# version 02
# baseline (speedup 1.0000x reference)
import numpy as np
import ml_dtypes
import concourse.bass as bass
import concourse.mybir as mybir
from concourse.bass_utils import run_bass_kernel_spmd

F32 = mybir.dt.float32
BF16 = mybir.dt.bfloat16
AF = mybir.ActivationFunctionType
ALU = mybir.AluOpType
AX = mybir.AxisListType.X
BF = ml_dtypes.bfloat16

NCORES = 8
B = 2
L = 8192
D = 1024
NTOK = B * L
NT = NTOK // NCORES
TT = NT // 128
BLK = 512
NBLK = NT // BLK


def _span(ap):
    try:
        lo = hi = int(ap.offset)
    except Exception:
        return None
    for step, count in ap.ap:
        try:
            step = int(step)
            count = int(count)
        except Exception:
            return None
        d = step * (count - 1)
        if d >= 0:
            hi += d
        else:
            lo += d
    return lo, hi + 1


class _Op:
    __slots__ = ("eng", "stream", "fn", "idx", "marked", "inc_no", "waits")

    def __init__(self, eng, stream, fn, idx):
        self.eng = eng
        self.stream = stream
        self.fn = fn
        self.idx = idx
        self.marked = False
        self.inc_no = 0
        self.waits = {}


class Prog:

    def __init__(self, name):
        self.nc = bass.Bass("TRN2", target_bir_lowering=False)
        self.name = name
        self._i = 0
        self.ops = []
        self._writes = {}
        self._reads = {}
        self._psum_names = set()
        self._flushed = False
        self._bankf = []
        self._bankb = []
        self._consts = {}

    def dram_in(self, name, shape, dtype=F32):
        return self.nc.dram_tensor(name, [int(s) for s in shape], dtype,
                                   kind="ExternalInput").ap()

    def dram_out(self, name, shape, dtype=F32):
        return self.nc.dram_tensor(name, [int(s) for s in shape], dtype,
                                   kind="ExternalOutput").ap()

    def sbuf(self, shape, dtype=F32, name=None):
        self._i += 1
        t = self.nc.alloc_sbuf_tensor(("s_" + name) if name else f"sb{self._i}",
                                      [int(s) for s in shape], dtype)
        return t.ap()

    def psum(self, shape, dtype=F32, name=None):
        self._i += 1
        nm = name or f"ps{self._i}"
        cm = self.nc.psum_tensor(nm, [int(s) for s in shape], dtype)
        t = cm.__enter__()
        ap = t.ap()
        self._psum_names.add(ap.tensor.name)
        return ap

    def bankf(self, i):
        while len(self._bankf) <= i:
            self._bankf.append(self.psum([128, 512], F32,
                                         f"bankf{len(self._bankf)}"))
        return self._bankf[i]

    def constf(self, val):
        if val not in self._consts:
            sb = self.sbuf([128, 1], F32, f"const{len(self._consts)}")
            self.memset(sb, float(val))
            self._consts[val] = sb
        return self._consts[val]

    def bankb(self, i):
        while len(self._bankb) <= i:
            self._bankb.append(self.psum([128, 1024], BF16,
                                         f"bankb{len(self._bankb)}"))
        return self._bankb[i]

    def _key_ranges(self, ap):
        name = ap.tensor.name
        if name in self._psum_names:
            return name, (0, 1 << 60)
        s = _span(ap)
        if s is None:
            s = (0, 1 << 59)
        es = mybir.dt.size(ap.dtype)
        return name, (s[0] * es, s[1] * es)

    def _dep_edges(self, op, aps, book):
        for ap in aps:
            name, (lo, hi) = self._key_ranges(ap)
            d = book.get(name)
            if not d:
                continue
            for (l2, h2), streams in d.items():
                if l2 < hi and lo < h2:
                    for stream, oidx in streams.items():
                        if stream == op.stream == "tensor":
                            continue
                        if oidx > op.waits.get(stream, -1):
                            op.waits[stream] = oidx

    def _record(self, book, aps, op):
        for ap in aps:
            name, rng = self._key_ranges(ap)
            book.setdefault(name, {}).setdefault(rng, {})[op.stream] = op.idx

    def op(self, eng, fn, reads=(), writes=(), dma=False):
        stream = eng + ("_dma" if dma else "")
        reads = [r for r in reads if r is not None]
        writes = [w for w in writes if w is not None]
        o = _Op(eng, stream, fn, len(self.ops))
        self._dep_edges(o, reads, self._writes)
        self._dep_edges(o, writes, self._writes)
        self._dep_edges(o, writes, self._reads)
        self._record(self._reads, reads, o)
        self._record(self._writes, writes, o)
        self.ops.append(o)
        return o

    def dma(self, out, in_, eng="sync"):
        h = getattr(self.nc, eng)
        return self.op(eng, lambda: h.dma_start(out=out, in_=in_),
                       reads=[in_], writes=[out], dma=True)

    def cp(self, out, in_, eng="vector"):
        h = getattr(self.nc, eng)
        if eng == "scalar":
            return self.op(eng, lambda: h.copy(out, in_),
                           reads=[in_], writes=[out])
        return self.op(eng, lambda: h.tensor_copy(out, in_),
                       reads=[in_], writes=[out])

    def memset(self, ap, val, eng="vector"):
        h = getattr(self.nc, eng)
        return self.op(eng, lambda: h.memset(ap, val), reads=[], writes=[ap])

    def act(self, out, in_, func, bias=0.0, scale=1.0, accum_out=None):
        nc = self.nc
        np_ = in_.shape[0]
        if isinstance(bias, bass.AP) and bias.shape[0] == 128 and np_ < 128:
            bias = bias[0:np_]
        if isinstance(scale, bass.AP) and scale.shape[0] == 128 and np_ < 128:
            scale = scale[0:np_]
        reads = [in_]
        if isinstance(bias, bass.AP):
            reads.append(bias)
        if isinstance(scale, bass.AP):
            reads.append(scale)
        writes = [out] + ([accum_out] if accum_out is not None else [])
        if accum_out is not None:
            assert in_.free_size() <= 512, "act accum_out broken for free>512"
        return self.op("scalar",
                       lambda: nc.scalar.activation(out, in_, func, bias=bias,
                                                    scale=scale,
                                                    accum_out=accum_out),
                       reads=reads, writes=writes)

    def smul(self, out, in_, mul):
        nc = self.nc
        reads = [in_] + ([mul] if isinstance(mul, bass.AP) else [])
        return self.op("scalar", lambda: nc.scalar.mul(out, in_, mul),
                       reads=reads, writes=[out])

    def ts(self, out, in0, s1, s2=None, op0=ALU.mult, op1=ALU.bypass,
           eng="vector", accum_out=None):
        h = getattr(self.nc, eng)
        reads = [in0] + [s for s in (s1, s2) if isinstance(s, bass.AP)]
        writes = [out] + ([accum_out] if accum_out is not None else [])
        return self.op(eng,
                       lambda: h.tensor_scalar(out, in0, s1, s2, op0=op0,
                                               op1=op1, accum_out=accum_out),
                       reads=reads, writes=writes)

    def tt(self, out, in0, in1, alu, eng="vector"):
        h = getattr(self.nc, eng)
        return self.op(eng, lambda: h.tensor_tensor(out, in0, in1, op=alu),
                       reads=[in0, in1], writes=[out])

    def stt(self, out, in0, scalar, in1, op0, op1, eng="vector"):
        h = getattr(self.nc, eng)
        reads = [in0, in1] + ([scalar] if isinstance(scalar, bass.AP) else [])
        return self.op(eng,
                       lambda: h.scalar_tensor_tensor(out, in0, scalar, in1,
                                                      op0=op0, op1=op1),
                       reads=reads, writes=[out])

    def red(self, out, in_, alu=ALU.add, eng="vector"):
        h = getattr(self.nc, eng)
        return self.op(eng,
                       lambda: h.tensor_reduce(out, in_, axis=AX, op=alu),
                       reads=[in_], writes=[out])

    def recip(self, out, in_):
        nc = self.nc
        return self.op("vector", lambda: nc.vector.reciprocal(out, in_),
                       reads=[in_], writes=[out])

    def sel(self, out, mask, on_true, on_false):
        nc = self.nc
        reads = [mask] + [x for x in (on_true, on_false)
                          if isinstance(x, bass.AP)]
        return self.op("vector",
                       lambda: nc.vector.select(out, mask, on_true, on_false),
                       reads=reads, writes=[out])

    def mm(self, ps, lhsT, rhs, start, stop):
        nc = self.nc
        return self.op("tensor",
                       lambda: nc.tensor.matmul(ps, lhsT, rhs, start=start,
                                                stop=stop),
                       reads=[lhsT, rhs], writes=[ps])

    def mm_acc(self, ps, pairs):
        n = len(pairs)
        for i, (lt, rh) in enumerate(pairs):
            self.mm(ps, lt, rh, i == 0, i == n - 1)

    def tp(self, ps_out, in_, ident):
        nc = self.nc
        k = in_.shape[0]
        b = int(in_.base_partition())
        idk = ident[b:b + k, b:b + k]
        return self.op("tensor",
                       lambda: nc.tensor.transpose(ps_out, in_, idk),
                       reads=[in_, idk], writes=[ps_out])

    def scan(self, out, data0, data1, init, op0=ALU.mult, op1=ALU.add):
        nc = self.nc
        reads = [data0, data1] + ([init] if isinstance(init, bass.AP) else [])
        return self.op("vector",
                       lambda: nc.vector.tensor_tensor_scan(out, data0, data1,
                                                            init, op0=op0,
                                                            op1=op1),
                       reads=reads, writes=[out])

    def iota(self, out, pattern, base=None, channel_multiplier=None):
        nc = self.nc
        return self.op("gpsimd",
                       lambda: nc.gpsimd.iota(
                           out, pattern, base=base,
                           channel_multiplier=channel_multiplier,
                           allow_small_or_imprecise_dtypes=True),
                       reads=[], writes=[out])

    def wload(self, hbm, K, N, name=None, dtype=BF16, eng="gpsimd"):
        if K <= 128:
            sb = self.sbuf([K, 1, N], dtype, name)
            self.dma(sb[:, 0, :], hbm, eng=eng)
            return sb
        nk = K // 128
        assert K % 128 == 0
        sb = self.sbuf([128, nk * N], dtype, name)
        v = sb.rearrange("p (k n) -> p k n", k=nk)
        self.dma(v, hbm.rearrange("(k p) n -> p k n", p=128), eng=eng)
        return v

    R_DMA = 8

    def flush(self):
        assert not self._flushed
        self._flushed = True
        nc = self.nc
        for o in self.ops:
            if o.stream.endswith("_dma"):
                o.marked = True
        for o in self.ops:
            for stream, idx in o.waits.items():
                self.ops[idx].marked = True
        totals = {}
        dma_k = {}
        dma_slot = {}
        for o in self.ops:
            if o.stream.endswith("_dma"):
                k = dma_k.get(o.stream, 0)
                dma_k[o.stream] = k + 1
                dma_slot[o.idx] = (k % self.R_DMA, k // self.R_DMA)
            elif o.marked:
                totals[o.stream] = totals.get(o.stream, 0) + 1
                o.inc_no = totals[o.stream]
        sems = {s: nc.alloc_semaphore(name=f"t_{s}") for s in totals}
        dsems = {}
        for s in dma_k:
            n = min(self.R_DMA, dma_k[s])
            dsems[s] = [nc.alloc_semaphore(name=f"t_{s}{i}") for i in range(n)]
        hw = {}
        for o in self.ops:
            eng = getattr(nc, o.eng)
            for stream in sorted(o.waits):
                src = self.ops[o.waits[stream]]
                if stream.endswith("_dma"):
                    slot, gen = dma_slot[src.idx]
                    sem = dsems[stream][slot]
                    val = 16 * (gen + 1)
                    key = (o.eng, stream, slot)
                else:
                    val = src.inc_no
                    if val == 0:
                        continue
                    sem = sems[stream]
                    key = (o.eng, stream)
                if hw.get(key, -1) >= val:
                    continue
                hw[key] = val
                eng.wait_ge(sem, val)
            if o.stream.endswith("_dma"):
                slot, gen = dma_slot[o.idx]
                if gen > 0:
                    key = (o.eng, o.stream, slot)
                    val = 16 * gen
                    if hw.get(key, -1) < val:
                        hw[key] = val
                        eng.wait_ge(dsems[o.stream][slot], val)
                instr = o.fn()
                instr.then_inc(dsems[o.stream][slot], 16)
            else:
                instr = o.fn()
                if o.marked:
                    instr.then_inc(sems[o.stream], 1)
        for s, k in dma_k.items():
            for slot in range(min(self.R_DMA, k)):
                uses = (k - slot - 1) // self.R_DMA + 1
                nc.sync.wait_ge(dsems[s][slot], 16 * uses)
        return nc


_NC_CACHE = {}


def _get_nc(name, builder):
    if name not in _NC_CACHE:
        p = builder()
        _NC_CACHE[name] = p.flush()
    return _NC_CACHE[name]


def _run(name, builder, in_maps, **kw):
    nc = _get_nc(name, builder)
    res = run_bass_kernel_spmd(nc, in_maps, core_ids=list(range(NCORES)), **kw)
    return res.results



def build_l0a():
    p = Prog("l0a")
    x = p.dram_in("x", (NT, D))
    wq_h = p.dram_in("wq", (D, D))
    wkv_h = p.dram_in("wkv", (D, 2 * D))
    med_h = p.dram_in("med", (64, D))
    wg_h = p.dram_in("wg", (D, 8))
    ident_h = p.dram_in("ident", (128, 128), BF16)

    o_xnfm = p.dram_out("xnfm", (8, 128, NT), BF16)
    o_qfm = p.dram_out("qfm", (8, 128, NT), BF16)
    o_S = p.dram_out("S", (8, NT))
    o_mx = p.dram_out("mx", (64, 16))
    o_se = p.dram_out("se", (64, 16))
    o_accv = p.dram_out("accv", (64, 16 * 64))

    xnfm = p.sbuf([128, 8 * NT], BF16, "xnfm")
    kfm = p.sbuf([128, 8 * NT], BF16, "kfm")
    vfm = p.sbuf([128, 8 * NT], BF16, "vfm")
    idsb = p.sbuf([128, 128], BF16, "idsb")
    p.dma(idsb, ident_h)
    wq = p.wload(wq_h, D, D, "wq")
    wkv = p.wload(wkv_h, D, 2 * D, "wkv")
    wg = p.wload(wg_h, D, 8, "wg")

    xt = [p.sbuf([128, D], F32, f"xt{i}") for i in range(2)]
    xc = [p.sbuf([128, D], F32, f"xc{i}") for i in range(2)]
    xnbf = [p.sbuf([128, D], BF16, f"xnbf{i}") for i in range(2)]
    xsq = p.sbuf([128, D], F32, "xsq")
    statsall = p.sbuf([128, TT * 8], F32, "stats")

    x3 = x.rearrange("(t p) d -> t p d", p=128)
    xnv = xnfm.rearrange("p (c t) -> p c t", c=8)
    for t in range(TT):
        bi = t % 2
        p.dma(xt[bi], x3[t])
        stats = statsall[:, t * 8:(t + 1) * 8]
        nm = stats[:, 0:1]
        vr = stats[:, 1:2]
        sd = stats[:, 2:3]
        rstd = stats[:, 3:4]
        vr2 = stats[:, 4:5]
        p.red(nm, xt[bi], ALU.add)
        p.ts(nm, nm, -1.0 / D, None, op0=ALU.mult)
        p.ts(xc[bi], xt[bi], nm, None, op0=ALU.add)
        p.act(xsq[:, 0:512], xc[bi][:, 0:512], AF.Square, accum_out=vr)
        p.act(xsq[:, 512:1024], xc[bi][:, 512:1024], AF.Square, accum_out=vr2)
        p.tt(vr, vr, vr2, ALU.add)
        p.act(sd, vr, AF.Sqrt, bias=p.constf(1e-5), scale=1.0 / D)
        p.recip(rstd, sd)
        p.ts(xnbf[bi], xc[bi], rstd, None, op0=ALU.mult)
        for ch in range(8):
            pb = p.bankb(ch % 2)[:, 0:128]
            p.tp(pb, xnbf[bi][:, ch * 128:(ch + 1) * 128], idsb)
            p.cp(xnv[:, ch, t * 128:(t + 1) * 128], pb, eng="vector")
    p.dma(o_xnfm.rearrange("c p t -> p c t"), xnv)

    qbf = [p.sbuf([128, BLK], BF16, f"qbf{i}") for i in range(2)]
    kvv = (kfm.rearrange("p (c t) -> p c t", c=8),
           vfm.rearrange("p (c t) -> p c t", c=8))
    pi = 0
    for nb in range(NBLK):
        tok = slice(nb * BLK, (nb + 1) * BLK)
        for po in range(8):
            ps = p.bankf(pi % 4)
            pi += 1
            p.mm_acc(ps, [(wq[:, ki, po * 128:(po + 1) * 128], xnv[:, ki, tok])
                          for ki in range(8)])
            qb = qbf[po % 2]
            p.cp(qb, ps, eng="scalar")
            p.dma(o_qfm[po][:, tok], qb)
        for po in range(16):
            ps = p.bankf(pi % 4)
            pi += 1
            p.mm_acc(ps, [(wkv[:, ki, po * 128:(po + 1) * 128], xnv[:, ki, tok])
                          for ki in range(8)])
            p.cp(kvv[po // 8][:, po % 8, tok], ps, eng="vector")
    ssb = p.sbuf([8, NT], F32, "ssb")
    for nb in range(NBLK):
        tok = slice(nb * BLK, (nb + 1) * BLK)
        ps8 = p.bankf(nb % 4)[0:8, :]
        p.mm_acc(ps8, [(wg[:, ki, :], xnv[:, ki, tok]) for ki in range(8)])
        p.act(ssb[:, tok], ps8, AF.Sigmoid)
    p.dma(o_S, ssb)

    med_sb = p.sbuf([64, D], F32, "medsb")
    medbf = p.sbuf([64, D], BF16, "medbf")
    medfm = p.sbuf([128, 8 * 64], BF16, "medfm")

    def medslot(h):
        bb = (h % 2) * 64
        return medfm[bb:bb + 64, (h // 2) * 64:(h // 2) * 64 + 64]

    p.dma(med_sb, med_h)
    p.smul(medbf, med_sb, 0.125)
    for h in range(16):
        ptm = p.bankb(h % 2)[0:64, 0:64]
        p.tp(ptm, medbf[:, h * 64:(h + 1) * 64], idsb)
        p.cp(medslot(h), ptm, eng="vector")

    scs = p.sbuf([64, NT], F32, "scs")
    esc = p.sbuf([64, NT], BF16, "esc")
    mxa = p.sbuf([64, 16], F32, "mxa")
    sea = p.sbuf([64, 16], F32, "sea")
    accva = p.sbuf([64, 16 * 64], F32, "accva")
    nmx = p.sbuf([64, 1], F32, "nmx")
    se4 = p.sbuf([64, 4], F32, "se4")
    esct = p.sbuf([128, 2 * 64], BF16, "esct")
    vtm = p.sbuf([128, 2 * 64], BF16, "vtm")

    for h in range(16):
        krows = kvv[0][(h % 2) * 64:(h % 2) * 64 + 64, h // 2, :]
        vrows = kvv[1][(h % 2) * 64:(h % 2) * 64 + 64, h // 2, :]
        for nb in range(NBLK):
            ps = p.bankf(nb % 2)[0:64, :]
            p.mm(ps, medslot(h),
                 krows[:, nb * BLK:(nb + 1) * BLK], True, True)
            p.cp(scs[:, nb * BLK:(nb + 1) * BLK], ps, eng="vector")
        p.red(mxa[:, h:h + 1], scs, ALU.max)
        p.ts(nmx, mxa[:, h:h + 1], -1.0, None, op0=ALU.mult)
        for q4 in range(4):
            p.act(esc[:, q4 * 512:(q4 + 1) * 512], scs[:, q4 * 512:(q4 + 1) * 512],
                  AF.Exp, bias=nmx, accum_out=se4[:, q4:q4 + 1])
        p.red(sea[:, h:h + 1], se4, ALU.add)
        pacc = p.bankf(2)[0:64, 0:64]
        for lc in range(TT):
            sl = slice(lc * 128, (lc + 1) * 128)
            pe = p.bankb(lc % 2)[:, 0:64]
            pv = p.bankb(2 + lc % 2)[:, 0:64]
            p.tp(pe, esc[:, sl], idsb)
            p.tp(pv, vrows[:, sl], idsb)
            eb = esct[:, (lc % 2) * 64:(lc % 2) * 64 + 64]
            vb = vtm[:, (lc % 2) * 64:(lc % 2) * 64 + 64]
            p.cp(eb, pe, eng="vector")
            p.cp(vb, pv, eng="scalar")
            p.mm(pacc, eb, vb, lc == 0, lc == TT - 1)
        p.cp(accva[:, h * 64:(h + 1) * 64], pacc, eng="vector")
    p.dma(o_mx, mxa)
    p.dma(o_se, sea)
    p.dma(o_accv, accva)
    return p


def build_l0b():
    p = Prog("l0b")
    qfm_h = p.dram_in("qfm", (8, 128, NT), BF16)
    med_h = p.dram_in("med", (64, D))
    wo_h = p.dram_in("wo", (D, D))
    bo_h = p.dram_in("bo", (1, D))
    mx_h = p.dram_in("mx", (16, 64, 4))
    se_h = p.dram_in("se", (16, 64, 4))
    accv_h = p.dram_in("accv", (16, 4, 64, 64))
    ident_h = p.dram_in("ident", (128, 128), BF16)

    o_s = p.dram_out("s", (NT, D), BF16)

    idsb = p.sbuf([128, 128], BF16, "idsb")
    p.dma(idsb, ident_h)
    qfm = p.sbuf([128, 8 * NT], BF16, "qfm")
    p.dma(qfm.rearrange("p (c t) -> p c t", c=8),
          qfm_h.rearrange("c p t -> p c t"))
    wo = p.wload(wo_h, D, D, "wo")
    bo = p.sbuf([1, D], BF16, "bo")
    p.dma(bo, bo_h, eng="gpsimd")
    ones1 = p.sbuf([1, 128], BF16, "ones1")
    p.memset(ones1, 1.0)

    med_sb = p.sbuf([64, D], F32, "medsb")
    medbf = p.sbuf([64, D], BF16, "medbf")
    medfm = p.sbuf([128, 8 * 64], BF16, "medfm")

    def medslot(h):
        bb = (h % 2) * 64
        return medfm[bb:bb + 64, (h // 2) * 64:(h // 2) * 64 + 64]

    p.dma(med_sb, med_h)
    p.smul(medbf, med_sb, 0.125)
    for h in range(16):
        ptm = p.bankb(h % 2)[0:64, 0:64]
        p.tp(ptm, medbf[:, h * 64:(h + 1) * 64], idsb)
        p.cp(medslot(h), ptm, eng="vector")

    mxs = p.sbuf([64, 4], F32, "mxs")
    ses = p.sbuf([64, 4], F32, "ses")
    accs = p.sbuf([64, 4 * 64], F32, "accs")
    M = p.sbuf([64, 1], F32, "M")
    nM = p.sbuf([64, 1], F32, "nM")
    w4 = p.sbuf([64, 4], F32, "w4")
    wse = p.sbuf([64, 4], F32, "wse")
    den = p.sbuf([64, 1], F32, "den")
    rden = p.sbuf([64, 1], F32, "rden")
    acc0 = p.sbuf([64, 64], F32, "acc0")
    acc1 = p.sbuf([64, 64], F32, "acc1")
    mkv = p.sbuf([128, 8 * 64], BF16, "mkv")

    def mkvslot(h):
        bb = (h % 2) * 64
        return mkv[bb:bb + 64, (h // 2) * 64:(h // 2) * 64 + 64]

    for h in range(16):
        p.dma(mxs, mx_h[h])
        p.dma(ses, se_h[h])
        p.dma(accs.rearrange("p (c d) -> p c d", c=4),
              accv_h[h].rearrange("c m d -> m c d"))
        p.red(M, mxs, ALU.max)
        p.ts(nM, M, -1.0, None, op0=ALU.mult)
        p.act(w4, mxs, AF.Exp, bias=nM)
        p.tt(wse, w4, ses, ALU.mult)
        p.red(den, wse, ALU.add)
        p.recip(rden, den)
        a3 = accs.rearrange("p (c d) -> p c d", c=4)
        p.ts(acc0, a3[:, 0, :], w4[:, 0:1], None, op0=ALU.mult)
        p.stt(acc1, a3[:, 1, :], w4[:, 1:2], acc0, op0=ALU.mult, op1=ALU.add)
        p.stt(acc0, a3[:, 2, :], w4[:, 2:3], acc1, op0=ALU.mult, op1=ALU.add)
        p.stt(acc1, a3[:, 3, :], w4[:, 3:4], acc0, op0=ALU.mult, op1=ALU.add)
        p.ts(mkvslot(h), acc1, rden, None, op0=ALU.mult)

    probs = p.sbuf([128, 2 * 64], BF16, "probs")
    probT = p.sbuf([128, 2 * 128], BF16, "probT")
    attn = p.sbuf([128, 8 * NT], BF16, "attn")
    rowmax = p.sbuf([128, 1], F32, "rowmax")
    nrowmax = p.sbuf([128, 1], F32, "nrowmax")
    rowsum = p.sbuf([128, 1], F32, "rowsum")
    rrsum = p.sbuf([128, 1], F32, "rrsum")
    qv = qfm.rearrange("p (c t) -> p c t", c=8)
    attnv = attn.rearrange("p (c t) -> p c t", c=8)

    for t in range(TT):
        tok = slice(t * 128, (t + 1) * 128)
        for h in range(16):
            ps = p.bankf(h % 2)[:, 0:64]
            qrows = qv[(h % 2) * 64:(h % 2) * 64 + 64, h // 2, tok]
            p.mm(ps, qrows, medslot(h), True, True)
            p.red(rowmax, ps, ALU.max)
            p.ts(nrowmax, rowmax, -1.0, None, op0=ALU.mult)
            prb = probs[:, (h % 2) * 64:(h % 2) * 64 + 64]
            p.act(prb, ps, AF.Exp, bias=nrowmax, accum_out=rowsum)
            p.recip(rrsum, rowsum)
            p.ts(prb, prb, rrsum, None, op0=ALU.mult)
            pq = p.bankb(h % 2)[0:64, 0:128]
            p.tp(pq, prb, idsb)
            bb = (h % 2) * 64
            pT = probT[bb:bb + 64, (h % 2) * 128:(h % 2) * 128 + 128]
            p.cp(pT, pq, eng="vector")
            po = p.bankf(2 + h % 2)[0:64, 0:128]
            p.mm(po, mkvslot(h), pT, True, True)
            p.cp(attnv[(h % 2) * 64:(h % 2) * 64 + 64, h // 2, tok], po,
                 eng="scalar")

    sbf = [p.sbuf([128, D], BF16, f"sbf{i}") for i in range(2)]
    s3 = o_s.rearrange("(t p) d -> t p d", p=128)
    for t in range(TT):
        tok = slice(t * 128, (t + 1) * 128)
        sb = sbf[t % 2]
        for fo in range(2):
            ps = p.bankf((t * 2 + fo) % 4)
            fcols = slice(fo * BLK, (fo + 1) * BLK)
            for ki in range(8):
                p.mm(ps, attnv[:, ki, tok], wo[:, ki, fcols], ki == 0, False)
            p.mm(ps, ones1, bo[:, fcols], False, True)
            p.cp(sb[:, fcols], ps, eng="vector")
        p.dma(s3[t], sb)
    return p


def build_l0c():
    p = Prog("l0c")
    xnfm_h = p.dram_in("xnfm", (8, 128, NT), BF16)
    S_h = p.dram_in("S", (8, 128, 128))
    colsel_h = p.dram_in("colsel", (128, 16))
    s_h = p.dram_in("s", (NT, D), BF16)
    x_h = p.dram_in("x", (NT, D))
    w1_h = p.dram_in("w1", (8, D, 4096))
    b1_h = p.dram_in("b1", (8, 128, 32))
    w2_h = p.dram_in("w2", (8, 4096, D))
    b2_h = p.dram_in("b2", (8, D))
    slt_h = p.dram_in("slt", (128, 128), BF16)

    o_x = p.dram_out("xout", (NT, D))

    xnfm = p.sbuf([128, 8 * NT], BF16, "xnfm")
    p.dma(xnfm.rearrange("p (c t) -> p c t", c=8),
          xnfm_h.rearrange("c p t -> p c t"))
    xnv = xnfm.rearrange("p (c t) -> p c t", c=8)
    ssb = p.sbuf([128, 8 * 128], F32, "ssb")
    p.dma(ssb.rearrange("p (e c) -> p e c", e=8),
          S_h.rearrange("e p c -> p e c"))
    s3 = ssb.rearrange("p (e c) -> p e c", e=8)
    colsel = p.sbuf([128, 16], F32, "colsel")
    p.dma(colsel, colsel_h)
    colselb = p.sbuf([128, 16], BF16, "colselb")
    p.cp(colselb, colsel)
    slt = p.sbuf([128, 128], BF16, "slt")
    p.dma(slt, slt_h)
    onesf = p.sbuf([1, 128], F32, "onesf")
    p.memset(onesf, 1.0)
    onesb = p.sbuf([128, 1], BF16, "onesb")
    p.memset(onesb, 1.0)

    lob = p.sbuf([128, 8], F32, "lob")
    hib = p.sbuf([128, 8], F32, "hib")
    tb = p.sbuf([128, 8], F32, "tb")
    selb = p.sbuf([128, 8], mybir.dt.int32, "selb")
    cntb = p.sbuf([128, 8], F32, "cntb")
    ge = p.sbuf([128, 8 * 128], F32, "ge")
    cnt2 = p.sbuf([128, 8], F32, "cnt2")
    cnt2b = p.sbuf([128, 8], BF16, "cnt2b")
    p.memset(lob, 0.0)
    p.memset(hib, 1.0001)
    KSEL = 4096.0
    ge3 = ge.rearrange("p (e c) -> p e c", e=8)

    srcb2 = p.sbuf([1, 8], BF16, "srcb2")
    srcrem = p.sbuf([1, 8], F32, "srcrem")
    srcb3 = p.sbuf([1, 8], BF16, "srcb3")
    onesc = p.sbuf([1, 128], BF16, "onesc")
    p.memset(onesc, 1.0)

    def bcast_row(src_row_f32):
        pc = p.bankf(5)[:, 0:8]
        p.cp(srcb2, src_row_f32)
        p.tt(srcrem, src_row_f32, srcb2, ALU.subtract)
        p.cp(srcb3, srcrem)
        p.mm(pc, onesc, srcb2, True, False)
        p.mm(pc, onesc, srcb3, False, True)
        return pc

    def tbb(t8):
        return bass.AP(t8.tensor, t8.offset, [t8.ap[0], [1, 8], [0, 128]])

    pcnt = p.bankf(4)[0:1, 0:8]
    cntr = p.sbuf([1, 8], F32, "cntr")
    for it in range(44):
        p.tt(tb, lob, hib, ALU.add)
        p.ts(tb, tb, 0.5, None, op0=ALU.mult)
        p.tt(ge3, s3, tbb(tb), ALU.is_ge)
        p.red(cnt2, ge3, ALU.add)
        p.cp(cnt2b, cnt2)
        p.mm(pcnt, onesb, cnt2b, True, True)
        p.cp(cntr, pcnt)
        pcb = bcast_row(cntr)
        p.cp(cntb, pcb, eng="scalar")
        p.ts(selb, cntb, KSEL, None, op0=ALU.is_ge)
        p.sel(lob, selb, tb, lob)
        p.sel(hib, selb, hib, tb)

    p.tt(ge3, s3, tbb(lob), ALU.is_gt)
    p.red(cnt2, ge3, ALU.add)
    p.cp(cnt2b, cnt2)
    p.mm(pcnt, onesb, cnt2b, True, True)
    p.cp(cntr, pcnt)
    c1b = bcast_row(cntr)
    remb8 = p.sbuf([128, 8], F32, "remb8")
    p.ts(remb8, c1b, -1.0, KSEL, op0=ALU.mult, op1=ALU.add)

    eq = p.sbuf([128, 128], F32, "eq")
    pref = p.sbuf([128, 128], F32, "pref")
    zrow = p.sbuf([128, 128], F32, "zrow")
    p.memset(zrow, 0.0)
    rowsum = p.sbuf([128, 1], F32, "rowsum")
    rowsumb = p.sbuf([128, 1], BF16, "rowsumb")
    seleq = p.sbuf([128, 128], F32, "seleq")
    gmask = p.sbuf([128, 8 * 128], F32, "gmask")
    gm3 = gmask.rearrange("p (e c) -> p e c", e=8)
    gloc = p.sbuf([128, 8 * 16], F32, "gloc")
    glv = gloc.rearrange("p (e t) -> p e t", e=8)
    gmaskb8 = p.sbuf([128, 8 * 128], BF16, "gmaskb8")
    gmb3 = gmaskb8.rearrange("p (e c) -> p e c", e=8)
    glrows = p.sbuf([1, TT * 128], BF16, "glrows")
    grv = glrows.rearrange("o (t j) -> o t j", t=TT)

    for e in range(8):
        p.tt(eq, s3[:, e, :], tbb(lob)[:, e, :], ALU.is_equal)
        p.scan(pref, zrow, eq, 0.0, op0=ALU.add, op1=ALU.add)
        p.tt(pref, pref, eq, ALU.subtract)
        p.red(rowsum, eq, ALU.add)
        p.cp(rowsumb, rowsum)
        poff = p.bankf(4)[:, 8:9]
        p.mm(poff, slt, rowsumb, True, True)
        poffs = p.sbuf([128, 1], F32, f"poffs{e}")
        p.cp(poffs, poff)
        p.ts(pref, pref, poffs, None, op0=ALU.add)
        p.ts(seleq, pref, remb8[:, e:e + 1], None, op0=ALU.is_lt)
        p.tt(seleq, seleq, eq, ALU.mult)
        p.tt(gm3[:, e, :], ge3[:, e, :], seleq, ALU.add)
        p.tt(gm3[:, e, :], gm3[:, e, :], s3[:, e, :], ALU.mult)
        pgl = p.bankf(5)[:, 16:32]
        p.mm(pgl, gm3[:, e, :], colsel, True, True)
        p.cp(glv[:, e, :], pgl)
        p.cp(gmb3[:, e, :], gm3[:, e, :])


    acc = p.sbuf([128, TT * D], F32, "acc")
    w1sb = p.sbuf([128, 8 * 2048], BF16, "w1sb")
    w2sb = p.sbuf([128, 16 * D], BF16, "w2sb")
    b1sb = p.sbuf([128, 32], F32, "b1sb")
    b2sb = p.sbuf([1, D], BF16, "b2sb")
    hidb = p.sbuf([128, 16 * BLK], BF16, "hidb")
    accv = acc.rearrange("p (t d) -> p t d", t=TT)
    w1v = w1sb.rearrange("p (k n) -> p k n", k=8)
    w2v = w2sb.rearrange("p (k n) -> p k n", k=16)
    hv = hidb.rearrange("p (k n) -> p k n", k=16)

    for e in range(8):
        p.dma(b1sb, b1_h[e], eng="gpsimd")
        p.dma(b2sb, b2_h[e:e + 1, :], eng="gpsimd")
        for t in range(TT):
            pgr = p.bankf(6)[0:1, 0:128]
            p.mm(pgr, colselb[:, t:t + 1], gmb3[:, e, :], True, True)
            p.cp(grv[:, t, :], pgr)
        for h2 in range(2):
            hcols = slice(h2 * 2048, (h2 + 1) * 2048)
            p.dma(w1v, w1_h[e, :, hcols].rearrange("(k p) n -> p k n", p=128),
                  eng="gpsimd")
            p.dma(w2v, w2_h[e, hcols, :].rearrange("(k p) n -> p k n", p=128),
                  eng="gpsimd")
            for nb in range(NBLK):
                tok = slice(nb * BLK, (nb + 1) * BLK)
                for po in range(16):
                    ps = p.bankf(po % 2)
                    p.mm_acc(ps, [(w1v[:, ki, po * 128:(po + 1) * 128],
                                   xnv[:, ki, tok]) for ki in range(8)])
                    p.act(hv[:, po, :], ps, AF.Gelu,
                          bias=b1sb[:, h2 * 16 + po: h2 * 16 + po + 1])
                for j in range(4):
                    t_glob = nb * 4 + j
                    for fo in range(2):
                        ps = p.bankf(2 + (j * 2 + fo) % 2)
                        fcols = slice(fo * BLK, (fo + 1) * BLK)
                        pairs = [(hv[:, ki, j * 128:(j + 1) * 128],
                                  w2v[:, ki, fcols]) for ki in range(16)]
                        if h2 == 0:
                            pairs.append((grv[:, t_glob, :], b2sb[:, fcols]))
                        p.mm_acc(ps, pairs)
                        gcol = glv[:, e, t_glob:t_glob + 1]
                        dst = accv[:, t_glob, fcols]
                        if e == 0 and h2 == 0:
                            p.ts(dst, ps, gcol, None, op0=ALU.mult)
                        else:
                            p.stt(dst, ps, gcol, dst, op0=ALU.mult, op1=ALU.add)

    xtl = p.sbuf([128, D], F32, "xtl0")
    stl = p.sbuf([128, D], BF16, "stl0")
    x3 = x_h.rearrange("(t p) d -> t p d", p=128)
    sS = s_h.rearrange("(t p) d -> t p d", p=128)
    o3 = o_x.rearrange("(t p) d -> t p d", p=128)
    for t in range(TT):
        at = accv[:, t, :]
        p.dma(xtl, x3[t])
        p.dma(stl, sS[t], eng="gpsimd")
        p.act(at, at, AF.Silu)
        p.tt(at, at, stl, ALU.mult)
        p.tt(at, at, xtl, ALU.add)
        p.dma(o3[t], at)
    return p




DI = 2048
DSL = DI // NCORES


def build_l1b1():
    p = Prog("l1b1")
    x = p.dram_in("x", (NT, D))
    win_h = p.dram_in("win", (D, 2 * DI))
    ident_h = p.dram_in("ident", (128, 128), BF16)
    o_xnfm = p.dram_out("xnfm", (8, 128, NT), BF16)
    o_xin = p.dram_out("xin", (16, 128, NT), BF16)
    o_z = p.dram_out("z", (16, 128, NT), BF16)

    xnfm = p.sbuf([128, 8 * NT], BF16, "xnfm")
    idsb = p.sbuf([128, 128], BF16, "idsb")
    p.dma(idsb, ident_h)
    win = p.wload(win_h, D, 2 * DI, "win")

    xt = [p.sbuf([128, D], F32, f"xt{i}") for i in range(2)]
    xc = [p.sbuf([128, D], F32, f"xc{i}") for i in range(2)]
    xnbf = [p.sbuf([128, D], BF16, f"xnbf{i}") for i in range(2)]
    xsq = p.sbuf([128, D], F32, "xsq")
    statsall = p.sbuf([128, TT * 8], F32, "stats")
    x3 = x.rearrange("(t p) d -> t p d", p=128)
    xnv = xnfm.rearrange("p (c t) -> p c t", c=8)
    for t in range(TT):
        bi = t % 2
        p.dma(xt[bi], x3[t])
        st = statsall[:, t * 8:(t + 1) * 8]
        nm, vr, sd, rstd, vr2 = (st[:, i:i + 1] for i in range(5))
        p.red(nm, xt[bi], ALU.add)
        p.ts(nm, nm, -1.0 / D, None, op0=ALU.mult)
        p.ts(xc[bi], xt[bi], nm, None, op0=ALU.add)
        p.act(xsq[:, 0:512], xc[bi][:, 0:512], AF.Square, accum_out=vr)
        p.act(xsq[:, 512:1024], xc[bi][:, 512:1024], AF.Square, accum_out=vr2)
        p.tt(vr, vr, vr2, ALU.add)
        p.act(sd, vr, AF.Sqrt, bias=p.constf(1e-5), scale=1.0 / D)
        p.recip(rstd, sd)
        p.ts(xnbf[bi], xc[bi], rstd, None, op0=ALU.mult)
        for ch in range(8):
            pb = p.bankb(ch % 2)[:, 0:128]
            p.tp(pb, xnbf[bi][:, ch * 128:(ch + 1) * 128], idsb)
            p.cp(xnv[:, ch, t * 128:(t + 1) * 128], pb, eng="vector")
    p.dma(o_xnfm.rearrange("c p t -> p c t"), xnv)

    ob = [p.sbuf([128, BLK], BF16, f"ob{i}") for i in range(4)]
    for nb in range(NBLK):
        tok = slice(nb * BLK, (nb + 1) * BLK)
        for po in range(32):
            ps = p.bankf(po % 4)
            p.mm_acc(ps, [(win[:, ki, po * 128:(po + 1) * 128], xnv[:, ki, tok])
                          for ki in range(8)])
            o = ob[po % 4]
            p.cp(o, ps, eng="scalar" if po % 2 else "vector")
            dst = o_xin if po < 16 else o_z
            p.dma(dst[po % 16][:, tok], o)
    return p


def build_l1b2():
    p = Prog("l1b2")
    xin_h = p.dram_in("xin", (2, 128, NTOK), BF16)
    convw_h = p.dram_in("convw", (128, 2, 4))
    convb_h = p.dram_in("convb", (128, 2))
    wxp_h = p.dram_in("wxp", (DSL, 96))
    o_u = p.dram_out("u", (2, 128, NTOK), BF16)
    o_xdp = p.dram_out("xdp", (96, NTOK))

    xin = p.sbuf([128, 2 * NTOK], BF16, "xin")
    xv = xin.rearrange("p (c t) -> p c t", c=2)
    p.dma(xv, xin_h.rearrange("c p t -> p c t"))
    u = p.sbuf([128, 2 * NTOK], BF16, "u")
    uv = u.rearrange("p (c t) -> p c t", c=2)
    cw = p.sbuf([128, 2 * 4], F32, "cw")
    p.dma(cw, convw_h.rearrange("p c k -> p (c k)"))
    cb = p.sbuf([128, 2], F32, "cb")
    p.dma(cb, convb_h)
    wxp = p.wload(wxp_h, DSL, 96, "wxp")

    acc = p.sbuf([128, L], F32, "acc")
    for pc in range(2):
        for b in range(B):
            xb = xv[:, pc, b * L:(b + 1) * L]
            p.ts(acc, xb, cw[:, pc * 4 + 3: pc * 4 + 4], None, op0=ALU.mult)
            for j in (2, 1, 0):
                sh = 3 - j
                p.stt(acc[:, sh:L], xb[:, 0:L - sh],
                      cw[:, pc * 4 + j: pc * 4 + j + 1], acc[:, sh:L],
                      op0=ALU.mult, op1=ALU.add)
            p.act(uv[:, pc, b * L:(b + 1) * L], acc, AF.Silu,
                  bias=cb[:, pc:pc + 1])
    p.dma(o_u.rearrange("c p t -> p c t"), uv)

    xdp = [p.sbuf([96, BLK], F32, f"xdp{i}") for i in range(2)]
    for nb in range(NTOK // BLK):
        tok = slice(nb * BLK, (nb + 1) * BLK)
        ps = p.bankf(nb % 2)[0:96, :]
        p.mm_acc(ps, [(wxp[:, ki, :], uv[:, ki, tok]) for ki in range(2)])
        xb = xdp[nb % 2]
        p.cp(xb, ps, eng="vector")
        p.dma(o_xdp[:, tok], xb)
    return p


def build_l1b3():
    p = Prog("l1b3")
    u_h = p.dram_in("u", (2, 128, NTOK), BF16)
    xdp_h = p.dram_in("xdp", (8, 96, NTOK))
    wdt_h = p.dram_in("wdt", (64, DSL))
    bdt_h = p.dram_in("bdt", (128, 2))
    aarr_h = p.dram_in("aarr", (128, 32))
    darr_h = p.dram_in("darr", (128, 2))
    sels_h = p.dram_in("sels", (16, 128, 128), BF16)
    nsel_h = p.dram_in("nsel", (128, 8), BF16)
    o_y = p.dram_out("y", (DSL, NTOK), BF16)
    o_du = p.dram_out("du", (2, 128, NTOK), BF16)

    u = p.sbuf([128, 2 * NTOK], BF16, "u")
    uv = u.rearrange("p (c t) -> p c t", c=2)
    p.dma(uv, u_h.rearrange("c p t -> p c t"))
    wdt = p.wload(wdt_h, 64, DSL, "wdt")
    bdt = p.sbuf([128, 2], F32, "bdt")
    p.dma(bdt, bdt_h)
    aarr = p.sbuf([128, 32], F32, "aarr")
    p.dma(aarr, aarr_h)
    A = p.sbuf([128, 32], F32, "A")
    p.act(A, aarr, AF.Exp)
    p.ts(A, A, -1.0, None, op0=ALU.mult)
    darr = p.sbuf([128, 2], F32, "darr")
    p.dma(darr, darr_h)
    sels = p.sbuf([128, 16 * 128], BF16, "sels")
    p.dma(sels.rearrange("p (g q) -> p g q", g=16),
          sels_h.rearrange("g p q -> p g q"))
    sv = sels.rearrange("p (g q) -> p g q", g=16)
    nsel = p.sbuf([128, 8], BF16, "nsel")
    p.dma(nsel, nsel_h)

    dub = p.sbuf([128, NTOK], BF16, "dub")
    for pc in range(2):
        p.ts(dub, uv[:, pc, :], darr[:, pc:pc + 1], None, op0=ALU.mult)
        p.dma(o_du[pc].rearrange("p t -> p t"), dub)

    CH = 2048
    NCH = L // CH
    xpart = p.sbuf([96, CH], F32, "xpart")
    xdbl = [p.sbuf([96, CH], F32, f"xdbl{i}") for i in range(2)]
    dtch = p.sbuf([128, 2 * CH], BF16, "dtch")
    dtv = dtch.rearrange("p (c t) -> p c t", c=2)
    dtlowb = p.sbuf([64, CH], BF16, "dtlowb")
    sp1 = p.sbuf([128, BLK], F32, "sp1")
    sp2 = p.sbuf([128, BLK], F32, "sp2")
    btile = p.sbuf([128, CH], BF16, "btile")
    ctile = p.sbuf([128, CH], BF16, "ctile")
    bcf = p.sbuf([32, CH], F32, "bcf")
    bcb = p.sbuf([32, CH], BF16, "bcb")
    dA = p.sbuf([128, BLK], F32, "dA")
    Xs = p.sbuf([128, BLK], F32, "Xs")
    hs = p.sbuf([128, BLK], F32, "hs")
    hc = p.sbuf([128, BLK], BF16, "hc")
    dtu = p.sbuf([128, 2 * CH], BF16, "dtu")
    dtuv = dtu.rearrange("p (c t) -> p c t", c=2)
    carry = p.sbuf([128, 64], F32, "carry")
    p.memset(carry, 0.0)
    ystage = p.sbuf([8, CH], F32, "ystage")

    NB = CH // BLK
    for b in range(B):
        for chn in range(NCH):
            cols = slice(b * L + chn * CH, b * L + (chn + 1) * CH)
            p.dma(xdbl[0], xdp_h[0, :, cols])
            for cc in range(1, 8):
                src, dst = xdbl[(cc + 1) % 2], xdbl[cc % 2]
                p.dma(xpart, xdp_h[cc, :, cols])
                p.tt(dst, src, xpart, ALU.add)
            xd = xdbl[7 % 2]
            p.cp(dtlowb, xd[0:64, :])
            for pc in range(2):
                for nb in range(NB):
                    ps = p.bankf(nb % 2)
                    p.mm(ps, wdt[0:64, 0, pc * 128:(pc + 1) * 128],
                         dtlowb[:, nb * BLK:(nb + 1) * BLK], True, True)
                    p.act(sp1, ps, AF.Abs, bias=bdt[:, pc:pc + 1])
                    p.act(sp1, sp1, AF.Exp, scale=-1.0)
                    p.act(sp1, sp1, AF.Ln, bias=p.constf(1.0))
                    p.act(sp2, ps, AF.Relu, bias=bdt[:, pc:pc + 1])
                    p.tt(dtv[:, pc, nb * BLK:(nb + 1) * BLK], sp1, sp2,
                         ALU.add)
                p.tt(dtuv[:, pc, :], dtv[:, pc, :], uv[:, pc, cols], ALU.mult)
            p.cp(bcf, xd[64:96, :])
            p.cp(bcb, bcf)
            for ds8 in range(8):
                p.dma(btile[ds8 * 16:(ds8 + 1) * 16, :], bcb[0:16, :],
                      eng="gpsimd")
                p.dma(ctile[ds8 * 16:(ds8 + 1) * 16, :], bcb[16:32, :],
                      eng="gpsimd")
            for g in range(32):
                pc = g // 16
                gi = g % 16
                ccol = b * 32 + g
                for nb in range(NB):
                    bcols = slice(nb * BLK, (nb + 1) * BLK)
                    pdt = p.bankf(2)
                    p.mm(pdt, sv[:, gi, :], dtv[:, pc, bcols], True, True)
                    p.act(dA, pdt, AF.Exp, scale=A[:, g:g + 1])
                    pdu = p.bankf(3)
                    p.mm(pdu, sv[:, gi, :], dtuv[:, pc, bcols], True, True)
                    p.tt(Xs, pdu, btile[:, bcols], ALU.mult)
                    p.scan(hs, dA, Xs, carry[:, ccol:ccol + 1])
                    p.cp(carry[:, ccol:ccol + 1], hs[:, BLK - 1:BLK])
                    p.tt(hc, hs, ctile[:, bcols], ALU.mult)
                    py = p.bankf(4)[0:8, :]
                    p.mm(py, nsel, hc, True, True)
                    p.cp(ystage[:, bcols], py, eng="scalar")
                rows = slice(pc * 128 + 8 * gi, pc * 128 + 8 * gi + 8)
                p.dma(o_y[rows, cols], ystage, eng="gpsimd")
    return p


def build_l1d1():
    p = Prog("l1d1")
    y_h = p.dram_in("y", (16, 128, NT), BF16)
    du_h = p.dram_in("du", (16, 128, NT), BF16)
    z_h = p.dram_in("z", (16, 128, NT), BF16)
    wout_h = p.dram_in("wout", (DI, D))
    x_h = p.dram_in("x", (NT, D))
    gam_h = p.dram_in("gam", (1, D))
    bet_h = p.dram_in("bet", (1, D))
    o_s = p.dram_out("s", (NT, D), BF16)

    wout = p.wload(wout_h, DI, D, "wout")
    yg = p.sbuf([128, 16 * NT], BF16, "yg")
    ygv = yg.rearrange("p (c t) -> p c t", c=16)
    ych = p.sbuf([128, NT], BF16, "ych")
    duch = p.sbuf([128, NT], BF16, "duch")
    zch = p.sbuf([128, NT], BF16, "zch")
    t1 = p.sbuf([128, NT], F32, "t1")
    t2 = p.sbuf([128, NT], F32, "t2")
    for ch in range(16):
        p.dma(ych, y_h[ch])
        p.dma(duch, du_h[ch])
        p.dma(zch, z_h[ch])
        p.act(t1, zch, AF.Silu)
        p.tt(t2, ych, duch, ALU.add)
        p.tt(ygv[:, ch, :], t1, t2, ALU.mult)

    gam = p.sbuf([1, D], BF16, "gam")
    bet = p.sbuf([1, D], BF16, "bet")
    p.dma(gam, gam_h, eng="gpsimd")
    p.dma(bet, bet_h, eng="gpsimd")
    ones1 = p.sbuf([1, 128], BF16, "ones1")
    p.memset(ones1, 1.0)
    gamb = p.sbuf([128, D], F32, "gamb")
    betb = p.sbuf([128, D], F32, "betb")
    for fo in range(2):
        ps = p.bankf(fo % 2)
        p.mm(ps, ones1, gam[:, fo * 512:(fo + 1) * 512], True, True)
        p.cp(gamb[:, fo * 512:(fo + 1) * 512], ps, eng="vector")
        p.mm(ps, ones1, bet[:, fo * 512:(fo + 1) * 512], True, True)
        p.cp(betb[:, fo * 512:(fo + 1) * 512], ps, eng="vector")

    xt = [p.sbuf([128, D], F32, f"xt{i}") for i in range(2)]
    xn = p.sbuf([128, D], F32, "xn")
    res = p.sbuf([128, D], F32, "res")
    xsq = p.sbuf([128, D], F32, "xsq")
    statsall = p.sbuf([128, TT * 8], F32, "stats")
    sout = [p.sbuf([128, D], BF16, f"sout{i}") for i in range(2)]
    x3 = x_h.rearrange("(t p) d -> t p d", p=128)
    s3 = o_s.rearrange("(t p) d -> t p d", p=128)
    for t in range(TT):
        bi = t % 2
        tok = slice(t * 128, (t + 1) * 128)
        p.dma(xt[bi], x3[t])
        st = statsall[:, t * 8:(t + 1) * 8]
        nm, vr, sd, rstd, vr2, nm2, vr3, sd2 = (st[:, i:i + 1] for i in range(8))
        p.red(nm, xt[bi], ALU.add)
        p.ts(nm, nm, -1.0 / D, None, op0=ALU.mult)
        p.ts(xn, xt[bi], nm, None, op0=ALU.add)
        p.act(xsq[:, 0:512], xn[:, 0:512], AF.Square, accum_out=vr)
        p.act(xsq[:, 512:1024], xn[:, 512:1024], AF.Square, accum_out=vr2)
        p.tt(vr, vr, vr2, ALU.add)
        p.act(sd, vr, AF.Sqrt, bias=p.constf(1e-5), scale=1.0 / D)
        p.recip(rstd, sd)
        p.ts(xn, xn, rstd, None, op0=ALU.mult)
        for fo in range(2):
            ps = p.bankf(2 + fo)
            fcols = slice(fo * BLK, (fo + 1) * BLK)
            p.mm_acc(ps, [(ygv[:, ki, tok], wout[:, ki, fcols])
                          for ki in range(16)])
            p.tt(res[:, fcols], ps, xn[:, fcols], ALU.add)
        p.red(nm2, res, ALU.add)
        p.ts(nm2, nm2, -1.0 / D, None, op0=ALU.mult)
        p.ts(res, res, nm2, None, op0=ALU.add)
        p.act(xsq[:, 0:512], res[:, 0:512], AF.Square, accum_out=vr3)
        p.act(xsq[:, 512:1024], res[:, 512:1024], AF.Square, accum_out=sd2)
        p.tt(vr3, vr3, sd2, ALU.add)
        p.act(sd2, vr3, AF.Sqrt, bias=p.constf(1e-5), scale=1.0 / D)
        p.recip(rstd, sd2)
        p.ts(res, res, rstd, None, op0=ALU.mult)
        so = sout[bi]
        p.tt(res, res, gamb, ALU.mult)
        p.tt(so, res, betb, ALU.add)
        p.dma(s3[t], so)
    return p


def build_l1d2():
    p = Prog("l1d2")
    xnfm_h = p.dram_in("xnfm", (8, 128, NT), BF16)
    wg_h = p.dram_in("wg", (D, 8))
    wu_h = p.dram_in("wu", (8, D, DI))
    wd_h = p.dram_in("wd", (8, DI, D))
    w1_h = p.dram_in("w1", (D, DI))
    w2_h = p.dram_in("w2", (DI, D))
    s_h = p.dram_in("s", (NT, D), BF16)
    x_h = p.dram_in("x", (NT, D))
    o_x = p.dram_out("xout", (NT, D))

    xnfm = p.sbuf([128, 8 * NT], BF16, "xnfm")
    p.dma(xnfm.rearrange("p (c t) -> p c t", c=8),
          xnfm_h.rearrange("c p t -> p c t"))
    xnv = xnfm.rearrange("p (c t) -> p c t", c=8)
    wg = p.wload(wg_h, D, 8, "wg")

    cwall = p.sbuf([128, TT * 8], F32, "cwall")
    iot8m = p.sbuf([128, 8], F32, "iot8m")
    p.iota(iot8m, [[1, 8]], base=-8, channel_multiplier=0)
    gp = p.sbuf([128, 8], F32, "gp")
    gp2 = p.sbuf([128, 8], F32, "gp2")
    cand = p.sbuf([128, 8], F32, "cand")
    mio = p.sbuf([128, 8], F32, "mio")
    oh1 = p.sbuf([128, 8], F32, "oh1")
    oh2 = p.sbuf([128, 8], F32, "oh2")
    v1 = p.sbuf([128, 1], F32, "v1")
    v2 = p.sbuf([128, 1], F32, "v2")
    i1 = p.sbuf([128, 1], F32, "i1")
    i2 = p.sbuf([128, 1], F32, "i2")
    r1 = p.sbuf([128, 1], F32, "r1")
    r2 = p.sbuf([128, 1], F32, "r2")
    vs = p.sbuf([128, 1], F32, "vs")
    BIG = 1.0e9
    for t in range(TT):
        tok = slice(t * 128, (t + 1) * 128)
        ps2 = p.bankf(0)[:, 0:8]
        p.mm_acc(ps2, [(xnv[:, ki, tok], wg[:, ki, :]) for ki in range(8)])
        p.red(r1, ps2, ALU.max)
        p.ts(r1, r1, -1.0, None, op0=ALU.mult)
        p.act(gp, ps2, AF.Exp, bias=r1, accum_out=r2)
        p.recip(r2, r2)
        p.ts(gp, gp, r2, None, op0=ALU.mult)
        p.red(v1, gp, ALU.max)
        p.ts(cand, gp, v1, None, op0=ALU.is_ge)
        p.tt(mio, cand, iot8m, ALU.mult)
        p.ts(mio, mio, 8.0, None, op0=ALU.add)
        p.red(i1, mio, ALU.min)
        p.ts(i1, i1, -8.0, None, op0=ALU.add)
        p.ts(oh1, iot8m, i1, None, op0=ALU.is_equal)
        p.stt(gp2, oh1, -BIG, gp, op0=ALU.mult, op1=ALU.add)
        p.red(v2, gp2, ALU.max)
        p.ts(cand, gp2, v2, None, op0=ALU.is_ge)
        p.tt(mio, cand, iot8m, ALU.mult)
        p.ts(mio, mio, 8.0, None, op0=ALU.add)
        p.red(i2, mio, ALU.min)
        p.ts(i2, i2, -8.0, None, op0=ALU.add)
        p.ts(oh2, iot8m, i2, None, op0=ALU.is_equal)
        p.tt(vs, v1, v2, ALU.add)
        p.recip(vs, vs)
        cw = cwall[:, t * 8:(t + 1) * 8]
        p.ts(cw, oh1, v1, None, op0=ALU.mult)
        p.stt(cw, oh2, v2, cw, op0=ALU.mult, op1=ALU.add)
        p.ts(cw, cw, vs, None, op0=ALU.mult)

    acc = p.sbuf([128, TT * D], F32, "acc")
    accv = acc.rearrange("p (t d) -> p t d", t=TT)
    wu = p.sbuf([128, 8 * DI], BF16, "wu")
    wuv = wu.rearrange("p (k n) -> p k n", k=8)
    wd = p.sbuf([128, 16 * D], BF16, "wd")
    wdv = wd.rearrange("p (k n) -> p k n", k=16)
    hidb = p.sbuf([128, 16 * BLK], BF16, "hidb")
    hv = hidb.rearrange("p (k n) -> p k n", k=16)

    for e in range(9):
        if e < 8:
            p.dma(wuv, wu_h[e].rearrange("(k p) n -> p k n", p=128),
                  eng="gpsimd")
            p.dma(wdv, wd_h[e].rearrange("(k p) n -> p k n", p=128),
                  eng="gpsimd")
        else:
            p.dma(wuv, w1_h.rearrange("(k p) n -> p k n", p=128),
                  eng="gpsimd")
            p.dma(wdv, w2_h.rearrange("(k p) n -> p k n", p=128),
                  eng="gpsimd")
        for nb in range(NBLK):
            tok = slice(nb * BLK, (nb + 1) * BLK)
            for po in range(16):
                ps = p.bankf(po % 2)
                p.mm_acc(ps, [(wuv[:, ki, po * 128:(po + 1) * 128],
                               xnv[:, ki, tok]) for ki in range(8)])
                p.act(hv[:, po, :], ps, AF.Gelu)
            for j in range(4):
                tg = nb * 4 + j
                for fo in range(2):
                    ps = p.bankf(2 + (j * 2 + fo) % 2)
                    fcols = slice(fo * BLK, (fo + 1) * BLK)
                    p.mm_acc(ps, [(hv[:, ki, j * 128:(j + 1) * 128],
                                   wdv[:, ki, fcols]) for ki in range(16)])
                    dst = accv[:, tg, fcols]
                    if e == 0:
                        p.ts(dst, ps, cwall[:, tg * 8:tg * 8 + 1], None,
                             op0=ALU.mult)
                    elif e < 8:
                        p.stt(dst, ps, cwall[:, tg * 8 + e:tg * 8 + e + 1],
                              dst, op0=ALU.mult, op1=ALU.add)
                    else:
                        p.stt(dst, ps, 1.0, dst, op0=ALU.mult, op1=ALU.add)

    xtl = p.sbuf([128, D], F32, "xtl0")
    stl = p.sbuf([128, D], BF16, "stl0")
    x3 = x_h.rearrange("(t p) d -> t p d", p=128)
    sS = s_h.rearrange("(t p) d -> t p d", p=128)
    o3 = o_x.rearrange("(t p) d -> t p d", p=128)
    for t in range(TT):
        at = accv[:, t, :]
        p.dma(xtl, x3[t])
        p.dma(stl, sS[t], eng="gpsimd")
        p.act(at, at, AF.Silu)
        p.tt(at, at, stl, ALU.mult)
        p.tt(at, at, xtl, ALU.add)
        p.dma(o3[t], at)
    return p


def run_l1(inp, x_full):
    ident, slt = _consts()
    xs = [np.ascontiguousarray(x_full[c * NT:(c + 1) * NT]) for c in range(NCORES)]
    im = [dict(x=xs[c], win=inp["L1_attn_W_in"], ident=ident)
          for c in range(NCORES)]
    r1 = _run("l1b1", build_l1b1, im)

    xin_full = np.concatenate([r["xin"] for r in r1], axis=2).reshape(DI, NTOK)
    im = []
    for c in range(NCORES):
        ds = slice(c * DSL, (c + 1) * DSL)
        cw = inp["L1_attn_conv_w"][ds].reshape(2, 128, 4).transpose(1, 0, 2)
        cb = inp["L1_attn_conv_b"][ds].reshape(2, 128).T
        im.append(dict(
            xin=np.ascontiguousarray(xin_full[ds].reshape(2, 128, NTOK)),
            convw=np.ascontiguousarray(cw), convb=np.ascontiguousarray(cb),
            wxp=np.ascontiguousarray(inp["L1_attn_W_xproj"][ds])))
    r2 = _run("l1b2", build_l1b2, im)

    xdp_all = np.stack([r["xdp"] for r in r2])
    sels = np.zeros((16, 128, 128), dtype=BF)
    for gi in range(16):
        for pp in range(128):
            sels[gi, 8 * gi + pp // 16, pp] = 1
    nsel = np.zeros((128, 8), dtype=BF)
    for pp in range(128):
        nsel[pp, pp // 16] = 1
    im = []
    for c in range(NCORES):
        ds = slice(c * DSL, (c + 1) * DSL)
        Asl = inp["L1_attn_A_log"][ds]
        aarr = np.zeros((128, 32), dtype=np.float32)
        for g in range(32):
            for pp in range(128):
                aarr[pp, g] = Asl[(g // 16) * 128 + 8 * (g % 16) + pp // 16,
                                  pp % 16]
        bdt = inp["L1_attn_b_dt"][ds].reshape(2, 128).T
        darr = inp["L1_attn_D"][ds].reshape(2, 128).T
        im.append(dict(
            u=r2[c]["u"], xdp=xdp_all,
            wdt=np.ascontiguousarray(inp["L1_attn_W_dt"][:, ds]),
            bdt=np.ascontiguousarray(bdt), aarr=aarr,
            darr=np.ascontiguousarray(darr), sels=sels, nsel=nsel))
    r3 = _run("l1b3", build_l1b3, im)

    y_full = np.concatenate([r["y"] for r in r3], axis=0)
    du_full = np.concatenate([r["du"].reshape(DSL, NTOK) for r in r3], axis=0)
    im = []
    for c in range(NCORES):
        cols = slice(c * NT, (c + 1) * NT)
        im.append(dict(
            y=np.ascontiguousarray(y_full[:, cols].reshape(16, 128, NT)),
            du=np.ascontiguousarray(du_full[:, cols].reshape(16, 128, NT)),
            z=r1[c]["z"], wout=inp["L1_attn_W_out"], x=xs[c],
            gam=np.ascontiguousarray(inp["L1_attn_gamma"][None, :]),
            bet=np.ascontiguousarray(inp["L1_attn_beta"][None, :])))
    r4 = _run("l1d1", build_l1d1, im)

    im = []
    for c in range(NCORES):
        im.append(dict(
            xnfm=r1[c]["xnfm"], wg=inp["L1_ffn_Wg"], wu=inp["L1_ffn_Wu"],
            wd=inp["L1_ffn_Wd"], w1=inp["L1_ffn_W1"], w2=inp["L1_ffn_W2"],
            s=r4[c]["s"], x=xs[c]))
    r5 = _run("l1d2", build_l1d2, im)
    x2 = np.concatenate([r["xout"] for r in r5], axis=0)
    return x2, r4




def build_l2e1():
    p = Prog("l2e1")
    x = p.dram_in("x", (NT, D))
    wx_h = p.dram_in("wx", (16, D, 64))
    bx_h = p.dram_in("bx", (64, 16))
    ident_h = p.dram_in("ident", (128, 128), BF16)
    o_xnfm = p.dram_out("xnfm", (8, 128, NT), BF16)
    o_h = p.dram_out("h", (16, 64, NT), BF16)
    o_hn = p.dram_out("hn", (16, 64, NT), BF16)

    xnfm = p.sbuf([128, 8 * NT], BF16, "xnfm")
    idsb = p.sbuf([128, 128], BF16, "idsb")
    p.dma(idsb, ident_h)
    xnv = xnfm.rearrange("p (c t) -> p c t", c=8)

    xt = [p.sbuf([128, D], F32, f"xt{i}") for i in range(2)]
    xc = [p.sbuf([128, D], F32, f"xc{i}") for i in range(2)]
    xnbf = [p.sbuf([128, D], BF16, f"xnbf{i}") for i in range(2)]
    xsq = p.sbuf([128, D], F32, "xsq")
    statsall = p.sbuf([128, TT * 8], F32, "stats")
    x3 = x.rearrange("(t p) d -> t p d", p=128)
    for t in range(TT):
        bi = t % 2
        p.dma(xt[bi], x3[t])
        st = statsall[:, t * 8:(t + 1) * 8]
        nm, vr, sd, rstd, vr2 = (st[:, i:i + 1] for i in range(5))
        p.red(nm, xt[bi], ALU.add)
        p.ts(nm, nm, -1.0 / D, None, op0=ALU.mult)
        p.ts(xc[bi], xt[bi], nm, None, op0=ALU.add)
        p.act(xsq[:, 0:512], xc[bi][:, 0:512], AF.Square, accum_out=vr)
        p.act(xsq[:, 512:1024], xc[bi][:, 512:1024], AF.Square, accum_out=vr2)
        p.tt(vr, vr, vr2, ALU.add)
        p.act(sd, vr, AF.Sqrt, bias=p.constf(1e-5), scale=1.0 / D)
        p.recip(rstd, sd)
        p.ts(xnbf[bi], xc[bi], rstd, None, op0=ALU.mult)
        for ch in range(8):
            pb = p.bankb(ch % 2)[:, 0:128]
            p.tp(pb, xnbf[bi][:, ch * 128:(ch + 1) * 128], idsb)
            p.cp(xnv[:, ch, t * 128:(t + 1) * 128], pb, eng="vector")
    p.dma(o_xnfm.rearrange("c p t -> p c t"), xnv)

    wx = p.sbuf([128, 8 * 16 * 64], BF16, "wx")
    wxv = wx.rearrange("p (h k n) -> p h k n", h=16, k=8)
    for hh in range(16):
        p.dma(wxv[:, hh], wx_h[hh].rearrange("(k p) n -> p k n", p=128),
              eng="gpsimd")
    bx = p.sbuf([64, 16], F32, "bx")
    p.dma(bx, bx_h)

    hsb = p.sbuf([64, BLK], F32, "hsb")
    hbf = p.sbuf([64, BLK], BF16, "hbf")
    hsq = p.sbuf([64, BLK], F32, "hsq")
    onesf = p.sbuf([64, 1], F32, "onesf")
    p.memset(onesf, 1.0)
    ones1b = p.sbuf([1, 64], BF16, "ones1b")
    p.memset(ones1b, 1.0)
    nr = p.sbuf([1, BLK], F32, "nr")
    nrb = p.sbuf([1, BLK], BF16, "nrb")
    hnb = p.sbuf([64, BLK], BF16, "hnb")
    for h in range(16):
        for nb in range(NBLK):
            tok = slice(nb * BLK, (nb + 1) * BLK)
            ps = p.bankf(0)[0:64, :]
            p.mm_acc(ps, [(wxv[:, h, ki, :], xnv[:, ki, tok])
                          for ki in range(8)])
            p.ts(hsb, ps, bx[:, h:h + 1], None, op0=ALU.add)
            p.cp(hbf, hsb)
            p.dma(o_h[h][:, tok], hbf)
            p.act(hsq, hsb, AF.Square)
            psn = p.bankf(1)[0:1, :]
            p.op("tensor", lambda ps2=psn, a=onesf, b=hsq:
                 p.nc.tensor.matmul(ps2, a, b, start=True, stop=True),
                 reads=[onesf, hsq], writes=[psn])
            p.act(nr, psn, AF.Sqrt)
            p.recip(nr, nr)
            p.cp(nrb, nr)
            psb = p.bankf(2)[0:64, :]
            p.mm(psb, ones1b, nrb, True, True)
            p.tt(hnb, hbf, psb, ALU.mult)
            p.dma(o_hn[h][:, tok], hnb)
    return p


def build_l2e2():
    p = Prog("l2e2")
    hn_h = p.dram_in("hn", (4, 64, L), BF16)
    h_h = p.dram_in("h", (4, 64, L), BF16)
    ident_h = p.dram_in("ident", (128, 128), BF16)
    o_out = p.dram_out("out", (4, 64, L), BF16)

    idsb = p.sbuf([128, 128], BF16, "idsb")
    p.dma(idsb, ident_h)
    hn = p.sbuf([64, L], BF16, "hn")
    hf = p.sbuf([64, L], BF16, "hf")
    htm = p.sbuf([128, 64 * 65], BF16, "htm")
    hv = htm.rearrange("p (k n) -> p k n", k=64)
    PT = [p.sbuf([128, 1024], BF16, f"pt{i}") for i in range(2)]
    ob = p.sbuf([64, 1024], BF16, "ob")
    rd = p.sbuf([1, 1024], F32, "rd")
    rdb = p.sbuf([1, 1024], BF16, "rdb")
    rdbc = p.sbuf([64, 512], F32, "rdbc")
    ones1b = p.sbuf([1, 64], BF16, "ones1b")
    p.memset(ones1b, 1.0)

    KB = 1024
    for bh in range(4):
        p.dma(hn, hn_h[bh])
        p.dma(hf, h_h[bh])
        p.memset(bass.AP(htm.tensor, htm.offset + 64,
                         [htm.ap[0], [65, 64], [1, 1]]), 1.0)
        for kc in range(64):
            pb = p.bankb(0)[:, 0:64]
            p.tp(pb, hf[:, kc * 128:(kc + 1) * 128], idsb)
            p.cp(hv[:, kc, 0:64], pb, eng="vector")
        for qb in range(L // KB):
            qs = slice(qb * KB, (qb + 1) * KB)
            for kc in range(64):
                ks = slice(kc * 128, (kc + 1) * 128)
                pS = p.bankf(kc % 2)
                pS2 = p.bankf(2 + kc % 2)
                p.mm(pS, hn[:, ks], hn[:, qb * KB: qb * KB + 512], True, True)
                p.mm(pS2, hn[:, ks], hn[:, qb * KB + 512: (qb + 1) * KB],
                     True, True)
                pt = PT[kc % 2]
                p.act(pt[:, 0:512], pS, AF.Exp)
                p.act(pt[:, 512:1024], pS2, AF.Exp)
                p.mm(p.bankf(4)[0:65, :], hv[:, kc, :], pt[:, 0:512],
                     kc == 0, kc == 63)
                p.mm(p.bankf(5)[0:65, :], hv[:, kc, :], pt[:, 512:1024],
                     kc == 0, kc == 63)
            for half in range(2):
                pb = p.bankf(4 + half)
                p.recip(rd[:, half * 512:(half + 1) * 512], pb[64:65, :])
            p.cp(rdb, rd)
            pbc = p.bankf(6)[0:64, :]
            for half in range(2):
                pb = p.bankf(4 + half)
                p.mm(pbc, ones1b, rdb[:, half * 512:(half + 1) * 512],
                     True, True)
                p.cp(rdbc, pbc, eng="scalar")
                p.tt(ob[:, half * 512:(half + 1) * 512], pb[0:64, :], rdbc,
                     ALU.mult)
            p.dma(o_out[bh][:, qs], ob)
    return p


def build_l2e3():
    p = Prog("l2e3")
    cat_h = p.dram_in("cat", (8, 128, NT), BF16)
    xnfm_h = p.dram_in("xnfm", (8, 128, NT), BF16)
    wp_h = p.dram_in("wp", (D, D))
    bp_h = p.dram_in("bp", (128, 8))
    At_h = p.dram_in("At", (6, 16, 100))
    Bm_h = p.dram_in("Bm", (6, 16, D))
    wgate_h = p.dram_in("wgate", (D, 6))
    wtwT_h = p.dram_in("wtwT", (D, D))
    wtb_h = p.dram_in("wtb", (1, D))
    wdwT_h = p.dram_in("wdwT", (D, D))
    wdb_h = p.dram_in("wdb", (128, 8))
    scal_h = p.dram_in("scal", (1, 1))
    x_h = p.dram_in("x", (NT, D))
    ident_h = p.dram_in("ident", (128, 128), BF16)
    o_x = p.dram_out("xout", (NT, D))

    idsb = p.sbuf([128, 128], BF16, "idsb")
    p.dma(idsb, ident_h)
    cat = p.sbuf([128, 8 * NT], BF16, "cat")
    p.dma(cat.rearrange("p (c t) -> p c t", c=8),
          cat_h.rearrange("c p t -> p c t"))
    catv = cat.rearrange("p (c t) -> p c t", c=8)
    xnfm = p.sbuf([128, 8 * NT], BF16, "xnfm")
    p.dma(xnfm.rearrange("p (c t) -> p c t", c=8),
          xnfm_h.rearrange("c p t -> p c t"))
    xnv = xnfm.rearrange("p (c t) -> p c t", c=8)
    wp = p.wload(wp_h, D, D, "wp")
    bp = p.sbuf([128, 8], F32, "bp")
    p.dma(bp, bp_h)
    wgate = p.wload(wgate_h, D, 6, "wgate")
    wtwT_sb = p.sbuf([128, 8 * D], BF16, "wtwT")
    wtwT = wtwT_sb.rearrange("p (k n) -> p k n", k=8)
    p.dma(wtwT, wtwT_h.rearrange("(k p) n -> p k n", p=128), eng="gpsimd")
    wtb = p.sbuf([1, D], BF16, "wtb")
    p.dma(wtb, wtb_h, eng="gpsimd")
    wdwT = wtwT
    wdb = p.sbuf([128, 8], F32, "wdb")
    p.dma(wdb, wdb_h)
    scal = p.sbuf([1, 1], F32, "scal")
    p.dma(scal, scal_h)
    ones1 = p.sbuf([1, 128], BF16, "ones1")
    p.memset(ones1, 1.0)
    onesf1 = p.sbuf([1, 128], F32, "onesf1")
    p.memset(onesf1, 1.0)

    shib = p.sbuf([1, 1], BF16, "shib")
    slof = p.sbuf([1, 1], F32, "slof")
    slob = p.sbuf([1, 1], BF16, "slob")
    sbc = p.sbuf([128, 1], F32, "sbc")
    p.cp(shib, scal)
    p.tt(slof, scal, shib, ALU.subtract)
    p.cp(slob, slof)
    psc = p.bankf(4)[:, 300:301]
    p.op("tensor", lambda: p.nc.tensor.matmul(psc, ones1, shib, start=True,
                                              stop=False),
         reads=[ones1, shib], writes=[psc])
    p.op("tensor", lambda: p.nc.tensor.matmul(psc, ones1, slob, start=False,
                                              stop=True),
         reads=[ones1, slob], writes=[psc])
    p.cp(sbc, psc)
    wdbs = p.sbuf([128, 8], F32, "wdbs")
    p.ts(wdbs, wdb, sbc[:, 0:1], None, op0=ALU.mult)

    At = p.sbuf([16, 6 * 100], F32, "At")
    p.dma(At.rearrange("p (e l) -> p e l", e=6), At_h.rearrange("e p l -> p e l"))
    Atb = p.sbuf([16, 6 * 100], BF16, "Atb")
    p.cp(Atb, At)
    Atv = Atb.rearrange("p (e l) -> p e l", e=6)
    Bm = p.sbuf([16, 6 * D], F32, "Bm")
    p.dma(Bm.rearrange("p (e d) -> p e d", e=6), Bm_h.rearrange("e p d -> p e d"))
    Bmb = p.sbuf([16, 6 * D], BF16, "Bmb")
    p.cp(Bmb, Bm)
    Bmv = Bmb.rearrange("p (e d) -> p e d", e=6)
    tok_fm = p.sbuf([128, 8 * 600], BF16, "tokfm")
    tv = tok_fm.rearrange("p (c el) -> p c el", c=8)
    for e in range(6):
        for ch in range(8):
            ps = p.bankf(ch % 2)[:, 0:100]
            p.mm(ps, Bmv[:, e, ch * 128:(ch + 1) * 128], Atv[:, e, :],
                 True, True)
            p.cp(tv[:, ch, e * 100:(e + 1) * 100], ps, eng="vector")
    val = p.sbuf([128, 5 * D], BF16, "val")
    vv = val.rearrange("p (c d) -> p c d", c=5)
    for elc in range(5):
        rows = min(128, 600 - elc * 128)
        for fo in range(2):
            ps = p.bankf(2 + fo % 2)[0:rows, :]
            fcols = slice(fo * 512, (fo + 1) * 512)
            pairs = [(tv[:, ki, elc * 128:elc * 128 + rows],
                      wtwT[:, ki, fcols]) for ki in range(8)]
            pairs.append((ones1[:, 0:rows], wtb[:, fcols]))
            p.mm_acc(ps, pairs)
            p.cp(vv[0:rows, elc, fcols], ps, eng="vector")

    p.dma(wdwT, wdwT_h.rearrange("(k p) n -> p k n", p=128), eng="gpsimd")

    iot6m = p.sbuf([128, 6], F32, "iot6m")
    p.iota(iot6m, [[1, 6]], base=-6, channel_multiplier=0)
    gl = p.sbuf([128, 6], F32, "gl")
    cand = p.sbuf([128, 6], F32, "cand")
    mio = p.sbuf([128, 6], F32, "mio")
    oh1 = p.sbuf([128, 6], F32, "oh1")
    v1 = p.sbuf([128, 1], F32, "v1")
    i1 = p.sbuf([128, 1], F32, "i1")
    r1 = p.sbuf([128, 1], F32, "r1")
    r2 = p.sbuf([128, 1], F32, "r2")
    att = p.sbuf([128, 600], F32, "att")
    attb = p.sbuf([128, 4 * 600], BF16, "attb")
    abv = attb.rearrange("p (j el) -> p j el", j=4)
    attT = p.sbuf([128, 5 * 512], BF16, "attT")
    atv = attT.rearrange("p (c j) -> p c j", c=5)
    delin = p.sbuf([128, 8 * BLK], BF16, "delin")
    dv = delin.rearrange("p (c t) -> p c t", c=8)
    expd = p.sbuf([128, BLK], F32, "expd")
    zs = p.sbuf([128, BLK], F32, "zs")
    gpre = p.sbuf([128, 8 * BLK], BF16, "gpre")
    gv = gpre.rearrange("p (c t) -> p c t", c=8)
    sfm = p.sbuf([128, 8 * BLK], BF16, "sfm")
    sv = sfm.rearrange("p (c t) -> p c t", c=8)
    sgb = p.sbuf([128, BLK], BF16, "sgb")
    xtl = p.sbuf([128, D], F32, "xtl")
    EPSF = float(np.finfo(np.float32).eps)

    x3v = x_h.rearrange("(t p) d -> t p d", p=128)
    o3 = o_x.rearrange("(t p) d -> t p d", p=128)
    for nb in range(NBLK):
        tokb = slice(nb * BLK, (nb + 1) * BLK)
        for j in range(4):
            t = nb * 4 + j
            tok = slice(t * 128, (t + 1) * 128)
            psg = p.bankf(0)[:, 0:6]
            p.mm_acc(psg, [(xnv[:, ki, tok], wgate[:, ki, :])
                           for ki in range(8)])
            p.cp(gl, psg)
            p.red(v1, gl, ALU.max)
            p.ts(cand, gl, v1, None, op0=ALU.is_ge)
            p.tt(mio, cand, iot6m, ALU.mult)
            p.ts(mio, mio, 6.0, None, op0=ALU.add)
            p.red(i1, mio, ALU.min)
            p.ts(i1, i1, -6.0, None, op0=ALU.add)
            p.ts(oh1, iot6m, i1, None, op0=ALU.is_equal)
            ps1 = p.bankf(1)
            ps2 = p.bankf(0)[:, 128:216]
            p.mm_acc(ps1, [(xnv[:, ki, tok], tv[:, ki, 0:512])
                           for ki in range(8)])
            p.mm_acc(ps2, [(xnv[:, ki, tok], tv[:, ki, 512:600])
                           for ki in range(8)])
            p.ts(att[:, 0:512], ps1, 1.0 / 32.0, None, op0=ALU.mult)
            p.ts(att[:, 512:600], ps2, 1.0 / 32.0, None, op0=ALU.mult)
            for e in range(6):
                sl = att[:, e * 100:(e + 1) * 100]
                p.red(r1, sl, ALU.max)
                p.ts(r1, r1, -1.0, None, op0=ALU.mult)
                p.act(sl, sl, AF.Exp, bias=r1, accum_out=r2)
                p.recip(r2, r2)
                p.ts(r2, r2, oh1[:, e:e + 1], None, op0=ALU.mult)
                p.ts(abv[:, j, e * 100:(e + 1) * 100], sl, r2, None,
                     op0=ALU.mult)
            p.memset(bass.AP(attb.tensor, attb.offset + j * 600,
                             [attb.ap[0], [100, 6], [1, 1]]), 0.0)
        for j in range(4):
            for elc in range(5):
                rows = min(128, 600 - elc * 128)
                pb = p.bankb(elc % 2)[0:rows, 0:128]
                p.tp(pb, abv[:, j, elc * 128:elc * 128 + rows], idsb)
                p.cp(atv[0:rows, elc, j * 128:(j + 1) * 128], pb,
                     eng="vector")
        for ch in range(8):
            ps = p.bankf(2 + ch % 2)
            pairs = []
            for elc in range(5):
                rows = min(128, 600 - elc * 128)
                pairs.append((vv[0:rows, elc, ch * 128:(ch + 1) * 128],
                              atv[0:rows, elc, :]))
            p.mm_acc(ps, pairs)
            p.tt(dv[:, ch, :], ps, xnv[:, ch, tokb], ALU.add)
        for ch in range(8):
            ps = p.bankf(4 + ch % 2)
            p.mm_acc(ps, [(wp[:, ki, ch * 128:(ch + 1) * 128],
                           catv[:, ki, tokb]) for ki in range(8)])
            p.ts(sv[:, ch, :], ps, bp[:, ch:ch + 1], None, op0=ALU.add)
        for ch in range(8):
            ps = p.bankf(2 + ch % 2)
            p.mm_acc(ps, [(wdwT[:, ki, ch * 128:(ch + 1) * 128],
                           dv[:, ki, :]) for ki in range(8)])
            p.act(expd, ps, AF.Exp, scale=sbc[:, 0:1],
                  bias=wdbs[:, ch:ch + 1])
            p.ts(zs, expd, 0.0, None, op0=ALU.is_equal)
            p.stt(expd, zs, EPSF, expd, op0=ALU.mult, op1=ALU.add)
            p.act(expd, expd, AF.Ln)
            p.tt(gv[:, ch, :], expd, xnv[:, ch, tokb], ALU.add)
        for j in range(4):
            t = nb * 4 + j
            p.dma(xtl, x3v[t])
            for ch in range(8):
                cols = slice(j * 128, (j + 1) * 128)
                p.act(zs[:, 0:128], gv[:, ch, cols], AF.Silu)
                p.tt(sgb[:, cols.start:cols.stop], zs[:, 0:128],
                     sv[:, ch, cols], ALU.mult)
                pb = p.bankb(ch % 2)[:, 0:128]
                p.tp(pb, sgb[:, cols.start:cols.stop], idsb)
                p.tt(xtl[:, ch * 128:(ch + 1) * 128],
                     xtl[:, ch * 128:(ch + 1) * 128], pb, ALU.add)
            p.dma(o3[t], xtl)
    return p


def run_l2(inp, x_full):
    ident, slt = _consts()
    xs = [np.ascontiguousarray(x_full[c * NT:(c + 1) * NT]) for c in range(NCORES)]
    im = [dict(x=xs[c], wx=inp["L2_attn_Wx"],
               bx=np.ascontiguousarray(inp["L2_attn_bx"].T), ident=ident)
          for c in range(NCORES)]
    r1 = _run("l2e1", build_l2e1, im)

    h_full = np.concatenate([r["h"] for r in r1], axis=2)
    hn_full = np.concatenate([r["hn"] for r in r1], axis=2)
    im = []
    for c in range(NCORES):
        b = c // 4
        hs = []
        hns = []
        for k in range(4):
            h = 4 * (c % 4) + k
            hs.append(h_full[h][:, b * L:(b + 1) * L])
            hns.append(hn_full[h][:, b * L:(b + 1) * L])
        im.append(dict(hn=np.ascontiguousarray(np.stack(hns)),
                       h=np.ascontiguousarray(np.stack(hs)), ident=ident))
    r2 = _run("l2e2", build_l2e2, im)

    out_full = np.zeros((D, NTOK), dtype=BF)
    for c in range(NCORES):
        b = c // 4
        for k in range(4):
            h = 4 * (c % 4) + k
            out_full[h * 64:(h + 1) * 64, b * L:(b + 1) * L] = r2[c]["out"][k]
    bp_arr = np.ascontiguousarray(inp["L2_attn_bp"].reshape(8, 128).T)
    wdb_arr = np.ascontiguousarray(inp["L2_ffn_wd_b"].reshape(8, 128).T)
    At = np.ascontiguousarray(inp["L2_ffn_A"].transpose(0, 2, 1))
    im = []
    for c in range(NCORES):
        cols = slice(c * NT, (c + 1) * NT)
        im.append(dict(
            cat=np.ascontiguousarray(out_full[:, cols].reshape(8, 128, NT)),
            xnfm=r1[c]["xnfm"], wp=inp["L2_attn_Wp"], bp=bp_arr,
            At=At, Bm=inp["L2_ffn_B"], wgate=inp["L2_ffn_w_gate"],
            wtwT=np.ascontiguousarray(inp["L2_ffn_wt_w"].T),
            wtb=np.ascontiguousarray(inp["L2_ffn_wt_b"][None, :]),
            wdwT=np.ascontiguousarray(inp["L2_ffn_wd_w"].T),
            wdb=wdb_arr,
            scal=np.ascontiguousarray(inp["L2_ffn_scale"][None, :]),
            x=xs[c], ident=ident))
    r3 = _run("l2e3", build_l2e3, im)
    x3 = np.concatenate([r["xout"] for r in r3], axis=0)
    return x3, r2





DIP = 4240
CONVD = 2176


def build_l3f1():
    p = Prog("l3f1")
    x = p.dram_in("x", (NT, D))
    ipw_h = p.dram_in("ipw", (DIP, D))
    xnw_h = p.dram_in("xnw", (1, D))
    xnb_h = p.dram_in("xnb", (1, D))
    ident_h = p.dram_in("ident", (128, 128), BF16)
    o_xnfm = p.dram_out("xnfm", (8, 128, NT), BF16)
    o_z = p.dram_out("z", (16, 128, NT), BF16)
    o_xbc = p.dram_out("xbc", (17, 128, NT), BF16)
    o_dtr = p.dram_out("dtr", (16, NT))

    idsb = p.sbuf([128, 128], BF16, "idsb")
    p.dma(idsb, ident_h)
    wnT = p.sbuf([128, 8 * DIP], BF16, "wnT")
    wv = wnT.rearrange("p (k n) -> p k n", k=8)
    wrow = p.sbuf([128, D], F32, "wrow")
    wsq = p.sbuf([128, D], F32, "wsq")
    wst = p.sbuf([128, 4], F32, "wst")
    wbf = p.sbuf([128, D], BF16, "wbf")
    NCHW = (DIP + 127) // 128
    for rc in range(NCHW):
        rows = min(128, DIP - rc * 128)
        p.dma(wrow[0:rows, :], ipw_h[rc * 128: rc * 128 + rows, :])
        rn, rn2, dv = wst[:, 0:1], wst[:, 1:2], wst[:, 2:3]
        p.act(wsq[0:rows, 0:512], wrow[0:rows, 0:512], AF.Square,
              accum_out=rn[0:rows])
        p.act(wsq[0:rows, 512:1024], wrow[0:rows, 512:1024], AF.Square,
              accum_out=rn2[0:rows])
        p.tt(rn[0:rows], rn[0:rows], rn2[0:rows], ALU.add)
        p.act(rn[0:rows], rn[0:rows], AF.Sqrt)
        p.ts(rn[0:rows], rn[0:rows], 1e-6, None, op0=ALU.max)
        p.recip(dv[0:rows], rn[0:rows])
        p.ts(wbf[0:rows, :], wrow[0:rows, :], dv[0:rows], 1.0 / 32.0,
             op0=ALU.mult, op1=ALU.mult)
        for ch in range(8):
            pb = p.bankb(ch % 2)[:, 0:rows]
            p.tp(pb, wbf[0:rows, ch * 128:(ch + 1) * 128], idsb)
            p.cp(wv[:, ch, rc * 128: rc * 128 + rows], pb, eng="vector")

    xnw = p.sbuf([1, D], BF16, "xnw")
    xnb = p.sbuf([1, D], BF16, "xnb")
    p.dma(xnw, xnw_h, eng="gpsimd")
    p.dma(xnb, xnb_h, eng="gpsimd")
    ones1 = p.sbuf([1, 128], BF16, "ones1")
    p.memset(ones1, 1.0)
    wb_ = p.sbuf([128, D], F32, "wb_")
    bb_ = p.sbuf([128, D], F32, "bb_")
    for fo in range(2):
        ps = p.bankf(0)
        p.mm(ps, ones1, xnw[:, fo * 512:(fo + 1) * 512], True, True)
        p.cp(wb_[:, fo * 512:(fo + 1) * 512], ps, eng="vector")
        p.mm(ps, ones1, xnb[:, fo * 512:(fo + 1) * 512], True, True)
        p.cp(bb_[:, fo * 512:(fo + 1) * 512], ps, eng="vector")

    xnfm = p.sbuf([128, 8 * NT], BF16, "xnfm")
    xnv = xnfm.rearrange("p (c t) -> p c t", c=8)
    ufm = p.sbuf([128, 8 * NT], BF16, "ufm")
    uv = ufm.rearrange("p (c t) -> p c t", c=8)
    xt = [p.sbuf([128, D], F32, f"xt{i}") for i in range(2)]
    xc = [p.sbuf([128, D], F32, f"xc{i}") for i in range(2)]
    ubf = [p.sbuf([128, D], BF16, f"ubf{i}") for i in range(2)]
    xnbf = [p.sbuf([128, D], BF16, f"xnbf{i}") for i in range(2)]
    xsq = p.sbuf([128, D], F32, "xsq")
    statsall = p.sbuf([128, TT * 8], F32, "stats")
    x3 = x.rearrange("(t p) d -> t p d", p=128)

    def ln(src, dst, st):
        nm, vr, sd, rstd, vr2 = (st[:, i:i + 1] for i in range(5))
        p.red(nm, src, ALU.add)
        p.ts(nm, nm, -1.0 / D, None, op0=ALU.mult)
        p.ts(dst, src, nm, None, op0=ALU.add)
        p.act(xsq[:, 0:512], dst[:, 0:512], AF.Square, accum_out=vr)
        p.act(xsq[:, 512:1024], dst[:, 512:1024], AF.Square, accum_out=vr2)
        p.tt(vr, vr, vr2, ALU.add)
        p.act(sd, vr, AF.Sqrt, bias=p.constf(1e-5), scale=1.0 / D)
        p.recip(rstd, sd)
        p.ts(dst, dst, rstd, None, op0=ALU.mult)

    for t in range(TT):
        bi = t % 2
        p.dma(xt[bi], x3[t])
        st = statsall[:, t * 8:(t + 1) * 8]
        ln(xt[bi], xc[bi], st)
        p.cp(xnbf[bi], xc[bi])
        ln(xc[bi], xt[bi], st)
        p.tt(xt[bi], xt[bi], wb_, ALU.mult)
        p.tt(xt[bi], xt[bi], bb_, ALU.add)
        p.cp(ubf[bi], xt[bi])
        for ch in range(8):
            pb = p.bankb(ch % 2)[:, 0:128]
            p.tp(pb, xnbf[bi][:, ch * 128:(ch + 1) * 128], idsb)
            p.cp(xnv[:, ch, t * 128:(t + 1) * 128], pb, eng="vector")
            pb2 = p.bankb(ch % 2)[:, 128:256]
            p.tp(pb2, ubf[bi][:, ch * 128:(ch + 1) * 128], idsb)
            p.cp(uv[:, ch, t * 128:(t + 1) * 128], pb2, eng="scalar")
    p.dma(o_xnfm.rearrange("c p t -> p c t"), xnv)

    ob = [p.sbuf([128, BLK], BF16, f"ob{i}") for i in range(4)]
    dtrb = p.sbuf([16, NT], F32, "dtrb")
    for nb in range(NBLK):
        tok = slice(nb * BLK, (nb + 1) * BLK)
        for po in range(34):
            rows = min(128, DIP - po * 128)
            ps = p.bankf(po % 4)[0:rows, :]
            p.mm_acc(ps, [(wv[:, ki, po * 128:po * 128 + rows],
                           uv[:, ki, tok]) for ki in range(8)])
            if po < 16:
                o = ob[po % 4]
                p.cp(o, ps, eng="scalar" if po % 2 else "vector")
                p.dma(o_z[po][:, tok], o)
            elif po < 33:
                o = ob[po % 4]
                p.cp(o, ps, eng="scalar" if po % 2 else "vector")
                p.dma(o_xbc[po - 16][:, tok], o)
            else:
                p.cp(dtrb[:, tok], ps, eng="vector")
    p.dma(o_dtr, dtrb)
    return p


def build_l3f2():
    p = Prog("l3f2")
    xh_h = p.dram_in("xh", (4, 128, L), BF16)
    bc_h = p.dram_in("bc", (128, L), BF16)
    cwx_h = p.dram_in("cwx", (4, 128, 4))
    cbx_h = p.dram_in("cbx", (4, 128, 1))
    cwbc_h = p.dram_in("cwbc", (128, 4))
    cbbc_h = p.dram_in("cbbc", (128, 1))
    dtr_h = p.dram_in("dtr", (4, L))
    dtb_h = p.dram_in("dtb", (4, 32, 1))
    alg_h = p.dram_in("alg", (4, 32, 1))
    drep_h = p.dram_in("drep", (4, 128, 1))
    z_h = p.dram_in("z", (4, 128, L), BF16)
    maskt_h = p.dram_in("maskt", (2, 128, 256))
    ident_h = p.dram_in("ident", (128, 128), BF16)
    identf_h = p.dram_in("identf", (128, 128))
    o_g = p.dram_out("g", (4, 128, L), BF16)

    idsb = p.sbuf([128, 128], BF16, "idsb")
    p.dma(idsb, ident_h)
    idf = p.sbuf([128, 128], F32, "idf")
    p.dma(idf, identf_h)
    masks = p.sbuf([128, 2 * 256], F32, "masks")
    p.dma(masks.rearrange("p (m t) -> p m t", m=2),
          maskt_h.rearrange("m p t -> p m t"))
    mk = masks.rearrange("p (m t) -> p m t", m=2)
    ones1 = p.sbuf([1, 128], BF16, "ones1")
    p.memset(ones1, 1.0)
    ones64 = p.sbuf([1, 64], BF16, "ones64")
    p.memset(ones64, 1.0)

    bcin = p.sbuf([128, L], BF16, "bcin")
    p.dma(bcin, bc_h)
    cwbc = p.sbuf([128, 4], F32, "cwbc")
    p.dma(cwbc, cwbc_h)
    cbbc = p.sbuf([128, 1], F32, "cbbc")
    p.dma(cbbc, cbbc_h)
    acc = p.sbuf([128, L], F32, "acc")
    p.ts(acc, bcin, cwbc[:, 3:4], None, op0=ALU.mult)
    for j in (2, 1, 0):
        sh = 3 - j
        p.stt(acc[:, sh:L], bcin[:, 0:L - sh], cwbc[:, j:j + 1],
              acc[:, sh:L], op0=ALU.mult, op1=ALU.add)
    bcf = p.sbuf([128, L], BF16, "bcf")
    p.act(bcf, acc, AF.Silu, bias=cbbc[:, 0:1])
    cfm = p.sbuf([64, L], BF16, "cfm")
    p.cp(cfm, bcf[64:128, :])

    xin = p.sbuf([128, L], BF16, "xin")
    xfm = p.sbuf([128, L], BF16, "xfm")
    zfm = p.sbuf([128, L], BF16, "zfm")
    cwx = p.sbuf([128, 4], F32, "cwx")
    cbx = p.sbuf([128, 1], F32, "cbx")
    dts = p.sbuf([32, 256], F32, "dts")
    sp1 = p.sbuf([32, 256], F32, "sp1")
    sp2 = p.sbuf([32, 256], F32, "sp2")
    dtn = p.sbuf([32, 256], F32, "dtn")
    dA = p.sbuf([32, 256], F32, "dA")
    cum = p.sbuf([32, 256], F32, "cum")
    ncum = p.sbuf([32, 256], F32, "ncum")
    dte = p.sbuf([32, 256], F32, "dte")
    wgt = p.sbuf([32, 256], F32, "wgt")
    cdec = p.sbuf([32, 1], F32, "cdec")
    zer = p.sbuf([32, 256], F32, "zer")
    p.memset(zer, 0.0)
    dtb = p.sbuf([32, 1], F32, "dtb")
    alg = p.sbuf([32, 1], F32, "alg")
    Acol = p.sbuf([32, 1], F32, "Acol")
    drep = p.sbuf([128, 1], F32, "drep")
    dtflat = p.sbuf([1, 256], F32, "dtflat")
    dtflatb = p.sbuf([1, 256], BF16, "dtflatb")
    cumflat = p.sbuf([1, 256], F32, "cumflat")
    cumhib = p.sbuf([1, 256], BF16, "cumhib")
    cumlo = p.sbuf([1, 256], F32, "cumlo")
    cumlob = p.sbuf([1, 256], BF16, "cumlob")
    ncumT = p.sbuf([128, 2 * 32], F32, "ncumT")
    dtT = p.sbuf([128, 2 * 32], F32, "dtT")
    wT = p.sbuf([128, 2 * 32], F32, "wT")
    cdecb = p.sbuf([128, 32], F32, "cdecb")
    cdrow = p.sbuf([1, 32], F32, "cdrow")
    cdrowb = p.sbuf([1, 32], BF16, "cdrowb")
    cdrlo = p.sbuf([1, 32], F32, "cdrlo")
    cdrlob = p.sbuf([1, 32], BF16, "cdrlob")
    Bd = p.sbuf([64, 256], BF16, "Bd")
    Ce = p.sbuf([64, 256], BF16, "Ce")
    xctm = p.sbuf([128, 2 * 128], BF16, "xctm")
    btmw = p.sbuf([128, 2 * 64], BF16, "btmw")
    dec = p.sbuf([128, 256], F32, "dec")
    attb = p.sbuf([128, 256], BF16, "attb")
    hst = [p.sbuf([128, 64], F32, f"hst{i}") for i in range(2)]
    hsb = p.sbuf([128, 64], BF16, "hsb")
    hTb = p.sbuf([64, 128], BF16, "hTb")
    ysb = p.sbuf([128, 256], F32, "ysb")
    zsil = p.sbuf([128, 256], F32, "zsil")
    gsb = p.sbuf([128, 256], BF16, "gsb")

    CH = 256
    for bh in range(4):
        p.dma(xin, xh_h[bh])
        p.dma(zfm, z_h[bh])
        p.dma(cwx, cwx_h[bh])
        p.dma(cbx[:, 0:1], cbx_h[bh])
        p.dma(dtb, dtb_h[bh])
        p.dma(alg, alg_h[bh])
        p.dma(drep, drep_h[bh])
        p.ts(acc, xin, cwx[:, 3:4], None, op0=ALU.mult)
        for j in (2, 1, 0):
            sh = 3 - j
            p.stt(acc[:, sh:L], xin[:, 0:L - sh], cwx[:, j:j + 1],
                  acc[:, sh:L], op0=ALU.mult, op1=ALU.add)
        p.act(xfm, acc, AF.Silu, bias=cbx[:, 0:1])
        p.act(Acol, alg, AF.Exp)
        p.ts(Acol, Acol, -1.0, None, op0=ALU.mult)
        p.dma(dts, dtr_h[bh].rearrange("(c t) -> c t", t=256))
        p.act(sp1, dts, AF.Abs, bias=dtb[:, 0:1])
        p.act(sp1, sp1, AF.Exp, scale=-1.0)
        p.act(sp1, sp1, AF.Ln, bias=p.constf(1.0))
        p.act(sp2, dts, AF.Relu, bias=dtb[:, 0:1])
        p.tt(dtn, sp1, sp2, ALU.add)
        p.ts(dA, dtn, Acol[:, 0:1], None, op0=ALU.mult)
        p.scan(cum, zer, dA, 0.0, op0=ALU.add, op1=ALU.add)
        p.ts(ncum, cum, -1.0, None, op0=ALU.mult)
        p.act(dte, ncum, AF.Exp, bias=cum[:, 255:256])
        p.tt(wgt, dte, dtn, ALU.mult)
        p.act(cdec, cum[:, 255:256], AF.Exp)
        for half in range(2):
            sl = slice(half * 128, (half + 1) * 128)
            pb = p.bankf(0)[:, 0:32]
            p.tp(pb, ncum[:, sl], idf)
            p.cp(ncumT[:, half * 32:(half + 1) * 32], pb, eng="vector")
            p.tp(pb, dtn[:, sl], idf)
            p.cp(dtT[:, half * 32:(half + 1) * 32], pb, eng="vector")
            p.tp(pb, wgt[:, sl], idf)
            p.cp(wT[:, half * 32:(half + 1) * 32], pb, eng="vector")
        pbc = p.bankf(0)[0:1, 0:32]
        p.tp(pbc, cdec, idf)
        p.cp(cdrow, pbc, eng="vector")
        p.cp(cdrowb, cdrow)
        p.tt(cdrlo, cdrow, cdrowb, ALU.subtract)
        p.cp(cdrlob, cdrlo)
        pcb = p.bankf(0)[:, 32:64]
        p.mm(pcb, ones1, cdrowb, True, False)
        p.mm(pcb, ones1, cdrlob, False, True)
        p.cp(cdecb, pcb, eng="vector")
        p.memset(hst[0], 0.0)

        for c in range(32):
            cols = slice(c * CH, (c + 1) * CH)
            p.dma(dtflat, dtn[c:c + 1, :], eng="gpsimd")
            p.cp(dtflatb, dtflat)
            p.dma(cumflat, cum[c:c + 1, :], eng="gpsimd")
            p.cp(cumhib, cumflat)
            p.tt(cumlo, cumflat, cumhib, ALU.subtract)
            p.cp(cumlob, cumlo)
            pb1 = p.bankf(1)[0:64, 0:256]
            p.mm(pb1, ones64, dtflatb, True, True)
            p.tt(Bd, bcf[0:64, cols], pb1, ALU.mult)
            pb2 = p.bankf(1)[0:64, 256:512]
            p.mm(pb2, ones64, cumhib, True, False)
            p.mm(pb2, ones64, cumlob, False, True)
            p.act(Ce, pb2, AF.Exp)
            p.tt(Ce, Ce, cfm[:, cols], ALU.mult)
            for half in range(2):
                sl = slice(c * CH + half * 128, c * CH + (half + 1) * 128)
                pb = p.bankb(0)[:, 0:128]
                p.tp(pb, xfm[:, sl], idsb)
                p.cp(xctm[:, half * 128:(half + 1) * 128], pb, eng="vector")
                pb3 = p.bankb(1)[:, 0:64]
                p.tp(pb3, bcf[0:64, sl], idsb)
                p.cp(btmw[:, half * 64:(half + 1) * 64], pb3, eng="vector")
                p.ts(btmw[:, half * 64:(half + 1) * 64],
                     btmw[:, half * 64:(half + 1) * 64],
                     wT[:, half * 32 + c:half * 32 + c + 1], None,
                     op0=ALU.mult)
            py = p.bankf(3)[:, 0:256]
            pst = p.bankf(4)[:, 0:64]
            for jt in range(2):
                jl = slice(jt * 128, (jt + 1) * 128)
                psc2 = p.bankf(5)[:, 0:256]
                p.mm(psc2, Bd[:, jl], cfm[:, cols], True, True)
                pcum = p.bankf(2)[:, 0:256]
                p.mm(pcum, ones1, cumhib, True, False)
                p.mm(pcum, ones1, cumlob, False, True)
                p.tt(pcum, pcum, mk[:, jt, :], ALU.add)
                p.act(dec, pcum, AF.Exp,
                      bias=ncumT[:, jt * 32 + c:jt * 32 + c + 1])
                p.tt(dec, dec, psc2, ALU.mult)
                p.ts(attb, dec, dtT[:, jt * 32 + c:jt * 32 + c + 1], None,
                     op0=ALU.mult)
                p.mm(py, xctm[:, jl], attb, jt == 0, False)
                p.mm(pst, xctm[:, jl], btmw[:, jt * 64:(jt + 1) * 64],
                     jt == 0, jt == 1)
            h0 = hst[c % 2]
            h1 = hst[(c + 1) % 2]
            p.cp(hsb, h0)
            pbh = p.bankb(0)[0:64, 0:128]
            p.tp(pbh, hsb, idsb)
            p.cp(hTb, pbh, eng="vector")
            p.mm(py, hTb, Ce, False, True)
            p.stt(h1, h0, cdecb[:, c:c + 1], pst, op0=ALU.mult, op1=ALU.add)
            p.stt(ysb, xfm[:, cols], drep[:, 0:1], py, op0=ALU.mult,
                  op1=ALU.add)
            p.act(zsil, zfm[:, cols], AF.Silu)
            p.tt(gsb, ysb, zsil, ALU.mult)
            p.dma(o_g[bh][:, cols], gsb)
    return p


def build_l3f3():
    p = Prog("l3f3")
    g_h = p.dram_in("g", (16, 128, NT), BF16)
    rmsw_h = p.dram_in("rmsw", (128, 16))
    opw_h = p.dram_in("opw", (D, DI))
    ident_h = p.dram_in("ident", (128, 128), BF16)
    o_s = p.dram_out("s", (NT, D), BF16)

    idsb = p.sbuf([128, 128], BF16, "idsb")
    p.dma(idsb, ident_h)
    g = p.sbuf([128, 16 * NT], BF16, "g")
    gv = g.rearrange("p (c t) -> p c t", c=16)
    p.dma(gv, g_h.rearrange("c p t -> p c t"))
    rmsw = p.sbuf([128, 16], F32, "rmsw")
    p.dma(rmsw, rmsw_h)

    wnT = p.sbuf([128, 16 * D], BF16, "wnT")
    wv = wnT.rearrange("p (k n) -> p k n", k=16)
    wrow = p.sbuf([128, DI], F32, "wrow")
    wsq = p.sbuf([128, DI], F32, "wsq")
    wst = p.sbuf([128, 4], F32, "wst")
    wbf = p.sbuf([128, DI], BF16, "wbf")
    for rc in range(8):
        p.dma(wrow, opw_h[rc * 128:(rc + 1) * 128, :])
        rn, rn2, dv_ = wst[:, 0:1], wst[:, 1:2], wst[:, 2:3]
        for q4 in range(4):
            tgt = rn if q4 == 0 else rn2
            p.act(wsq[:, q4 * 512:(q4 + 1) * 512],
                  wrow[:, q4 * 512:(q4 + 1) * 512], AF.Square,
                  accum_out=tgt)
            if q4 > 0:
                p.tt(rn, rn, rn2, ALU.add)
        p.act(rn, rn, AF.Sqrt)
        p.ts(rn, rn, 1e-6, None, op0=ALU.max)
        p.recip(dv_, rn)
        p.ts(wbf, wrow, dv_, None, op0=ALU.mult)
        for ch in range(16):
            pb = p.bankb(ch % 2)[:, 0:128]
            p.tp(pb, wbf[:, ch * 128:(ch + 1) * 128], idsb)
            p.cp(wv[:, ch, rc * 128:(rc + 1) * 128], pb, eng="vector")

    gsq = p.sbuf([128, BLK], BF16, "gsq")
    onesb = p.sbuf([128, 1], BF16, "onesb")
    p.memset(onesb, 1.0)
    rr = p.sbuf([1, BLK], F32, "rr")
    rrb = p.sbuf([1, BLK], BF16, "rrb")
    ghat = p.sbuf([128, 16 * BLK], BF16, "ghat")
    gh = ghat.rearrange("p (c t) -> p c t", c=16)
    rstdb = p.sbuf([128, BLK], F32, "rstdb")
    sbf = p.sbuf([128, D], BF16, "sbf")
    ones1 = p.sbuf([1, 128], BF16, "ones1")
    p.memset(ones1, 1.0)
    s3 = o_s.rearrange("(t p) d -> t p d", p=128)
    for nb in range(NBLK):
        tok = slice(nb * BLK, (nb + 1) * BLK)
        psn = p.bankf(0)[0:1, :]
        for ch in range(16):
            p.act(gsq, gv[:, ch, tok], AF.Square)
            p.op("tensor", lambda a=onesb, b=gsq, ps=psn, st=(ch == 0),
                 sp=(ch == 15): p.nc.tensor.matmul(ps, a, b, start=st,
                                                   stop=sp),
                 reads=[onesb, gsq], writes=[psn])
        p.act(rr, psn, AF.Sqrt, scale=1.0 / DI, bias=p.constf(1e-5))
        p.recip(rr, rr)
        p.cp(rrb, rr)
        prb = p.bankf(1)
        p.mm(prb, ones1, rrb, True, True)
        p.cp(rstdb, prb, eng="scalar")
        for ch in range(16):
            p.tt(gh[:, ch, :], gv[:, ch, tok], rstdb, ALU.mult)
            p.ts(gh[:, ch, :], gh[:, ch, :], rmsw[:, ch:ch + 1], None,
                 op0=ALU.mult)
        for j in range(4):
            t = nb * 4 + j
            for fo in range(2):
                ps = p.bankf(2 + fo)
                fcols = slice(fo * 512, (fo + 1) * 512)
                p.mm_acc(ps, [(gh[:, ki, j * 128:(j + 1) * 128],
                               wv[:, ki, fcols]) for ki in range(16)])
                p.cp(sbf[:, fcols], ps, eng="vector")
            p.dma(s3[t], sbf)
    return p


def build_l3g1():
    p = Prog("l3g1")
    xnfm_h = p.dram_in("xnfm", (8, 128, NT), BF16)
    wln1_h = p.dram_in("wln1", (128, 8))
    wq_h = p.dram_in("wq", (D, D))
    wkva_h = p.dram_in("wkva", (D, 160))
    wkvb_h = p.dram_in("wkvb", (128, 512))
    cs_h = p.dram_in("cs", (NT, 32))
    ident_h = p.dram_in("ident", (128, 128), BF16)
    o_q = p.dram_out("q", (8, 128, NT), BF16)
    o_kv = p.dram_out("kv", (4, 128, NT), BF16)
    o_kr = p.dram_out("kr", (32, NT), BF16)

    idsb = p.sbuf([128, 128], BF16, "idsb")
    p.dma(idsb, ident_h)
    xnfm = p.sbuf([128, 8 * NT], BF16, "xnfm")
    p.dma(xnfm.rearrange("p (c t) -> p c t", c=8),
          xnfm_h.rearrange("c p t -> p c t"))
    xnv = xnfm.rearrange("p (c t) -> p c t", c=8)
    wln1 = p.sbuf([128, 8], F32, "wln1")
    p.dma(wln1, wln1_h)
    wq = p.wload(wq_h, D, D, "wq")
    wkva = p.wload(wkva_h, D, 160, "wkva")
    wkvb = p.wload(wkvb_h, 128, 512, "wkvb")
    cs = p.sbuf([128, TT * 32], F32, "cs")
    p.dma(cs.rearrange("p (t k) -> p t k", t=TT),
          cs_h.rearrange("(t p) k -> p t k", p=128))
    csv = cs.rearrange("p (t k) -> p t k", t=TT)

    xn1 = p.sbuf([128, 8 * NT], BF16, "xn1")
    x1v = xn1.rearrange("p (c t) -> p c t", c=8)
    xsq = p.sbuf([128, BLK], BF16, "xsq")
    onesb = p.sbuf([128, 1], BF16, "onesb")
    p.memset(onesb, 1.0)
    ones1 = p.sbuf([1, 128], BF16, "ones1")
    p.memset(ones1, 1.0)
    rr = p.sbuf([1, BLK], F32, "rr")
    rrb = p.sbuf([1, BLK], BF16, "rrb")
    rstdb = p.sbuf([128, BLK], F32, "rstdb")
    for nb in range(NBLK):
        tok = slice(nb * BLK, (nb + 1) * BLK)
        psn = p.bankf(0)[0:1, :]
        for ch in range(8):
            p.act(xsq, xnv[:, ch, tok], AF.Square)
            p.op("tensor", lambda a=onesb, b=xsq, ps=psn, st=(ch == 0),
                 sp=(ch == 7): p.nc.tensor.matmul(ps, a, b, start=st, stop=sp),
                 reads=[onesb, xsq], writes=[psn])
        p.act(rr, psn, AF.Sqrt, scale=1.0 / D, bias=p.constf(1e-5))
        p.recip(rr, rr)
        p.cp(rrb, rr)
        prb = p.bankf(1)
        p.mm(prb, ones1, rrb, True, True)
        p.cp(rstdb, prb, eng="scalar")
        for ch in range(8):
            p.tt(x1v[:, ch, tok], xnv[:, ch, tok], rstdb, ALU.mult)
            p.ts(x1v[:, ch, tok], x1v[:, ch, tok], wln1[:, ch:ch + 1], None,
                 op0=ALU.mult)

    qfm = p.sbuf([128, 8 * NT], BF16, "qfm")
    qv = qfm.rearrange("p (c t) -> p c t", c=8)
    rtm = p.sbuf([128, 64], BF16, "rtm")
    rot = p.sbuf([128, 32], BF16, "rot")
    for nb in range(NBLK):
        tok = slice(nb * BLK, (nb + 1) * BLK)
        for po in range(8):
            ps = p.bankf(po % 2)
            p.mm_acc(ps, [(wq[:, ki, po * 128:(po + 1) * 128],
                           xnv2 := x1v[:, ki, tok]) for ki in range(8)])
            p.smul(qv[:, po, tok], ps, 0.125)
    tmpf = p.sbuf([128, 16], F32, "tmpf")
    for h in range(16):
        ch = h // 2
        rw = (h % 2) * 64
        for t in range(TT):
            tok = slice(t * 128, (t + 1) * 128)
            pb = p.bankb(0)[:, 0:64]
            p.tp(pb, qv[rw:rw + 64, ch, tok], idsb)
            p.cp(rtm, pb, eng="vector")
            xe = rtm[:, 32:64:2]
            xo = rtm[:, 33:64:2]
            co = csv[:, t, 0:16]
            si = csv[:, t, 16:32]
            re_ = rot[:, 0:32:2]
            ro_ = rot[:, 1:32:2]
            p.tt(re_, xe, co, ALU.mult)
            p.tt(ro_, xo, co, ALU.mult)
            p.tt(tmpf, xo, si, ALU.mult)
            p.tt(re_, re_, tmpf, ALU.subtract)
            p.tt(tmpf, xe, si, ALU.mult)
            p.tt(ro_, ro_, tmpf, ALU.add)
            p.cp(rtm[:, 32:64], rot)
            pb2 = p.bankb(1)[0:64, 0:128]
            p.tp(pb2, rtm, idsb)
            p.cp(qv[rw:rw + 64, ch, tok], pb2, eng="vector")
    p.dma(o_q.rearrange("c p t -> p c t"), qv)

    ckv = p.sbuf([128, 2 * NT], BF16, "ckv")
    cv = ckv.rearrange("p (c t) -> p c t", c=2)
    for nb in range(NBLK):
        tok = slice(nb * BLK, (nb + 1) * BLK)
        ps = p.bankf(0)
        p.mm_acc(ps, [(wkva[:, ki, 0:128], x1v[:, ki, tok])
                      for ki in range(8)])
        p.cp(cv[:, 0, tok], ps, eng="vector")
        ps2 = p.bankf(1)[0:32, :]
        p.mm_acc(ps2, [(wkva[:, ki, 128:160], x1v[:, ki, tok])
                       for ki in range(8)])
        p.cp(cv[0:32, 1, tok], ps2, eng="vector")
        for po in range(4):
            ps3 = p.bankf(2 + po % 2)
            p.mm(ps3, wkvb[:, 0, po * 128:(po + 1) * 128], cv[:, 0, tok],
                 True, True)
            p.op("vector", lambda o=None, ps_=ps3, dst=o_kv, poo=po, tk=tok:
                 None) if False else None
            ob = p.sbuf([128, BLK], BF16)
            p.cp(ob, ps3, eng="scalar")
            p.dma(o_kv[po][:, tok], ob)
    for t in range(TT):
        tok = slice(t * 128, (t + 1) * 128)
        pb = p.bankb(0)[:, 0:32]
        p.tp(pb, cv[0:32, 1, tok], idsb)
        p.cp(rtm[:, 0:32], pb, eng="vector")
        xe = rtm[:, 0:32:2]
        xo = rtm[:, 1:32:2]
        co = csv[:, t, 0:16]
        si = csv[:, t, 16:32]
        re_ = rot[:, 0:32:2]
        ro_ = rot[:, 1:32:2]
        p.tt(re_, xe, co, ALU.mult)
        p.tt(ro_, xo, co, ALU.mult)
        p.tt(tmpf, xo, si, ALU.mult)
        p.tt(re_, re_, tmpf, ALU.subtract)
        p.tt(tmpf, xe, si, ALU.mult)
        p.tt(ro_, ro_, tmpf, ALU.add)
        pb2 = p.bankb(1)[0:32, 0:128]
        p.tp(pb2, rot, idsb)
        krb = p.sbuf([32, 128], BF16)
        p.cp(krb, pb2, eng="vector")
        p.dma(o_kr[:, tok], krb)
    return p


def build_l3g2():
    p = Prog("l3g2")
    q_h = p.dram_in("q", (4, 64, L), BF16)
    k_h = p.dram_in("k", (4, 64, L), BF16)
    v_h = p.dram_in("v", (4, 32, L), BF16)
    masks_h = p.dram_in("masks", (4, 128, 512))
    ident_h = p.dram_in("ident", (128, 128), BF16)
    o_out = p.dram_out("out", (4, 32, L), BF16)

    idsb = p.sbuf([128, 128], BF16, "idsb")
    p.dma(idsb, ident_h)
    masks = p.sbuf([128, 4 * 512], F32, "masks")
    p.dma(masks.rearrange("p (m t) -> p m t", m=4),
          masks_h.rearrange("m p t -> p m t"))
    mk = masks.rearrange("p (m t) -> p m t", m=4)
    q = p.sbuf([64, L], BF16, "q")
    k = p.sbuf([64, L], BF16, "k")
    vtm = p.sbuf([128, 64 * 32], BF16, "vtm")
    vv = vtm.rearrange("p (c n) -> p c n", c=64)
    vb = p.sbuf([32, L], BF16, "vb")
    PT = p.sbuf([128, 512], BF16, "PT")
    Pb = p.sbuf([128, 512], BF16, "Pb")
    m_ = [p.sbuf([128, 1], F32, f"m{i}") for i in range(2)]
    l_ = [p.sbuf([128, 1], F32, f"l{i}") for i in range(2)]
    ac = [p.sbuf([128, 32], F32, f"ac{i}") for i in range(2)]
    mb = p.sbuf([128, 1], F32, "mb")
    nmn = p.sbuf([128, 1], F32, "nmn")
    alp = p.sbuf([128, 1], F32, "alp")
    sblk = p.sbuf([128, 1], F32, "sblk")
    rl = p.sbuf([128, 1], F32, "rl")
    otb = p.sbuf([128, 32], BF16, "otb")

    for bh in range(4):
        p.dma(q, q_h[bh])
        p.dma(k, k_h[bh])
        p.dma(vb, v_h[bh])
        for kc in range(64):
            pb = p.bankb(0)[:, 0:32]
            p.tp(pb, vb[:, kc * 128:(kc + 1) * 128], idsb)
            p.cp(vv[:, kc, :], pb, eng="vector")
        for qt in range(64):
            nkb = qt // 4 + 1
            p.memset(m_[0], -1.0e30)
            p.memset(l_[0], 0.0)
            p.memset(ac[0], 0.0)
            for kb in range(nkb):
                pi = kb % 2
                po = (kb + 1) % 2
                pS = p.bankf(pi)
                p.mm(pS, q[:, qt * 128:(qt + 1) * 128],
                     k[:, kb * 512:(kb + 1) * 512], True, True)
                if kb == qt // 4:
                    p.tt(pS, pS, mk[:, qt % 4, :], ALU.add)
                p.red(mb, pS, ALU.max)
                p.tt(mb, mb, m_[pi], ALU.max)
                p.ts(nmn, mb, -1.0, None, op0=ALU.mult)
                p.act(alp, m_[pi], AF.Exp, bias=nmn)
                p.act(Pb, pS, AF.Exp, bias=nmn, accum_out=sblk)
                p.stt(l_[po], l_[pi], alp, sblk, op0=ALU.mult, op1=ALU.add)
                p.cp(m_[po], mb)
                pacc = p.bankf(2 + pi)[:, 0:32]
                for kc in range(4):
                    pbt = p.bankb(1)[:, 0:128]
                    p.tp(pbt, Pb[:, kc * 128:(kc + 1) * 128], idsb)
                    p.cp(PT[:, kc * 128:(kc + 1) * 128], pbt, eng="scalar")
                    p.mm(pacc, PT[:, kc * 128:(kc + 1) * 128],
                         vv[:, kb * 4 + kc, :], kc == 0, kc == 3)
                p.stt(ac[po], ac[pi], alp, pacc, op0=ALU.mult, op1=ALU.add)
            fin = nkb % 2
            p.recip(rl, l_[fin])
            p.ts(otb, ac[fin], rl, None, op0=ALU.mult)
            pb2 = p.bankb(0)[0:32, 0:128]
            p.tp(pb2, otb, idsb)
            otf = p.sbuf([32, 128], BF16)
            p.cp(otf, pb2, eng="vector")
            p.dma(o_out[bh][:, qt * 128:(qt + 1) * 128], otf)
    return p


def build_l3g3():
    p = Prog("l3g3")
    y_h = p.dram_in("y", (4, 128, NT), BF16)
    wonz_h = p.dram_in("wonz", (512, D))
    xnfm_h = p.dram_in("xnfm", (8, 128, NT), BF16)
    x_h = p.dram_in("x", (NT, D))
    wln2_h = p.dram_in("wln2", (1, D))
    wgate_h = p.dram_in("wgate", (D, 8))
    ebias_h = p.dram_in("ebias", (1, 8))
    wgm_h = p.dram_in("wgm", (9, D, 512))
    wum_h = p.dram_in("wum", (9, D, 512))
    wdm_h = p.dram_in("wdm", (9, 512, D))
    s_h = p.dram_in("s", (NT, D), BF16)
    ident_h = p.dram_in("ident", (128, 128), BF16)
    o_x = p.dram_out("xout", (NT, D))

    idsb = p.sbuf([128, 128], BF16, "idsb")
    p.dma(idsb, ident_h)
    y = p.sbuf([128, 4 * NT], BF16, "y")
    yv = y.rearrange("p (c t) -> p c t", c=4)
    p.dma(yv, y_h.rearrange("c p t -> p c t"))
    wonz = p.wload(wonz_h, 512, D, "wonz")
    xnfm = p.sbuf([128, 8 * NT], BF16, "xnfm")
    p.dma(xnfm.rearrange("p (c t) -> p c t", c=8),
          xnfm_h.rearrange("c p t -> p c t"))
    xnv = xnfm.rearrange("p (c t) -> p c t", c=8)
    wgate = p.wload(wgate_h, D, 8, "wgate")
    ones1 = p.sbuf([1, 128], BF16, "ones1")
    p.memset(ones1, 1.0)
    wln2 = p.sbuf([1, D], BF16, "wln2")
    p.dma(wln2, wln2_h, eng="gpsimd")
    ebias = p.sbuf([1, 8], F32, "ebias")
    p.dma(ebias, ebias_h)
    ebb = p.sbuf([128, 8], F32, "ebb")
    ebhib = p.sbuf([1, 8], BF16, "ebhib")
    eblo = p.sbuf([1, 8], F32, "eblo")
    eblob = p.sbuf([1, 8], BF16, "eblob")
    p.cp(ebhib, ebias)
    p.tt(eblo, ebias, ebhib, ALU.subtract)
    p.cp(eblob, eblo)
    peb = p.bankf(0)[:, 0:8]
    p.mm(peb, ones1, ebhib, True, False)
    p.mm(peb, ones1, eblob, False, True)
    p.cp(ebb, peb)
    wlnb = p.sbuf([128, D], F32, "wlnb")
    for fo in range(2):
        ps = p.bankf(1)
        p.mm(ps, ones1, wln2[:, fo * 512:(fo + 1) * 512], True, True)
        p.cp(wlnb[:, fo * 512:(fo + 1) * 512], ps, eng="vector")

    x1i = p.sbuf([128, TT * D], BF16, "x1i")
    x1v = x1i.rearrange("p (t d) -> p t d", t=TT)
    xn2 = p.sbuf([128, 8 * NT], BF16, "xn2")
    x2v = xn2.rearrange("p (c t) -> p c t", c=8)
    xt = p.sbuf([128, D], F32, "xt")
    xsq = p.sbuf([128, D], F32, "xsq")
    st8 = p.sbuf([128, 8], F32, "st8")
    xnbf = p.sbuf([128, D], BF16, "xnbf")
    x3 = x_h.rearrange("(t p) d -> t p d", p=128)
    for t in range(TT):
        tok = slice(t * 128, (t + 1) * 128)
        p.dma(xt, x3[t])
        nm, vr, sd, rstd, vr2 = (st8[:, i:i + 1] for i in range(5))
        p.red(nm, xt, ALU.add)
        p.ts(nm, nm, -1.0 / D, None, op0=ALU.mult)
        p.ts(xt, xt, nm, None, op0=ALU.add)
        p.act(xsq[:, 0:512], xt[:, 0:512], AF.Square, accum_out=vr)
        p.act(xsq[:, 512:1024], xt[:, 512:1024], AF.Square, accum_out=vr2)
        p.tt(vr, vr, vr2, ALU.add)
        p.act(sd, vr, AF.Sqrt, bias=p.constf(1e-5), scale=1.0 / D)
        p.recip(rstd, sd)
        p.ts(xt, xt, rstd, None, op0=ALU.mult)
        for fo in range(2):
            ps = p.bankf(2 + fo)
            fcols = slice(fo * 512, (fo + 1) * 512)
            p.mm_acc(ps, [(yv[:, ki, tok], wonz[:, ki, fcols])
                          for ki in range(4)])
            p.tt(x1v[:, t, fcols], ps, xt[:, fcols], ALU.add)
        x1t = x1v[:, t, :]
        p.act(xsq[:, 0:512], x1t[:, 0:512], AF.Square, accum_out=vr)
        p.act(xsq[:, 512:1024], x1t[:, 512:1024], AF.Square, accum_out=vr2)
        p.tt(vr, vr, vr2, ALU.add)
        p.act(sd, vr, AF.Sqrt, bias=p.constf(1e-5), scale=1.0 / D)
        p.recip(rstd, sd)
        p.ts(xsq, x1t, rstd, None, op0=ALU.mult)
        p.tt(xnbf, xsq, wlnb, ALU.mult)
        for ch in range(8):
            pb = p.bankb(ch % 2)[:, 0:128]
            p.tp(pb, xnbf[:, ch * 128:(ch + 1) * 128], idsb)
            p.cp(x2v[:, ch, tok], pb, eng="vector")

    cwall = p.sbuf([128, TT * 8], F32, "cwall")
    iot8m = p.sbuf([128, 8], F32, "iot8m")
    p.iota(iot8m, [[1, 8]], base=-8, channel_multiplier=0)
    gp = p.sbuf([128, 8], F32, "gp")
    gp2 = p.sbuf([128, 8], F32, "gp2")
    cand = p.sbuf([128, 8], F32, "cand")
    mio = p.sbuf([128, 8], F32, "mio")
    oh1 = p.sbuf([128, 8], F32, "oh1")
    oh2 = p.sbuf([128, 8], F32, "oh2")
    v1 = p.sbuf([128, 1], F32, "v1")
    v2 = p.sbuf([128, 1], F32, "v2")
    i1 = p.sbuf([128, 1], F32, "i1")
    i2 = p.sbuf([128, 1], F32, "i2")
    r1 = p.sbuf([128, 1], F32, "r1")
    r2 = p.sbuf([128, 1], F32, "r2")
    vs = p.sbuf([128, 1], F32, "vs")
    BIG = 1.0e9
    for t in range(TT):
        tok = slice(t * 128, (t + 1) * 128)
        ps2 = p.bankf(0)[:, 0:8]
        p.mm_acc(ps2, [(x2v[:, ki, tok], wgate[:, ki, :]) for ki in range(8)])
        p.tt(gp, ps2, ebb, ALU.add)
        p.red(r1, gp, ALU.max)
        p.ts(r1, r1, -1.0, None, op0=ALU.mult)
        p.act(gp, gp, AF.Exp, bias=r1, accum_out=r2)
        p.recip(r2, r2)
        p.ts(gp, gp, r2, None, op0=ALU.mult)
        p.red(v1, gp, ALU.max)
        p.ts(cand, gp, v1, None, op0=ALU.is_ge)
        p.tt(mio, cand, iot8m, ALU.mult)
        p.ts(mio, mio, 8.0, None, op0=ALU.add)
        p.red(i1, mio, ALU.min)
        p.ts(i1, i1, -8.0, None, op0=ALU.add)
        p.ts(oh1, iot8m, i1, None, op0=ALU.is_equal)
        p.stt(gp2, oh1, -BIG, gp, op0=ALU.mult, op1=ALU.add)
        p.red(v2, gp2, ALU.max)
        p.ts(cand, gp2, v2, None, op0=ALU.is_ge)
        p.tt(mio, cand, iot8m, ALU.mult)
        p.ts(mio, mio, 8.0, None, op0=ALU.add)
        p.red(i2, mio, ALU.min)
        p.ts(i2, i2, -8.0, None, op0=ALU.add)
        p.ts(oh2, iot8m, i2, None, op0=ALU.is_equal)
        p.tt(vs, v1, v2, ALU.add)
        p.recip(vs, vs)
        cw = cwall[:, t * 8:(t + 1) * 8]
        p.ts(cw, oh1, v1, None, op0=ALU.mult)
        p.stt(cw, oh2, v2, cw, op0=ALU.mult, op1=ALU.add)
        p.ts(cw, cw, vs, None, op0=ALU.mult)

    acc = p.sbuf([128, TT * D], BF16, "acc")
    accv = acc.rearrange("p (t d) -> p t d", t=TT)
    wgm = p.sbuf([128, 8 * 512], BF16, "wgm")
    wgv = wgm.rearrange("p (k n) -> p k n", k=8)
    wum = p.sbuf([128, 8 * 512], BF16, "wum")
    wuv = wum.rearrange("p (k n) -> p k n", k=8)
    wdm = p.sbuf([128, 4 * D], BF16, "wdm")
    wdv = wdm.rearrange("p (k n) -> p k n", k=4)
    geb = p.sbuf([128, 4 * BLK], BF16, "geb")
    gev = geb.rearrange("p (k n) -> p k n", k=4)
    silb = p.sbuf([128, BLK], F32, "silb")
    for e in range(9):
        p.dma(wgv, wgm_h[e].rearrange("(k p) n -> p k n", p=128), eng="gpsimd")
        p.dma(wuv, wum_h[e].rearrange("(k p) n -> p k n", p=128), eng="gpsimd")
        p.dma(wdv, wdm_h[e].rearrange("(k p) n -> p k n", p=128), eng="gpsimd")
        for nb in range(NBLK):
            tok = slice(nb * BLK, (nb + 1) * BLK)
            for po in range(4):
                psg = p.bankf(po % 2)
                p.mm_acc(psg, [(wgv[:, ki, po * 128:(po + 1) * 128],
                                x2v[:, ki, tok]) for ki in range(8)])
                p.act(silb, psg, AF.Silu)
                psu = p.bankf(2 + po % 2)
                p.mm_acc(psu, [(wuv[:, ki, po * 128:(po + 1) * 128],
                                x2v[:, ki, tok]) for ki in range(8)])
                p.tt(gev[:, po, :], silb, psu, ALU.mult)
            for j in range(4):
                tg = nb * 4 + j
                for fo in range(2):
                    ps = p.bankf(4 + fo)
                    fcols = slice(fo * BLK, (fo + 1) * BLK)
                    p.mm_acc(ps, [(gev[:, ki, j * 128:(j + 1) * 128],
                                   wdv[:, ki, fcols]) for ki in range(4)])
                    dst = accv[:, tg, fcols]
                    if e == 0:
                        p.ts(dst, ps, cwall[:, tg * 8:tg * 8 + 1], None,
                             op0=ALU.mult)
                    elif e < 8:
                        p.stt(dst, ps, cwall[:, tg * 8 + e:tg * 8 + e + 1],
                              dst, op0=ALU.mult, op1=ALU.add)
                    else:
                        p.stt(dst, ps, 1.0, dst, op0=ALU.mult, op1=ALU.add)

    stl = p.sbuf([128, D], BF16, "stl")
    gtl = xsq
    sS = s_h.rearrange("(t p) d -> t p d", p=128)
    o3 = o_x.rearrange("(t p) d -> t p d", p=128)
    for t in range(TT):
        p.dma(xt, x3[t])
        p.dma(stl, sS[t], eng="gpsimd")
        p.tt(gtl, accv[:, t, :], x1v[:, t, :], ALU.add)
        p.act(gtl, gtl, AF.Silu)
        p.tt(gtl, gtl, stl, ALU.mult)
        p.tt(gtl, gtl, xt, ALU.add)
        p.dma(o3[t], gtl)
    return p


def run_l3(inp, x_full):
    ident, slt = _consts()
    identf = np.eye(128, dtype=np.float32)
    xs = [np.ascontiguousarray(x_full[c * NT:(c + 1) * NT]) for c in range(NCORES)]
    im = [dict(x=xs[c], ipw=inp["L3_attn_in_proj_w"],
               xnw=np.ascontiguousarray(inp["L3_attn_x_norm_w"][None, :]),
               xnb=np.ascontiguousarray(inp["L3_attn_x_norm_b"][None, :]),
               ident=ident) for c in range(NCORES)]
    rf1 = _run("l3f1", build_l3f1, im)

    xbc_full = np.concatenate([r["xbc"] for r in rf1], axis=2).reshape(17 * 128, NTOK)
    z_full = np.concatenate([r["z"] for r in rf1], axis=2).reshape(DI, NTOK)
    dtr_full = np.concatenate([r["dtr"] for r in rf1], axis=1)
    cw_all = inp["L3_attn_conv_w"]
    cb_all = inp["L3_attn_conv_b"]
    maskt = np.full((2, 128, 256), -1.0e30, dtype=np.float32)
    for pp in range(128):
        maskt[0, pp, pp:] = 0.0
        maskt[1, pp, 128 + pp:] = 0.0
    im = []
    for c in range(NCORES):
        b = c // 4
        hs = [4 * (c % 4) + kk for kk in range(4)]
        colb = slice(b * L, (b + 1) * L)
        xh = np.stack([xbc_full[128 * h:128 * (h + 1), colb] for h in hs])
        bc = xbc_full[2048:2176, colb]
        cwx = np.stack([cw_all[128 * h:128 * (h + 1)] for h in hs])
        cbx = np.stack([cb_all[128 * h:128 * (h + 1), None] for h in hs])
        zz = np.stack([z_full[128 * h:128 * (h + 1), colb] for h in hs])
        dtr = np.stack([dtr_full[h, colb] for h in hs])
        dtb = np.stack([np.full((32, 1), inp["L3_attn_dt_bias"][h],
                                np.float32) for h in hs])
        alg = np.stack([np.full((32, 1), inp["L3_attn_A_log"][h], np.float32)
                        for h in hs])
        drep = np.stack([np.full((128, 1), inp["L3_attn_D"][h], np.float32)
                         for h in hs])
        im.append(dict(xh=np.ascontiguousarray(xh),
                       bc=np.ascontiguousarray(bc), cwx=cwx, cbx=cbx,
                       cwbc=np.ascontiguousarray(cw_all[2048:2176]),
                       cbbc=np.ascontiguousarray(cb_all[2048:2176, None]),
                       dtr=np.ascontiguousarray(dtr), dtb=dtb, alg=alg,
                       drep=drep, z=np.ascontiguousarray(zz), maskt=maskt,
                       ident=ident, identf=identf))
    rf2 = _run("l3f2", build_l3f2, im)

    g_full = np.zeros((DI, NTOK), dtype=BF)
    for c in range(NCORES):
        b = c // 4
        for kk in range(4):
            h = 4 * (c % 4) + kk
            g_full[128 * h:128 * (h + 1), b * L:(b + 1) * L] = rf2[c]["g"][kk]
    rmsw = np.ascontiguousarray(inp["L3_attn_rms_w"].reshape(16, 128).T)
    im = []
    for c in range(NCORES):
        cols = slice(c * NT, (c + 1) * NT)
        im.append(dict(g=np.ascontiguousarray(
                           g_full[:, cols].reshape(16, 128, NT)),
                       rmsw=rmsw, opw=inp["L3_attn_out_proj_w"], ident=ident))
    rf3 = _run("l3f3", build_l3f3, im)

    THETA = 1.0e5
    freqs = (1.0 / THETA ** (np.arange(0, 32, 2, dtype=np.float32) / 32.0))
    im = []
    for c in range(NCORES):
        pos = (np.arange(NT, dtype=np.float32) + (c % 4) * NT)
        ang = np.float32(pos[:, None]) * np.float32(freqs[None, :])
        cs = np.concatenate([np.cos(ang), np.sin(ang)], axis=1).astype(np.float32)
        im.append(dict(xnfm=rf1[c]["xnfm"],
                       wln1=np.ascontiguousarray(
                           inp["L3_ffn_w_ln1"].reshape(8, 128).T),
                       wq=inp["L3_ffn_Wq"], wkva=inp["L3_ffn_Wkva"],
                       wkvb=inp["L3_ffn_Wkvb"], cs=cs, ident=ident))
    rg1 = _run("l3g1", build_l3g1, im)

    q_full = np.concatenate([r["q"] for r in rg1], axis=2).reshape(D, NTOK)
    kv_full = np.concatenate([r["kv"] for r in rg1], axis=2).reshape(512, NTOK)
    kr_full = np.concatenate([r["kr"] for r in rg1], axis=1)
    masks4 = np.zeros((4, 128, 512), dtype=np.float32)
    for m4 in range(4):
        for pp in range(128):
            masks4[m4, pp, m4 * 128 + pp + 1:] = -1.0e30
    im = []
    for c in range(NCORES):
        b = c // 4
        colb = slice(b * L, (b + 1) * L)
        qs, ks, vsl = [], [], []
        for kk in range(4):
            h = 4 * (c % 4) + kk
            qs.append(q_full[64 * h:64 * (h + 1), colb])
            ks.append(np.concatenate([kv_full[32 * h:32 * (h + 1), colb],
                                      kr_full[:, colb]], axis=0))
            vsl.append(kv_full[32 * h:32 * (h + 1), colb])
        im.append(dict(q=np.ascontiguousarray(np.stack(qs)),
                       k=np.ascontiguousarray(np.stack(ks)),
                       v=np.ascontiguousarray(np.stack(vsl)),
                       masks=masks4, ident=ident))
    rg2 = _run("l3g2", build_l3g2, im)

    yout_full = np.zeros((512, NTOK), dtype=BF)
    for c in range(NCORES):
        b = c // 4
        for kk in range(4):
            h = 4 * (c % 4) + kk
            yout_full[32 * h:32 * (h + 1), b * L:(b + 1) * L] = rg2[c]["out"][kk]
    wonz = np.ascontiguousarray(
        inp["L3_ffn_Wo"].reshape(16, 64, D)[:, :32].reshape(512, D))
    wgm = np.concatenate([inp["L3_ffn_Wg"], inp["L3_ffn_sWg"]], axis=0)
    wum = np.concatenate([inp["L3_ffn_Wu"], inp["L3_ffn_sWu"]], axis=0)
    wdm = np.concatenate([inp["L3_ffn_Wd"], inp["L3_ffn_sWd"]], axis=0)
    im = []
    for c in range(NCORES):
        cols = slice(c * NT, (c + 1) * NT)
        im.append(dict(
            y=np.ascontiguousarray(yout_full[:, cols].reshape(4, 128, NT)),
            wonz=wonz, xnfm=rf1[c]["xnfm"], x=xs[c],
            wln2=np.ascontiguousarray(inp["L3_ffn_w_ln2"][None, :]),
            wgate=inp["L3_ffn_Wgate"],
            ebias=np.ascontiguousarray(inp["L3_ffn_expert_bias"][None, :]),
            wgm=wgm, wum=wum, wdm=wdm, s=rf3[c]["s"], ident=ident))
    rg3 = _run("l3g3", build_l3g3, im)
    x4 = np.concatenate([r["xout"] for r in rg3], axis=0)
    return x4, rf3




def _consts():
    ident = np.eye(128, dtype=BF)
    slt = (np.arange(128)[:, None] < np.arange(128)[None, :]).astype(BF)
    return ident, slt


def run_l0(inp, x_full):
    ident, slt = _consts()
    med = np.ascontiguousarray(inp["L0_attn_mediator"][0])
    xs = [np.ascontiguousarray(x_full[c * NT:(c + 1) * NT]) for c in range(NCORES)]

    im_a = [dict(x=xs[c], wq=inp["L0_attn_Wq"], wkv=inp["L0_attn_Wkv"],
                 med=med, wg=inp["L0_ffn_Wg"], ident=ident)
            for c in range(NCORES)]
    ra = _run("l0a", build_l0a, im_a)

    mx_all = np.stack([r["mx"] for r in ra])
    se_all = np.stack([r["se"] for r in ra])
    accv_all = np.stack([r["accv"] for r in ra]).reshape(8, 64, 16, 64)
    S_full = np.concatenate([r["S"] for r in ra], axis=1)

    im_b = []
    for c in range(NCORES):
        b = c // 4
        cs = slice(4 * b, 4 * b + 4)
        im_b.append(dict(
            qfm=ra[c]["qfm"], med=med, wo=inp["L0_attn_Wo"],
            bo=np.ascontiguousarray(inp["L0_attn_bo"][None, :]),
            mx=np.ascontiguousarray(mx_all[cs].transpose(2, 1, 0)),
            se=np.ascontiguousarray(se_all[cs].transpose(2, 1, 0)),
            accv=np.ascontiguousarray(accv_all[cs].transpose(2, 0, 1, 3)),
            ident=ident))
    rb = _run("l0b", build_l0b, im_b)

    S_pm = np.ascontiguousarray(S_full.reshape(8, 128, 128))
    im_c = []
    for c in range(NCORES):
        colsel = np.zeros((128, 16), dtype=np.float32)
        for t in range(16):
            colsel[c * 16 + t, t] = 1.0
        b1t = np.ascontiguousarray(
            inp["L0_ffn_b1"].reshape(8, 32, 128).transpose(0, 2, 1))
        im_c.append(dict(
            xnfm=ra[c]["xnfm"], S=S_pm, colsel=colsel, s=rb[c]["s"], x=xs[c],
            w1=inp["L0_ffn_W1"], b1=b1t, w2=inp["L0_ffn_W2"],
            b2=inp["L0_ffn_b2"], slt=slt))
    rc = _run("l0c", build_l0c, im_c)
    return np.concatenate([r["xout"] for r in rc], axis=0), ra, rb


def kernel(**inputs):
    inp = {k: np.ascontiguousarray(np.asarray(v, dtype=np.float32))
           for k, v in inputs.items()}
    x = np.ascontiguousarray(inp["L0_attn_x"].reshape(NTOK, D))
    x1, _, _ = run_l0(inp, x)
    x2, _ = run_l1(inp, x1)
    x3, _ = run_l2(inp, x2)
    x4, _ = run_l3(inp, x3)
    return x4.reshape(B, L, D)


# version 03
# speedup vs baseline: 1.1021x; 1.1021x over previous
import numpy as np
import ml_dtypes
import concourse.bass as bass
import concourse.mybir as mybir
from concourse.bass_utils import run_bass_kernel_spmd

F32 = mybir.dt.float32
BF16 = mybir.dt.bfloat16
AF = mybir.ActivationFunctionType
ALU = mybir.AluOpType
AX = mybir.AxisListType.X
BF = ml_dtypes.bfloat16

NCORES = 8
B = 2
L = 8192
D = 1024
NTOK = B * L
NT = NTOK // NCORES
TT = NT // 128
BLK = 512
NBLK = NT // BLK


def _span(ap):
    try:
        lo = hi = int(ap.offset)
    except Exception:
        return None
    for step, count in ap.ap:
        try:
            step = int(step)
            count = int(count)
        except Exception:
            return None
        d = step * (count - 1)
        if d >= 0:
            hi += d
        else:
            lo += d
    return lo, hi + 1


class _Op:
    __slots__ = ("eng", "stream", "fn", "idx", "marked", "inc_no", "waits")

    def __init__(self, eng, stream, fn, idx):
        self.eng = eng
        self.stream = stream
        self.fn = fn
        self.idx = idx
        self.marked = False
        self.inc_no = 0
        self.waits = {}


class Prog:

    def __init__(self, name):
        self.nc = bass.Bass("TRN2", target_bir_lowering=False)
        self.name = name
        self._i = 0
        self.ops = []
        self._writes = {}
        self._reads = {}
        self._psum_names = set()
        self._flushed = False
        self._bankf = []
        self._bankb = []
        self._consts = {}

    def dram_in(self, name, shape, dtype=F32):
        return self.nc.dram_tensor(name, [int(s) for s in shape], dtype,
                                   kind="ExternalInput").ap()

    def dram_out(self, name, shape, dtype=F32):
        return self.nc.dram_tensor(name, [int(s) for s in shape], dtype,
                                   kind="ExternalOutput").ap()

    def sbuf(self, shape, dtype=F32, name=None):
        self._i += 1
        t = self.nc.alloc_sbuf_tensor(("s_" + name) if name else f"sb{self._i}",
                                      [int(s) for s in shape], dtype)
        return t.ap()

    def psum(self, shape, dtype=F32, name=None):
        self._i += 1
        nm = name or f"ps{self._i}"
        cm = self.nc.psum_tensor(nm, [int(s) for s in shape], dtype)
        t = cm.__enter__()
        ap = t.ap()
        self._psum_names.add(ap.tensor.name)
        return ap

    def bankf(self, i):
        while len(self._bankf) <= i:
            self._bankf.append(self.psum([128, 512], F32,
                                         f"bankf{len(self._bankf)}"))
        return self._bankf[i]

    def constf(self, val):
        if val not in self._consts:
            sb = self.sbuf([128, 1], F32, f"const{len(self._consts)}")
            self.memset(sb, float(val))
            self._consts[val] = sb
        return self._consts[val]

    def bankb(self, i):
        while len(self._bankb) <= i:
            self._bankb.append(self.psum([128, 1024], BF16,
                                         f"bankb{len(self._bankb)}"))
        return self._bankb[i]

    def _key_ranges(self, ap):
        name = ap.tensor.name
        if name in self._psum_names:
            return name, (0, 1 << 60)
        s = _span(ap)
        if s is None:
            s = (0, 1 << 59)
        es = mybir.dt.size(ap.dtype)
        return name, (s[0] * es, s[1] * es)

    def _dep_edges(self, op, aps, book):
        for ap in aps:
            name, (lo, hi) = self._key_ranges(ap)
            d = book.get(name)
            if not d:
                continue
            for (l2, h2), streams in d.items():
                if l2 < hi and lo < h2:
                    for stream, oidx in streams.items():
                        if stream == op.stream == "tensor":
                            continue
                        if oidx > op.waits.get(stream, -1):
                            op.waits[stream] = oidx

    def _record(self, book, aps, op):
        for ap in aps:
            name, rng = self._key_ranges(ap)
            book.setdefault(name, {}).setdefault(rng, {})[op.stream] = op.idx

    def op(self, eng, fn, reads=(), writes=(), dma=False):
        stream = eng + ("_dma" if dma else "")
        reads = [r for r in reads if r is not None]
        writes = [w for w in writes if w is not None]
        o = _Op(eng, stream, fn, len(self.ops))
        self._dep_edges(o, reads, self._writes)
        self._dep_edges(o, writes, self._writes)
        self._dep_edges(o, writes, self._reads)
        self._record(self._reads, reads, o)
        self._record(self._writes, writes, o)
        self.ops.append(o)
        return o

    def dma(self, out, in_, eng="sync"):
        h = getattr(self.nc, eng)
        return self.op(eng, lambda: h.dma_start(out=out, in_=in_),
                       reads=[in_], writes=[out], dma=True)

    def cp(self, out, in_, eng="vector"):
        h = getattr(self.nc, eng)
        if eng == "scalar":
            return self.op(eng, lambda: h.copy(out, in_),
                           reads=[in_], writes=[out])
        return self.op(eng, lambda: h.tensor_copy(out, in_),
                       reads=[in_], writes=[out])

    def memset(self, ap, val, eng="vector"):
        h = getattr(self.nc, eng)
        return self.op(eng, lambda: h.memset(ap, val), reads=[], writes=[ap])

    def act(self, out, in_, func, bias=0.0, scale=1.0, accum_out=None):
        nc = self.nc
        np_ = in_.shape[0]
        if isinstance(bias, bass.AP) and bias.shape[0] == 128 and np_ < 128:
            bias = bias[0:np_]
        if isinstance(scale, bass.AP) and scale.shape[0] == 128 and np_ < 128:
            scale = scale[0:np_]
        reads = [in_]
        if isinstance(bias, bass.AP):
            reads.append(bias)
        if isinstance(scale, bass.AP):
            reads.append(scale)
        writes = [out] + ([accum_out] if accum_out is not None else [])
        if accum_out is not None:
            assert in_.free_size() <= 512, "act accum_out broken for free>512"
        return self.op("scalar",
                       lambda: nc.scalar.activation(out, in_, func, bias=bias,
                                                    scale=scale,
                                                    accum_out=accum_out),
                       reads=reads, writes=writes)

    def smul(self, out, in_, mul):
        nc = self.nc
        reads = [in_] + ([mul] if isinstance(mul, bass.AP) else [])
        return self.op("scalar", lambda: nc.scalar.mul(out, in_, mul),
                       reads=reads, writes=[out])

    def ts(self, out, in0, s1, s2=None, op0=ALU.mult, op1=ALU.bypass,
           eng="vector", accum_out=None):
        h = getattr(self.nc, eng)
        reads = [in0] + [s for s in (s1, s2) if isinstance(s, bass.AP)]
        writes = [out] + ([accum_out] if accum_out is not None else [])
        return self.op(eng,
                       lambda: h.tensor_scalar(out, in0, s1, s2, op0=op0,
                                               op1=op1, accum_out=accum_out),
                       reads=reads, writes=writes)

    def tt(self, out, in0, in1, alu, eng="vector"):
        h = getattr(self.nc, eng)
        return self.op(eng, lambda: h.tensor_tensor(out, in0, in1, op=alu),
                       reads=[in0, in1], writes=[out])

    def stt(self, out, in0, scalar, in1, op0, op1, eng="vector"):
        h = getattr(self.nc, eng)
        reads = [in0, in1] + ([scalar] if isinstance(scalar, bass.AP) else [])
        return self.op(eng,
                       lambda: h.scalar_tensor_tensor(out, in0, scalar, in1,
                                                      op0=op0, op1=op1),
                       reads=reads, writes=[out])

    def red(self, out, in_, alu=ALU.add, eng="vector"):
        h = getattr(self.nc, eng)
        return self.op(eng,
                       lambda: h.tensor_reduce(out, in_, axis=AX, op=alu),
                       reads=[in_], writes=[out])

    def recip(self, out, in_):
        nc = self.nc
        return self.op("vector", lambda: nc.vector.reciprocal(out, in_),
                       reads=[in_], writes=[out])

    def sel(self, out, mask, on_true, on_false):
        nc = self.nc
        reads = [mask] + [x for x in (on_true, on_false)
                          if isinstance(x, bass.AP)]
        return self.op("vector",
                       lambda: nc.vector.select(out, mask, on_true, on_false),
                       reads=reads, writes=[out])

    def mm(self, ps, lhsT, rhs, start, stop):
        nc = self.nc
        return self.op("tensor",
                       lambda: nc.tensor.matmul(ps, lhsT, rhs, start=start,
                                                stop=stop),
                       reads=[lhsT, rhs], writes=[ps])

    def mm_acc(self, ps, pairs):
        n = len(pairs)
        for i, (lt, rh) in enumerate(pairs):
            self.mm(ps, lt, rh, i == 0, i == n - 1)

    def tp(self, ps_out, in_, ident):
        nc = self.nc
        k = in_.shape[0]
        b = int(in_.base_partition())
        idk = ident[b:b + k, b:b + k]
        return self.op("tensor",
                       lambda: nc.tensor.transpose(ps_out, in_, idk),
                       reads=[in_, idk], writes=[ps_out])

    def scan(self, out, data0, data1, init, op0=ALU.mult, op1=ALU.add):
        nc = self.nc
        reads = [data0, data1] + ([init] if isinstance(init, bass.AP) else [])
        return self.op("vector",
                       lambda: nc.vector.tensor_tensor_scan(out, data0, data1,
                                                            init, op0=op0,
                                                            op1=op1),
                       reads=reads, writes=[out])

    def iota(self, out, pattern, base=None, channel_multiplier=None):
        nc = self.nc
        return self.op("gpsimd",
                       lambda: nc.gpsimd.iota(
                           out, pattern, base=base,
                           channel_multiplier=channel_multiplier,
                           allow_small_or_imprecise_dtypes=True),
                       reads=[], writes=[out])

    def wload(self, hbm, K, N, name=None, dtype=BF16, eng="gpsimd"):
        if K <= 128:
            sb = self.sbuf([K, 1, N], dtype, name)
            self.dma(sb[:, 0, :], hbm, eng=eng)
            return sb
        nk = K // 128
        assert K % 128 == 0
        sb = self.sbuf([128, nk * N], dtype, name)
        v = sb.rearrange("p (k n) -> p k n", k=nk)
        self.dma(v, hbm.rearrange("(k p) n -> p k n", p=128), eng=eng)
        return v

    R_DMA = 8

    def flush(self):
        assert not self._flushed
        self._flushed = True
        nc = self.nc
        for o in self.ops:
            if o.stream.endswith("_dma"):
                o.marked = True
        for o in self.ops:
            for stream, idx in o.waits.items():
                self.ops[idx].marked = True
        totals = {}
        dma_k = {}
        dma_slot = {}
        for o in self.ops:
            if o.stream.endswith("_dma"):
                k = dma_k.get(o.stream, 0)
                dma_k[o.stream] = k + 1
                dma_slot[o.idx] = (k % self.R_DMA, k // self.R_DMA)
            elif o.marked:
                totals[o.stream] = totals.get(o.stream, 0) + 1
                o.inc_no = totals[o.stream]
        sems = {s: nc.alloc_semaphore(name=f"t_{s}") for s in totals}
        dsems = {}
        for s in dma_k:
            n = min(self.R_DMA, dma_k[s])
            dsems[s] = [nc.alloc_semaphore(name=f"t_{s}{i}") for i in range(n)]
        hw = {}
        for o in self.ops:
            eng = getattr(nc, o.eng)
            for stream in sorted(o.waits):
                src = self.ops[o.waits[stream]]
                if stream.endswith("_dma"):
                    slot, gen = dma_slot[src.idx]
                    sem = dsems[stream][slot]
                    val = 16 * (gen + 1)
                    key = (o.eng, stream, slot)
                else:
                    val = src.inc_no
                    if val == 0:
                        continue
                    sem = sems[stream]
                    key = (o.eng, stream)
                if hw.get(key, -1) >= val:
                    continue
                hw[key] = val
                eng.wait_ge(sem, val)
            if o.stream.endswith("_dma"):
                slot, gen = dma_slot[o.idx]
                if gen > 0:
                    key = (o.eng, o.stream, slot)
                    val = 16 * gen
                    if hw.get(key, -1) < val:
                        hw[key] = val
                        eng.wait_ge(dsems[o.stream][slot], val)
                instr = o.fn()
                instr.then_inc(dsems[o.stream][slot], 16)
            else:
                instr = o.fn()
                if o.marked:
                    instr.then_inc(sems[o.stream], 1)
        for s, k in dma_k.items():
            for slot in range(min(self.R_DMA, k)):
                uses = (k - slot - 1) // self.R_DMA + 1
                nc.sync.wait_ge(dsems[s][slot], 16 * uses)
        return nc


_NC_CACHE = {}


def _get_nc(name, builder):
    if name not in _NC_CACHE:
        p = builder()
        _NC_CACHE[name] = p.flush()
    return _NC_CACHE[name]


def _run(name, builder, in_maps, **kw):
    nc = _get_nc(name, builder)
    res = run_bass_kernel_spmd(nc, in_maps, core_ids=list(range(NCORES)), **kw)
    return res.results



def build_l0a():
    p = Prog("l0a")
    x = p.dram_in("x", (NT, D))
    wq_h = p.dram_in("wq", (D, D))
    wkv_h = p.dram_in("wkv", (D, 2 * D))
    med_h = p.dram_in("med", (64, D))
    wg_h = p.dram_in("wg", (D, 8))
    ident_h = p.dram_in("ident", (128, 128), BF16)

    o_xnfm = p.dram_out("xnfm", (8, 128, NT), BF16)
    o_qfm = p.dram_out("qfm", (8, 128, NT), BF16)
    o_S = p.dram_out("S", (8, NT))
    o_mx = p.dram_out("mx", (64, 16))
    o_se = p.dram_out("se", (64, 16))
    o_accv = p.dram_out("accv", (64, 16 * 64))

    xnfm = p.sbuf([128, 8 * NT], BF16, "xnfm")
    kfm = p.sbuf([128, 8 * NT], BF16, "kfm")
    vfm = p.sbuf([128, 8 * NT], BF16, "vfm")
    idsb = p.sbuf([128, 128], BF16, "idsb")
    p.dma(idsb, ident_h)
    wq = p.wload(wq_h, D, D, "wq")
    wkv = p.wload(wkv_h, D, 2 * D, "wkv")
    wg = p.wload(wg_h, D, 8, "wg")

    xt = [p.sbuf([128, D], F32, f"xt{i}") for i in range(2)]
    xc = [p.sbuf([128, D], F32, f"xc{i}") for i in range(2)]
    xnbf = [p.sbuf([128, D], BF16, f"xnbf{i}") for i in range(2)]
    xsq = p.sbuf([128, D], F32, "xsq")
    statsall = p.sbuf([128, TT * 8], F32, "stats")

    x3 = x.rearrange("(t p) d -> t p d", p=128)
    xnv = xnfm.rearrange("p (c t) -> p c t", c=8)
    for t in range(TT):
        bi = t % 2
        p.dma(xt[bi], x3[t])
        stats = statsall[:, t * 8:(t + 1) * 8]
        nm = stats[:, 0:1]
        vr = stats[:, 1:2]
        sd = stats[:, 2:3]
        rstd = stats[:, 3:4]
        vr2 = stats[:, 4:5]
        p.red(nm, xt[bi], ALU.add)
        p.ts(nm, nm, -1.0 / D, None, op0=ALU.mult)
        p.ts(xc[bi], xt[bi], nm, None, op0=ALU.add)
        p.act(xsq[:, 0:512], xc[bi][:, 0:512], AF.Square, accum_out=vr)
        p.act(xsq[:, 512:1024], xc[bi][:, 512:1024], AF.Square, accum_out=vr2)
        p.tt(vr, vr, vr2, ALU.add)
        p.act(sd, vr, AF.Sqrt, bias=p.constf(1e-5), scale=1.0 / D)
        p.recip(rstd, sd)
        p.ts(xnbf[bi], xc[bi], rstd, None, op0=ALU.mult)
        for ch in range(8):
            pb = p.bankb(ch % 2)[:, 0:128]
            p.tp(pb, xnbf[bi][:, ch * 128:(ch + 1) * 128], idsb)
            p.cp(xnv[:, ch, t * 128:(t + 1) * 128], pb, eng="vector")
    p.dma(o_xnfm.rearrange("c p t -> p c t"), xnv)

    qbf = [p.sbuf([128, BLK], BF16, f"qbf{i}") for i in range(2)]
    kvv = (kfm.rearrange("p (c t) -> p c t", c=8),
           vfm.rearrange("p (c t) -> p c t", c=8))
    pi = 0
    for nb in range(NBLK):
        tok = slice(nb * BLK, (nb + 1) * BLK)
        for po in range(8):
            ps = p.bankf(pi % 4)
            pi += 1
            p.mm_acc(ps, [(wq[:, ki, po * 128:(po + 1) * 128], xnv[:, ki, tok])
                          for ki in range(8)])
            qb = qbf[po % 2]
            p.cp(qb, ps, eng="scalar")
            p.dma(o_qfm[po][:, tok], qb)
        for po in range(16):
            ps = p.bankf(pi % 4)
            pi += 1
            p.mm_acc(ps, [(wkv[:, ki, po * 128:(po + 1) * 128], xnv[:, ki, tok])
                          for ki in range(8)])
            p.cp(kvv[po // 8][:, po % 8, tok], ps, eng="vector")
    ssb = p.sbuf([8, NT], F32, "ssb")
    for nb in range(NBLK):
        tok = slice(nb * BLK, (nb + 1) * BLK)
        ps8 = p.bankf(nb % 4)[0:8, :]
        p.mm_acc(ps8, [(wg[:, ki, :], xnv[:, ki, tok]) for ki in range(8)])
        p.act(ssb[:, tok], ps8, AF.Sigmoid)
    p.dma(o_S, ssb)

    med_sb = p.sbuf([64, D], F32, "medsb")
    medbf = p.sbuf([64, D], BF16, "medbf")
    medfm = p.sbuf([128, 8 * 64], BF16, "medfm")

    def medslot(h):
        bb = (h % 2) * 64
        return medfm[bb:bb + 64, (h // 2) * 64:(h // 2) * 64 + 64]

    p.dma(med_sb, med_h)
    p.smul(medbf, med_sb, 0.125)
    for h in range(16):
        ptm = p.bankb(h % 2)[0:64, 0:64]
        p.tp(ptm, medbf[:, h * 64:(h + 1) * 64], idsb)
        p.cp(medslot(h), ptm, eng="vector")

    scs = p.sbuf([64, NT], F32, "scs")
    esc = p.sbuf([64, NT], BF16, "esc")
    mxa = p.sbuf([64, 16], F32, "mxa")
    sea = p.sbuf([64, 16], F32, "sea")
    accva = p.sbuf([64, 16 * 64], F32, "accva")
    nmx = p.sbuf([64, 1], F32, "nmx")
    se4 = p.sbuf([64, 4], F32, "se4")
    esct = p.sbuf([128, 2 * 64], BF16, "esct")
    vtm = p.sbuf([128, 2 * 64], BF16, "vtm")

    for h in range(16):
        krows = kvv[0][(h % 2) * 64:(h % 2) * 64 + 64, h // 2, :]
        vrows = kvv[1][(h % 2) * 64:(h % 2) * 64 + 64, h // 2, :]
        for nb in range(NBLK):
            ps = p.bankf(nb % 2)[0:64, :]
            p.mm(ps, medslot(h),
                 krows[:, nb * BLK:(nb + 1) * BLK], True, True)
            p.cp(scs[:, nb * BLK:(nb + 1) * BLK], ps, eng="vector")
        p.red(mxa[:, h:h + 1], scs, ALU.max)
        p.ts(nmx, mxa[:, h:h + 1], -1.0, None, op0=ALU.mult)
        for q4 in range(4):
            p.act(esc[:, q4 * 512:(q4 + 1) * 512], scs[:, q4 * 512:(q4 + 1) * 512],
                  AF.Exp, bias=nmx, accum_out=se4[:, q4:q4 + 1])
        p.red(sea[:, h:h + 1], se4, ALU.add)
        pacc = p.bankf(2)[0:64, 0:64]
        for lc in range(TT):
            sl = slice(lc * 128, (lc + 1) * 128)
            pe = p.bankb(lc % 2)[:, 0:64]
            pv = p.bankb(2 + lc % 2)[:, 0:64]
            p.tp(pe, esc[:, sl], idsb)
            p.tp(pv, vrows[:, sl], idsb)
            eb = esct[:, (lc % 2) * 64:(lc % 2) * 64 + 64]
            vb = vtm[:, (lc % 2) * 64:(lc % 2) * 64 + 64]
            p.cp(eb, pe, eng="vector")
            p.cp(vb, pv, eng="scalar")
            p.mm(pacc, eb, vb, lc == 0, lc == TT - 1)
        p.cp(accva[:, h * 64:(h + 1) * 64], pacc, eng="vector")
    p.dma(o_mx, mxa)
    p.dma(o_se, sea)
    p.dma(o_accv, accva)
    return p


def build_l0b():
    p = Prog("l0b")
    qfm_h = p.dram_in("qfm", (8, 128, NT), BF16)
    med_h = p.dram_in("med", (64, D))
    wo_h = p.dram_in("wo", (D, D))
    bo_h = p.dram_in("bo", (1, D))
    mx_h = p.dram_in("mx", (16, 64, 4))
    se_h = p.dram_in("se", (16, 64, 4))
    accv_h = p.dram_in("accv", (16, 4, 64, 64))
    ident_h = p.dram_in("ident", (128, 128), BF16)

    o_s = p.dram_out("s", (NT, D), BF16)

    idsb = p.sbuf([128, 128], BF16, "idsb")
    p.dma(idsb, ident_h)
    qfm = p.sbuf([128, 8 * NT], BF16, "qfm")
    p.dma(qfm.rearrange("p (c t) -> p c t", c=8),
          qfm_h.rearrange("c p t -> p c t"))
    wo = p.wload(wo_h, D, D, "wo")
    bo = p.sbuf([1, D], BF16, "bo")
    p.dma(bo, bo_h, eng="gpsimd")
    ones1 = p.sbuf([1, 128], BF16, "ones1")
    p.memset(ones1, 1.0)

    med_sb = p.sbuf([64, D], F32, "medsb")
    medbf = p.sbuf([64, D], BF16, "medbf")
    medfm = p.sbuf([128, 8 * 64], BF16, "medfm")

    def medslot(h):
        bb = (h % 2) * 64
        return medfm[bb:bb + 64, (h // 2) * 64:(h // 2) * 64 + 64]

    p.dma(med_sb, med_h)
    p.smul(medbf, med_sb, 0.125)
    for h in range(16):
        ptm = p.bankb(h % 2)[0:64, 0:64]
        p.tp(ptm, medbf[:, h * 64:(h + 1) * 64], idsb)
        p.cp(medslot(h), ptm, eng="vector")

    mxs = p.sbuf([64, 4], F32, "mxs")
    ses = p.sbuf([64, 4], F32, "ses")
    accs = p.sbuf([64, 4 * 64], F32, "accs")
    M = p.sbuf([64, 1], F32, "M")
    nM = p.sbuf([64, 1], F32, "nM")
    w4 = p.sbuf([64, 4], F32, "w4")
    wse = p.sbuf([64, 4], F32, "wse")
    den = p.sbuf([64, 1], F32, "den")
    rden = p.sbuf([64, 1], F32, "rden")
    acc0 = p.sbuf([64, 64], F32, "acc0")
    acc1 = p.sbuf([64, 64], F32, "acc1")
    mkv = p.sbuf([128, 8 * 64], BF16, "mkv")

    def mkvslot(h):
        bb = (h % 2) * 64
        return mkv[bb:bb + 64, (h // 2) * 64:(h // 2) * 64 + 64]

    for h in range(16):
        p.dma(mxs, mx_h[h])
        p.dma(ses, se_h[h])
        p.dma(accs.rearrange("p (c d) -> p c d", c=4),
              accv_h[h].rearrange("c m d -> m c d"))
        p.red(M, mxs, ALU.max)
        p.ts(nM, M, -1.0, None, op0=ALU.mult)
        p.act(w4, mxs, AF.Exp, bias=nM)
        p.tt(wse, w4, ses, ALU.mult)
        p.red(den, wse, ALU.add)
        p.recip(rden, den)
        a3 = accs.rearrange("p (c d) -> p c d", c=4)
        p.ts(acc0, a3[:, 0, :], w4[:, 0:1], None, op0=ALU.mult)
        p.stt(acc1, a3[:, 1, :], w4[:, 1:2], acc0, op0=ALU.mult, op1=ALU.add)
        p.stt(acc0, a3[:, 2, :], w4[:, 2:3], acc1, op0=ALU.mult, op1=ALU.add)
        p.stt(acc1, a3[:, 3, :], w4[:, 3:4], acc0, op0=ALU.mult, op1=ALU.add)
        p.ts(mkvslot(h), acc1, rden, None, op0=ALU.mult)

    probs = p.sbuf([128, 2 * 64], BF16, "probs")
    probT = p.sbuf([128, 2 * 128], BF16, "probT")
    attn = p.sbuf([128, 8 * NT], BF16, "attn")
    rowmax = p.sbuf([128, 1], F32, "rowmax")
    nrowmax = p.sbuf([128, 1], F32, "nrowmax")
    rowsum = p.sbuf([128, 1], F32, "rowsum")
    rrsum = p.sbuf([128, 1], F32, "rrsum")
    qv = qfm.rearrange("p (c t) -> p c t", c=8)
    attnv = attn.rearrange("p (c t) -> p c t", c=8)

    for t in range(TT):
        tok = slice(t * 128, (t + 1) * 128)
        for h in range(16):
            ps = p.bankf(h % 2)[:, 0:64]
            qrows = qv[(h % 2) * 64:(h % 2) * 64 + 64, h // 2, tok]
            p.mm(ps, qrows, medslot(h), True, True)
            p.red(rowmax, ps, ALU.max)
            p.ts(nrowmax, rowmax, -1.0, None, op0=ALU.mult)
            prb = probs[:, (h % 2) * 64:(h % 2) * 64 + 64]
            p.act(prb, ps, AF.Exp, bias=nrowmax, accum_out=rowsum)
            p.recip(rrsum, rowsum)
            p.ts(prb, prb, rrsum, None, op0=ALU.mult)
            pq = p.bankb(h % 2)[0:64, 0:128]
            p.tp(pq, prb, idsb)
            bb = (h % 2) * 64
            pT = probT[bb:bb + 64, (h % 2) * 128:(h % 2) * 128 + 128]
            p.cp(pT, pq, eng="vector")
            po = p.bankf(2 + h % 2)[0:64, 0:128]
            p.mm(po, mkvslot(h), pT, True, True)
            p.cp(attnv[(h % 2) * 64:(h % 2) * 64 + 64, h // 2, tok], po,
                 eng="scalar")

    sbf = [p.sbuf([128, D], BF16, f"sbf{i}") for i in range(2)]
    s3 = o_s.rearrange("(t p) d -> t p d", p=128)
    for t in range(TT):
        tok = slice(t * 128, (t + 1) * 128)
        sb = sbf[t % 2]
        for fo in range(2):
            ps = p.bankf((t * 2 + fo) % 4)
            fcols = slice(fo * BLK, (fo + 1) * BLK)
            for ki in range(8):
                p.mm(ps, attnv[:, ki, tok], wo[:, ki, fcols], ki == 0, False)
            p.mm(ps, ones1, bo[:, fcols], False, True)
            p.cp(sb[:, fcols], ps, eng="vector")
        p.dma(s3[t], sb)
    return p


def build_l0c():
    p = Prog("l0c")
    xnfm_h = p.dram_in("xnfm", (8, 128, NT), BF16)
    S_h = p.dram_in("S", (8, 128, 128))
    colsel_h = p.dram_in("colsel", (128, 16))
    s_h = p.dram_in("s", (NT, D), BF16)
    x_h = p.dram_in("x", (NT, D))
    w1_h = p.dram_in("w1", (8, D, 4096))
    b1_h = p.dram_in("b1", (8, 128, 32))
    w2_h = p.dram_in("w2", (8, 4096, D))
    b2_h = p.dram_in("b2", (8, D))
    slt_h = p.dram_in("slt", (128, 128), BF16)

    o_x = p.dram_out("xout", (NT, D))

    xnfm = p.sbuf([128, 8 * NT], BF16, "xnfm")
    p.dma(xnfm.rearrange("p (c t) -> p c t", c=8),
          xnfm_h.rearrange("c p t -> p c t"))
    xnv = xnfm.rearrange("p (c t) -> p c t", c=8)
    ssb = p.sbuf([128, 8 * 128], F32, "ssb")
    p.dma(ssb.rearrange("p (e c) -> p e c", e=8),
          S_h.rearrange("e p c -> p e c"))
    s3 = ssb.rearrange("p (e c) -> p e c", e=8)
    colsel = p.sbuf([128, 16], F32, "colsel")
    p.dma(colsel, colsel_h)
    colselb = p.sbuf([128, 16], BF16, "colselb")
    p.cp(colselb, colsel)
    slt = p.sbuf([128, 128], BF16, "slt")
    p.dma(slt, slt_h)
    onesf = p.sbuf([1, 128], F32, "onesf")
    p.memset(onesf, 1.0)
    onesb = p.sbuf([128, 1], BF16, "onesb")
    p.memset(onesb, 1.0)

    lob = p.sbuf([128, 8], F32, "lob")
    hib = p.sbuf([128, 8], F32, "hib")
    tb = p.sbuf([128, 8], F32, "tb")
    selb = p.sbuf([128, 8], mybir.dt.int32, "selb")
    cntb = p.sbuf([128, 8], F32, "cntb")
    ge = p.sbuf([128, 8 * 128], F32, "ge")
    cnt2 = p.sbuf([128, 8], F32, "cnt2")
    cnt2b = p.sbuf([128, 8], BF16, "cnt2b")
    p.memset(lob, 0.0)
    p.memset(hib, 1.0001)
    KSEL = 4096.0
    ge3 = ge.rearrange("p (e c) -> p e c", e=8)

    srcb2 = p.sbuf([1, 8], BF16, "srcb2")
    srcrem = p.sbuf([1, 8], F32, "srcrem")
    srcb3 = p.sbuf([1, 8], BF16, "srcb3")
    onesc = p.sbuf([1, 128], BF16, "onesc")
    p.memset(onesc, 1.0)

    def bcast_row(src_row_f32):
        pc = p.bankf(5)[:, 0:8]
        p.cp(srcb2, src_row_f32)
        p.tt(srcrem, src_row_f32, srcb2, ALU.subtract)
        p.cp(srcb3, srcrem)
        p.mm(pc, onesc, srcb2, True, False)
        p.mm(pc, onesc, srcb3, False, True)
        return pc

    def tbb(t8):
        return bass.AP(t8.tensor, t8.offset, [t8.ap[0], [1, 8], [0, 128]])

    pcnt = p.bankf(4)[0:1, 0:8]
    cntr = p.sbuf([1, 8], F32, "cntr")
    for it in range(44):
        p.tt(tb, lob, hib, ALU.add)
        p.ts(tb, tb, 0.5, None, op0=ALU.mult)
        p.tt(ge3, s3, tbb(tb), ALU.is_ge)
        p.red(cnt2, ge3, ALU.add)
        p.cp(cnt2b, cnt2)
        p.mm(pcnt, onesb, cnt2b, True, True)
        p.cp(cntr, pcnt)
        pcb = bcast_row(cntr)
        p.cp(cntb, pcb, eng="scalar")
        p.ts(selb, cntb, KSEL, None, op0=ALU.is_ge)
        p.sel(lob, selb, tb, lob)
        p.sel(hib, selb, hib, tb)

    p.tt(ge3, s3, tbb(lob), ALU.is_gt)
    p.red(cnt2, ge3, ALU.add)
    p.cp(cnt2b, cnt2)
    p.mm(pcnt, onesb, cnt2b, True, True)
    p.cp(cntr, pcnt)
    c1b = bcast_row(cntr)
    remb8 = p.sbuf([128, 8], F32, "remb8")
    p.ts(remb8, c1b, -1.0, KSEL, op0=ALU.mult, op1=ALU.add)

    eq = p.sbuf([128, 128], F32, "eq")
    pref = p.sbuf([128, 128], F32, "pref")
    zrow = p.sbuf([128, 128], F32, "zrow")
    p.memset(zrow, 0.0)
    rowsum = p.sbuf([128, 1], F32, "rowsum")
    rowsumb = p.sbuf([128, 1], BF16, "rowsumb")
    seleq = p.sbuf([128, 128], F32, "seleq")
    gmask = p.sbuf([128, 8 * 128], F32, "gmask")
    gm3 = gmask.rearrange("p (e c) -> p e c", e=8)
    gloc = p.sbuf([128, 8 * 16], F32, "gloc")
    glv = gloc.rearrange("p (e t) -> p e t", e=8)
    gmaskb8 = p.sbuf([128, 8 * 128], BF16, "gmaskb8")
    gmb3 = gmaskb8.rearrange("p (e c) -> p e c", e=8)
    glrows = p.sbuf([1, TT * 128], BF16, "glrows")
    grv = glrows.rearrange("o (t j) -> o t j", t=TT)

    for e in range(8):
        p.tt(eq, s3[:, e, :], tbb(lob)[:, e, :], ALU.is_equal)
        p.scan(pref, zrow, eq, 0.0, op0=ALU.add, op1=ALU.add)
        p.tt(pref, pref, eq, ALU.subtract)
        p.red(rowsum, eq, ALU.add)
        p.cp(rowsumb, rowsum)
        poff = p.bankf(4)[:, 8:9]
        p.mm(poff, slt, rowsumb, True, True)
        poffs = p.sbuf([128, 1], F32, f"poffs{e}")
        p.cp(poffs, poff)
        p.ts(pref, pref, poffs, None, op0=ALU.add)
        p.ts(seleq, pref, remb8[:, e:e + 1], None, op0=ALU.is_lt)
        p.tt(seleq, seleq, eq, ALU.mult)
        p.tt(gm3[:, e, :], ge3[:, e, :], seleq, ALU.add)
        p.tt(gm3[:, e, :], gm3[:, e, :], s3[:, e, :], ALU.mult)
        pgl = p.bankf(5)[:, 16:32]
        p.mm(pgl, gm3[:, e, :], colsel, True, True)
        p.cp(glv[:, e, :], pgl)
        p.cp(gmb3[:, e, :], gm3[:, e, :])


    acc = p.sbuf([128, TT * D], F32, "acc")
    w1sb = p.sbuf([128, 8 * 2048], BF16, "w1sb")
    w2sb = p.sbuf([128, 16 * D], BF16, "w2sb")
    b1sb = p.sbuf([128, 32], F32, "b1sb")
    b2sb = p.sbuf([1, D], BF16, "b2sb")
    hidb = p.sbuf([128, 16 * BLK], BF16, "hidb")
    accv = acc.rearrange("p (t d) -> p t d", t=TT)
    w1v = w1sb.rearrange("p (k n) -> p k n", k=8)
    w2v = w2sb.rearrange("p (k n) -> p k n", k=16)
    hv = hidb.rearrange("p (k n) -> p k n", k=16)

    for e in range(8):
        p.dma(b1sb, b1_h[e], eng="gpsimd")
        p.dma(b2sb, b2_h[e:e + 1, :], eng="gpsimd")
        for t in range(TT):
            pgr = p.bankf(6)[0:1, 0:128]
            p.mm(pgr, colselb[:, t:t + 1], gmb3[:, e, :], True, True)
            p.cp(grv[:, t, :], pgr)
        for h2 in range(2):
            hcols = slice(h2 * 2048, (h2 + 1) * 2048)
            p.dma(w1v, w1_h[e, :, hcols].rearrange("(k p) n -> p k n", p=128),
                  eng="gpsimd")
            p.dma(w2v, w2_h[e, hcols, :].rearrange("(k p) n -> p k n", p=128),
                  eng="gpsimd")
            for nb in range(NBLK):
                tok = slice(nb * BLK, (nb + 1) * BLK)
                for po in range(16):
                    ps = p.bankf(po % 2)
                    p.mm_acc(ps, [(w1v[:, ki, po * 128:(po + 1) * 128],
                                   xnv[:, ki, tok]) for ki in range(8)])
                    p.act(hv[:, po, :], ps, AF.Gelu,
                          bias=b1sb[:, h2 * 16 + po: h2 * 16 + po + 1])
                for j in range(4):
                    t_glob = nb * 4 + j
                    for fo in range(2):
                        ps = p.bankf(2 + (j * 2 + fo) % 2)
                        fcols = slice(fo * BLK, (fo + 1) * BLK)
                        pairs = [(hv[:, ki, j * 128:(j + 1) * 128],
                                  w2v[:, ki, fcols]) for ki in range(16)]
                        if h2 == 0:
                            pairs.append((grv[:, t_glob, :], b2sb[:, fcols]))
                        p.mm_acc(ps, pairs)
                        gcol = glv[:, e, t_glob:t_glob + 1]
                        dst = accv[:, t_glob, fcols]
                        if e == 0 and h2 == 0:
                            p.ts(dst, ps, gcol, None, op0=ALU.mult)
                        else:
                            p.stt(dst, ps, gcol, dst, op0=ALU.mult, op1=ALU.add)

    xtl = p.sbuf([128, D], F32, "xtl0")
    stl = p.sbuf([128, D], BF16, "stl0")
    x3 = x_h.rearrange("(t p) d -> t p d", p=128)
    sS = s_h.rearrange("(t p) d -> t p d", p=128)
    o3 = o_x.rearrange("(t p) d -> t p d", p=128)
    for t in range(TT):
        at = accv[:, t, :]
        p.dma(xtl, x3[t])
        p.dma(stl, sS[t], eng="gpsimd")
        p.act(at, at, AF.Silu)
        p.tt(at, at, stl, ALU.mult)
        p.tt(at, at, xtl, ALU.add)
        p.dma(o3[t], at)
    return p




DI = 2048
DSL = DI // NCORES


def build_l1b1():
    p = Prog("l1b1")
    x = p.dram_in("x", (NT, D))
    win_h = p.dram_in("win", (D, 2 * DI))
    ident_h = p.dram_in("ident", (128, 128), BF16)
    o_xnfm = p.dram_out("xnfm", (8, 128, NT), BF16)
    o_xin = p.dram_out("xin", (16, 128, NT), BF16)
    o_z = p.dram_out("z", (16, 128, NT), BF16)

    xnfm = p.sbuf([128, 8 * NT], BF16, "xnfm")
    idsb = p.sbuf([128, 128], BF16, "idsb")
    p.dma(idsb, ident_h)
    win = p.wload(win_h, D, 2 * DI, "win")

    xt = [p.sbuf([128, D], F32, f"xt{i}") for i in range(2)]
    xc = [p.sbuf([128, D], F32, f"xc{i}") for i in range(2)]
    xnbf = [p.sbuf([128, D], BF16, f"xnbf{i}") for i in range(2)]
    xsq = p.sbuf([128, D], F32, "xsq")
    statsall = p.sbuf([128, TT * 8], F32, "stats")
    x3 = x.rearrange("(t p) d -> t p d", p=128)
    xnv = xnfm.rearrange("p (c t) -> p c t", c=8)
    for t in range(TT):
        bi = t % 2
        p.dma(xt[bi], x3[t])
        st = statsall[:, t * 8:(t + 1) * 8]
        nm, vr, sd, rstd, vr2 = (st[:, i:i + 1] for i in range(5))
        p.red(nm, xt[bi], ALU.add)
        p.ts(nm, nm, -1.0 / D, None, op0=ALU.mult)
        p.ts(xc[bi], xt[bi], nm, None, op0=ALU.add)
        p.act(xsq[:, 0:512], xc[bi][:, 0:512], AF.Square, accum_out=vr)
        p.act(xsq[:, 512:1024], xc[bi][:, 512:1024], AF.Square, accum_out=vr2)
        p.tt(vr, vr, vr2, ALU.add)
        p.act(sd, vr, AF.Sqrt, bias=p.constf(1e-5), scale=1.0 / D)
        p.recip(rstd, sd)
        p.ts(xnbf[bi], xc[bi], rstd, None, op0=ALU.mult)
        for ch in range(8):
            pb = p.bankb(ch % 2)[:, 0:128]
            p.tp(pb, xnbf[bi][:, ch * 128:(ch + 1) * 128], idsb)
            p.cp(xnv[:, ch, t * 128:(t + 1) * 128], pb, eng="vector")
    p.dma(o_xnfm.rearrange("c p t -> p c t"), xnv)

    ob = [p.sbuf([128, BLK], BF16, f"ob{i}") for i in range(4)]
    for nb in range(NBLK):
        tok = slice(nb * BLK, (nb + 1) * BLK)
        for po in range(32):
            ps = p.bankf(po % 4)
            p.mm_acc(ps, [(win[:, ki, po * 128:(po + 1) * 128], xnv[:, ki, tok])
                          for ki in range(8)])
            o = ob[po % 4]
            p.cp(o, ps, eng="scalar" if po % 2 else "vector")
            dst = o_xin if po < 16 else o_z
            p.dma(dst[po % 16][:, tok], o)
    return p


def build_l1b2():
    p = Prog("l1b2")
    xin_h = p.dram_in("xin", (2, 128, NTOK), BF16)
    convw_h = p.dram_in("convw", (128, 2, 4))
    convb_h = p.dram_in("convb", (128, 2))
    wxp_h = p.dram_in("wxp", (DSL, 96))
    o_u = p.dram_out("u", (2, 128, NTOK), BF16)
    o_xdp = p.dram_out("xdp", (96, NTOK))

    xin = p.sbuf([128, 2 * NTOK], BF16, "xin")
    xv = xin.rearrange("p (c t) -> p c t", c=2)
    p.dma(xv, xin_h.rearrange("c p t -> p c t"))
    u = p.sbuf([128, 2 * NTOK], BF16, "u")
    uv = u.rearrange("p (c t) -> p c t", c=2)
    cw = p.sbuf([128, 2 * 4], F32, "cw")
    p.dma(cw, convw_h.rearrange("p c k -> p (c k)"))
    cb = p.sbuf([128, 2], F32, "cb")
    p.dma(cb, convb_h)
    wxp = p.wload(wxp_h, DSL, 96, "wxp")

    acc = p.sbuf([128, L], F32, "acc")
    for pc in range(2):
        for b in range(B):
            xb = xv[:, pc, b * L:(b + 1) * L]
            p.ts(acc, xb, cw[:, pc * 4 + 3: pc * 4 + 4], None, op0=ALU.mult)
            for j in (2, 1, 0):
                sh = 3 - j
                p.stt(acc[:, sh:L], xb[:, 0:L - sh],
                      cw[:, pc * 4 + j: pc * 4 + j + 1], acc[:, sh:L],
                      op0=ALU.mult, op1=ALU.add)
            p.act(uv[:, pc, b * L:(b + 1) * L], acc, AF.Silu,
                  bias=cb[:, pc:pc + 1])
    p.dma(o_u.rearrange("c p t -> p c t"), uv)

    xdp = [p.sbuf([96, BLK], F32, f"xdp{i}") for i in range(2)]
    for nb in range(NTOK // BLK):
        tok = slice(nb * BLK, (nb + 1) * BLK)
        ps = p.bankf(nb % 2)[0:96, :]
        p.mm_acc(ps, [(wxp[:, ki, :], uv[:, ki, tok]) for ki in range(2)])
        xb = xdp[nb % 2]
        p.cp(xb, ps, eng="vector")
        p.dma(o_xdp[:, tok], xb)
    return p


def build_l1b3():
    p = Prog("l1b3")
    u_h = p.dram_in("u", (2, 128, NTOK), BF16)
    xdp_h = p.dram_in("xdp", (8, 96, NTOK))
    wdt_h = p.dram_in("wdt", (64, DSL))
    bdt_h = p.dram_in("bdt", (128, 2))
    aarr_h = p.dram_in("aarr", (128, 32))
    darr_h = p.dram_in("darr", (128, 2))
    sels_h = p.dram_in("sels", (16, 128, 128), BF16)
    nsel_h = p.dram_in("nsel", (128, 8), BF16)
    o_y = p.dram_out("y", (DSL, NTOK), BF16)
    o_du = p.dram_out("du", (2, 128, NTOK), BF16)

    u = p.sbuf([128, 2 * NTOK], BF16, "u")
    uv = u.rearrange("p (c t) -> p c t", c=2)
    p.dma(uv, u_h.rearrange("c p t -> p c t"))
    wdt = p.wload(wdt_h, 64, DSL, "wdt")
    bdt = p.sbuf([128, 2], F32, "bdt")
    p.dma(bdt, bdt_h)
    aarr = p.sbuf([128, 32], F32, "aarr")
    p.dma(aarr, aarr_h)
    A = p.sbuf([128, 32], F32, "A")
    p.act(A, aarr, AF.Exp)
    p.ts(A, A, -1.0, None, op0=ALU.mult)
    darr = p.sbuf([128, 2], F32, "darr")
    p.dma(darr, darr_h)
    sels = p.sbuf([128, 16 * 128], BF16, "sels")
    p.dma(sels.rearrange("p (g q) -> p g q", g=16),
          sels_h.rearrange("g p q -> p g q"))
    sv = sels.rearrange("p (g q) -> p g q", g=16)
    nsel = p.sbuf([128, 8], BF16, "nsel")
    p.dma(nsel, nsel_h)

    dub = p.sbuf([128, NTOK], BF16, "dub")
    for pc in range(2):
        p.ts(dub, uv[:, pc, :], darr[:, pc:pc + 1], None, op0=ALU.mult)
        p.dma(o_du[pc].rearrange("p t -> p t"), dub)

    CH = 2048
    NCH = L // CH
    xpart = p.sbuf([96, CH], F32, "xpart")
    xdbl = [p.sbuf([96, CH], F32, f"xdbl{i}") for i in range(2)]
    dtch = p.sbuf([128, 2 * CH], BF16, "dtch")
    dtv = dtch.rearrange("p (c t) -> p c t", c=2)
    dtlowb = p.sbuf([64, CH], BF16, "dtlowb")
    sp1 = p.sbuf([128, BLK], F32, "sp1")
    sp2 = p.sbuf([128, BLK], F32, "sp2")
    btile = p.sbuf([128, CH], BF16, "btile")
    ctile = p.sbuf([128, CH], BF16, "ctile")
    bcf = p.sbuf([32, CH], F32, "bcf")
    bcb = p.sbuf([32, CH], BF16, "bcb")
    dA = [p.sbuf([128, BLK], F32, f"dA{i}") for i in range(2)]
    Xs = [p.sbuf([128, BLK], F32, f"Xs{i}") for i in range(2)]
    hs = [p.sbuf([128, BLK], F32, f"hs{i}") for i in range(2)]
    hc = [p.sbuf([128, BLK], BF16, f"hc{i}") for i in range(2)]
    dtu = p.sbuf([128, 2 * CH], BF16, "dtu")
    dtuv = dtu.rearrange("p (c t) -> p c t", c=2)
    carry = p.sbuf([128, 64], F32, "carry")
    p.memset(carry, 0.0)
    ystage = [p.sbuf([8, CH], F32, f"ystage{i}") for i in range(2)]

    NB = CH // BLK
    for b in range(B):
        for chn in range(NCH):
            cols = slice(b * L + chn * CH, b * L + (chn + 1) * CH)
            p.dma(xdbl[0], xdp_h[0, :, cols])
            for cc in range(1, 8):
                src, dst = xdbl[(cc + 1) % 2], xdbl[cc % 2]
                p.dma(xpart, xdp_h[cc, :, cols])
                p.tt(dst, src, xpart, ALU.add)
            xd = xdbl[7 % 2]
            p.cp(dtlowb, xd[0:64, :])
            for pc in range(2):
                for nb in range(NB):
                    ps = p.bankf(nb % 2)
                    p.mm(ps, wdt[0:64, 0, pc * 128:(pc + 1) * 128],
                         dtlowb[:, nb * BLK:(nb + 1) * BLK], True, True)
                    p.act(sp1, ps, AF.Abs, bias=bdt[:, pc:pc + 1])
                    p.act(sp1, sp1, AF.Exp, scale=-1.0)
                    p.act(sp1, sp1, AF.Ln, bias=p.constf(1.0))
                    p.act(sp2, ps, AF.Relu, bias=bdt[:, pc:pc + 1])
                    p.tt(dtv[:, pc, nb * BLK:(nb + 1) * BLK], sp1, sp2,
                         ALU.add)
                p.tt(dtuv[:, pc, :], dtv[:, pc, :], uv[:, pc, cols], ALU.mult)
            p.cp(bcf, xd[64:96, :])
            p.cp(bcb, bcf)
            for ds8 in range(8):
                p.dma(btile[ds8 * 16:(ds8 + 1) * 16, :], bcb[0:16, :],
                      eng="gpsimd")
                p.dma(ctile[ds8 * 16:(ds8 + 1) * 16, :], bcb[16:32, :],
                      eng="gpsimd")
            for g in range(32):
                pc = g // 16
                gi = g % 16
                ccol = b * 32 + g
                gr = g % 2
                ys = ystage[gr]
                for nb in range(NB):
                    ri = nb % 2
                    bcols = slice(nb * BLK, (nb + 1) * BLK)
                    pdt = p.bankf(2 + ri)
                    p.mm(pdt, sv[:, gi, :], dtv[:, pc, bcols], True, True)
                    p.act(dA[ri], pdt, AF.Exp, scale=A[:, g:g + 1])
                    pdu = p.bankf(4 + ri)
                    p.mm(pdu, sv[:, gi, :], dtuv[:, pc, bcols], True, True)
                    p.tt(Xs[ri], pdu, btile[:, bcols], ALU.mult)
                    p.scan(hs[ri], dA[ri], Xs[ri], carry[:, ccol:ccol + 1])
                    p.cp(carry[:, ccol:ccol + 1], hs[ri][:, BLK - 1:BLK])
                    p.tt(hc[ri], hs[ri], ctile[:, bcols], ALU.mult)
                    py = p.bankf(6 + ri)[0:8, :]
                    p.mm(py, nsel, hc[ri], True, True)
                    p.cp(ys[:, bcols], py, eng="scalar")
                rows = slice(pc * 128 + 8 * gi, pc * 128 + 8 * gi + 8)
                p.dma(o_y[rows, cols], ys, eng="gpsimd")
    return p


def build_l1d1():
    p = Prog("l1d1")
    y_h = p.dram_in("y", (16, 128, NT), BF16)
    du_h = p.dram_in("du", (16, 128, NT), BF16)
    z_h = p.dram_in("z", (16, 128, NT), BF16)
    wout_h = p.dram_in("wout", (DI, D))
    x_h = p.dram_in("x", (NT, D))
    gam_h = p.dram_in("gam", (1, D))
    bet_h = p.dram_in("bet", (1, D))
    o_s = p.dram_out("s", (NT, D), BF16)

    wout = p.wload(wout_h, DI, D, "wout")
    yg = p.sbuf([128, 16 * NT], BF16, "yg")
    ygv = yg.rearrange("p (c t) -> p c t", c=16)
    ych = p.sbuf([128, NT], BF16, "ych")
    duch = p.sbuf([128, NT], BF16, "duch")
    zch = p.sbuf([128, NT], BF16, "zch")
    t1 = p.sbuf([128, NT], F32, "t1")
    t2 = p.sbuf([128, NT], F32, "t2")
    for ch in range(16):
        p.dma(ych, y_h[ch])
        p.dma(duch, du_h[ch])
        p.dma(zch, z_h[ch])
        p.act(t1, zch, AF.Silu)
        p.tt(t2, ych, duch, ALU.add)
        p.tt(ygv[:, ch, :], t1, t2, ALU.mult)

    gam = p.sbuf([1, D], BF16, "gam")
    bet = p.sbuf([1, D], BF16, "bet")
    p.dma(gam, gam_h, eng="gpsimd")
    p.dma(bet, bet_h, eng="gpsimd")
    ones1 = p.sbuf([1, 128], BF16, "ones1")
    p.memset(ones1, 1.0)
    gamb = p.sbuf([128, D], F32, "gamb")
    betb = p.sbuf([128, D], F32, "betb")
    for fo in range(2):
        ps = p.bankf(fo % 2)
        p.mm(ps, ones1, gam[:, fo * 512:(fo + 1) * 512], True, True)
        p.cp(gamb[:, fo * 512:(fo + 1) * 512], ps, eng="vector")
        p.mm(ps, ones1, bet[:, fo * 512:(fo + 1) * 512], True, True)
        p.cp(betb[:, fo * 512:(fo + 1) * 512], ps, eng="vector")

    xt = [p.sbuf([128, D], F32, f"xt{i}") for i in range(2)]
    xn = p.sbuf([128, D], F32, "xn")
    res = p.sbuf([128, D], F32, "res")
    xsq = p.sbuf([128, D], F32, "xsq")
    statsall = p.sbuf([128, TT * 8], F32, "stats")
    sout = [p.sbuf([128, D], BF16, f"sout{i}") for i in range(2)]
    x3 = x_h.rearrange("(t p) d -> t p d", p=128)
    s3 = o_s.rearrange("(t p) d -> t p d", p=128)
    for t in range(TT):
        bi = t % 2
        tok = slice(t * 128, (t + 1) * 128)
        p.dma(xt[bi], x3[t])
        st = statsall[:, t * 8:(t + 1) * 8]
        nm, vr, sd, rstd, vr2, nm2, vr3, sd2 = (st[:, i:i + 1] for i in range(8))
        p.red(nm, xt[bi], ALU.add)
        p.ts(nm, nm, -1.0 / D, None, op0=ALU.mult)
        p.ts(xn, xt[bi], nm, None, op0=ALU.add)
        p.act(xsq[:, 0:512], xn[:, 0:512], AF.Square, accum_out=vr)
        p.act(xsq[:, 512:1024], xn[:, 512:1024], AF.Square, accum_out=vr2)
        p.tt(vr, vr, vr2, ALU.add)
        p.act(sd, vr, AF.Sqrt, bias=p.constf(1e-5), scale=1.0 / D)
        p.recip(rstd, sd)
        p.ts(xn, xn, rstd, None, op0=ALU.mult)
        for fo in range(2):
            ps = p.bankf(2 + fo)
            fcols = slice(fo * BLK, (fo + 1) * BLK)
            p.mm_acc(ps, [(ygv[:, ki, tok], wout[:, ki, fcols])
                          for ki in range(16)])
            p.tt(res[:, fcols], ps, xn[:, fcols], ALU.add)
        p.red(nm2, res, ALU.add)
        p.ts(nm2, nm2, -1.0 / D, None, op0=ALU.mult)
        p.ts(res, res, nm2, None, op0=ALU.add)
        p.act(xsq[:, 0:512], res[:, 0:512], AF.Square, accum_out=vr3)
        p.act(xsq[:, 512:1024], res[:, 512:1024], AF.Square, accum_out=sd2)
        p.tt(vr3, vr3, sd2, ALU.add)
        p.act(sd2, vr3, AF.Sqrt, bias=p.constf(1e-5), scale=1.0 / D)
        p.recip(rstd, sd2)
        p.ts(res, res, rstd, None, op0=ALU.mult)
        so = sout[bi]
        p.tt(res, res, gamb, ALU.mult)
        p.tt(so, res, betb, ALU.add)
        p.dma(s3[t], so)
    return p


def build_l1d2():
    p = Prog("l1d2")
    xnfm_h = p.dram_in("xnfm", (8, 128, NT), BF16)
    wg_h = p.dram_in("wg", (D, 8))
    wu_h = p.dram_in("wu", (8, D, DI))
    wd_h = p.dram_in("wd", (8, DI, D))
    w1_h = p.dram_in("w1", (D, DI))
    w2_h = p.dram_in("w2", (DI, D))
    s_h = p.dram_in("s", (NT, D), BF16)
    x_h = p.dram_in("x", (NT, D))
    o_x = p.dram_out("xout", (NT, D))

    xnfm = p.sbuf([128, 8 * NT], BF16, "xnfm")
    p.dma(xnfm.rearrange("p (c t) -> p c t", c=8),
          xnfm_h.rearrange("c p t -> p c t"))
    xnv = xnfm.rearrange("p (c t) -> p c t", c=8)
    wg = p.wload(wg_h, D, 8, "wg")

    cwall = p.sbuf([128, TT * 8], F32, "cwall")
    iot8m = p.sbuf([128, 8], F32, "iot8m")
    p.iota(iot8m, [[1, 8]], base=-8, channel_multiplier=0)
    gp = p.sbuf([128, 8], F32, "gp")
    gp2 = p.sbuf([128, 8], F32, "gp2")
    cand = p.sbuf([128, 8], F32, "cand")
    mio = p.sbuf([128, 8], F32, "mio")
    oh1 = p.sbuf([128, 8], F32, "oh1")
    oh2 = p.sbuf([128, 8], F32, "oh2")
    v1 = p.sbuf([128, 1], F32, "v1")
    v2 = p.sbuf([128, 1], F32, "v2")
    i1 = p.sbuf([128, 1], F32, "i1")
    i2 = p.sbuf([128, 1], F32, "i2")
    r1 = p.sbuf([128, 1], F32, "r1")
    r2 = p.sbuf([128, 1], F32, "r2")
    vs = p.sbuf([128, 1], F32, "vs")
    BIG = 1.0e9
    for t in range(TT):
        tok = slice(t * 128, (t + 1) * 128)
        ps2 = p.bankf(0)[:, 0:8]
        p.mm_acc(ps2, [(xnv[:, ki, tok], wg[:, ki, :]) for ki in range(8)])
        p.red(r1, ps2, ALU.max)
        p.ts(r1, r1, -1.0, None, op0=ALU.mult)
        p.act(gp, ps2, AF.Exp, bias=r1, accum_out=r2)
        p.recip(r2, r2)
        p.ts(gp, gp, r2, None, op0=ALU.mult)
        p.red(v1, gp, ALU.max)
        p.ts(cand, gp, v1, None, op0=ALU.is_ge)
        p.tt(mio, cand, iot8m, ALU.mult)
        p.ts(mio, mio, 8.0, None, op0=ALU.add)
        p.red(i1, mio, ALU.min)
        p.ts(i1, i1, -8.0, None, op0=ALU.add)
        p.ts(oh1, iot8m, i1, None, op0=ALU.is_equal)
        p.stt(gp2, oh1, -BIG, gp, op0=ALU.mult, op1=ALU.add)
        p.red(v2, gp2, ALU.max)
        p.ts(cand, gp2, v2, None, op0=ALU.is_ge)
        p.tt(mio, cand, iot8m, ALU.mult)
        p.ts(mio, mio, 8.0, None, op0=ALU.add)
        p.red(i2, mio, ALU.min)
        p.ts(i2, i2, -8.0, None, op0=ALU.add)
        p.ts(oh2, iot8m, i2, None, op0=ALU.is_equal)
        p.tt(vs, v1, v2, ALU.add)
        p.recip(vs, vs)
        cw = cwall[:, t * 8:(t + 1) * 8]
        p.ts(cw, oh1, v1, None, op0=ALU.mult)
        p.stt(cw, oh2, v2, cw, op0=ALU.mult, op1=ALU.add)
        p.ts(cw, cw, vs, None, op0=ALU.mult)

    acc = p.sbuf([128, TT * D], F32, "acc")
    accv = acc.rearrange("p (t d) -> p t d", t=TT)
    wu = p.sbuf([128, 8 * DI], BF16, "wu")
    wuv = wu.rearrange("p (k n) -> p k n", k=8)
    wd = p.sbuf([128, 16 * D], BF16, "wd")
    wdv = wd.rearrange("p (k n) -> p k n", k=16)
    hidb = p.sbuf([128, 16 * BLK], BF16, "hidb")
    hv = hidb.rearrange("p (k n) -> p k n", k=16)

    for e in range(9):
        if e < 8:
            p.dma(wuv, wu_h[e].rearrange("(k p) n -> p k n", p=128),
                  eng="gpsimd")
            p.dma(wdv, wd_h[e].rearrange("(k p) n -> p k n", p=128),
                  eng="gpsimd")
        else:
            p.dma(wuv, w1_h.rearrange("(k p) n -> p k n", p=128),
                  eng="gpsimd")
            p.dma(wdv, w2_h.rearrange("(k p) n -> p k n", p=128),
                  eng="gpsimd")
        for nb in range(NBLK):
            tok = slice(nb * BLK, (nb + 1) * BLK)
            for po in range(16):
                ps = p.bankf(po % 2)
                p.mm_acc(ps, [(wuv[:, ki, po * 128:(po + 1) * 128],
                               xnv[:, ki, tok]) for ki in range(8)])
                p.act(hv[:, po, :], ps, AF.Gelu)
            for j in range(4):
                tg = nb * 4 + j
                for fo in range(2):
                    ps = p.bankf(2 + (j * 2 + fo) % 2)
                    fcols = slice(fo * BLK, (fo + 1) * BLK)
                    p.mm_acc(ps, [(hv[:, ki, j * 128:(j + 1) * 128],
                                   wdv[:, ki, fcols]) for ki in range(16)])
                    dst = accv[:, tg, fcols]
                    if e == 0:
                        p.ts(dst, ps, cwall[:, tg * 8:tg * 8 + 1], None,
                             op0=ALU.mult)
                    elif e < 8:
                        p.stt(dst, ps, cwall[:, tg * 8 + e:tg * 8 + e + 1],
                              dst, op0=ALU.mult, op1=ALU.add)
                    else:
                        p.stt(dst, ps, 1.0, dst, op0=ALU.mult, op1=ALU.add)

    xtl = p.sbuf([128, D], F32, "xtl0")
    stl = p.sbuf([128, D], BF16, "stl0")
    x3 = x_h.rearrange("(t p) d -> t p d", p=128)
    sS = s_h.rearrange("(t p) d -> t p d", p=128)
    o3 = o_x.rearrange("(t p) d -> t p d", p=128)
    for t in range(TT):
        at = accv[:, t, :]
        p.dma(xtl, x3[t])
        p.dma(stl, sS[t], eng="gpsimd")
        p.act(at, at, AF.Silu)
        p.tt(at, at, stl, ALU.mult)
        p.tt(at, at, xtl, ALU.add)
        p.dma(o3[t], at)
    return p


def run_l1(inp, x_full):
    ident, slt = _consts()
    xs = [np.ascontiguousarray(x_full[c * NT:(c + 1) * NT]) for c in range(NCORES)]
    im = [dict(x=xs[c], win=inp["L1_attn_W_in"], ident=ident)
          for c in range(NCORES)]
    r1 = _run("l1b1", build_l1b1, im)

    xin_full = np.concatenate([r["xin"] for r in r1], axis=2).reshape(DI, NTOK)
    im = []
    for c in range(NCORES):
        ds = slice(c * DSL, (c + 1) * DSL)
        cw = inp["L1_attn_conv_w"][ds].reshape(2, 128, 4).transpose(1, 0, 2)
        cb = inp["L1_attn_conv_b"][ds].reshape(2, 128).T
        im.append(dict(
            xin=np.ascontiguousarray(xin_full[ds].reshape(2, 128, NTOK)),
            convw=np.ascontiguousarray(cw), convb=np.ascontiguousarray(cb),
            wxp=np.ascontiguousarray(inp["L1_attn_W_xproj"][ds])))
    r2 = _run("l1b2", build_l1b2, im)

    xdp_all = np.stack([r["xdp"] for r in r2])
    sels = np.zeros((16, 128, 128), dtype=BF)
    for gi in range(16):
        for pp in range(128):
            sels[gi, 8 * gi + pp // 16, pp] = 1
    nsel = np.zeros((128, 8), dtype=BF)
    for pp in range(128):
        nsel[pp, pp // 16] = 1
    im = []
    for c in range(NCORES):
        ds = slice(c * DSL, (c + 1) * DSL)
        Asl = inp["L1_attn_A_log"][ds]
        aarr = np.zeros((128, 32), dtype=np.float32)
        for g in range(32):
            for pp in range(128):
                aarr[pp, g] = Asl[(g // 16) * 128 + 8 * (g % 16) + pp // 16,
                                  pp % 16]
        bdt = inp["L1_attn_b_dt"][ds].reshape(2, 128).T
        darr = inp["L1_attn_D"][ds].reshape(2, 128).T
        im.append(dict(
            u=r2[c]["u"], xdp=xdp_all,
            wdt=np.ascontiguousarray(inp["L1_attn_W_dt"][:, ds]),
            bdt=np.ascontiguousarray(bdt), aarr=aarr,
            darr=np.ascontiguousarray(darr), sels=sels, nsel=nsel))
    r3 = _run("l1b3", build_l1b3, im)

    y_full = np.concatenate([r["y"] for r in r3], axis=0)
    du_full = np.concatenate([r["du"].reshape(DSL, NTOK) for r in r3], axis=0)
    im = []
    for c in range(NCORES):
        cols = slice(c * NT, (c + 1) * NT)
        im.append(dict(
            y=np.ascontiguousarray(y_full[:, cols].reshape(16, 128, NT)),
            du=np.ascontiguousarray(du_full[:, cols].reshape(16, 128, NT)),
            z=r1[c]["z"], wout=inp["L1_attn_W_out"], x=xs[c],
            gam=np.ascontiguousarray(inp["L1_attn_gamma"][None, :]),
            bet=np.ascontiguousarray(inp["L1_attn_beta"][None, :])))
    r4 = _run("l1d1", build_l1d1, im)

    im = []
    for c in range(NCORES):
        im.append(dict(
            xnfm=r1[c]["xnfm"], wg=inp["L1_ffn_Wg"], wu=inp["L1_ffn_Wu"],
            wd=inp["L1_ffn_Wd"], w1=inp["L1_ffn_W1"], w2=inp["L1_ffn_W2"],
            s=r4[c]["s"], x=xs[c]))
    r5 = _run("l1d2", build_l1d2, im)
    x2 = np.concatenate([r["xout"] for r in r5], axis=0)
    return x2, r4




def build_l2e1():
    p = Prog("l2e1")
    x = p.dram_in("x", (NT, D))
    wx_h = p.dram_in("wx", (16, D, 64))
    bx_h = p.dram_in("bx", (64, 16))
    ident_h = p.dram_in("ident", (128, 128), BF16)
    o_xnfm = p.dram_out("xnfm", (8, 128, NT), BF16)
    o_h = p.dram_out("h", (16, 64, NT), BF16)
    o_hn = p.dram_out("hn", (16, 64, NT), BF16)

    xnfm = p.sbuf([128, 8 * NT], BF16, "xnfm")
    idsb = p.sbuf([128, 128], BF16, "idsb")
    p.dma(idsb, ident_h)
    xnv = xnfm.rearrange("p (c t) -> p c t", c=8)

    xt = [p.sbuf([128, D], F32, f"xt{i}") for i in range(2)]
    xc = [p.sbuf([128, D], F32, f"xc{i}") for i in range(2)]
    xnbf = [p.sbuf([128, D], BF16, f"xnbf{i}") for i in range(2)]
    xsq = p.sbuf([128, D], F32, "xsq")
    statsall = p.sbuf([128, TT * 8], F32, "stats")
    x3 = x.rearrange("(t p) d -> t p d", p=128)
    for t in range(TT):
        bi = t % 2
        p.dma(xt[bi], x3[t])
        st = statsall[:, t * 8:(t + 1) * 8]
        nm, vr, sd, rstd, vr2 = (st[:, i:i + 1] for i in range(5))
        p.red(nm, xt[bi], ALU.add)
        p.ts(nm, nm, -1.0 / D, None, op0=ALU.mult)
        p.ts(xc[bi], xt[bi], nm, None, op0=ALU.add)
        p.act(xsq[:, 0:512], xc[bi][:, 0:512], AF.Square, accum_out=vr)
        p.act(xsq[:, 512:1024], xc[bi][:, 512:1024], AF.Square, accum_out=vr2)
        p.tt(vr, vr, vr2, ALU.add)
        p.act(sd, vr, AF.Sqrt, bias=p.constf(1e-5), scale=1.0 / D)
        p.recip(rstd, sd)
        p.ts(xnbf[bi], xc[bi], rstd, None, op0=ALU.mult)
        for ch in range(8):
            pb = p.bankb(ch % 2)[:, 0:128]
            p.tp(pb, xnbf[bi][:, ch * 128:(ch + 1) * 128], idsb)
            p.cp(xnv[:, ch, t * 128:(t + 1) * 128], pb, eng="vector")
    p.dma(o_xnfm.rearrange("c p t -> p c t"), xnv)

    wx = p.sbuf([128, 8 * 16 * 64], BF16, "wx")
    wxv = wx.rearrange("p (h k n) -> p h k n", h=16, k=8)
    for hh in range(16):
        p.dma(wxv[:, hh], wx_h[hh].rearrange("(k p) n -> p k n", p=128),
              eng="gpsimd")
    bx = p.sbuf([64, 16], F32, "bx")
    p.dma(bx, bx_h)

    hsb = p.sbuf([64, BLK], F32, "hsb")
    hbf = p.sbuf([64, BLK], BF16, "hbf")
    hsq = p.sbuf([64, BLK], F32, "hsq")
    onesf = p.sbuf([64, 1], F32, "onesf")
    p.memset(onesf, 1.0)
    ones1b = p.sbuf([1, 64], BF16, "ones1b")
    p.memset(ones1b, 1.0)
    nr = p.sbuf([1, BLK], F32, "nr")
    nrb = p.sbuf([1, BLK], BF16, "nrb")
    hnb = p.sbuf([64, BLK], BF16, "hnb")
    for h in range(16):
        for nb in range(NBLK):
            tok = slice(nb * BLK, (nb + 1) * BLK)
            ps = p.bankf(0)[0:64, :]
            p.mm_acc(ps, [(wxv[:, h, ki, :], xnv[:, ki, tok])
                          for ki in range(8)])
            p.ts(hsb, ps, bx[:, h:h + 1], None, op0=ALU.add)
            p.cp(hbf, hsb)
            p.dma(o_h[h][:, tok], hbf)
            p.act(hsq, hsb, AF.Square)
            psn = p.bankf(1)[0:1, :]
            p.op("tensor", lambda ps2=psn, a=onesf, b=hsq:
                 p.nc.tensor.matmul(ps2, a, b, start=True, stop=True),
                 reads=[onesf, hsq], writes=[psn])
            p.act(nr, psn, AF.Sqrt)
            p.recip(nr, nr)
            p.cp(nrb, nr)
            psb = p.bankf(2)[0:64, :]
            p.mm(psb, ones1b, nrb, True, True)
            p.tt(hnb, hbf, psb, ALU.mult)
            p.dma(o_hn[h][:, tok], hnb)
    return p


def build_l2e2():
    p = Prog("l2e2")
    hn_h = p.dram_in("hn", (4, 64, L), BF16)
    h_h = p.dram_in("h", (4, 64, L), BF16)
    ident_h = p.dram_in("ident", (128, 128), BF16)
    o_out = p.dram_out("out", (4, 64, L), BF16)

    idsb = p.sbuf([128, 128], BF16, "idsb")
    p.dma(idsb, ident_h)
    hn = p.sbuf([64, L], BF16, "hn")
    hf = p.sbuf([64, L], BF16, "hf")
    htm = p.sbuf([128, 64 * 65], BF16, "htm")
    hv = htm.rearrange("p (k n) -> p k n", k=64)
    PT = [p.sbuf([128, 1024], BF16, f"pt{i}") for i in range(4)]
    ps1024 = [p.psum([128, 1024], F32, f"ps1024_{i}") for i in range(2)]
    ob = p.sbuf([64, 1024], BF16, "ob")
    rd = p.sbuf([1, 1024], F32, "rd")
    rdb = p.sbuf([1, 1024], BF16, "rdb")
    rdbc = p.sbuf([64, 512], F32, "rdbc")
    ones1b = p.sbuf([1, 64], BF16, "ones1b")
    p.memset(ones1b, 1.0)

    KB = 1024
    for bh in range(4):
        p.dma(hn, hn_h[bh])
        p.dma(hf, h_h[bh])
        p.memset(bass.AP(htm.tensor, htm.offset + 64,
                         [htm.ap[0], [65, 64], [1, 1]]), 1.0)
        for kc in range(64):
            pb = p.bankb(0)[:, 0:64]
            p.tp(pb, hf[:, kc * 128:(kc + 1) * 128], idsb)
            p.cp(hv[:, kc, 0:64], pb, eng="vector")
        for qb in range(L // KB):
            qs = slice(qb * KB, (qb + 1) * KB)
            for kc in range(64):
                ks = slice(kc * 128, (kc + 1) * 128)
                sc = ps1024[kc % 2]
                p.mm(sc[:, 0:512], hn[:, ks], hn[:, qb * KB: qb * KB + 512],
                     True, True)
                p.mm(sc[:, 512:1024], hn[:, ks],
                     hn[:, qb * KB + 512: (qb + 1) * KB], True, True)
                pt = PT[kc % 4]
                p.act(pt, sc, AF.Exp)
                p.mm(p.bankf(0)[0:65, :], hv[:, kc, :], pt[:, 0:512],
                     kc == 0, kc == 63)
                p.mm(p.bankf(1)[0:65, :], hv[:, kc, :], pt[:, 512:1024],
                     kc == 0, kc == 63)
            for half in range(2):
                pb = p.bankf(half)
                p.recip(rd[:, half * 512:(half + 1) * 512], pb[64:65, :])
            p.cp(rdb, rd)
            pbc = p.bankf(2)[0:64, :]
            for half in range(2):
                pb = p.bankf(half)
                p.mm(pbc, ones1b, rdb[:, half * 512:(half + 1) * 512],
                     True, True)
                p.cp(rdbc, pbc, eng="scalar")
                p.tt(ob[:, half * 512:(half + 1) * 512], pb[0:64, :], rdbc,
                     ALU.mult)
            p.dma(o_out[bh][:, qs], ob)
    return p


def build_l2e3():
    p = Prog("l2e3")
    cat_h = p.dram_in("cat", (8, 128, NT), BF16)
    xnfm_h = p.dram_in("xnfm", (8, 128, NT), BF16)
    wp_h = p.dram_in("wp", (D, D))
    bp_h = p.dram_in("bp", (128, 8))
    At_h = p.dram_in("At", (6, 16, 100))
    Bm_h = p.dram_in("Bm", (6, 16, D))
    wgate_h = p.dram_in("wgate", (D, 6))
    wtwT_h = p.dram_in("wtwT", (D, D))
    wtb_h = p.dram_in("wtb", (1, D))
    wdwT_h = p.dram_in("wdwT", (D, D))
    wdb_h = p.dram_in("wdb", (128, 8))
    scal_h = p.dram_in("scal", (1, 1))
    x_h = p.dram_in("x", (NT, D))
    ident_h = p.dram_in("ident", (128, 128), BF16)
    o_x = p.dram_out("xout", (NT, D))

    idsb = p.sbuf([128, 128], BF16, "idsb")
    p.dma(idsb, ident_h)
    cat = p.sbuf([128, 8 * NT], BF16, "cat")
    p.dma(cat.rearrange("p (c t) -> p c t", c=8),
          cat_h.rearrange("c p t -> p c t"))
    catv = cat.rearrange("p (c t) -> p c t", c=8)
    xnfm = p.sbuf([128, 8 * NT], BF16, "xnfm")
    p.dma(xnfm.rearrange("p (c t) -> p c t", c=8),
          xnfm_h.rearrange("c p t -> p c t"))
    xnv = xnfm.rearrange("p (c t) -> p c t", c=8)
    wp = p.wload(wp_h, D, D, "wp")
    bp = p.sbuf([128, 8], F32, "bp")
    p.dma(bp, bp_h)
    wgate = p.wload(wgate_h, D, 6, "wgate")
    wtwT_sb = p.sbuf([128, 8 * D], BF16, "wtwT")
    wtwT = wtwT_sb.rearrange("p (k n) -> p k n", k=8)
    p.dma(wtwT, wtwT_h.rearrange("(k p) n -> p k n", p=128), eng="gpsimd")
    wtb = p.sbuf([1, D], BF16, "wtb")
    p.dma(wtb, wtb_h, eng="gpsimd")
    wdwT = wtwT
    wdb = p.sbuf([128, 8], F32, "wdb")
    p.dma(wdb, wdb_h)
    scal = p.sbuf([1, 1], F32, "scal")
    p.dma(scal, scal_h)
    ones1 = p.sbuf([1, 128], BF16, "ones1")
    p.memset(ones1, 1.0)
    onesf1 = p.sbuf([1, 128], F32, "onesf1")
    p.memset(onesf1, 1.0)

    shib = p.sbuf([1, 1], BF16, "shib")
    slof = p.sbuf([1, 1], F32, "slof")
    slob = p.sbuf([1, 1], BF16, "slob")
    sbc = p.sbuf([128, 1], F32, "sbc")
    p.cp(shib, scal)
    p.tt(slof, scal, shib, ALU.subtract)
    p.cp(slob, slof)
    psc = p.bankf(4)[:, 300:301]
    p.op("tensor", lambda: p.nc.tensor.matmul(psc, ones1, shib, start=True,
                                              stop=False),
         reads=[ones1, shib], writes=[psc])
    p.op("tensor", lambda: p.nc.tensor.matmul(psc, ones1, slob, start=False,
                                              stop=True),
         reads=[ones1, slob], writes=[psc])
    p.cp(sbc, psc)
    wdbs = p.sbuf([128, 8], F32, "wdbs")
    p.ts(wdbs, wdb, sbc[:, 0:1], None, op0=ALU.mult)

    At = p.sbuf([16, 6 * 100], F32, "At")
    p.dma(At.rearrange("p (e l) -> p e l", e=6), At_h.rearrange("e p l -> p e l"))
    Atb = p.sbuf([16, 6 * 100], BF16, "Atb")
    p.cp(Atb, At)
    Atv = Atb.rearrange("p (e l) -> p e l", e=6)
    Bm = p.sbuf([16, 6 * D], F32, "Bm")
    p.dma(Bm.rearrange("p (e d) -> p e d", e=6), Bm_h.rearrange("e p d -> p e d"))
    Bmb = p.sbuf([16, 6 * D], BF16, "Bmb")
    p.cp(Bmb, Bm)
    Bmv = Bmb.rearrange("p (e d) -> p e d", e=6)
    tok_fm = p.sbuf([128, 8 * 600], BF16, "tokfm")
    tv = tok_fm.rearrange("p (c el) -> p c el", c=8)
    for e in range(6):
        for ch in range(8):
            ps = p.bankf(ch % 2)[:, 0:100]
            p.mm(ps, Bmv[:, e, ch * 128:(ch + 1) * 128], Atv[:, e, :],
                 True, True)
            p.cp(tv[:, ch, e * 100:(e + 1) * 100], ps, eng="vector")
    val = p.sbuf([128, 5 * D], BF16, "val")
    vv = val.rearrange("p (c d) -> p c d", c=5)
    for elc in range(5):
        rows = min(128, 600 - elc * 128)
        for fo in range(2):
            ps = p.bankf(2 + fo % 2)[0:rows, :]
            fcols = slice(fo * 512, (fo + 1) * 512)
            pairs = [(tv[:, ki, elc * 128:elc * 128 + rows],
                      wtwT[:, ki, fcols]) for ki in range(8)]
            pairs.append((ones1[:, 0:rows], wtb[:, fcols]))
            p.mm_acc(ps, pairs)
            p.cp(vv[0:rows, elc, fcols], ps, eng="vector")

    p.dma(wdwT, wdwT_h.rearrange("(k p) n -> p k n", p=128), eng="gpsimd")

    iot6m = p.sbuf([128, 6], F32, "iot6m")
    p.iota(iot6m, [[1, 6]], base=-6, channel_multiplier=0)
    gl = p.sbuf([128, 6], F32, "gl")
    cand = p.sbuf([128, 6], F32, "cand")
    mio = p.sbuf([128, 6], F32, "mio")
    oh1 = p.sbuf([128, 6], F32, "oh1")
    v1 = p.sbuf([128, 1], F32, "v1")
    i1 = p.sbuf([128, 1], F32, "i1")
    r1 = p.sbuf([128, 1], F32, "r1")
    r2 = p.sbuf([128, 1], F32, "r2")
    att = p.sbuf([128, 600], F32, "att")
    attb = p.sbuf([128, 4 * 600], BF16, "attb")
    abv = attb.rearrange("p (j el) -> p j el", j=4)
    attT = p.sbuf([128, 5 * 512], BF16, "attT")
    atv = attT.rearrange("p (c j) -> p c j", c=5)
    delin = p.sbuf([128, 8 * BLK], BF16, "delin")
    dv = delin.rearrange("p (c t) -> p c t", c=8)
    expd = p.sbuf([128, BLK], F32, "expd")
    zs = p.sbuf([128, BLK], F32, "zs")
    gpre = p.sbuf([128, 8 * BLK], BF16, "gpre")
    gv = gpre.rearrange("p (c t) -> p c t", c=8)
    sfm = p.sbuf([128, 8 * BLK], BF16, "sfm")
    sv = sfm.rearrange("p (c t) -> p c t", c=8)
    sgb = p.sbuf([128, BLK], BF16, "sgb")
    xtl = p.sbuf([128, D], F32, "xtl")
    EPSF = float(np.finfo(np.float32).eps)

    x3v = x_h.rearrange("(t p) d -> t p d", p=128)
    o3 = o_x.rearrange("(t p) d -> t p d", p=128)
    for nb in range(NBLK):
        tokb = slice(nb * BLK, (nb + 1) * BLK)
        for j in range(4):
            t = nb * 4 + j
            tok = slice(t * 128, (t + 1) * 128)
            psg = p.bankf(0)[:, 0:6]
            p.mm_acc(psg, [(xnv[:, ki, tok], wgate[:, ki, :])
                           for ki in range(8)])
            p.cp(gl, psg)
            p.red(v1, gl, ALU.max)
            p.ts(cand, gl, v1, None, op0=ALU.is_ge)
            p.tt(mio, cand, iot6m, ALU.mult)
            p.ts(mio, mio, 6.0, None, op0=ALU.add)
            p.red(i1, mio, ALU.min)
            p.ts(i1, i1, -6.0, None, op0=ALU.add)
            p.ts(oh1, iot6m, i1, None, op0=ALU.is_equal)
            ps1 = p.bankf(1)
            ps2 = p.bankf(0)[:, 128:216]
            p.mm_acc(ps1, [(xnv[:, ki, tok], tv[:, ki, 0:512])
                           for ki in range(8)])
            p.mm_acc(ps2, [(xnv[:, ki, tok], tv[:, ki, 512:600])
                           for ki in range(8)])
            p.ts(att[:, 0:512], ps1, 1.0 / 32.0, None, op0=ALU.mult)
            p.ts(att[:, 512:600], ps2, 1.0 / 32.0, None, op0=ALU.mult)
            for e in range(6):
                sl = att[:, e * 100:(e + 1) * 100]
                p.red(r1, sl, ALU.max)
                p.ts(r1, r1, -1.0, None, op0=ALU.mult)
                p.act(sl, sl, AF.Exp, bias=r1, accum_out=r2)
                p.recip(r2, r2)
                p.ts(r2, r2, oh1[:, e:e + 1], None, op0=ALU.mult)
                p.ts(abv[:, j, e * 100:(e + 1) * 100], sl, r2, None,
                     op0=ALU.mult)
            p.memset(bass.AP(attb.tensor, attb.offset + j * 600,
                             [attb.ap[0], [100, 6], [1, 1]]), 0.0)
        for j in range(4):
            for elc in range(5):
                rows = min(128, 600 - elc * 128)
                pb = p.bankb(elc % 2)[0:rows, 0:128]
                p.tp(pb, abv[:, j, elc * 128:elc * 128 + rows], idsb)
                p.cp(atv[0:rows, elc, j * 128:(j + 1) * 128], pb,
                     eng="vector")
        for ch in range(8):
            ps = p.bankf(2 + ch % 2)
            pairs = []
            for elc in range(5):
                rows = min(128, 600 - elc * 128)
                pairs.append((vv[0:rows, elc, ch * 128:(ch + 1) * 128],
                              atv[0:rows, elc, :]))
            p.mm_acc(ps, pairs)
            p.tt(dv[:, ch, :], ps, xnv[:, ch, tokb], ALU.add)
        for ch in range(8):
            ps = p.bankf(4 + ch % 2)
            p.mm_acc(ps, [(wp[:, ki, ch * 128:(ch + 1) * 128],
                           catv[:, ki, tokb]) for ki in range(8)])
            p.ts(sv[:, ch, :], ps, bp[:, ch:ch + 1], None, op0=ALU.add)
        for ch in range(8):
            ps = p.bankf(2 + ch % 2)
            p.mm_acc(ps, [(wdwT[:, ki, ch * 128:(ch + 1) * 128],
                           dv[:, ki, :]) for ki in range(8)])
            p.act(expd, ps, AF.Exp, scale=sbc[:, 0:1],
                  bias=wdbs[:, ch:ch + 1])
            p.ts(zs, expd, 0.0, None, op0=ALU.is_equal)
            p.stt(expd, zs, EPSF, expd, op0=ALU.mult, op1=ALU.add)
            p.act(expd, expd, AF.Ln)
            p.tt(gv[:, ch, :], expd, xnv[:, ch, tokb], ALU.add)
        for j in range(4):
            t = nb * 4 + j
            p.dma(xtl, x3v[t])
            for ch in range(8):
                cols = slice(j * 128, (j + 1) * 128)
                p.act(zs[:, 0:128], gv[:, ch, cols], AF.Silu)
                p.tt(sgb[:, cols.start:cols.stop], zs[:, 0:128],
                     sv[:, ch, cols], ALU.mult)
                pb = p.bankb(ch % 2)[:, 0:128]
                p.tp(pb, sgb[:, cols.start:cols.stop], idsb)
                p.tt(xtl[:, ch * 128:(ch + 1) * 128],
                     xtl[:, ch * 128:(ch + 1) * 128], pb, ALU.add)
            p.dma(o3[t], xtl)
    return p


def run_l2(inp, x_full):
    ident, slt = _consts()
    xs = [np.ascontiguousarray(x_full[c * NT:(c + 1) * NT]) for c in range(NCORES)]
    im = [dict(x=xs[c], wx=inp["L2_attn_Wx"],
               bx=np.ascontiguousarray(inp["L2_attn_bx"].T), ident=ident)
          for c in range(NCORES)]
    r1 = _run("l2e1", build_l2e1, im)

    h_full = np.concatenate([r["h"] for r in r1], axis=2)
    hn_full = np.concatenate([r["hn"] for r in r1], axis=2)
    im = []
    for c in range(NCORES):
        b = c // 4
        hs = []
        hns = []
        for k in range(4):
            h = 4 * (c % 4) + k
            hs.append(h_full[h][:, b * L:(b + 1) * L])
            hns.append(hn_full[h][:, b * L:(b + 1) * L])
        im.append(dict(hn=np.ascontiguousarray(np.stack(hns)),
                       h=np.ascontiguousarray(np.stack(hs)), ident=ident))
    r2 = _run("l2e2", build_l2e2, im)

    out_full = np.zeros((D, NTOK), dtype=BF)
    for c in range(NCORES):
        b = c // 4
        for k in range(4):
            h = 4 * (c % 4) + k
            out_full[h * 64:(h + 1) * 64, b * L:(b + 1) * L] = r2[c]["out"][k]
    bp_arr = np.ascontiguousarray(inp["L2_attn_bp"].reshape(8, 128).T)
    wdb_arr = np.ascontiguousarray(inp["L2_ffn_wd_b"].reshape(8, 128).T)
    At = np.ascontiguousarray(inp["L2_ffn_A"].transpose(0, 2, 1))
    im = []
    for c in range(NCORES):
        cols = slice(c * NT, (c + 1) * NT)
        im.append(dict(
            cat=np.ascontiguousarray(out_full[:, cols].reshape(8, 128, NT)),
            xnfm=r1[c]["xnfm"], wp=inp["L2_attn_Wp"], bp=bp_arr,
            At=At, Bm=inp["L2_ffn_B"], wgate=inp["L2_ffn_w_gate"],
            wtwT=np.ascontiguousarray(inp["L2_ffn_wt_w"].T),
            wtb=np.ascontiguousarray(inp["L2_ffn_wt_b"][None, :]),
            wdwT=np.ascontiguousarray(inp["L2_ffn_wd_w"].T),
            wdb=wdb_arr,
            scal=np.ascontiguousarray(inp["L2_ffn_scale"][None, :]),
            x=xs[c], ident=ident))
    r3 = _run("l2e3", build_l2e3, im)
    x3 = np.concatenate([r["xout"] for r in r3], axis=0)
    return x3, r2





DIP = 4240
CONVD = 2176


def build_l3f1():
    p = Prog("l3f1")
    x = p.dram_in("x", (NT, D))
    ipw_h = p.dram_in("ipw", (DIP, D))
    xnw_h = p.dram_in("xnw", (1, D))
    xnb_h = p.dram_in("xnb", (1, D))
    ident_h = p.dram_in("ident", (128, 128), BF16)
    o_xnfm = p.dram_out("xnfm", (8, 128, NT), BF16)
    o_z = p.dram_out("z", (16, 128, NT), BF16)
    o_xbc = p.dram_out("xbc", (17, 128, NT), BF16)
    o_dtr = p.dram_out("dtr", (16, NT))

    idsb = p.sbuf([128, 128], BF16, "idsb")
    p.dma(idsb, ident_h)
    wnT = p.sbuf([128, 8 * DIP], BF16, "wnT")
    wv = wnT.rearrange("p (k n) -> p k n", k=8)
    wrow = p.sbuf([128, D], F32, "wrow")
    wsq = p.sbuf([128, D], F32, "wsq")
    wst = p.sbuf([128, 4], F32, "wst")
    wbf = p.sbuf([128, D], BF16, "wbf")
    NCHW = (DIP + 127) // 128
    for rc in range(NCHW):
        rows = min(128, DIP - rc * 128)
        p.dma(wrow[0:rows, :], ipw_h[rc * 128: rc * 128 + rows, :])
        rn, rn2, dv = wst[:, 0:1], wst[:, 1:2], wst[:, 2:3]
        p.act(wsq[0:rows, 0:512], wrow[0:rows, 0:512], AF.Square,
              accum_out=rn[0:rows])
        p.act(wsq[0:rows, 512:1024], wrow[0:rows, 512:1024], AF.Square,
              accum_out=rn2[0:rows])
        p.tt(rn[0:rows], rn[0:rows], rn2[0:rows], ALU.add)
        p.act(rn[0:rows], rn[0:rows], AF.Sqrt)
        p.ts(rn[0:rows], rn[0:rows], 1e-6, None, op0=ALU.max)
        p.recip(dv[0:rows], rn[0:rows])
        p.ts(wbf[0:rows, :], wrow[0:rows, :], dv[0:rows], 1.0 / 32.0,
             op0=ALU.mult, op1=ALU.mult)
        for ch in range(8):
            pb = p.bankb(ch % 2)[:, 0:rows]
            p.tp(pb, wbf[0:rows, ch * 128:(ch + 1) * 128], idsb)
            p.cp(wv[:, ch, rc * 128: rc * 128 + rows], pb, eng="vector")

    xnw = p.sbuf([1, D], BF16, "xnw")
    xnb = p.sbuf([1, D], BF16, "xnb")
    p.dma(xnw, xnw_h, eng="gpsimd")
    p.dma(xnb, xnb_h, eng="gpsimd")
    ones1 = p.sbuf([1, 128], BF16, "ones1")
    p.memset(ones1, 1.0)
    wb_ = p.sbuf([128, D], F32, "wb_")
    bb_ = p.sbuf([128, D], F32, "bb_")
    for fo in range(2):
        ps = p.bankf(0)
        p.mm(ps, ones1, xnw[:, fo * 512:(fo + 1) * 512], True, True)
        p.cp(wb_[:, fo * 512:(fo + 1) * 512], ps, eng="vector")
        p.mm(ps, ones1, xnb[:, fo * 512:(fo + 1) * 512], True, True)
        p.cp(bb_[:, fo * 512:(fo + 1) * 512], ps, eng="vector")

    xnfm = p.sbuf([128, 8 * NT], BF16, "xnfm")
    xnv = xnfm.rearrange("p (c t) -> p c t", c=8)
    ufm = p.sbuf([128, 8 * NT], BF16, "ufm")
    uv = ufm.rearrange("p (c t) -> p c t", c=8)
    xt = [p.sbuf([128, D], F32, f"xt{i}") for i in range(2)]
    xc = [p.sbuf([128, D], F32, f"xc{i}") for i in range(2)]
    ubf = [p.sbuf([128, D], BF16, f"ubf{i}") for i in range(2)]
    xnbf = [p.sbuf([128, D], BF16, f"xnbf{i}") for i in range(2)]
    xsq = p.sbuf([128, D], F32, "xsq")
    statsall = p.sbuf([128, TT * 8], F32, "stats")
    x3 = x.rearrange("(t p) d -> t p d", p=128)

    def ln(src, dst, st):
        nm, vr, sd, rstd, vr2 = (st[:, i:i + 1] for i in range(5))
        p.red(nm, src, ALU.add)
        p.ts(nm, nm, -1.0 / D, None, op0=ALU.mult)
        p.ts(dst, src, nm, None, op0=ALU.add)
        p.act(xsq[:, 0:512], dst[:, 0:512], AF.Square, accum_out=vr)
        p.act(xsq[:, 512:1024], dst[:, 512:1024], AF.Square, accum_out=vr2)
        p.tt(vr, vr, vr2, ALU.add)
        p.act(sd, vr, AF.Sqrt, bias=p.constf(1e-5), scale=1.0 / D)
        p.recip(rstd, sd)
        p.ts(dst, dst, rstd, None, op0=ALU.mult)

    for t in range(TT):
        bi = t % 2
        p.dma(xt[bi], x3[t])
        st = statsall[:, t * 8:(t + 1) * 8]
        ln(xt[bi], xc[bi], st)
        p.cp(xnbf[bi], xc[bi])
        ln(xc[bi], xt[bi], st)
        p.tt(xt[bi], xt[bi], wb_, ALU.mult)
        p.tt(xt[bi], xt[bi], bb_, ALU.add)
        p.cp(ubf[bi], xt[bi])
        for ch in range(8):
            pb = p.bankb(ch % 2)[:, 0:128]
            p.tp(pb, xnbf[bi][:, ch * 128:(ch + 1) * 128], idsb)
            p.cp(xnv[:, ch, t * 128:(t + 1) * 128], pb, eng="vector")
            pb2 = p.bankb(ch % 2)[:, 128:256]
            p.tp(pb2, ubf[bi][:, ch * 128:(ch + 1) * 128], idsb)
            p.cp(uv[:, ch, t * 128:(t + 1) * 128], pb2, eng="scalar")
    p.dma(o_xnfm.rearrange("c p t -> p c t"), xnv)

    ob = [p.sbuf([128, BLK], BF16, f"ob{i}") for i in range(4)]
    dtrb = p.sbuf([16, NT], F32, "dtrb")
    for nb in range(NBLK):
        tok = slice(nb * BLK, (nb + 1) * BLK)
        for po in range(34):
            rows = min(128, DIP - po * 128)
            ps = p.bankf(po % 4)[0:rows, :]
            p.mm_acc(ps, [(wv[:, ki, po * 128:po * 128 + rows],
                           uv[:, ki, tok]) for ki in range(8)])
            if po < 16:
                o = ob[po % 4]
                p.cp(o, ps, eng="scalar" if po % 2 else "vector")
                p.dma(o_z[po][:, tok], o)
            elif po < 33:
                o = ob[po % 4]
                p.cp(o, ps, eng="scalar" if po % 2 else "vector")
                p.dma(o_xbc[po - 16][:, tok], o)
            else:
                p.cp(dtrb[:, tok], ps, eng="vector")
    p.dma(o_dtr, dtrb)
    return p


def build_l3f2():
    p = Prog("l3f2")
    xh_h = p.dram_in("xh", (4, 128, L), BF16)
    bc_h = p.dram_in("bc", (128, L), BF16)
    cwx_h = p.dram_in("cwx", (4, 128, 4))
    cbx_h = p.dram_in("cbx", (4, 128, 1))
    cwbc_h = p.dram_in("cwbc", (128, 4))
    cbbc_h = p.dram_in("cbbc", (128, 1))
    dtr_h = p.dram_in("dtr", (4, L))
    dtb_h = p.dram_in("dtb", (4, 32, 1))
    alg_h = p.dram_in("alg", (4, 32, 1))
    drep_h = p.dram_in("drep", (4, 128, 1))
    z_h = p.dram_in("z", (4, 128, L), BF16)
    maskt_h = p.dram_in("maskt", (2, 128, 256))
    ident_h = p.dram_in("ident", (128, 128), BF16)
    identf_h = p.dram_in("identf", (128, 128))
    o_g = p.dram_out("g", (4, 128, L), BF16)

    idsb = p.sbuf([128, 128], BF16, "idsb")
    p.dma(idsb, ident_h)
    idf = p.sbuf([128, 128], F32, "idf")
    p.dma(idf, identf_h)
    masks = p.sbuf([128, 2 * 256], F32, "masks")
    p.dma(masks.rearrange("p (m t) -> p m t", m=2),
          maskt_h.rearrange("m p t -> p m t"))
    mk = masks.rearrange("p (m t) -> p m t", m=2)
    ones1 = p.sbuf([1, 128], BF16, "ones1")
    p.memset(ones1, 1.0)
    ones64 = p.sbuf([1, 64], BF16, "ones64")
    p.memset(ones64, 1.0)

    bcin = p.sbuf([128, L], BF16, "bcin")
    p.dma(bcin, bc_h)
    cwbc = p.sbuf([128, 4], F32, "cwbc")
    p.dma(cwbc, cwbc_h)
    cbbc = p.sbuf([128, 1], F32, "cbbc")
    p.dma(cbbc, cbbc_h)
    acc = p.sbuf([128, L], F32, "acc")
    p.ts(acc, bcin, cwbc[:, 3:4], None, op0=ALU.mult)
    for j in (2, 1, 0):
        sh = 3 - j
        p.stt(acc[:, sh:L], bcin[:, 0:L - sh], cwbc[:, j:j + 1],
              acc[:, sh:L], op0=ALU.mult, op1=ALU.add)
    bcf = p.sbuf([128, L], BF16, "bcf")
    p.act(bcf, acc, AF.Silu, bias=cbbc[:, 0:1])
    cfm = p.sbuf([64, L], BF16, "cfm")
    p.cp(cfm, bcf[64:128, :])

    xin = p.sbuf([128, L], BF16, "xin")
    xfm = p.sbuf([128, L], BF16, "xfm")
    zfm = p.sbuf([128, L], BF16, "zfm")
    cwx = p.sbuf([128, 4], F32, "cwx")
    cbx = p.sbuf([128, 1], F32, "cbx")
    dts = p.sbuf([32, 256], F32, "dts")
    sp1 = p.sbuf([32, 256], F32, "sp1")
    sp2 = p.sbuf([32, 256], F32, "sp2")
    dtn = p.sbuf([32, 256], F32, "dtn")
    dA = p.sbuf([32, 256], F32, "dA")
    cum = p.sbuf([32, 256], F32, "cum")
    ncum = p.sbuf([32, 256], F32, "ncum")
    dte = p.sbuf([32, 256], F32, "dte")
    wgt = p.sbuf([32, 256], F32, "wgt")
    cdec = p.sbuf([32, 1], F32, "cdec")
    zer = p.sbuf([32, 256], F32, "zer")
    p.memset(zer, 0.0)
    dtb = p.sbuf([32, 1], F32, "dtb")
    alg = p.sbuf([32, 1], F32, "alg")
    Acol = p.sbuf([32, 1], F32, "Acol")
    drep = p.sbuf([128, 1], F32, "drep")
    dtflat = p.sbuf([1, 256], F32, "dtflat")
    dtflatb = p.sbuf([1, 256], BF16, "dtflatb")
    cumflat = p.sbuf([1, 256], F32, "cumflat")
    cumhib = p.sbuf([1, 256], BF16, "cumhib")
    cumlo = p.sbuf([1, 256], F32, "cumlo")
    cumlob = p.sbuf([1, 256], BF16, "cumlob")
    ncumT = p.sbuf([128, 2 * 32], F32, "ncumT")
    dtT = p.sbuf([128, 2 * 32], F32, "dtT")
    wT = p.sbuf([128, 2 * 32], F32, "wT")
    cdecb = p.sbuf([128, 32], F32, "cdecb")
    cdrow = p.sbuf([1, 32], F32, "cdrow")
    cdrowb = p.sbuf([1, 32], BF16, "cdrowb")
    cdrlo = p.sbuf([1, 32], F32, "cdrlo")
    cdrlob = p.sbuf([1, 32], BF16, "cdrlob")
    Bd = p.sbuf([64, 256], BF16, "Bd")
    Ce = p.sbuf([64, 256], BF16, "Ce")
    xctm = p.sbuf([128, 2 * 128], BF16, "xctm")
    btmw = p.sbuf([128, 2 * 64], BF16, "btmw")
    dec = p.sbuf([128, 256], F32, "dec")
    attb = p.sbuf([128, 256], BF16, "attb")
    hst = [p.sbuf([128, 64], F32, f"hst{i}") for i in range(2)]
    hsb = p.sbuf([128, 64], BF16, "hsb")
    hTb = p.sbuf([64, 128], BF16, "hTb")
    ysb = p.sbuf([128, 256], F32, "ysb")
    zsil = p.sbuf([128, 256], F32, "zsil")
    gsb = p.sbuf([128, 256], BF16, "gsb")

    CH = 256
    for bh in range(4):
        p.dma(xin, xh_h[bh])
        p.dma(zfm, z_h[bh])
        p.dma(cwx, cwx_h[bh])
        p.dma(cbx[:, 0:1], cbx_h[bh])
        p.dma(dtb, dtb_h[bh])
        p.dma(alg, alg_h[bh])
        p.dma(drep, drep_h[bh])
        p.ts(acc, xin, cwx[:, 3:4], None, op0=ALU.mult)
        for j in (2, 1, 0):
            sh = 3 - j
            p.stt(acc[:, sh:L], xin[:, 0:L - sh], cwx[:, j:j + 1],
                  acc[:, sh:L], op0=ALU.mult, op1=ALU.add)
        p.act(xfm, acc, AF.Silu, bias=cbx[:, 0:1])
        p.act(Acol, alg, AF.Exp)
        p.ts(Acol, Acol, -1.0, None, op0=ALU.mult)
        p.dma(dts, dtr_h[bh].rearrange("(c t) -> c t", t=256))
        p.act(sp1, dts, AF.Abs, bias=dtb[:, 0:1])
        p.act(sp1, sp1, AF.Exp, scale=-1.0)
        p.act(sp1, sp1, AF.Ln, bias=p.constf(1.0))
        p.act(sp2, dts, AF.Relu, bias=dtb[:, 0:1])
        p.tt(dtn, sp1, sp2, ALU.add)
        p.ts(dA, dtn, Acol[:, 0:1], None, op0=ALU.mult)
        p.scan(cum, zer, dA, 0.0, op0=ALU.add, op1=ALU.add)
        p.ts(ncum, cum, -1.0, None, op0=ALU.mult)
        p.act(dte, ncum, AF.Exp, bias=cum[:, 255:256])
        p.tt(wgt, dte, dtn, ALU.mult)
        p.act(cdec, cum[:, 255:256], AF.Exp)
        for half in range(2):
            sl = slice(half * 128, (half + 1) * 128)
            pb = p.bankf(0)[:, 0:32]
            p.tp(pb, ncum[:, sl], idf)
            p.cp(ncumT[:, half * 32:(half + 1) * 32], pb, eng="vector")
            p.tp(pb, dtn[:, sl], idf)
            p.cp(dtT[:, half * 32:(half + 1) * 32], pb, eng="vector")
            p.tp(pb, wgt[:, sl], idf)
            p.cp(wT[:, half * 32:(half + 1) * 32], pb, eng="vector")
        pbc = p.bankf(0)[0:1, 0:32]
        p.tp(pbc, cdec, idf)
        p.cp(cdrow, pbc, eng="vector")
        p.cp(cdrowb, cdrow)
        p.tt(cdrlo, cdrow, cdrowb, ALU.subtract)
        p.cp(cdrlob, cdrlo)
        pcb = p.bankf(0)[:, 32:64]
        p.mm(pcb, ones1, cdrowb, True, False)
        p.mm(pcb, ones1, cdrlob, False, True)
        p.cp(cdecb, pcb, eng="vector")
        p.memset(hst[0], 0.0)

        for c in range(32):
            cols = slice(c * CH, (c + 1) * CH)
            p.dma(dtflat, dtn[c:c + 1, :], eng="gpsimd")
            p.cp(dtflatb, dtflat)
            p.dma(cumflat, cum[c:c + 1, :], eng="gpsimd")
            p.cp(cumhib, cumflat)
            p.tt(cumlo, cumflat, cumhib, ALU.subtract)
            p.cp(cumlob, cumlo)
            pb1 = p.bankf(1)[0:64, 0:256]
            p.mm(pb1, ones64, dtflatb, True, True)
            p.tt(Bd, bcf[0:64, cols], pb1, ALU.mult)
            pb2 = p.bankf(1)[0:64, 256:512]
            p.mm(pb2, ones64, cumhib, True, False)
            p.mm(pb2, ones64, cumlob, False, True)
            p.act(Ce, pb2, AF.Exp)
            p.tt(Ce, Ce, cfm[:, cols], ALU.mult)
            for half in range(2):
                sl = slice(c * CH + half * 128, c * CH + (half + 1) * 128)
                pb = p.bankb(0)[:, 0:128]
                p.tp(pb, xfm[:, sl], idsb)
                p.cp(xctm[:, half * 128:(half + 1) * 128], pb, eng="vector")
                pb3 = p.bankb(1)[:, 0:64]
                p.tp(pb3, bcf[0:64, sl], idsb)
                p.cp(btmw[:, half * 64:(half + 1) * 64], pb3, eng="vector")
                p.ts(btmw[:, half * 64:(half + 1) * 64],
                     btmw[:, half * 64:(half + 1) * 64],
                     wT[:, half * 32 + c:half * 32 + c + 1], None,
                     op0=ALU.mult)
            py = p.bankf(3)[:, 0:256]
            pst = p.bankf(4)[:, 0:64]
            for jt in range(2):
                jl = slice(jt * 128, (jt + 1) * 128)
                psc2 = p.bankf(5)[:, 0:256]
                p.mm(psc2, Bd[:, jl], cfm[:, cols], True, True)
                pcum = p.bankf(2)[:, 0:256]
                p.mm(pcum, ones1, cumhib, True, False)
                p.mm(pcum, ones1, cumlob, False, True)
                p.tt(pcum, pcum, mk[:, jt, :], ALU.add)
                p.act(dec, pcum, AF.Exp,
                      bias=ncumT[:, jt * 32 + c:jt * 32 + c + 1])
                p.tt(dec, dec, psc2, ALU.mult)
                p.ts(attb, dec, dtT[:, jt * 32 + c:jt * 32 + c + 1], None,
                     op0=ALU.mult)
                p.mm(py, xctm[:, jl], attb, jt == 0, False)
                p.mm(pst, xctm[:, jl], btmw[:, jt * 64:(jt + 1) * 64],
                     jt == 0, jt == 1)
            h0 = hst[c % 2]
            h1 = hst[(c + 1) % 2]
            p.cp(hsb, h0)
            pbh = p.bankb(0)[0:64, 0:128]
            p.tp(pbh, hsb, idsb)
            p.cp(hTb, pbh, eng="vector")
            p.mm(py, hTb, Ce, False, True)
            p.stt(h1, h0, cdecb[:, c:c + 1], pst, op0=ALU.mult, op1=ALU.add)
            p.stt(ysb, xfm[:, cols], drep[:, 0:1], py, op0=ALU.mult,
                  op1=ALU.add)
            p.act(zsil, zfm[:, cols], AF.Silu)
            p.tt(gsb, ysb, zsil, ALU.mult)
            p.dma(o_g[bh][:, cols], gsb)
    return p


def build_l3f3():
    p = Prog("l3f3")
    g_h = p.dram_in("g", (16, 128, NT), BF16)
    rmsw_h = p.dram_in("rmsw", (128, 16))
    opw_h = p.dram_in("opw", (D, DI))
    ident_h = p.dram_in("ident", (128, 128), BF16)
    o_s = p.dram_out("s", (NT, D), BF16)

    idsb = p.sbuf([128, 128], BF16, "idsb")
    p.dma(idsb, ident_h)
    g = p.sbuf([128, 16 * NT], BF16, "g")
    gv = g.rearrange("p (c t) -> p c t", c=16)
    p.dma(gv, g_h.rearrange("c p t -> p c t"))
    rmsw = p.sbuf([128, 16], F32, "rmsw")
    p.dma(rmsw, rmsw_h)

    wnT = p.sbuf([128, 16 * D], BF16, "wnT")
    wv = wnT.rearrange("p (k n) -> p k n", k=16)
    wrow = p.sbuf([128, DI], F32, "wrow")
    wsq = p.sbuf([128, DI], F32, "wsq")
    wst = p.sbuf([128, 4], F32, "wst")
    wbf = p.sbuf([128, DI], BF16, "wbf")
    for rc in range(8):
        p.dma(wrow, opw_h[rc * 128:(rc + 1) * 128, :])
        rn, rn2, dv_ = wst[:, 0:1], wst[:, 1:2], wst[:, 2:3]
        for q4 in range(4):
            tgt = rn if q4 == 0 else rn2
            p.act(wsq[:, q4 * 512:(q4 + 1) * 512],
                  wrow[:, q4 * 512:(q4 + 1) * 512], AF.Square,
                  accum_out=tgt)
            if q4 > 0:
                p.tt(rn, rn, rn2, ALU.add)
        p.act(rn, rn, AF.Sqrt)
        p.ts(rn, rn, 1e-6, None, op0=ALU.max)
        p.recip(dv_, rn)
        p.ts(wbf, wrow, dv_, None, op0=ALU.mult)
        for ch in range(16):
            pb = p.bankb(ch % 2)[:, 0:128]
            p.tp(pb, wbf[:, ch * 128:(ch + 1) * 128], idsb)
            p.cp(wv[:, ch, rc * 128:(rc + 1) * 128], pb, eng="vector")

    gsq = p.sbuf([128, BLK], BF16, "gsq")
    onesb = p.sbuf([128, 1], BF16, "onesb")
    p.memset(onesb, 1.0)
    rr = p.sbuf([1, BLK], F32, "rr")
    rrb = p.sbuf([1, BLK], BF16, "rrb")
    ghat = p.sbuf([128, 16 * BLK], BF16, "ghat")
    gh = ghat.rearrange("p (c t) -> p c t", c=16)
    rstdb = p.sbuf([128, BLK], F32, "rstdb")
    sbf = p.sbuf([128, D], BF16, "sbf")
    ones1 = p.sbuf([1, 128], BF16, "ones1")
    p.memset(ones1, 1.0)
    s3 = o_s.rearrange("(t p) d -> t p d", p=128)
    for nb in range(NBLK):
        tok = slice(nb * BLK, (nb + 1) * BLK)
        psn = p.bankf(0)[0:1, :]
        for ch in range(16):
            p.act(gsq, gv[:, ch, tok], AF.Square)
            p.op("tensor", lambda a=onesb, b=gsq, ps=psn, st=(ch == 0),
                 sp=(ch == 15): p.nc.tensor.matmul(ps, a, b, start=st,
                                                   stop=sp),
                 reads=[onesb, gsq], writes=[psn])
        p.act(rr, psn, AF.Sqrt, scale=1.0 / DI, bias=p.constf(1e-5))
        p.recip(rr, rr)
        p.cp(rrb, rr)
        prb = p.bankf(1)
        p.mm(prb, ones1, rrb, True, True)
        p.cp(rstdb, prb, eng="scalar")
        for ch in range(16):
            p.tt(gh[:, ch, :], gv[:, ch, tok], rstdb, ALU.mult)
            p.ts(gh[:, ch, :], gh[:, ch, :], rmsw[:, ch:ch + 1], None,
                 op0=ALU.mult)
        for j in range(4):
            t = nb * 4 + j
            for fo in range(2):
                ps = p.bankf(2 + fo)
                fcols = slice(fo * 512, (fo + 1) * 512)
                p.mm_acc(ps, [(gh[:, ki, j * 128:(j + 1) * 128],
                               wv[:, ki, fcols]) for ki in range(16)])
                p.cp(sbf[:, fcols], ps, eng="vector")
            p.dma(s3[t], sbf)
    return p


def build_l3g1():
    p = Prog("l3g1")
    xnfm_h = p.dram_in("xnfm", (8, 128, NT), BF16)
    wln1_h = p.dram_in("wln1", (128, 8))
    wq_h = p.dram_in("wq", (D, D))
    wkva_h = p.dram_in("wkva", (D, 160))
    wkvb_h = p.dram_in("wkvb", (128, 512))
    cs_h = p.dram_in("cs", (NT, 32))
    ident_h = p.dram_in("ident", (128, 128), BF16)
    o_q = p.dram_out("q", (8, 128, NT), BF16)
    o_kv = p.dram_out("kv", (4, 128, NT), BF16)
    o_kr = p.dram_out("kr", (32, NT), BF16)

    idsb = p.sbuf([128, 128], BF16, "idsb")
    p.dma(idsb, ident_h)
    xnfm = p.sbuf([128, 8 * NT], BF16, "xnfm")
    p.dma(xnfm.rearrange("p (c t) -> p c t", c=8),
          xnfm_h.rearrange("c p t -> p c t"))
    xnv = xnfm.rearrange("p (c t) -> p c t", c=8)
    wln1 = p.sbuf([128, 8], F32, "wln1")
    p.dma(wln1, wln1_h)
    wq = p.wload(wq_h, D, D, "wq")
    wkva = p.wload(wkva_h, D, 160, "wkva")
    wkvb = p.wload(wkvb_h, 128, 512, "wkvb")
    cs = p.sbuf([128, TT * 32], F32, "cs")
    p.dma(cs.rearrange("p (t k) -> p t k", t=TT),
          cs_h.rearrange("(t p) k -> p t k", p=128))
    csv = cs.rearrange("p (t k) -> p t k", t=TT)

    xn1 = p.sbuf([128, 8 * NT], BF16, "xn1")
    x1v = xn1.rearrange("p (c t) -> p c t", c=8)
    xsq = p.sbuf([128, BLK], BF16, "xsq")
    onesb = p.sbuf([128, 1], BF16, "onesb")
    p.memset(onesb, 1.0)
    ones1 = p.sbuf([1, 128], BF16, "ones1")
    p.memset(ones1, 1.0)
    rr = p.sbuf([1, BLK], F32, "rr")
    rrb = p.sbuf([1, BLK], BF16, "rrb")
    rstdb = p.sbuf([128, BLK], F32, "rstdb")
    for nb in range(NBLK):
        tok = slice(nb * BLK, (nb + 1) * BLK)
        psn = p.bankf(0)[0:1, :]
        for ch in range(8):
            p.act(xsq, xnv[:, ch, tok], AF.Square)
            p.op("tensor", lambda a=onesb, b=xsq, ps=psn, st=(ch == 0),
                 sp=(ch == 7): p.nc.tensor.matmul(ps, a, b, start=st, stop=sp),
                 reads=[onesb, xsq], writes=[psn])
        p.act(rr, psn, AF.Sqrt, scale=1.0 / D, bias=p.constf(1e-5))
        p.recip(rr, rr)
        p.cp(rrb, rr)
        prb = p.bankf(1)
        p.mm(prb, ones1, rrb, True, True)
        p.cp(rstdb, prb, eng="scalar")
        for ch in range(8):
            p.tt(x1v[:, ch, tok], xnv[:, ch, tok], rstdb, ALU.mult)
            p.ts(x1v[:, ch, tok], x1v[:, ch, tok], wln1[:, ch:ch + 1], None,
                 op0=ALU.mult)

    qfm = p.sbuf([128, 8 * NT], BF16, "qfm")
    qv = qfm.rearrange("p (c t) -> p c t", c=8)
    rtm = p.sbuf([128, 64], BF16, "rtm")
    rot = p.sbuf([128, 32], BF16, "rot")
    for nb in range(NBLK):
        tok = slice(nb * BLK, (nb + 1) * BLK)
        for po in range(8):
            ps = p.bankf(po % 2)
            p.mm_acc(ps, [(wq[:, ki, po * 128:(po + 1) * 128],
                           xnv2 := x1v[:, ki, tok]) for ki in range(8)])
            p.smul(qv[:, po, tok], ps, 0.125)
    tmpf = p.sbuf([128, 16], F32, "tmpf")
    for h in range(16):
        ch = h // 2
        rw = (h % 2) * 64
        for t in range(TT):
            tok = slice(t * 128, (t + 1) * 128)
            pb = p.bankb(0)[:, 0:64]
            p.tp(pb, qv[rw:rw + 64, ch, tok], idsb)
            p.cp(rtm, pb, eng="vector")
            xe = rtm[:, 32:64:2]
            xo = rtm[:, 33:64:2]
            co = csv[:, t, 0:16]
            si = csv[:, t, 16:32]
            re_ = rot[:, 0:32:2]
            ro_ = rot[:, 1:32:2]
            p.tt(re_, xe, co, ALU.mult)
            p.tt(ro_, xo, co, ALU.mult)
            p.tt(tmpf, xo, si, ALU.mult)
            p.tt(re_, re_, tmpf, ALU.subtract)
            p.tt(tmpf, xe, si, ALU.mult)
            p.tt(ro_, ro_, tmpf, ALU.add)
            p.cp(rtm[:, 32:64], rot)
            pb2 = p.bankb(1)[0:64, 0:128]
            p.tp(pb2, rtm, idsb)
            p.cp(qv[rw:rw + 64, ch, tok], pb2, eng="vector")
    p.dma(o_q.rearrange("c p t -> p c t"), qv)

    ckv = p.sbuf([128, 2 * NT], BF16, "ckv")
    cv = ckv.rearrange("p (c t) -> p c t", c=2)
    for nb in range(NBLK):
        tok = slice(nb * BLK, (nb + 1) * BLK)
        ps = p.bankf(0)
        p.mm_acc(ps, [(wkva[:, ki, 0:128], x1v[:, ki, tok])
                      for ki in range(8)])
        p.cp(cv[:, 0, tok], ps, eng="vector")
        ps2 = p.bankf(1)[0:32, :]
        p.mm_acc(ps2, [(wkva[:, ki, 128:160], x1v[:, ki, tok])
                       for ki in range(8)])
        p.cp(cv[0:32, 1, tok], ps2, eng="vector")
        for po in range(4):
            ps3 = p.bankf(2 + po % 2)
            p.mm(ps3, wkvb[:, 0, po * 128:(po + 1) * 128], cv[:, 0, tok],
                 True, True)
            p.op("vector", lambda o=None, ps_=ps3, dst=o_kv, poo=po, tk=tok:
                 None) if False else None
            ob = p.sbuf([128, BLK], BF16)
            p.cp(ob, ps3, eng="scalar")
            p.dma(o_kv[po][:, tok], ob)
    for t in range(TT):
        tok = slice(t * 128, (t + 1) * 128)
        pb = p.bankb(0)[:, 0:32]
        p.tp(pb, cv[0:32, 1, tok], idsb)
        p.cp(rtm[:, 0:32], pb, eng="vector")
        xe = rtm[:, 0:32:2]
        xo = rtm[:, 1:32:2]
        co = csv[:, t, 0:16]
        si = csv[:, t, 16:32]
        re_ = rot[:, 0:32:2]
        ro_ = rot[:, 1:32:2]
        p.tt(re_, xe, co, ALU.mult)
        p.tt(ro_, xo, co, ALU.mult)
        p.tt(tmpf, xo, si, ALU.mult)
        p.tt(re_, re_, tmpf, ALU.subtract)
        p.tt(tmpf, xe, si, ALU.mult)
        p.tt(ro_, ro_, tmpf, ALU.add)
        pb2 = p.bankb(1)[0:32, 0:128]
        p.tp(pb2, rot, idsb)
        krb = p.sbuf([32, 128], BF16)
        p.cp(krb, pb2, eng="vector")
        p.dma(o_kr[:, tok], krb)
    return p


def build_l3g2():
    p = Prog("l3g2")
    q_h = p.dram_in("q", (4, 64, L), BF16)
    k_h = p.dram_in("k", (4, 64, L), BF16)
    v_h = p.dram_in("v", (4, 32, L), BF16)
    masks_h = p.dram_in("masks", (4, 128, 512))
    ident_h = p.dram_in("ident", (128, 128), BF16)
    o_out = p.dram_out("out", (4, 32, L), BF16)

    idsb = p.sbuf([128, 128], BF16, "idsb")
    p.dma(idsb, ident_h)
    masks = p.sbuf([128, 4 * 512], F32, "masks")
    p.dma(masks.rearrange("p (m t) -> p m t", m=4),
          masks_h.rearrange("m p t -> p m t"))
    mk = masks.rearrange("p (m t) -> p m t", m=4)
    q = p.sbuf([64, L], BF16, "q")
    k = p.sbuf([64, L], BF16, "k")
    vtm = p.sbuf([128, 64 * 32], BF16, "vtm")
    vv = vtm.rearrange("p (c n) -> p c n", c=64)
    vb = p.sbuf([32, L], BF16, "vb")
    NROT = 4
    PT = [p.sbuf([128, 512], BF16, f"PT{i}") for i in range(NROT)]
    Pb = [p.sbuf([128, 512], BF16, f"Pb{i}") for i in range(NROT)]
    mcol = [p.sbuf([128, 16], F32, f"mcol{i}") for i in range(2)]
    m_ = [p.sbuf([128, 1], F32, f"m{i}") for i in range(2)]
    l_ = [p.sbuf([128, 16], F32, f"l{i}") for i in range(2)]
    lsum = [p.sbuf([128, 1], F32, f"ls{i}") for i in range(2)]
    rl = [p.sbuf([128, 1], F32, f"rl{i}") for i in range(2)]
    otb = [p.sbuf([128, 32], BF16, f"otb{i}") for i in range(2)]
    otf = [p.sbuf([32, 128], BF16, f"otf{i}") for i in range(2)]

    for bh in range(4):
        p.dma(q, q_h[bh])
        p.dma(k, k_h[bh])
        p.dma(vb, v_h[bh])
        for kc in range(64):
            pb = p.bankb(kc % 2)[:, 0:32]
            p.tp(pb, vb[:, kc * 128:(kc + 1) * 128], idsb)
            p.cp(vv[:, kc, :], pb, eng="vector")
        for qt in range(64):
            nkb = qt // 4 + 1
            qi = qt % 2
            mc = mcol[qi]
            for kb in range(nkb):
                pS = p.bankf(kb % 4)
                p.mm(pS, q[:, qt * 128:(qt + 1) * 128],
                     k[:, kb * 512:(kb + 1) * 512], True, True)
                if kb == qt // 4:
                    p.tt(pS, pS, mk[:, qt % 4, :], ALU.add)
                p.red(mc[:, kb:kb + 1], pS, ALU.max)
            p.red(m_[qi], mc[:, 0:nkb], ALU.max)
            p.ts(m_[qi], m_[qi], -1.0, None, op0=ALU.mult)
            pacc = p.bankf(4 + qi)[:, 0:32]
            for kb in range(nkb):
                ri = kb % NROT
                pS = p.bankf(kb % 4)
                p.mm(pS, q[:, qt * 128:(qt + 1) * 128],
                     k[:, kb * 512:(kb + 1) * 512], True, True)
                if kb == qt // 4:
                    p.tt(pS, pS, mk[:, qt % 4, :], ALU.add)
                p.act(Pb[ri], pS, AF.Exp, bias=m_[qi],
                      accum_out=l_[qi][:, kb:kb + 1])
                for kc in range(4):
                    pbt = p.bankb((kb * 4 + kc) % 2)[:, 256:384]
                    p.tp(pbt, Pb[ri][:, kc * 128:(kc + 1) * 128], idsb)
                    p.cp(PT[ri][:, kc * 128:(kc + 1) * 128], pbt,
                         eng="scalar" if kc % 2 else "vector")
                for kc in range(4):
                    p.mm(pacc, PT[ri][:, kc * 128:(kc + 1) * 128],
                         vv[:, kb * 4 + kc, :],
                         kb == 0 and kc == 0, kb == nkb - 1 and kc == 3)
            p.red(lsum[qi], l_[qi][:, 0:nkb], ALU.add)
            p.recip(rl[qi], lsum[qi])
            p.ts(otb[qi], pacc, rl[qi], None, op0=ALU.mult)
            pb2 = p.bankb(qi)[0:32, 128:256]
            p.tp(pb2, otb[qi], idsb)
            p.cp(otf[qi], pb2, eng="vector")
            p.dma(o_out[bh][:, qt * 128:(qt + 1) * 128], otf[qi])
    return p


def build_l3g3():
    p = Prog("l3g3")
    y_h = p.dram_in("y", (4, 128, NT), BF16)
    wonz_h = p.dram_in("wonz", (512, D))
    xnfm_h = p.dram_in("xnfm", (8, 128, NT), BF16)
    x_h = p.dram_in("x", (NT, D))
    wln2_h = p.dram_in("wln2", (1, D))
    wgate_h = p.dram_in("wgate", (D, 8))
    ebias_h = p.dram_in("ebias", (1, 8))
    wgm_h = p.dram_in("wgm", (9, D, 512))
    wum_h = p.dram_in("wum", (9, D, 512))
    wdm_h = p.dram_in("wdm", (9, 512, D))
    s_h = p.dram_in("s", (NT, D), BF16)
    ident_h = p.dram_in("ident", (128, 128), BF16)
    o_x = p.dram_out("xout", (NT, D))

    idsb = p.sbuf([128, 128], BF16, "idsb")
    p.dma(idsb, ident_h)
    y = p.sbuf([128, 4 * NT], BF16, "y")
    yv = y.rearrange("p (c t) -> p c t", c=4)
    p.dma(yv, y_h.rearrange("c p t -> p c t"))
    wonz = p.wload(wonz_h, 512, D, "wonz")
    xnfm = p.sbuf([128, 8 * NT], BF16, "xnfm")
    p.dma(xnfm.rearrange("p (c t) -> p c t", c=8),
          xnfm_h.rearrange("c p t -> p c t"))
    xnv = xnfm.rearrange("p (c t) -> p c t", c=8)
    wgate = p.wload(wgate_h, D, 8, "wgate")
    ones1 = p.sbuf([1, 128], BF16, "ones1")
    p.memset(ones1, 1.0)
    wln2 = p.sbuf([1, D], BF16, "wln2")
    p.dma(wln2, wln2_h, eng="gpsimd")
    ebias = p.sbuf([1, 8], F32, "ebias")
    p.dma(ebias, ebias_h)
    ebb = p.sbuf([128, 8], F32, "ebb")
    ebhib = p.sbuf([1, 8], BF16, "ebhib")
    eblo = p.sbuf([1, 8], F32, "eblo")
    eblob = p.sbuf([1, 8], BF16, "eblob")
    p.cp(ebhib, ebias)
    p.tt(eblo, ebias, ebhib, ALU.subtract)
    p.cp(eblob, eblo)
    peb = p.bankf(0)[:, 0:8]
    p.mm(peb, ones1, ebhib, True, False)
    p.mm(peb, ones1, eblob, False, True)
    p.cp(ebb, peb)
    wlnb = p.sbuf([128, D], F32, "wlnb")
    for fo in range(2):
        ps = p.bankf(1)
        p.mm(ps, ones1, wln2[:, fo * 512:(fo + 1) * 512], True, True)
        p.cp(wlnb[:, fo * 512:(fo + 1) * 512], ps, eng="vector")

    x1i = p.sbuf([128, TT * D], BF16, "x1i")
    x1v = x1i.rearrange("p (t d) -> p t d", t=TT)
    xn2 = p.sbuf([128, 8 * NT], BF16, "xn2")
    x2v = xn2.rearrange("p (c t) -> p c t", c=8)
    xt = p.sbuf([128, D], F32, "xt")
    xsq = p.sbuf([128, D], F32, "xsq")
    st8 = p.sbuf([128, 8], F32, "st8")
    xnbf = p.sbuf([128, D], BF16, "xnbf")
    x3 = x_h.rearrange("(t p) d -> t p d", p=128)
    for t in range(TT):
        tok = slice(t * 128, (t + 1) * 128)
        p.dma(xt, x3[t])
        nm, vr, sd, rstd, vr2 = (st8[:, i:i + 1] for i in range(5))
        p.red(nm, xt, ALU.add)
        p.ts(nm, nm, -1.0 / D, None, op0=ALU.mult)
        p.ts(xt, xt, nm, None, op0=ALU.add)
        p.act(xsq[:, 0:512], xt[:, 0:512], AF.Square, accum_out=vr)
        p.act(xsq[:, 512:1024], xt[:, 512:1024], AF.Square, accum_out=vr2)
        p.tt(vr, vr, vr2, ALU.add)
        p.act(sd, vr, AF.Sqrt, bias=p.constf(1e-5), scale=1.0 / D)
        p.recip(rstd, sd)
        p.ts(xt, xt, rstd, None, op0=ALU.mult)
        for fo in range(2):
            ps = p.bankf(2 + fo)
            fcols = slice(fo * 512, (fo + 1) * 512)
            p.mm_acc(ps, [(yv[:, ki, tok], wonz[:, ki, fcols])
                          for ki in range(4)])
            p.tt(x1v[:, t, fcols], ps, xt[:, fcols], ALU.add)
        x1t = x1v[:, t, :]
        p.act(xsq[:, 0:512], x1t[:, 0:512], AF.Square, accum_out=vr)
        p.act(xsq[:, 512:1024], x1t[:, 512:1024], AF.Square, accum_out=vr2)
        p.tt(vr, vr, vr2, ALU.add)
        p.act(sd, vr, AF.Sqrt, bias=p.constf(1e-5), scale=1.0 / D)
        p.recip(rstd, sd)
        p.ts(xsq, x1t, rstd, None, op0=ALU.mult)
        p.tt(xnbf, xsq, wlnb, ALU.mult)
        for ch in range(8):
            pb = p.bankb(ch % 2)[:, 0:128]
            p.tp(pb, xnbf[:, ch * 128:(ch + 1) * 128], idsb)
            p.cp(x2v[:, ch, tok], pb, eng="vector")

    cwall = p.sbuf([128, TT * 8], F32, "cwall")
    iot8m = p.sbuf([128, 8], F32, "iot8m")
    p.iota(iot8m, [[1, 8]], base=-8, channel_multiplier=0)
    gp = p.sbuf([128, 8], F32, "gp")
    gp2 = p.sbuf([128, 8], F32, "gp2")
    cand = p.sbuf([128, 8], F32, "cand")
    mio = p.sbuf([128, 8], F32, "mio")
    oh1 = p.sbuf([128, 8], F32, "oh1")
    oh2 = p.sbuf([128, 8], F32, "oh2")
    v1 = p.sbuf([128, 1], F32, "v1")
    v2 = p.sbuf([128, 1], F32, "v2")
    i1 = p.sbuf([128, 1], F32, "i1")
    i2 = p.sbuf([128, 1], F32, "i2")
    r1 = p.sbuf([128, 1], F32, "r1")
    r2 = p.sbuf([128, 1], F32, "r2")
    vs = p.sbuf([128, 1], F32, "vs")
    BIG = 1.0e9
    for t in range(TT):
        tok = slice(t * 128, (t + 1) * 128)
        ps2 = p.bankf(0)[:, 0:8]
        p.mm_acc(ps2, [(x2v[:, ki, tok], wgate[:, ki, :]) for ki in range(8)])
        p.tt(gp, ps2, ebb, ALU.add)
        p.red(r1, gp, ALU.max)
        p.ts(r1, r1, -1.0, None, op0=ALU.mult)
        p.act(gp, gp, AF.Exp, bias=r1, accum_out=r2)
        p.recip(r2, r2)
        p.ts(gp, gp, r2, None, op0=ALU.mult)
        p.red(v1, gp, ALU.max)
        p.ts(cand, gp, v1, None, op0=ALU.is_ge)
        p.tt(mio, cand, iot8m, ALU.mult)
        p.ts(mio, mio, 8.0, None, op0=ALU.add)
        p.red(i1, mio, ALU.min)
        p.ts(i1, i1, -8.0, None, op0=ALU.add)
        p.ts(oh1, iot8m, i1, None, op0=ALU.is_equal)
        p.stt(gp2, oh1, -BIG, gp, op0=ALU.mult, op1=ALU.add)
        p.red(v2, gp2, ALU.max)
        p.ts(cand, gp2, v2, None, op0=ALU.is_ge)
        p.tt(mio, cand, iot8m, ALU.mult)
        p.ts(mio, mio, 8.0, None, op0=ALU.add)
        p.red(i2, mio, ALU.min)
        p.ts(i2, i2, -8.0, None, op0=ALU.add)
        p.ts(oh2, iot8m, i2, None, op0=ALU.is_equal)
        p.tt(vs, v1, v2, ALU.add)
        p.recip(vs, vs)
        cw = cwall[:, t * 8:(t + 1) * 8]
        p.ts(cw, oh1, v1, None, op0=ALU.mult)
        p.stt(cw, oh2, v2, cw, op0=ALU.mult, op1=ALU.add)
        p.ts(cw, cw, vs, None, op0=ALU.mult)

    acc = p.sbuf([128, TT * D], BF16, "acc")
    accv = acc.rearrange("p (t d) -> p t d", t=TT)
    wgm = p.sbuf([128, 8 * 512], BF16, "wgm")
    wgv = wgm.rearrange("p (k n) -> p k n", k=8)
    wum = p.sbuf([128, 8 * 512], BF16, "wum")
    wuv = wum.rearrange("p (k n) -> p k n", k=8)
    wdm = p.sbuf([128, 4 * D], BF16, "wdm")
    wdv = wdm.rearrange("p (k n) -> p k n", k=4)
    geb = p.sbuf([128, 4 * BLK], BF16, "geb")
    gev = geb.rearrange("p (k n) -> p k n", k=4)
    silb = p.sbuf([128, BLK], F32, "silb")
    for e in range(9):
        p.dma(wgv, wgm_h[e].rearrange("(k p) n -> p k n", p=128), eng="gpsimd")
        p.dma(wuv, wum_h[e].rearrange("(k p) n -> p k n", p=128), eng="gpsimd")
        p.dma(wdv, wdm_h[e].rearrange("(k p) n -> p k n", p=128), eng="gpsimd")
        for nb in range(NBLK):
            tok = slice(nb * BLK, (nb + 1) * BLK)
            for po in range(4):
                psg = p.bankf(po % 2)
                p.mm_acc(psg, [(wgv[:, ki, po * 128:(po + 1) * 128],
                                x2v[:, ki, tok]) for ki in range(8)])
                p.act(silb, psg, AF.Silu)
                psu = p.bankf(2 + po % 2)
                p.mm_acc(psu, [(wuv[:, ki, po * 128:(po + 1) * 128],
                                x2v[:, ki, tok]) for ki in range(8)])
                p.tt(gev[:, po, :], silb, psu, ALU.mult)
            for j in range(4):
                tg = nb * 4 + j
                for fo in range(2):
                    ps = p.bankf(4 + fo)
                    fcols = slice(fo * BLK, (fo + 1) * BLK)
                    p.mm_acc(ps, [(gev[:, ki, j * 128:(j + 1) * 128],
                                   wdv[:, ki, fcols]) for ki in range(4)])
                    dst = accv[:, tg, fcols]
                    if e == 0:
                        p.ts(dst, ps, cwall[:, tg * 8:tg * 8 + 1], None,
                             op0=ALU.mult)
                    elif e < 8:
                        p.stt(dst, ps, cwall[:, tg * 8 + e:tg * 8 + e + 1],
                              dst, op0=ALU.mult, op1=ALU.add)
                    else:
                        p.stt(dst, ps, 1.0, dst, op0=ALU.mult, op1=ALU.add)

    stl = p.sbuf([128, D], BF16, "stl")
    gtl = xsq
    sS = s_h.rearrange("(t p) d -> t p d", p=128)
    o3 = o_x.rearrange("(t p) d -> t p d", p=128)
    for t in range(TT):
        p.dma(xt, x3[t])
        p.dma(stl, sS[t], eng="gpsimd")
        p.tt(gtl, accv[:, t, :], x1v[:, t, :], ALU.add)
        p.act(gtl, gtl, AF.Silu)
        p.tt(gtl, gtl, stl, ALU.mult)
        p.tt(gtl, gtl, xt, ALU.add)
        p.dma(o3[t], gtl)
    return p


def run_l3(inp, x_full):
    ident, slt = _consts()
    identf = np.eye(128, dtype=np.float32)
    xs = [np.ascontiguousarray(x_full[c * NT:(c + 1) * NT]) for c in range(NCORES)]
    im = [dict(x=xs[c], ipw=inp["L3_attn_in_proj_w"],
               xnw=np.ascontiguousarray(inp["L3_attn_x_norm_w"][None, :]),
               xnb=np.ascontiguousarray(inp["L3_attn_x_norm_b"][None, :]),
               ident=ident) for c in range(NCORES)]
    rf1 = _run("l3f1", build_l3f1, im)

    xbc_full = np.concatenate([r["xbc"] for r in rf1], axis=2).reshape(17 * 128, NTOK)
    z_full = np.concatenate([r["z"] for r in rf1], axis=2).reshape(DI, NTOK)
    dtr_full = np.concatenate([r["dtr"] for r in rf1], axis=1)
    cw_all = inp["L3_attn_conv_w"]
    cb_all = inp["L3_attn_conv_b"]
    maskt = np.full((2, 128, 256), -1.0e30, dtype=np.float32)
    for pp in range(128):
        maskt[0, pp, pp:] = 0.0
        maskt[1, pp, 128 + pp:] = 0.0
    im = []
    for c in range(NCORES):
        b = c // 4
        hs = [4 * (c % 4) + kk for kk in range(4)]
        colb = slice(b * L, (b + 1) * L)
        xh = np.stack([xbc_full[128 * h:128 * (h + 1), colb] for h in hs])
        bc = xbc_full[2048:2176, colb]
        cwx = np.stack([cw_all[128 * h:128 * (h + 1)] for h in hs])
        cbx = np.stack([cb_all[128 * h:128 * (h + 1), None] for h in hs])
        zz = np.stack([z_full[128 * h:128 * (h + 1), colb] for h in hs])
        dtr = np.stack([dtr_full[h, colb] for h in hs])
        dtb = np.stack([np.full((32, 1), inp["L3_attn_dt_bias"][h],
                                np.float32) for h in hs])
        alg = np.stack([np.full((32, 1), inp["L3_attn_A_log"][h], np.float32)
                        for h in hs])
        drep = np.stack([np.full((128, 1), inp["L3_attn_D"][h], np.float32)
                         for h in hs])
        im.append(dict(xh=np.ascontiguousarray(xh),
                       bc=np.ascontiguousarray(bc), cwx=cwx, cbx=cbx,
                       cwbc=np.ascontiguousarray(cw_all[2048:2176]),
                       cbbc=np.ascontiguousarray(cb_all[2048:2176, None]),
                       dtr=np.ascontiguousarray(dtr), dtb=dtb, alg=alg,
                       drep=drep, z=np.ascontiguousarray(zz), maskt=maskt,
                       ident=ident, identf=identf))
    rf2 = _run("l3f2", build_l3f2, im)

    g_full = np.zeros((DI, NTOK), dtype=BF)
    for c in range(NCORES):
        b = c // 4
        for kk in range(4):
            h = 4 * (c % 4) + kk
            g_full[128 * h:128 * (h + 1), b * L:(b + 1) * L] = rf2[c]["g"][kk]
    rmsw = np.ascontiguousarray(inp["L3_attn_rms_w"].reshape(16, 128).T)
    im = []
    for c in range(NCORES):
        cols = slice(c * NT, (c + 1) * NT)
        im.append(dict(g=np.ascontiguousarray(
                           g_full[:, cols].reshape(16, 128, NT)),
                       rmsw=rmsw, opw=inp["L3_attn_out_proj_w"], ident=ident))
    rf3 = _run("l3f3", build_l3f3, im)

    THETA = 1.0e5
    freqs = (1.0 / THETA ** (np.arange(0, 32, 2, dtype=np.float32) / 32.0))
    im = []
    for c in range(NCORES):
        pos = (np.arange(NT, dtype=np.float32) + (c % 4) * NT)
        ang = np.float32(pos[:, None]) * np.float32(freqs[None, :])
        cs = np.concatenate([np.cos(ang), np.sin(ang)], axis=1).astype(np.float32)
        im.append(dict(xnfm=rf1[c]["xnfm"],
                       wln1=np.ascontiguousarray(
                           inp["L3_ffn_w_ln1"].reshape(8, 128).T),
                       wq=inp["L3_ffn_Wq"], wkva=inp["L3_ffn_Wkva"],
                       wkvb=inp["L3_ffn_Wkvb"], cs=cs, ident=ident))
    rg1 = _run("l3g1", build_l3g1, im)

    q_full = np.concatenate([r["q"] for r in rg1], axis=2).reshape(D, NTOK)
    kv_full = np.concatenate([r["kv"] for r in rg1], axis=2).reshape(512, NTOK)
    kr_full = np.concatenate([r["kr"] for r in rg1], axis=1)
    masks4 = np.zeros((4, 128, 512), dtype=np.float32)
    for m4 in range(4):
        for pp in range(128):
            masks4[m4, pp, m4 * 128 + pp + 1:] = -1.0e30
    im = []
    for c in range(NCORES):
        b = c // 4
        colb = slice(b * L, (b + 1) * L)
        qs, ks, vsl = [], [], []
        for kk in range(4):
            h = 4 * (c % 4) + kk
            qs.append(q_full[64 * h:64 * (h + 1), colb])
            ks.append(np.concatenate([kv_full[32 * h:32 * (h + 1), colb],
                                      kr_full[:, colb]], axis=0))
            vsl.append(kv_full[32 * h:32 * (h + 1), colb])
        im.append(dict(q=np.ascontiguousarray(np.stack(qs)),
                       k=np.ascontiguousarray(np.stack(ks)),
                       v=np.ascontiguousarray(np.stack(vsl)),
                       masks=masks4, ident=ident))
    rg2 = _run("l3g2", build_l3g2, im)

    yout_full = np.zeros((512, NTOK), dtype=BF)
    for c in range(NCORES):
        b = c // 4
        for kk in range(4):
            h = 4 * (c % 4) + kk
            yout_full[32 * h:32 * (h + 1), b * L:(b + 1) * L] = rg2[c]["out"][kk]
    wonz = np.ascontiguousarray(
        inp["L3_ffn_Wo"].reshape(16, 64, D)[:, :32].reshape(512, D))
    wgm = np.concatenate([inp["L3_ffn_Wg"], inp["L3_ffn_sWg"]], axis=0)
    wum = np.concatenate([inp["L3_ffn_Wu"], inp["L3_ffn_sWu"]], axis=0)
    wdm = np.concatenate([inp["L3_ffn_Wd"], inp["L3_ffn_sWd"]], axis=0)
    im = []
    for c in range(NCORES):
        cols = slice(c * NT, (c + 1) * NT)
        im.append(dict(
            y=np.ascontiguousarray(yout_full[:, cols].reshape(4, 128, NT)),
            wonz=wonz, xnfm=rf1[c]["xnfm"], x=xs[c],
            wln2=np.ascontiguousarray(inp["L3_ffn_w_ln2"][None, :]),
            wgate=inp["L3_ffn_Wgate"],
            ebias=np.ascontiguousarray(inp["L3_ffn_expert_bias"][None, :]),
            wgm=wgm, wum=wum, wdm=wdm, s=rf3[c]["s"], ident=ident))
    rg3 = _run("l3g3", build_l3g3, im)
    x4 = np.concatenate([r["xout"] for r in rg3], axis=0)
    return x4, rf3




def _consts():
    ident = np.eye(128, dtype=BF)
    slt = (np.arange(128)[:, None] < np.arange(128)[None, :]).astype(BF)
    return ident, slt


def run_l0(inp, x_full):
    ident, slt = _consts()
    med = np.ascontiguousarray(inp["L0_attn_mediator"][0])
    xs = [np.ascontiguousarray(x_full[c * NT:(c + 1) * NT]) for c in range(NCORES)]

    im_a = [dict(x=xs[c], wq=inp["L0_attn_Wq"], wkv=inp["L0_attn_Wkv"],
                 med=med, wg=inp["L0_ffn_Wg"], ident=ident)
            for c in range(NCORES)]
    ra = _run("l0a", build_l0a, im_a)

    mx_all = np.stack([r["mx"] for r in ra])
    se_all = np.stack([r["se"] for r in ra])
    accv_all = np.stack([r["accv"] for r in ra]).reshape(8, 64, 16, 64)
    S_full = np.concatenate([r["S"] for r in ra], axis=1)

    im_b = []
    for c in range(NCORES):
        b = c // 4
        cs = slice(4 * b, 4 * b + 4)
        im_b.append(dict(
            qfm=ra[c]["qfm"], med=med, wo=inp["L0_attn_Wo"],
            bo=np.ascontiguousarray(inp["L0_attn_bo"][None, :]),
            mx=np.ascontiguousarray(mx_all[cs].transpose(2, 1, 0)),
            se=np.ascontiguousarray(se_all[cs].transpose(2, 1, 0)),
            accv=np.ascontiguousarray(accv_all[cs].transpose(2, 0, 1, 3)),
            ident=ident))
    rb = _run("l0b", build_l0b, im_b)

    S_pm = np.ascontiguousarray(S_full.reshape(8, 128, 128))
    im_c = []
    for c in range(NCORES):
        colsel = np.zeros((128, 16), dtype=np.float32)
        for t in range(16):
            colsel[c * 16 + t, t] = 1.0
        b1t = np.ascontiguousarray(
            inp["L0_ffn_b1"].reshape(8, 32, 128).transpose(0, 2, 1))
        im_c.append(dict(
            xnfm=ra[c]["xnfm"], S=S_pm, colsel=colsel, s=rb[c]["s"], x=xs[c],
            w1=inp["L0_ffn_W1"], b1=b1t, w2=inp["L0_ffn_W2"],
            b2=inp["L0_ffn_b2"], slt=slt))
    rc = _run("l0c", build_l0c, im_c)
    return np.concatenate([r["xout"] for r in rc], axis=0), ra, rb


def kernel(**inputs):
    inp = {k: np.ascontiguousarray(np.asarray(v, dtype=np.float32))
           for k, v in inputs.items()}
    x = np.ascontiguousarray(inp["L0_attn_x"].reshape(NTOK, D))
    x1, _, _ = run_l0(inp, x)
    x2, _ = run_l1(inp, x1)
    x3, _ = run_l2(inp, x2)
    x4, _ = run_l3(inp, x3)
    return x4.reshape(B, L, D)


# version 04 (final)
# speedup vs baseline: 1.2117x; 1.0994x over previous
import numpy as np
import ml_dtypes
import concourse.bass as bass
import concourse.mybir as mybir
from concourse.bass_utils import run_bass_kernel_spmd

F32 = mybir.dt.float32
BF16 = mybir.dt.bfloat16
AF = mybir.ActivationFunctionType
ALU = mybir.AluOpType
AX = mybir.AxisListType.X
BF = ml_dtypes.bfloat16

NCORES = 8
B = 2
L = 8192
D = 1024
NTOK = B * L
NT = NTOK // NCORES
TT = NT // 128
BLK = 512
NBLK = NT // BLK


def _span(ap):
    try:
        lo = hi = int(ap.offset)
    except Exception:
        return None
    for step, count in ap.ap:
        try:
            step = int(step)
            count = int(count)
        except Exception:
            return None
        d = step * (count - 1)
        if d >= 0:
            hi += d
        else:
            lo += d
    return lo, hi + 1


class _Op:
    __slots__ = ("eng", "stream", "fn", "idx", "marked", "inc_no", "waits")

    def __init__(self, eng, stream, fn, idx):
        self.eng = eng
        self.stream = stream
        self.fn = fn
        self.idx = idx
        self.marked = False
        self.inc_no = 0
        self.waits = {}


class Prog:

    def __init__(self, name):
        self.nc = bass.Bass("TRN2", target_bir_lowering=False)
        self.name = name
        self._i = 0
        self.ops = []
        self._writes = {}
        self._reads = {}
        self._psum_names = set()
        self._flushed = False
        self._bankf = []
        self._bankb = []
        self._consts = {}

    def dram_in(self, name, shape, dtype=F32):
        return self.nc.dram_tensor(name, [int(s) for s in shape], dtype,
                                   kind="ExternalInput").ap()

    def dram_out(self, name, shape, dtype=F32):
        return self.nc.dram_tensor(name, [int(s) for s in shape], dtype,
                                   kind="ExternalOutput").ap()

    def sbuf(self, shape, dtype=F32, name=None):
        self._i += 1
        t = self.nc.alloc_sbuf_tensor(("s_" + name) if name else f"sb{self._i}",
                                      [int(s) for s in shape], dtype)
        return t.ap()

    def psum(self, shape, dtype=F32, name=None):
        self._i += 1
        nm = name or f"ps{self._i}"
        cm = self.nc.psum_tensor(nm, [int(s) for s in shape], dtype)
        t = cm.__enter__()
        ap = t.ap()
        self._psum_names.add(ap.tensor.name)
        return ap

    def bankf(self, i):
        while len(self._bankf) <= i:
            self._bankf.append(self.psum([128, 512], F32,
                                         f"bankf{len(self._bankf)}"))
        return self._bankf[i]

    def constf(self, val):
        if val not in self._consts:
            sb = self.sbuf([128, 1], F32, f"const{len(self._consts)}")
            self.memset(sb, float(val))
            self._consts[val] = sb
        return self._consts[val]

    def bankb(self, i):
        while len(self._bankb) <= i:
            self._bankb.append(self.psum([128, 1024], BF16,
                                         f"bankb{len(self._bankb)}"))
        return self._bankb[i]

    def _key_ranges(self, ap):
        name = ap.tensor.name
        if name in self._psum_names:
            return name, (0, 1 << 60)
        s = _span(ap)
        if s is None:
            s = (0, 1 << 59)
        es = mybir.dt.size(ap.dtype)
        return name, (s[0] * es, s[1] * es)

    def _dep_edges(self, op, aps, book):
        for ap in aps:
            name, (lo, hi) = self._key_ranges(ap)
            d = book.get(name)
            if not d:
                continue
            for (l2, h2), streams in d.items():
                if l2 < hi and lo < h2:
                    for stream, oidx in streams.items():
                        if stream == op.stream == "tensor":
                            continue
                        if oidx > op.waits.get(stream, -1):
                            op.waits[stream] = oidx

    def _record(self, book, aps, op):
        for ap in aps:
            name, rng = self._key_ranges(ap)
            book.setdefault(name, {}).setdefault(rng, {})[op.stream] = op.idx

    def op(self, eng, fn, reads=(), writes=(), dma=False):
        stream = eng + ("_dma" if dma else "")
        reads = [r for r in reads if r is not None]
        writes = [w for w in writes if w is not None]
        o = _Op(eng, stream, fn, len(self.ops))
        self._dep_edges(o, reads, self._writes)
        self._dep_edges(o, writes, self._writes)
        self._dep_edges(o, writes, self._reads)
        self._record(self._reads, reads, o)
        self._record(self._writes, writes, o)
        self.ops.append(o)
        return o

    def dma(self, out, in_, eng="sync"):
        h = getattr(self.nc, eng)
        return self.op(eng, lambda: h.dma_start(out=out, in_=in_),
                       reads=[in_], writes=[out], dma=True)

    def cp(self, out, in_, eng="vector"):
        h = getattr(self.nc, eng)
        if eng == "scalar":
            return self.op(eng, lambda: h.copy(out, in_),
                           reads=[in_], writes=[out])
        return self.op(eng, lambda: h.tensor_copy(out, in_),
                       reads=[in_], writes=[out])

    def memset(self, ap, val, eng="vector"):
        h = getattr(self.nc, eng)
        return self.op(eng, lambda: h.memset(ap, val), reads=[], writes=[ap])

    def act(self, out, in_, func, bias=0.0, scale=1.0, accum_out=None):
        nc = self.nc
        np_ = in_.shape[0]
        if isinstance(bias, bass.AP) and bias.shape[0] == 128 and np_ < 128:
            bias = bias[0:np_]
        if isinstance(scale, bass.AP) and scale.shape[0] == 128 and np_ < 128:
            scale = scale[0:np_]
        reads = [in_]
        if isinstance(bias, bass.AP):
            reads.append(bias)
        if isinstance(scale, bass.AP):
            reads.append(scale)
        writes = [out] + ([accum_out] if accum_out is not None else [])
        if accum_out is not None:
            assert in_.free_size() <= 512, "act accum_out broken for free>512"
        return self.op("scalar",
                       lambda: nc.scalar.activation(out, in_, func, bias=bias,
                                                    scale=scale,
                                                    accum_out=accum_out),
                       reads=reads, writes=writes)

    def smul(self, out, in_, mul):
        nc = self.nc
        reads = [in_] + ([mul] if isinstance(mul, bass.AP) else [])
        return self.op("scalar", lambda: nc.scalar.mul(out, in_, mul),
                       reads=reads, writes=[out])

    def ts(self, out, in0, s1, s2=None, op0=ALU.mult, op1=ALU.bypass,
           eng="vector", accum_out=None):
        h = getattr(self.nc, eng)
        reads = [in0] + [s for s in (s1, s2) if isinstance(s, bass.AP)]
        writes = [out] + ([accum_out] if accum_out is not None else [])
        return self.op(eng,
                       lambda: h.tensor_scalar(out, in0, s1, s2, op0=op0,
                                               op1=op1, accum_out=accum_out),
                       reads=reads, writes=writes)

    def tt(self, out, in0, in1, alu, eng="vector"):
        h = getattr(self.nc, eng)
        return self.op(eng, lambda: h.tensor_tensor(out, in0, in1, op=alu),
                       reads=[in0, in1], writes=[out])

    def stt(self, out, in0, scalar, in1, op0, op1, eng="vector"):
        h = getattr(self.nc, eng)
        reads = [in0, in1] + ([scalar] if isinstance(scalar, bass.AP) else [])
        return self.op(eng,
                       lambda: h.scalar_tensor_tensor(out, in0, scalar, in1,
                                                      op0=op0, op1=op1),
                       reads=reads, writes=[out])

    def red(self, out, in_, alu=ALU.add, eng="vector"):
        h = getattr(self.nc, eng)
        return self.op(eng,
                       lambda: h.tensor_reduce(out, in_, axis=AX, op=alu),
                       reads=[in_], writes=[out])

    def recip(self, out, in_):
        nc = self.nc
        return self.op("vector", lambda: nc.vector.reciprocal(out, in_),
                       reads=[in_], writes=[out])

    def sel(self, out, mask, on_true, on_false):
        nc = self.nc
        reads = [mask] + [x for x in (on_true, on_false)
                          if isinstance(x, bass.AP)]
        return self.op("vector",
                       lambda: nc.vector.select(out, mask, on_true, on_false),
                       reads=reads, writes=[out])

    def mm(self, ps, lhsT, rhs, start, stop):
        nc = self.nc
        return self.op("tensor",
                       lambda: nc.tensor.matmul(ps, lhsT, rhs, start=start,
                                                stop=stop),
                       reads=[lhsT, rhs], writes=[ps])

    def mm_acc(self, ps, pairs):
        n = len(pairs)
        for i, (lt, rh) in enumerate(pairs):
            self.mm(ps, lt, rh, i == 0, i == n - 1)

    def tp(self, ps_out, in_, ident):
        nc = self.nc
        k = in_.shape[0]
        b = int(in_.base_partition())
        idk = ident[b:b + k, b:b + k]
        return self.op("tensor",
                       lambda: nc.tensor.transpose(ps_out, in_, idk),
                       reads=[in_, idk], writes=[ps_out])

    def scan(self, out, data0, data1, init, op0=ALU.mult, op1=ALU.add):
        nc = self.nc
        reads = [data0, data1] + ([init] if isinstance(init, bass.AP) else [])
        return self.op("vector",
                       lambda: nc.vector.tensor_tensor_scan(out, data0, data1,
                                                            init, op0=op0,
                                                            op1=op1),
                       reads=reads, writes=[out])

    def iota(self, out, pattern, base=None, channel_multiplier=None):
        nc = self.nc
        return self.op("gpsimd",
                       lambda: nc.gpsimd.iota(
                           out, pattern, base=base,
                           channel_multiplier=channel_multiplier,
                           allow_small_or_imprecise_dtypes=True),
                       reads=[], writes=[out])

    def wload(self, hbm, K, N, name=None, dtype=BF16, eng="gpsimd"):
        if K <= 128:
            sb = self.sbuf([K, 1, N], dtype, name)
            self.dma(sb[:, 0, :], hbm, eng=eng)
            return sb
        nk = K // 128
        assert K % 128 == 0
        sb = self.sbuf([128, nk * N], dtype, name)
        v = sb.rearrange("p (k n) -> p k n", k=nk)
        self.dma(v, hbm.rearrange("(k p) n -> p k n", p=128), eng=eng)
        return v

    R_DMA = 8

    def flush(self):
        assert not self._flushed
        self._flushed = True
        nc = self.nc
        for o in self.ops:
            if o.stream.endswith("_dma"):
                o.marked = True
        for o in self.ops:
            for stream, idx in o.waits.items():
                self.ops[idx].marked = True
        totals = {}
        dma_k = {}
        dma_slot = {}
        for o in self.ops:
            if o.stream.endswith("_dma"):
                k = dma_k.get(o.stream, 0)
                dma_k[o.stream] = k + 1
                dma_slot[o.idx] = (k % self.R_DMA, k // self.R_DMA)
            elif o.marked:
                totals[o.stream] = totals.get(o.stream, 0) + 1
                o.inc_no = totals[o.stream]
        sems = {s: nc.alloc_semaphore(name=f"t_{s}") for s in totals}
        dsems = {}
        for s in dma_k:
            n = min(self.R_DMA, dma_k[s])
            dsems[s] = [nc.alloc_semaphore(name=f"t_{s}{i}") for i in range(n)]
        hw = {}
        for o in self.ops:
            eng = getattr(nc, o.eng)
            for stream in sorted(o.waits):
                src = self.ops[o.waits[stream]]
                if stream.endswith("_dma"):
                    slot, gen = dma_slot[src.idx]
                    sem = dsems[stream][slot]
                    val = 16 * (gen + 1)
                    key = (o.eng, stream, slot)
                else:
                    val = src.inc_no
                    if val == 0:
                        continue
                    sem = sems[stream]
                    key = (o.eng, stream)
                if hw.get(key, -1) >= val:
                    continue
                hw[key] = val
                eng.wait_ge(sem, val)
            if o.stream.endswith("_dma"):
                slot, gen = dma_slot[o.idx]
                if gen > 0:
                    key = (o.eng, o.stream, slot)
                    val = 16 * gen
                    if hw.get(key, -1) < val:
                        hw[key] = val
                        eng.wait_ge(dsems[o.stream][slot], val)
                instr = o.fn()
                instr.then_inc(dsems[o.stream][slot], 16)
            else:
                instr = o.fn()
                if o.marked:
                    instr.then_inc(sems[o.stream], 1)
        for s, k in dma_k.items():
            for slot in range(min(self.R_DMA, k)):
                uses = (k - slot - 1) // self.R_DMA + 1
                nc.sync.wait_ge(dsems[s][slot], 16 * uses)
        return nc


_NC_CACHE = {}


def _get_nc(name, builder):
    if name not in _NC_CACHE:
        p = builder()
        _NC_CACHE[name] = p.flush()
    return _NC_CACHE[name]


def _run(name, builder, in_maps, **kw):
    nc = _get_nc(name, builder)
    res = run_bass_kernel_spmd(nc, in_maps, core_ids=list(range(NCORES)), **kw)
    return res.results



def build_l0a():
    p = Prog("l0a")
    x = p.dram_in("x", (NT, D))
    wq_h = p.dram_in("wq", (D, D))
    wkv_h = p.dram_in("wkv", (D, 2 * D))
    med_h = p.dram_in("med", (64, D))
    wg_h = p.dram_in("wg", (D, 8))
    ident_h = p.dram_in("ident", (128, 128), BF16)

    o_xnfm = p.dram_out("xnfm", (8, 128, NT), BF16)
    o_qfm = p.dram_out("qfm", (8, 128, NT), BF16)
    o_S = p.dram_out("S", (8, NT))
    o_mx = p.dram_out("mx", (64, 16))
    o_se = p.dram_out("se", (64, 16))
    o_accv = p.dram_out("accv", (64, 16 * 64))

    xnfm = p.sbuf([128, 8 * NT], BF16, "xnfm")
    kfm = p.sbuf([128, 8 * NT], BF16, "kfm")
    vfm = p.sbuf([128, 8 * NT], BF16, "vfm")
    idsb = p.sbuf([128, 128], BF16, "idsb")
    p.dma(idsb, ident_h)
    wq = p.wload(wq_h, D, D, "wq")
    wkv = p.wload(wkv_h, D, 2 * D, "wkv")
    wg = p.wload(wg_h, D, 8, "wg")

    xt = [p.sbuf([128, D], F32, f"xt{i}") for i in range(2)]
    xc = [p.sbuf([128, D], F32, f"xc{i}") for i in range(2)]
    xnbf = [p.sbuf([128, D], BF16, f"xnbf{i}") for i in range(2)]
    xsq = p.sbuf([128, D], F32, "xsq")
    statsall = p.sbuf([128, TT * 8], F32, "stats")

    x3 = x.rearrange("(t p) d -> t p d", p=128)
    xnv = xnfm.rearrange("p (c t) -> p c t", c=8)
    for t in range(TT):
        bi = t % 2
        p.dma(xt[bi], x3[t])
        stats = statsall[:, t * 8:(t + 1) * 8]
        nm = stats[:, 0:1]
        vr = stats[:, 1:2]
        sd = stats[:, 2:3]
        rstd = stats[:, 3:4]
        vr2 = stats[:, 4:5]
        p.red(nm, xt[bi], ALU.add)
        p.ts(nm, nm, -1.0 / D, None, op0=ALU.mult)
        p.ts(xc[bi], xt[bi], nm, None, op0=ALU.add)
        p.act(xsq[:, 0:512], xc[bi][:, 0:512], AF.Square, accum_out=vr)
        p.act(xsq[:, 512:1024], xc[bi][:, 512:1024], AF.Square, accum_out=vr2)
        p.tt(vr, vr, vr2, ALU.add)
        p.act(sd, vr, AF.Sqrt, bias=p.constf(1e-5), scale=1.0 / D)
        p.recip(rstd, sd)
        p.ts(xnbf[bi], xc[bi], rstd, None, op0=ALU.mult)
        for ch in range(8):
            pb = p.bankb(ch % 2)[:, 0:128]
            p.tp(pb, xnbf[bi][:, ch * 128:(ch + 1) * 128], idsb)
            p.cp(xnv[:, ch, t * 128:(t + 1) * 128], pb, eng="vector")
    p.dma(o_xnfm.rearrange("c p t -> p c t"), xnv)

    qbf = [p.sbuf([128, BLK], BF16, f"qbf{i}") for i in range(2)]
    kvv = (kfm.rearrange("p (c t) -> p c t", c=8),
           vfm.rearrange("p (c t) -> p c t", c=8))
    pi = 0
    for nb in range(NBLK):
        tok = slice(nb * BLK, (nb + 1) * BLK)
        for po in range(8):
            ps = p.bankf(pi % 4)
            pi += 1
            p.mm_acc(ps, [(wq[:, ki, po * 128:(po + 1) * 128], xnv[:, ki, tok])
                          for ki in range(8)])
            qb = qbf[po % 2]
            p.cp(qb, ps, eng="scalar")
            p.dma(o_qfm[po][:, tok], qb)
        for po in range(16):
            ps = p.bankf(pi % 4)
            pi += 1
            p.mm_acc(ps, [(wkv[:, ki, po * 128:(po + 1) * 128], xnv[:, ki, tok])
                          for ki in range(8)])
            p.cp(kvv[po // 8][:, po % 8, tok], ps, eng="vector")
    ssb = p.sbuf([8, NT], F32, "ssb")
    for nb in range(NBLK):
        tok = slice(nb * BLK, (nb + 1) * BLK)
        ps8 = p.bankf(nb % 4)[0:8, :]
        p.mm_acc(ps8, [(wg[:, ki, :], xnv[:, ki, tok]) for ki in range(8)])
        p.act(ssb[:, tok], ps8, AF.Sigmoid)
    p.dma(o_S, ssb)

    med_sb = p.sbuf([64, D], F32, "medsb")
    medbf = p.sbuf([64, D], BF16, "medbf")
    medfm = p.sbuf([128, 8 * 64], BF16, "medfm")

    def medslot(h):
        bb = (h % 2) * 64
        return medfm[bb:bb + 64, (h // 2) * 64:(h // 2) * 64 + 64]

    p.dma(med_sb, med_h)
    p.smul(medbf, med_sb, 0.125)
    for h in range(16):
        ptm = p.bankb(h % 2)[0:64, 0:64]
        p.tp(ptm, medbf[:, h * 64:(h + 1) * 64], idsb)
        p.cp(medslot(h), ptm, eng="vector")

    scs = p.sbuf([64, NT], F32, "scs")
    esc = p.sbuf([64, NT], BF16, "esc")
    mxa = p.sbuf([64, 16], F32, "mxa")
    sea = p.sbuf([64, 16], F32, "sea")
    accva = p.sbuf([64, 16 * 64], F32, "accva")
    nmx = p.sbuf([64, 1], F32, "nmx")
    se4 = p.sbuf([64, 4], F32, "se4")
    esct = p.sbuf([128, 2 * 64], BF16, "esct")
    vtm = p.sbuf([128, 2 * 64], BF16, "vtm")

    for h in range(16):
        krows = kvv[0][(h % 2) * 64:(h % 2) * 64 + 64, h // 2, :]
        vrows = kvv[1][(h % 2) * 64:(h % 2) * 64 + 64, h // 2, :]
        for nb in range(NBLK):
            ps = p.bankf(nb % 2)[0:64, :]
            p.mm(ps, medslot(h),
                 krows[:, nb * BLK:(nb + 1) * BLK], True, True)
            p.cp(scs[:, nb * BLK:(nb + 1) * BLK], ps, eng="vector")
        p.red(mxa[:, h:h + 1], scs, ALU.max)
        p.ts(nmx, mxa[:, h:h + 1], -1.0, None, op0=ALU.mult)
        for q4 in range(4):
            p.act(esc[:, q4 * 512:(q4 + 1) * 512], scs[:, q4 * 512:(q4 + 1) * 512],
                  AF.Exp, bias=nmx, accum_out=se4[:, q4:q4 + 1])
        p.red(sea[:, h:h + 1], se4, ALU.add)
        pacc = p.bankf(2)[0:64, 0:64]
        for lc in range(TT):
            sl = slice(lc * 128, (lc + 1) * 128)
            pe = p.bankb(lc % 2)[:, 0:64]
            pv = p.bankb(2 + lc % 2)[:, 0:64]
            p.tp(pe, esc[:, sl], idsb)
            p.tp(pv, vrows[:, sl], idsb)
            eb = esct[:, (lc % 2) * 64:(lc % 2) * 64 + 64]
            vb = vtm[:, (lc % 2) * 64:(lc % 2) * 64 + 64]
            p.cp(eb, pe, eng="vector")
            p.cp(vb, pv, eng="scalar")
            p.mm(pacc, eb, vb, lc == 0, lc == TT - 1)
        p.cp(accva[:, h * 64:(h + 1) * 64], pacc, eng="vector")
    p.dma(o_mx, mxa)
    p.dma(o_se, sea)
    p.dma(o_accv, accva)
    return p


def build_l0b():
    p = Prog("l0b")
    qfm_h = p.dram_in("qfm", (8, 128, NT), BF16)
    med_h = p.dram_in("med", (64, D))
    wo_h = p.dram_in("wo", (D, D))
    bo_h = p.dram_in("bo", (1, D))
    mx_h = p.dram_in("mx", (16, 64, 4))
    se_h = p.dram_in("se", (16, 64, 4))
    accv_h = p.dram_in("accv", (16, 4, 64, 64))
    ident_h = p.dram_in("ident", (128, 128), BF16)

    o_s = p.dram_out("s", (NT, D), BF16)

    idsb = p.sbuf([128, 128], BF16, "idsb")
    p.dma(idsb, ident_h)
    qfm = p.sbuf([128, 8 * NT], BF16, "qfm")
    p.dma(qfm.rearrange("p (c t) -> p c t", c=8),
          qfm_h.rearrange("c p t -> p c t"))
    wo = p.wload(wo_h, D, D, "wo")
    bo = p.sbuf([1, D], BF16, "bo")
    p.dma(bo, bo_h, eng="gpsimd")
    ones1 = p.sbuf([1, 128], BF16, "ones1")
    p.memset(ones1, 1.0)

    med_sb = p.sbuf([64, D], F32, "medsb")
    medbf = p.sbuf([64, D], BF16, "medbf")
    medfm = p.sbuf([128, 8 * 64], BF16, "medfm")

    def medslot(h):
        bb = (h % 2) * 64
        return medfm[bb:bb + 64, (h // 2) * 64:(h // 2) * 64 + 64]

    p.dma(med_sb, med_h)
    p.smul(medbf, med_sb, 0.125)
    for h in range(16):
        ptm = p.bankb(h % 2)[0:64, 0:64]
        p.tp(ptm, medbf[:, h * 64:(h + 1) * 64], idsb)
        p.cp(medslot(h), ptm, eng="vector")

    mxs = p.sbuf([64, 4], F32, "mxs")
    ses = p.sbuf([64, 4], F32, "ses")
    accs = p.sbuf([64, 4 * 64], F32, "accs")
    M = p.sbuf([64, 1], F32, "M")
    nM = p.sbuf([64, 1], F32, "nM")
    w4 = p.sbuf([64, 4], F32, "w4")
    wse = p.sbuf([64, 4], F32, "wse")
    den = p.sbuf([64, 1], F32, "den")
    rden = p.sbuf([64, 1], F32, "rden")
    acc0 = p.sbuf([64, 64], F32, "acc0")
    acc1 = p.sbuf([64, 64], F32, "acc1")
    mkv = p.sbuf([128, 8 * 64], BF16, "mkv")

    def mkvslot(h):
        bb = (h % 2) * 64
        return mkv[bb:bb + 64, (h // 2) * 64:(h // 2) * 64 + 64]

    for h in range(16):
        p.dma(mxs, mx_h[h])
        p.dma(ses, se_h[h])
        p.dma(accs.rearrange("p (c d) -> p c d", c=4),
              accv_h[h].rearrange("c m d -> m c d"))
        p.red(M, mxs, ALU.max)
        p.ts(nM, M, -1.0, None, op0=ALU.mult)
        p.act(w4, mxs, AF.Exp, bias=nM)
        p.tt(wse, w4, ses, ALU.mult)
        p.red(den, wse, ALU.add)
        p.recip(rden, den)
        a3 = accs.rearrange("p (c d) -> p c d", c=4)
        p.ts(acc0, a3[:, 0, :], w4[:, 0:1], None, op0=ALU.mult)
        p.stt(acc1, a3[:, 1, :], w4[:, 1:2], acc0, op0=ALU.mult, op1=ALU.add)
        p.stt(acc0, a3[:, 2, :], w4[:, 2:3], acc1, op0=ALU.mult, op1=ALU.add)
        p.stt(acc1, a3[:, 3, :], w4[:, 3:4], acc0, op0=ALU.mult, op1=ALU.add)
        p.ts(mkvslot(h), acc1, rden, None, op0=ALU.mult)

    probs = p.sbuf([128, 2 * 64], BF16, "probs")
    probT = p.sbuf([128, 2 * 128], BF16, "probT")
    attn = p.sbuf([128, 8 * NT], BF16, "attn")
    rowmax = p.sbuf([128, 1], F32, "rowmax")
    nrowmax = p.sbuf([128, 1], F32, "nrowmax")
    rowsum = p.sbuf([128, 1], F32, "rowsum")
    rrsum = p.sbuf([128, 1], F32, "rrsum")
    qv = qfm.rearrange("p (c t) -> p c t", c=8)
    attnv = attn.rearrange("p (c t) -> p c t", c=8)

    for t in range(TT):
        tok = slice(t * 128, (t + 1) * 128)
        for h in range(16):
            ps = p.bankf(h % 2)[:, 0:64]
            qrows = qv[(h % 2) * 64:(h % 2) * 64 + 64, h // 2, tok]
            p.mm(ps, qrows, medslot(h), True, True)
            p.red(rowmax, ps, ALU.max)
            p.ts(nrowmax, rowmax, -1.0, None, op0=ALU.mult)
            prb = probs[:, (h % 2) * 64:(h % 2) * 64 + 64]
            p.act(prb, ps, AF.Exp, bias=nrowmax, accum_out=rowsum)
            p.recip(rrsum, rowsum)
            p.ts(prb, prb, rrsum, None, op0=ALU.mult)
            pq = p.bankb(h % 2)[0:64, 0:128]
            p.tp(pq, prb, idsb)
            bb = (h % 2) * 64
            pT = probT[bb:bb + 64, (h % 2) * 128:(h % 2) * 128 + 128]
            p.cp(pT, pq, eng="vector")
            po = p.bankf(2 + h % 2)[0:64, 0:128]
            p.mm(po, mkvslot(h), pT, True, True)
            p.cp(attnv[(h % 2) * 64:(h % 2) * 64 + 64, h // 2, tok], po,
                 eng="scalar")

    sbf = [p.sbuf([128, D], BF16, f"sbf{i}") for i in range(2)]
    s3 = o_s.rearrange("(t p) d -> t p d", p=128)
    for t in range(TT):
        tok = slice(t * 128, (t + 1) * 128)
        sb = sbf[t % 2]
        for fo in range(2):
            ps = p.bankf((t * 2 + fo) % 4)
            fcols = slice(fo * BLK, (fo + 1) * BLK)
            for ki in range(8):
                p.mm(ps, attnv[:, ki, tok], wo[:, ki, fcols], ki == 0, False)
            p.mm(ps, ones1, bo[:, fcols], False, True)
            p.cp(sb[:, fcols], ps, eng="vector")
        p.dma(s3[t], sb)
    return p


def build_l0c():
    p = Prog("l0c")
    xnfm_h = p.dram_in("xnfm", (8, 128, NT), BF16)
    S_h = p.dram_in("S", (8, 128, 128))
    colsel_h = p.dram_in("colsel", (128, 16))
    s_h = p.dram_in("s", (NT, D), BF16)
    x_h = p.dram_in("x", (NT, D))
    w1_h = p.dram_in("w1", (8, D, 4096))
    b1_h = p.dram_in("b1", (8, 128, 32))
    w2_h = p.dram_in("w2", (8, 4096, D))
    b2_h = p.dram_in("b2", (8, D))
    slt_h = p.dram_in("slt", (128, 128), BF16)

    o_x = p.dram_out("xout", (NT, D))

    xnfm = p.sbuf([128, 8 * NT], BF16, "xnfm")
    p.dma(xnfm.rearrange("p (c t) -> p c t", c=8),
          xnfm_h.rearrange("c p t -> p c t"))
    xnv = xnfm.rearrange("p (c t) -> p c t", c=8)
    ssb = p.sbuf([128, 8 * 128], F32, "ssb")
    p.dma(ssb.rearrange("p (e c) -> p e c", e=8),
          S_h.rearrange("e p c -> p e c"))
    s3 = ssb.rearrange("p (e c) -> p e c", e=8)
    colsel = p.sbuf([128, 16], F32, "colsel")
    p.dma(colsel, colsel_h)
    colselb = p.sbuf([128, 16], BF16, "colselb")
    p.cp(colselb, colsel)
    slt = p.sbuf([128, 128], BF16, "slt")
    p.dma(slt, slt_h)
    onesf = p.sbuf([1, 128], F32, "onesf")
    p.memset(onesf, 1.0)
    onesb = p.sbuf([128, 1], BF16, "onesb")
    p.memset(onesb, 1.0)

    lob = p.sbuf([128, 8], F32, "lob")
    hib = p.sbuf([128, 8], F32, "hib")
    tb = p.sbuf([128, 8], F32, "tb")
    selb = p.sbuf([128, 8], mybir.dt.int32, "selb")
    cntb = p.sbuf([128, 8], F32, "cntb")
    ge = p.sbuf([128, 8 * 128], F32, "ge")
    cnt2 = p.sbuf([128, 8], F32, "cnt2")
    cnt2b = p.sbuf([128, 8], BF16, "cnt2b")
    p.memset(lob, 0.0)
    p.memset(hib, 1.0001)
    KSEL = 4096.0
    ge3 = ge.rearrange("p (e c) -> p e c", e=8)

    srcb2 = p.sbuf([1, 8], BF16, "srcb2")
    srcrem = p.sbuf([1, 8], F32, "srcrem")
    srcb3 = p.sbuf([1, 8], BF16, "srcb3")
    onesc = p.sbuf([1, 128], BF16, "onesc")
    p.memset(onesc, 1.0)

    def bcast_row(src_row_f32):
        pc = p.bankf(5)[:, 0:8]
        p.cp(srcb2, src_row_f32)
        p.tt(srcrem, src_row_f32, srcb2, ALU.subtract)
        p.cp(srcb3, srcrem)
        p.mm(pc, onesc, srcb2, True, False)
        p.mm(pc, onesc, srcb3, False, True)
        return pc

    def tbb(t8):
        return bass.AP(t8.tensor, t8.offset, [t8.ap[0], [1, 8], [0, 128]])

    pcnt = p.bankf(4)[0:1, 0:8]
    cntr = p.sbuf([1, 8], F32, "cntr")
    for it in range(44):
        p.tt(tb, lob, hib, ALU.add)
        p.ts(tb, tb, 0.5, None, op0=ALU.mult)
        p.tt(ge3, s3, tbb(tb), ALU.is_ge)
        p.red(cnt2, ge3, ALU.add)
        p.cp(cnt2b, cnt2)
        p.mm(pcnt, onesb, cnt2b, True, True)
        p.cp(cntr, pcnt)
        pcb = bcast_row(cntr)
        p.cp(cntb, pcb, eng="scalar")
        p.ts(selb, cntb, KSEL, None, op0=ALU.is_ge)
        p.sel(lob, selb, tb, lob)
        p.sel(hib, selb, hib, tb)

    p.tt(ge3, s3, tbb(lob), ALU.is_gt)
    p.red(cnt2, ge3, ALU.add)
    p.cp(cnt2b, cnt2)
    p.mm(pcnt, onesb, cnt2b, True, True)
    p.cp(cntr, pcnt)
    c1b = bcast_row(cntr)
    remb8 = p.sbuf([128, 8], F32, "remb8")
    p.ts(remb8, c1b, -1.0, KSEL, op0=ALU.mult, op1=ALU.add)

    eq = p.sbuf([128, 128], F32, "eq")
    pref = p.sbuf([128, 128], F32, "pref")
    zrow = p.sbuf([128, 128], F32, "zrow")
    p.memset(zrow, 0.0)
    rowsum = p.sbuf([128, 1], F32, "rowsum")
    rowsumb = p.sbuf([128, 1], BF16, "rowsumb")
    seleq = p.sbuf([128, 128], F32, "seleq")
    gmask = p.sbuf([128, 8 * 128], F32, "gmask")
    gm3 = gmask.rearrange("p (e c) -> p e c", e=8)
    gloc = p.sbuf([128, 8 * 16], F32, "gloc")
    glv = gloc.rearrange("p (e t) -> p e t", e=8)
    gmaskb8 = p.sbuf([128, 8 * 128], BF16, "gmaskb8")
    gmb3 = gmaskb8.rearrange("p (e c) -> p e c", e=8)
    glrows = p.sbuf([1, TT * 128], BF16, "glrows")
    grv = glrows.rearrange("o (t j) -> o t j", t=TT)

    for e in range(8):
        p.tt(eq, s3[:, e, :], tbb(lob)[:, e, :], ALU.is_equal)
        p.scan(pref, zrow, eq, 0.0, op0=ALU.add, op1=ALU.add)
        p.tt(pref, pref, eq, ALU.subtract)
        p.red(rowsum, eq, ALU.add)
        p.cp(rowsumb, rowsum)
        poff = p.bankf(4)[:, 8:9]
        p.mm(poff, slt, rowsumb, True, True)
        poffs = p.sbuf([128, 1], F32, f"poffs{e}")
        p.cp(poffs, poff)
        p.ts(pref, pref, poffs, None, op0=ALU.add)
        p.ts(seleq, pref, remb8[:, e:e + 1], None, op0=ALU.is_lt)
        p.tt(seleq, seleq, eq, ALU.mult)
        p.tt(gm3[:, e, :], ge3[:, e, :], seleq, ALU.add)
        p.tt(gm3[:, e, :], gm3[:, e, :], s3[:, e, :], ALU.mult)
        pgl = p.bankf(5)[:, 16:32]
        p.mm(pgl, gm3[:, e, :], colsel, True, True)
        p.cp(glv[:, e, :], pgl)
        p.cp(gmb3[:, e, :], gm3[:, e, :])


    acc = p.sbuf([128, TT * D], F32, "acc")
    w1sb = p.sbuf([128, 8 * 2048], BF16, "w1sb")
    w2sb = p.sbuf([128, 16 * D], BF16, "w2sb")
    b1sb = p.sbuf([128, 32], F32, "b1sb")
    b2sb = p.sbuf([1, D], BF16, "b2sb")
    hidb = p.sbuf([128, 16 * BLK], BF16, "hidb")
    accv = acc.rearrange("p (t d) -> p t d", t=TT)
    w1v = w1sb.rearrange("p (k n) -> p k n", k=8)
    w2v = w2sb.rearrange("p (k n) -> p k n", k=16)
    hv = hidb.rearrange("p (k n) -> p k n", k=16)

    for e in range(8):
        p.dma(b1sb, b1_h[e], eng="gpsimd")
        p.dma(b2sb, b2_h[e:e + 1, :], eng="gpsimd")
        for t in range(TT):
            pgr = p.bankf(6)[0:1, 0:128]
            p.mm(pgr, colselb[:, t:t + 1], gmb3[:, e, :], True, True)
            p.cp(grv[:, t, :], pgr)
        for h2 in range(2):
            hcols = slice(h2 * 2048, (h2 + 1) * 2048)
            p.dma(w1v, w1_h[e, :, hcols].rearrange("(k p) n -> p k n", p=128),
                  eng="gpsimd")
            p.dma(w2v, w2_h[e, hcols, :].rearrange("(k p) n -> p k n", p=128),
                  eng="gpsimd")
            for nb in range(NBLK):
                tok = slice(nb * BLK, (nb + 1) * BLK)
                for po in range(16):
                    ps = p.bankf(po % 2)
                    p.mm_acc(ps, [(w1v[:, ki, po * 128:(po + 1) * 128],
                                   xnv[:, ki, tok]) for ki in range(8)])
                    p.act(hv[:, po, :], ps, AF.Gelu,
                          bias=b1sb[:, h2 * 16 + po: h2 * 16 + po + 1])
                for j in range(4):
                    t_glob = nb * 4 + j
                    for fo in range(2):
                        ps = p.bankf(2 + (j * 2 + fo) % 2)
                        fcols = slice(fo * BLK, (fo + 1) * BLK)
                        pairs = [(hv[:, ki, j * 128:(j + 1) * 128],
                                  w2v[:, ki, fcols]) for ki in range(16)]
                        if h2 == 0:
                            pairs.append((grv[:, t_glob, :], b2sb[:, fcols]))
                        p.mm_acc(ps, pairs)
                        gcol = glv[:, e, t_glob:t_glob + 1]
                        dst = accv[:, t_glob, fcols]
                        if e == 0 and h2 == 0:
                            p.ts(dst, ps, gcol, None, op0=ALU.mult)
                        else:
                            p.stt(dst, ps, gcol, dst, op0=ALU.mult, op1=ALU.add)

    xtl = p.sbuf([128, D], F32, "xtl0")
    stl = p.sbuf([128, D], BF16, "stl0")
    x3 = x_h.rearrange("(t p) d -> t p d", p=128)
    sS = s_h.rearrange("(t p) d -> t p d", p=128)
    o3 = o_x.rearrange("(t p) d -> t p d", p=128)
    for t in range(TT):
        at = accv[:, t, :]
        p.dma(xtl, x3[t])
        p.dma(stl, sS[t], eng="gpsimd")
        p.act(at, at, AF.Silu)
        p.tt(at, at, stl, ALU.mult)
        p.tt(at, at, xtl, ALU.add)
        p.dma(o3[t], at)
    return p




DI = 2048
DSL = DI // NCORES


def build_l1b1():
    p = Prog("l1b1")
    x = p.dram_in("x", (NT, D))
    win_h = p.dram_in("win", (D, 2 * DI))
    ident_h = p.dram_in("ident", (128, 128), BF16)
    o_xnfm = p.dram_out("xnfm", (8, 128, NT), BF16)
    o_xin = p.dram_out("xin", (16, 128, NT), BF16)
    o_z = p.dram_out("z", (16, 128, NT), BF16)

    xnfm = p.sbuf([128, 8 * NT], BF16, "xnfm")
    idsb = p.sbuf([128, 128], BF16, "idsb")
    p.dma(idsb, ident_h)
    win = p.wload(win_h, D, 2 * DI, "win")

    xt = [p.sbuf([128, D], F32, f"xt{i}") for i in range(2)]
    xc = [p.sbuf([128, D], F32, f"xc{i}") for i in range(2)]
    xnbf = [p.sbuf([128, D], BF16, f"xnbf{i}") for i in range(2)]
    xsq = p.sbuf([128, D], F32, "xsq")
    statsall = p.sbuf([128, TT * 8], F32, "stats")
    x3 = x.rearrange("(t p) d -> t p d", p=128)
    xnv = xnfm.rearrange("p (c t) -> p c t", c=8)
    for t in range(TT):
        bi = t % 2
        p.dma(xt[bi], x3[t])
        st = statsall[:, t * 8:(t + 1) * 8]
        nm, vr, sd, rstd, vr2 = (st[:, i:i + 1] for i in range(5))
        p.red(nm, xt[bi], ALU.add)
        p.ts(nm, nm, -1.0 / D, None, op0=ALU.mult)
        p.ts(xc[bi], xt[bi], nm, None, op0=ALU.add)
        p.act(xsq[:, 0:512], xc[bi][:, 0:512], AF.Square, accum_out=vr)
        p.act(xsq[:, 512:1024], xc[bi][:, 512:1024], AF.Square, accum_out=vr2)
        p.tt(vr, vr, vr2, ALU.add)
        p.act(sd, vr, AF.Sqrt, bias=p.constf(1e-5), scale=1.0 / D)
        p.recip(rstd, sd)
        p.ts(xnbf[bi], xc[bi], rstd, None, op0=ALU.mult)
        for ch in range(8):
            pb = p.bankb(ch % 2)[:, 0:128]
            p.tp(pb, xnbf[bi][:, ch * 128:(ch + 1) * 128], idsb)
            p.cp(xnv[:, ch, t * 128:(t + 1) * 128], pb, eng="vector")
    p.dma(o_xnfm.rearrange("c p t -> p c t"), xnv)

    ob = [p.sbuf([128, BLK], BF16, f"ob{i}") for i in range(4)]
    for nb in range(NBLK):
        tok = slice(nb * BLK, (nb + 1) * BLK)
        for po in range(32):
            ps = p.bankf(po % 4)
            p.mm_acc(ps, [(win[:, ki, po * 128:(po + 1) * 128], xnv[:, ki, tok])
                          for ki in range(8)])
            o = ob[po % 4]
            p.cp(o, ps, eng="scalar" if po % 2 else "vector")
            dst = o_xin if po < 16 else o_z
            p.dma(dst[po % 16][:, tok], o)
    return p


def build_l1b2():
    p = Prog("l1b2")
    xin_h = p.dram_in("xin", (2, 128, NTOK), BF16)
    convw_h = p.dram_in("convw", (128, 2, 4))
    convb_h = p.dram_in("convb", (128, 2))
    wxp_h = p.dram_in("wxp", (DSL, 96))
    o_u = p.dram_out("u", (2, 128, NTOK), BF16)
    o_xdp = p.dram_out("xdp", (96, NTOK))

    xin = p.sbuf([128, 2 * NTOK], BF16, "xin")
    xv = xin.rearrange("p (c t) -> p c t", c=2)
    p.dma(xv, xin_h.rearrange("c p t -> p c t"))
    u = p.sbuf([128, 2 * NTOK], BF16, "u")
    uv = u.rearrange("p (c t) -> p c t", c=2)
    cw = p.sbuf([128, 2 * 4], F32, "cw")
    p.dma(cw, convw_h.rearrange("p c k -> p (c k)"))
    cb = p.sbuf([128, 2], F32, "cb")
    p.dma(cb, convb_h)
    wxp = p.wload(wxp_h, DSL, 96, "wxp")

    acc = p.sbuf([128, L], F32, "acc")
    for pc in range(2):
        for b in range(B):
            xb = xv[:, pc, b * L:(b + 1) * L]
            p.ts(acc, xb, cw[:, pc * 4 + 3: pc * 4 + 4], None, op0=ALU.mult)
            for j in (2, 1, 0):
                sh = 3 - j
                p.stt(acc[:, sh:L], xb[:, 0:L - sh],
                      cw[:, pc * 4 + j: pc * 4 + j + 1], acc[:, sh:L],
                      op0=ALU.mult, op1=ALU.add)
            p.act(uv[:, pc, b * L:(b + 1) * L], acc, AF.Silu,
                  bias=cb[:, pc:pc + 1])
    p.dma(o_u.rearrange("c p t -> p c t"), uv)

    xdp = [p.sbuf([96, BLK], F32, f"xdp{i}") for i in range(2)]
    for nb in range(NTOK // BLK):
        tok = slice(nb * BLK, (nb + 1) * BLK)
        ps = p.bankf(nb % 2)[0:96, :]
        p.mm_acc(ps, [(wxp[:, ki, :], uv[:, ki, tok]) for ki in range(2)])
        xb = xdp[nb % 2]
        p.cp(xb, ps, eng="vector")
        p.dma(o_xdp[:, tok], xb)
    return p


def build_l1b3():
    p = Prog("l1b3")
    u_h = p.dram_in("u", (2, 128, NTOK), BF16)
    xdp_h = p.dram_in("xdp", (8, 96, NTOK))
    wdt_h = p.dram_in("wdt", (64, DSL))
    bdt_h = p.dram_in("bdt", (128, 2))
    aarr_h = p.dram_in("aarr", (128, 32))
    darr_h = p.dram_in("darr", (128, 2))
    sels_h = p.dram_in("sels", (16, 128, 128), BF16)
    nsel_h = p.dram_in("nsel", (128, 8), BF16)
    o_y = p.dram_out("y", (DSL, NTOK), BF16)
    o_du = p.dram_out("du", (2, 128, NTOK), BF16)

    u = p.sbuf([128, 2 * NTOK], BF16, "u")
    uv = u.rearrange("p (c t) -> p c t", c=2)
    p.dma(uv, u_h.rearrange("c p t -> p c t"))
    wdt = p.wload(wdt_h, 64, DSL, "wdt")
    bdt = p.sbuf([128, 2], F32, "bdt")
    p.dma(bdt, bdt_h)
    aarr = p.sbuf([128, 32], F32, "aarr")
    p.dma(aarr, aarr_h)
    A = p.sbuf([128, 32], F32, "A")
    p.act(A, aarr, AF.Exp)
    p.ts(A, A, -1.0, None, op0=ALU.mult)
    darr = p.sbuf([128, 2], F32, "darr")
    p.dma(darr, darr_h)
    sels = p.sbuf([128, 16 * 128], BF16, "sels")
    p.dma(sels.rearrange("p (g q) -> p g q", g=16),
          sels_h.rearrange("g p q -> p g q"))
    sv = sels.rearrange("p (g q) -> p g q", g=16)
    nsel = p.sbuf([128, 8], BF16, "nsel")
    p.dma(nsel, nsel_h)

    dub = p.sbuf([128, NTOK], BF16, "dub")
    for pc in range(2):
        p.ts(dub, uv[:, pc, :], darr[:, pc:pc + 1], None, op0=ALU.mult)
        p.dma(o_du[pc].rearrange("p t -> p t"), dub)

    CH = 2048
    NCH = L // CH
    xpart = p.sbuf([96, CH], F32, "xpart")
    xdbl = [p.sbuf([96, CH], F32, f"xdbl{i}") for i in range(2)]
    dtch = p.sbuf([128, 2 * CH], BF16, "dtch")
    dtv = dtch.rearrange("p (c t) -> p c t", c=2)
    dtlowb = p.sbuf([64, CH], BF16, "dtlowb")
    sp1 = p.sbuf([128, BLK], F32, "sp1")
    sp2 = p.sbuf([128, BLK], F32, "sp2")
    btile = p.sbuf([128, CH], BF16, "btile")
    ctile = p.sbuf([128, CH], BF16, "ctile")
    bcf = p.sbuf([32, CH], F32, "bcf")
    bcb = p.sbuf([32, CH], BF16, "bcb")
    dA = [p.sbuf([128, BLK], F32, f"dA{i}") for i in range(2)]
    Xs = [p.sbuf([128, BLK], F32, f"Xs{i}") for i in range(2)]
    hs = [p.sbuf([128, BLK], F32, f"hs{i}") for i in range(2)]
    hc = [p.sbuf([128, BLK], BF16, f"hc{i}") for i in range(2)]
    dtu = p.sbuf([128, 2 * CH], BF16, "dtu")
    dtuv = dtu.rearrange("p (c t) -> p c t", c=2)
    carry = p.sbuf([128, 64], F32, "carry")
    p.memset(carry, 0.0)
    ystage = [p.sbuf([8, CH], F32, f"ystage{i}") for i in range(2)]

    NB = CH // BLK
    for b in range(B):
        for chn in range(NCH):
            cols = slice(b * L + chn * CH, b * L + (chn + 1) * CH)
            p.dma(xdbl[0], xdp_h[0, :, cols])
            for cc in range(1, 8):
                src, dst = xdbl[(cc + 1) % 2], xdbl[cc % 2]
                p.dma(xpart, xdp_h[cc, :, cols])
                p.tt(dst, src, xpart, ALU.add)
            xd = xdbl[7 % 2]
            p.cp(dtlowb, xd[0:64, :])
            for pc in range(2):
                for nb in range(NB):
                    ps = p.bankf(nb % 2)
                    p.mm(ps, wdt[0:64, 0, pc * 128:(pc + 1) * 128],
                         dtlowb[:, nb * BLK:(nb + 1) * BLK], True, True)
                    p.act(sp1, ps, AF.Abs, bias=bdt[:, pc:pc + 1])
                    p.act(sp1, sp1, AF.Exp, scale=-1.0)
                    p.act(sp1, sp1, AF.Ln, bias=p.constf(1.0))
                    p.act(sp2, ps, AF.Relu, bias=bdt[:, pc:pc + 1])
                    p.tt(dtv[:, pc, nb * BLK:(nb + 1) * BLK], sp1, sp2,
                         ALU.add)
                p.tt(dtuv[:, pc, :], dtv[:, pc, :], uv[:, pc, cols], ALU.mult)
            p.cp(bcf, xd[64:96, :])
            p.cp(bcb, bcf)
            for ds8 in range(8):
                p.dma(btile[ds8 * 16:(ds8 + 1) * 16, :], bcb[0:16, :],
                      eng="gpsimd")
                p.dma(ctile[ds8 * 16:(ds8 + 1) * 16, :], bcb[16:32, :],
                      eng="gpsimd")
            for g in range(32):
                pc = g // 16
                gi = g % 16
                ccol = b * 32 + g
                gr = g % 2
                ys = ystage[gr]
                for nb in range(NB):
                    ri = nb % 2
                    bcols = slice(nb * BLK, (nb + 1) * BLK)
                    pdt = p.bankf(2 + ri)
                    p.mm(pdt, sv[:, gi, :], dtv[:, pc, bcols], True, True)
                    p.act(dA[ri], pdt, AF.Exp, scale=A[:, g:g + 1])
                    pdu = p.bankf(4 + ri)
                    p.mm(pdu, sv[:, gi, :], dtuv[:, pc, bcols], True, True)
                    p.tt(Xs[ri], pdu, btile[:, bcols], ALU.mult)
                    p.scan(hs[ri], dA[ri], Xs[ri], carry[:, ccol:ccol + 1])
                    p.cp(carry[:, ccol:ccol + 1], hs[ri][:, BLK - 1:BLK])
                    p.tt(hc[ri], hs[ri], ctile[:, bcols], ALU.mult)
                    py = p.bankf(6 + ri)[0:8, :]
                    p.mm(py, nsel, hc[ri], True, True)
                    p.cp(ys[:, bcols], py, eng="scalar")
                rows = slice(pc * 128 + 8 * gi, pc * 128 + 8 * gi + 8)
                p.dma(o_y[rows, cols], ys, eng="gpsimd")
    return p


def build_l1d1():
    p = Prog("l1d1")
    y_h = p.dram_in("y", (16, 128, NT), BF16)
    du_h = p.dram_in("du", (16, 128, NT), BF16)
    z_h = p.dram_in("z", (16, 128, NT), BF16)
    wout_h = p.dram_in("wout", (DI, D))
    x_h = p.dram_in("x", (NT, D))
    gam_h = p.dram_in("gam", (1, D))
    bet_h = p.dram_in("bet", (1, D))
    o_s = p.dram_out("s", (NT, D), BF16)

    wout = p.wload(wout_h, DI, D, "wout")
    yg = p.sbuf([128, 16 * NT], BF16, "yg")
    ygv = yg.rearrange("p (c t) -> p c t", c=16)
    ych = p.sbuf([128, NT], BF16, "ych")
    duch = p.sbuf([128, NT], BF16, "duch")
    zch = p.sbuf([128, NT], BF16, "zch")
    t1 = p.sbuf([128, NT], F32, "t1")
    t2 = p.sbuf([128, NT], F32, "t2")
    for ch in range(16):
        p.dma(ych, y_h[ch])
        p.dma(duch, du_h[ch])
        p.dma(zch, z_h[ch])
        p.act(t1, zch, AF.Silu)
        p.tt(t2, ych, duch, ALU.add)
        p.tt(ygv[:, ch, :], t1, t2, ALU.mult)

    gam = p.sbuf([1, D], BF16, "gam")
    bet = p.sbuf([1, D], BF16, "bet")
    p.dma(gam, gam_h, eng="gpsimd")
    p.dma(bet, bet_h, eng="gpsimd")
    ones1 = p.sbuf([1, 128], BF16, "ones1")
    p.memset(ones1, 1.0)
    gamb = p.sbuf([128, D], F32, "gamb")
    betb = p.sbuf([128, D], F32, "betb")
    for fo in range(2):
        ps = p.bankf(fo % 2)
        p.mm(ps, ones1, gam[:, fo * 512:(fo + 1) * 512], True, True)
        p.cp(gamb[:, fo * 512:(fo + 1) * 512], ps, eng="vector")
        p.mm(ps, ones1, bet[:, fo * 512:(fo + 1) * 512], True, True)
        p.cp(betb[:, fo * 512:(fo + 1) * 512], ps, eng="vector")

    xt = [p.sbuf([128, D], F32, f"xt{i}") for i in range(2)]
    xn = p.sbuf([128, D], F32, "xn")
    res = p.sbuf([128, D], F32, "res")
    xsq = p.sbuf([128, D], F32, "xsq")
    statsall = p.sbuf([128, TT * 8], F32, "stats")
    sout = [p.sbuf([128, D], BF16, f"sout{i}") for i in range(2)]
    x3 = x_h.rearrange("(t p) d -> t p d", p=128)
    s3 = o_s.rearrange("(t p) d -> t p d", p=128)
    for t in range(TT):
        bi = t % 2
        tok = slice(t * 128, (t + 1) * 128)
        p.dma(xt[bi], x3[t])
        st = statsall[:, t * 8:(t + 1) * 8]
        nm, vr, sd, rstd, vr2, nm2, vr3, sd2 = (st[:, i:i + 1] for i in range(8))
        p.red(nm, xt[bi], ALU.add)
        p.ts(nm, nm, -1.0 / D, None, op0=ALU.mult)
        p.ts(xn, xt[bi], nm, None, op0=ALU.add)
        p.act(xsq[:, 0:512], xn[:, 0:512], AF.Square, accum_out=vr)
        p.act(xsq[:, 512:1024], xn[:, 512:1024], AF.Square, accum_out=vr2)
        p.tt(vr, vr, vr2, ALU.add)
        p.act(sd, vr, AF.Sqrt, bias=p.constf(1e-5), scale=1.0 / D)
        p.recip(rstd, sd)
        p.ts(xn, xn, rstd, None, op0=ALU.mult)
        for fo in range(2):
            ps = p.bankf(2 + fo)
            fcols = slice(fo * BLK, (fo + 1) * BLK)
            p.mm_acc(ps, [(ygv[:, ki, tok], wout[:, ki, fcols])
                          for ki in range(16)])
            p.tt(res[:, fcols], ps, xn[:, fcols], ALU.add)
        p.red(nm2, res, ALU.add)
        p.ts(nm2, nm2, -1.0 / D, None, op0=ALU.mult)
        p.ts(res, res, nm2, None, op0=ALU.add)
        p.act(xsq[:, 0:512], res[:, 0:512], AF.Square, accum_out=vr3)
        p.act(xsq[:, 512:1024], res[:, 512:1024], AF.Square, accum_out=sd2)
        p.tt(vr3, vr3, sd2, ALU.add)
        p.act(sd2, vr3, AF.Sqrt, bias=p.constf(1e-5), scale=1.0 / D)
        p.recip(rstd, sd2)
        p.ts(res, res, rstd, None, op0=ALU.mult)
        so = sout[bi]
        p.tt(res, res, gamb, ALU.mult)
        p.tt(so, res, betb, ALU.add)
        p.dma(s3[t], so)
    return p


def build_l1d2():
    p = Prog("l1d2")
    xnfm_h = p.dram_in("xnfm", (8, 128, NT), BF16)
    wg_h = p.dram_in("wg", (D, 8))
    wu_h = p.dram_in("wu", (8, D, DI))
    wd_h = p.dram_in("wd", (8, DI, D))
    w1_h = p.dram_in("w1", (D, DI))
    w2_h = p.dram_in("w2", (DI, D))
    s_h = p.dram_in("s", (NT, D), BF16)
    x_h = p.dram_in("x", (NT, D))
    o_x = p.dram_out("xout", (NT, D))

    xnfm = p.sbuf([128, 8 * NT], BF16, "xnfm")
    p.dma(xnfm.rearrange("p (c t) -> p c t", c=8),
          xnfm_h.rearrange("c p t -> p c t"))
    xnv = xnfm.rearrange("p (c t) -> p c t", c=8)
    wg = p.wload(wg_h, D, 8, "wg")

    cwall = p.sbuf([128, TT * 8], F32, "cwall")
    iot8m = p.sbuf([128, 8], F32, "iot8m")
    p.iota(iot8m, [[1, 8]], base=-8, channel_multiplier=0)
    gp = p.sbuf([128, 8], F32, "gp")
    gp2 = p.sbuf([128, 8], F32, "gp2")
    cand = p.sbuf([128, 8], F32, "cand")
    mio = p.sbuf([128, 8], F32, "mio")
    oh1 = p.sbuf([128, 8], F32, "oh1")
    oh2 = p.sbuf([128, 8], F32, "oh2")
    v1 = p.sbuf([128, 1], F32, "v1")
    v2 = p.sbuf([128, 1], F32, "v2")
    i1 = p.sbuf([128, 1], F32, "i1")
    i2 = p.sbuf([128, 1], F32, "i2")
    r1 = p.sbuf([128, 1], F32, "r1")
    r2 = p.sbuf([128, 1], F32, "r2")
    vs = p.sbuf([128, 1], F32, "vs")
    BIG = 1.0e9
    for t in range(TT):
        tok = slice(t * 128, (t + 1) * 128)
        ps2 = p.bankf(0)[:, 0:8]
        p.mm_acc(ps2, [(xnv[:, ki, tok], wg[:, ki, :]) for ki in range(8)])
        p.red(r1, ps2, ALU.max)
        p.ts(r1, r1, -1.0, None, op0=ALU.mult)
        p.act(gp, ps2, AF.Exp, bias=r1, accum_out=r2)
        p.recip(r2, r2)
        p.ts(gp, gp, r2, None, op0=ALU.mult)
        p.red(v1, gp, ALU.max)
        p.ts(cand, gp, v1, None, op0=ALU.is_ge)
        p.tt(mio, cand, iot8m, ALU.mult)
        p.ts(mio, mio, 8.0, None, op0=ALU.add)
        p.red(i1, mio, ALU.min)
        p.ts(i1, i1, -8.0, None, op0=ALU.add)
        p.ts(oh1, iot8m, i1, None, op0=ALU.is_equal)
        p.stt(gp2, oh1, -BIG, gp, op0=ALU.mult, op1=ALU.add)
        p.red(v2, gp2, ALU.max)
        p.ts(cand, gp2, v2, None, op0=ALU.is_ge)
        p.tt(mio, cand, iot8m, ALU.mult)
        p.ts(mio, mio, 8.0, None, op0=ALU.add)
        p.red(i2, mio, ALU.min)
        p.ts(i2, i2, -8.0, None, op0=ALU.add)
        p.ts(oh2, iot8m, i2, None, op0=ALU.is_equal)
        p.tt(vs, v1, v2, ALU.add)
        p.recip(vs, vs)
        cw = cwall[:, t * 8:(t + 1) * 8]
        p.ts(cw, oh1, v1, None, op0=ALU.mult)
        p.stt(cw, oh2, v2, cw, op0=ALU.mult, op1=ALU.add)
        p.ts(cw, cw, vs, None, op0=ALU.mult)

    acc = p.sbuf([128, TT * D], F32, "acc")
    accv = acc.rearrange("p (t d) -> p t d", t=TT)
    wu = p.sbuf([128, 8 * DI], BF16, "wu")
    wuv = wu.rearrange("p (k n) -> p k n", k=8)
    wd = p.sbuf([128, 16 * D], BF16, "wd")
    wdv = wd.rearrange("p (k n) -> p k n", k=16)
    hidb = p.sbuf([128, 16 * BLK], BF16, "hidb")
    hv = hidb.rearrange("p (k n) -> p k n", k=16)

    for e in range(9):
        if e < 8:
            p.dma(wuv, wu_h[e].rearrange("(k p) n -> p k n", p=128),
                  eng="gpsimd")
            p.dma(wdv, wd_h[e].rearrange("(k p) n -> p k n", p=128),
                  eng="gpsimd")
        else:
            p.dma(wuv, w1_h.rearrange("(k p) n -> p k n", p=128),
                  eng="gpsimd")
            p.dma(wdv, w2_h.rearrange("(k p) n -> p k n", p=128),
                  eng="gpsimd")
        for nb in range(NBLK):
            tok = slice(nb * BLK, (nb + 1) * BLK)
            for po in range(16):
                ps = p.bankf(po % 2)
                p.mm_acc(ps, [(wuv[:, ki, po * 128:(po + 1) * 128],
                               xnv[:, ki, tok]) for ki in range(8)])
                p.act(hv[:, po, :], ps, AF.Gelu)
            for j in range(4):
                tg = nb * 4 + j
                for fo in range(2):
                    ps = p.bankf(2 + (j * 2 + fo) % 2)
                    fcols = slice(fo * BLK, (fo + 1) * BLK)
                    p.mm_acc(ps, [(hv[:, ki, j * 128:(j + 1) * 128],
                                   wdv[:, ki, fcols]) for ki in range(16)])
                    dst = accv[:, tg, fcols]
                    if e == 0:
                        p.ts(dst, ps, cwall[:, tg * 8:tg * 8 + 1], None,
                             op0=ALU.mult)
                    elif e < 8:
                        p.stt(dst, ps, cwall[:, tg * 8 + e:tg * 8 + e + 1],
                              dst, op0=ALU.mult, op1=ALU.add)
                    else:
                        p.stt(dst, ps, 1.0, dst, op0=ALU.mult, op1=ALU.add)

    xtl = p.sbuf([128, D], F32, "xtl0")
    stl = p.sbuf([128, D], BF16, "stl0")
    x3 = x_h.rearrange("(t p) d -> t p d", p=128)
    sS = s_h.rearrange("(t p) d -> t p d", p=128)
    o3 = o_x.rearrange("(t p) d -> t p d", p=128)
    for t in range(TT):
        at = accv[:, t, :]
        p.dma(xtl, x3[t])
        p.dma(stl, sS[t], eng="gpsimd")
        p.act(at, at, AF.Silu)
        p.tt(at, at, stl, ALU.mult)
        p.tt(at, at, xtl, ALU.add)
        p.dma(o3[t], at)
    return p


def run_l1(inp, x_full):
    ident, slt = _consts()
    xs = [np.ascontiguousarray(x_full[c * NT:(c + 1) * NT]) for c in range(NCORES)]
    im = [dict(x=xs[c], win=inp["L1_attn_W_in"], ident=ident)
          for c in range(NCORES)]
    r1 = _run("l1b1", build_l1b1, im)

    xin_full = np.concatenate([r["xin"] for r in r1], axis=2).reshape(DI, NTOK)
    im = []
    for c in range(NCORES):
        ds = slice(c * DSL, (c + 1) * DSL)
        cw = inp["L1_attn_conv_w"][ds].reshape(2, 128, 4).transpose(1, 0, 2)
        cb = inp["L1_attn_conv_b"][ds].reshape(2, 128).T
        im.append(dict(
            xin=np.ascontiguousarray(xin_full[ds].reshape(2, 128, NTOK)),
            convw=np.ascontiguousarray(cw), convb=np.ascontiguousarray(cb),
            wxp=np.ascontiguousarray(inp["L1_attn_W_xproj"][ds])))
    r2 = _run("l1b2", build_l1b2, im)

    xdp_all = np.stack([r["xdp"] for r in r2])
    sels = np.zeros((16, 128, 128), dtype=BF)
    for gi in range(16):
        for pp in range(128):
            sels[gi, 8 * gi + pp // 16, pp] = 1
    nsel = np.zeros((128, 8), dtype=BF)
    for pp in range(128):
        nsel[pp, pp // 16] = 1
    im = []
    for c in range(NCORES):
        ds = slice(c * DSL, (c + 1) * DSL)
        Asl = inp["L1_attn_A_log"][ds]
        aarr = np.zeros((128, 32), dtype=np.float32)
        for g in range(32):
            for pp in range(128):
                aarr[pp, g] = Asl[(g // 16) * 128 + 8 * (g % 16) + pp // 16,
                                  pp % 16]
        bdt = inp["L1_attn_b_dt"][ds].reshape(2, 128).T
        darr = inp["L1_attn_D"][ds].reshape(2, 128).T
        im.append(dict(
            u=r2[c]["u"], xdp=xdp_all,
            wdt=np.ascontiguousarray(inp["L1_attn_W_dt"][:, ds]),
            bdt=np.ascontiguousarray(bdt), aarr=aarr,
            darr=np.ascontiguousarray(darr), sels=sels, nsel=nsel))
    r3 = _run("l1b3", build_l1b3, im)

    y_full = np.concatenate([r["y"] for r in r3], axis=0)
    du_full = np.concatenate([r["du"].reshape(DSL, NTOK) for r in r3], axis=0)
    im = []
    for c in range(NCORES):
        cols = slice(c * NT, (c + 1) * NT)
        im.append(dict(
            y=np.ascontiguousarray(y_full[:, cols].reshape(16, 128, NT)),
            du=np.ascontiguousarray(du_full[:, cols].reshape(16, 128, NT)),
            z=r1[c]["z"], wout=inp["L1_attn_W_out"], x=xs[c],
            gam=np.ascontiguousarray(inp["L1_attn_gamma"][None, :]),
            bet=np.ascontiguousarray(inp["L1_attn_beta"][None, :])))
    r4 = _run("l1d1", build_l1d1, im)

    im = []
    for c in range(NCORES):
        im.append(dict(
            xnfm=r1[c]["xnfm"], wg=inp["L1_ffn_Wg"], wu=inp["L1_ffn_Wu"],
            wd=inp["L1_ffn_Wd"], w1=inp["L1_ffn_W1"], w2=inp["L1_ffn_W2"],
            s=r4[c]["s"], x=xs[c]))
    r5 = _run("l1d2", build_l1d2, im)
    x2 = np.concatenate([r["xout"] for r in r5], axis=0)
    return x2, r4




def build_l2e1():
    p = Prog("l2e1")
    x = p.dram_in("x", (NT, D))
    wx_h = p.dram_in("wx", (16, D, 64))
    bx_h = p.dram_in("bx", (64, 16))
    ident_h = p.dram_in("ident", (128, 128), BF16)
    o_xnfm = p.dram_out("xnfm", (8, 128, NT), BF16)
    o_h = p.dram_out("h", (16, 64, NT), BF16)
    o_hn = p.dram_out("hn", (16, 64, NT), BF16)

    xnfm = p.sbuf([128, 8 * NT], BF16, "xnfm")
    idsb = p.sbuf([128, 128], BF16, "idsb")
    p.dma(idsb, ident_h)
    xnv = xnfm.rearrange("p (c t) -> p c t", c=8)

    xt = [p.sbuf([128, D], F32, f"xt{i}") for i in range(2)]
    xc = [p.sbuf([128, D], F32, f"xc{i}") for i in range(2)]
    xnbf = [p.sbuf([128, D], BF16, f"xnbf{i}") for i in range(2)]
    xsq = p.sbuf([128, D], F32, "xsq")
    statsall = p.sbuf([128, TT * 8], F32, "stats")
    x3 = x.rearrange("(t p) d -> t p d", p=128)
    for t in range(TT):
        bi = t % 2
        p.dma(xt[bi], x3[t])
        st = statsall[:, t * 8:(t + 1) * 8]
        nm, vr, sd, rstd, vr2 = (st[:, i:i + 1] for i in range(5))
        p.red(nm, xt[bi], ALU.add)
        p.ts(nm, nm, -1.0 / D, None, op0=ALU.mult)
        p.ts(xc[bi], xt[bi], nm, None, op0=ALU.add)
        p.act(xsq[:, 0:512], xc[bi][:, 0:512], AF.Square, accum_out=vr)
        p.act(xsq[:, 512:1024], xc[bi][:, 512:1024], AF.Square, accum_out=vr2)
        p.tt(vr, vr, vr2, ALU.add)
        p.act(sd, vr, AF.Sqrt, bias=p.constf(1e-5), scale=1.0 / D)
        p.recip(rstd, sd)
        p.ts(xnbf[bi], xc[bi], rstd, None, op0=ALU.mult)
        for ch in range(8):
            pb = p.bankb(ch % 2)[:, 0:128]
            p.tp(pb, xnbf[bi][:, ch * 128:(ch + 1) * 128], idsb)
            p.cp(xnv[:, ch, t * 128:(t + 1) * 128], pb, eng="vector")
    p.dma(o_xnfm.rearrange("c p t -> p c t"), xnv)

    wx = p.sbuf([128, 8 * 16 * 64], BF16, "wx")
    wxv = wx.rearrange("p (h k n) -> p h k n", h=16, k=8)
    for hh in range(16):
        p.dma(wxv[:, hh], wx_h[hh].rearrange("(k p) n -> p k n", p=128),
              eng="gpsimd")
    bx = p.sbuf([64, 16], F32, "bx")
    p.dma(bx, bx_h)

    hsb = p.sbuf([64, BLK], F32, "hsb")
    hbf = p.sbuf([64, BLK], BF16, "hbf")
    hsq = p.sbuf([64, BLK], F32, "hsq")
    onesf = p.sbuf([64, 1], F32, "onesf")
    p.memset(onesf, 1.0)
    ones1b = p.sbuf([1, 64], BF16, "ones1b")
    p.memset(ones1b, 1.0)
    nr = p.sbuf([1, BLK], F32, "nr")
    nrb = p.sbuf([1, BLK], BF16, "nrb")
    hnb = p.sbuf([64, BLK], BF16, "hnb")
    for h in range(16):
        for nb in range(NBLK):
            tok = slice(nb * BLK, (nb + 1) * BLK)
            ps = p.bankf(0)[0:64, :]
            p.mm_acc(ps, [(wxv[:, h, ki, :], xnv[:, ki, tok])
                          for ki in range(8)])
            p.ts(hsb, ps, bx[:, h:h + 1], None, op0=ALU.add)
            p.cp(hbf, hsb)
            p.dma(o_h[h][:, tok], hbf)
            p.act(hsq, hsb, AF.Square)
            psn = p.bankf(1)[0:1, :]
            p.op("tensor", lambda ps2=psn, a=onesf, b=hsq:
                 p.nc.tensor.matmul(ps2, a, b, start=True, stop=True),
                 reads=[onesf, hsq], writes=[psn])
            p.act(nr, psn, AF.Sqrt)
            p.recip(nr, nr)
            p.cp(nrb, nr)
            psb = p.bankf(2)[0:64, :]
            p.mm(psb, ones1b, nrb, True, True)
            p.tt(hnb, hbf, psb, ALU.mult)
            p.dma(o_hn[h][:, tok], hnb)
    return p


def build_l2e2():
    p = Prog("l2e2")
    hn_h = p.dram_in("hn", (4, 64, L), BF16)
    h_h = p.dram_in("h", (4, 64, L), BF16)
    ident_h = p.dram_in("ident", (128, 128), BF16)
    o_out = p.dram_out("out", (4, 64, L), BF16)

    idsb = p.sbuf([128, 128], BF16, "idsb")
    p.dma(idsb, ident_h)
    hn = p.sbuf([64, L], BF16, "hn")
    hf = p.sbuf([64, L], BF16, "hf")
    htm = p.sbuf([128, 64 * 65], BF16, "htm")
    hv = htm.rearrange("p (k n) -> p k n", k=64)
    PT = [p.sbuf([128, 1024], BF16, f"pt{i}") for i in range(4)]
    ps1024 = [p.psum([128, 1024], F32, f"ps1024_{i}") for i in range(2)]
    ob = p.sbuf([64, 1024], BF16, "ob")
    rd = p.sbuf([1, 1024], F32, "rd")
    rdb = p.sbuf([1, 1024], BF16, "rdb")
    rdbc = p.sbuf([64, 512], F32, "rdbc")
    ones1b = p.sbuf([1, 64], BF16, "ones1b")
    p.memset(ones1b, 1.0)

    KB = 1024
    for bh in range(4):
        p.dma(hn, hn_h[bh])
        p.dma(hf, h_h[bh])
        p.memset(bass.AP(htm.tensor, htm.offset + 64,
                         [htm.ap[0], [65, 64], [1, 1]]), 1.0)
        for kc in range(64):
            pb = p.bankb(0)[:, 0:64]
            p.tp(pb, hf[:, kc * 128:(kc + 1) * 128], idsb)
            p.cp(hv[:, kc, 0:64], pb, eng="vector")
        for qb in range(L // KB):
            qs = slice(qb * KB, (qb + 1) * KB)
            for kc in range(64):
                ks = slice(kc * 128, (kc + 1) * 128)
                sc = ps1024[kc % 2]
                p.mm(sc[:, 0:512], hn[:, ks], hn[:, qb * KB: qb * KB + 512],
                     True, True)
                p.mm(sc[:, 512:1024], hn[:, ks],
                     hn[:, qb * KB + 512: (qb + 1) * KB], True, True)
                pt = PT[kc % 4]
                p.act(pt, sc, AF.Exp)
                p.mm(p.bankf(0)[0:65, :], hv[:, kc, :], pt[:, 0:512],
                     kc == 0, kc == 63)
                p.mm(p.bankf(1)[0:65, :], hv[:, kc, :], pt[:, 512:1024],
                     kc == 0, kc == 63)
            for half in range(2):
                pb = p.bankf(half)
                p.recip(rd[:, half * 512:(half + 1) * 512], pb[64:65, :])
            p.cp(rdb, rd)
            pbc = p.bankf(2)[0:64, :]
            for half in range(2):
                pb = p.bankf(half)
                p.mm(pbc, ones1b, rdb[:, half * 512:(half + 1) * 512],
                     True, True)
                p.cp(rdbc, pbc, eng="scalar")
                p.tt(ob[:, half * 512:(half + 1) * 512], pb[0:64, :], rdbc,
                     ALU.mult)
            p.dma(o_out[bh][:, qs], ob)
    return p


def build_l2e3():
    p = Prog("l2e3")
    cat_h = p.dram_in("cat", (8, 128, NT), BF16)
    xnfm_h = p.dram_in("xnfm", (8, 128, NT), BF16)
    wp_h = p.dram_in("wp", (D, D))
    bp_h = p.dram_in("bp", (128, 8))
    At_h = p.dram_in("At", (6, 16, 100))
    Bm_h = p.dram_in("Bm", (6, 16, D))
    wgate_h = p.dram_in("wgate", (D, 6))
    wtwT_h = p.dram_in("wtwT", (D, D))
    wtb_h = p.dram_in("wtb", (1, D))
    wdwT_h = p.dram_in("wdwT", (D, D))
    wdb_h = p.dram_in("wdb", (128, 8))
    scal_h = p.dram_in("scal", (1, 1))
    x_h = p.dram_in("x", (NT, D))
    ident_h = p.dram_in("ident", (128, 128), BF16)
    o_x = p.dram_out("xout", (NT, D))

    idsb = p.sbuf([128, 128], BF16, "idsb")
    p.dma(idsb, ident_h)
    cat = p.sbuf([128, 8 * NT], BF16, "cat")
    p.dma(cat.rearrange("p (c t) -> p c t", c=8),
          cat_h.rearrange("c p t -> p c t"))
    catv = cat.rearrange("p (c t) -> p c t", c=8)
    xnfm = p.sbuf([128, 8 * NT], BF16, "xnfm")
    p.dma(xnfm.rearrange("p (c t) -> p c t", c=8),
          xnfm_h.rearrange("c p t -> p c t"))
    xnv = xnfm.rearrange("p (c t) -> p c t", c=8)
    wp = p.wload(wp_h, D, D, "wp")
    bp = p.sbuf([128, 8], F32, "bp")
    p.dma(bp, bp_h)
    wgate = p.wload(wgate_h, D, 6, "wgate")
    wtwT_sb = p.sbuf([128, 8 * D], BF16, "wtwT")
    wtwT = wtwT_sb.rearrange("p (k n) -> p k n", k=8)
    p.dma(wtwT, wtwT_h.rearrange("(k p) n -> p k n", p=128), eng="gpsimd")
    wtb = p.sbuf([1, D], BF16, "wtb")
    p.dma(wtb, wtb_h, eng="gpsimd")
    wdwT = wtwT
    wdb = p.sbuf([128, 8], F32, "wdb")
    p.dma(wdb, wdb_h)
    scal = p.sbuf([1, 1], F32, "scal")
    p.dma(scal, scal_h)
    ones1 = p.sbuf([1, 128], BF16, "ones1")
    p.memset(ones1, 1.0)
    onesf1 = p.sbuf([1, 128], F32, "onesf1")
    p.memset(onesf1, 1.0)

    shib = p.sbuf([1, 1], BF16, "shib")
    slof = p.sbuf([1, 1], F32, "slof")
    slob = p.sbuf([1, 1], BF16, "slob")
    sbc = p.sbuf([128, 1], F32, "sbc")
    p.cp(shib, scal)
    p.tt(slof, scal, shib, ALU.subtract)
    p.cp(slob, slof)
    psc = p.bankf(4)[:, 300:301]
    p.op("tensor", lambda: p.nc.tensor.matmul(psc, ones1, shib, start=True,
                                              stop=False),
         reads=[ones1, shib], writes=[psc])
    p.op("tensor", lambda: p.nc.tensor.matmul(psc, ones1, slob, start=False,
                                              stop=True),
         reads=[ones1, slob], writes=[psc])
    p.cp(sbc, psc)
    wdbs = p.sbuf([128, 8], F32, "wdbs")
    p.ts(wdbs, wdb, sbc[:, 0:1], None, op0=ALU.mult)

    At = p.sbuf([16, 6 * 100], F32, "At")
    p.dma(At.rearrange("p (e l) -> p e l", e=6), At_h.rearrange("e p l -> p e l"))
    Atb = p.sbuf([16, 6 * 100], BF16, "Atb")
    p.cp(Atb, At)
    Atv = Atb.rearrange("p (e l) -> p e l", e=6)
    Bm = p.sbuf([16, 6 * D], F32, "Bm")
    p.dma(Bm.rearrange("p (e d) -> p e d", e=6), Bm_h.rearrange("e p d -> p e d"))
    Bmb = p.sbuf([16, 6 * D], BF16, "Bmb")
    p.cp(Bmb, Bm)
    Bmv = Bmb.rearrange("p (e d) -> p e d", e=6)
    tok_fm = p.sbuf([128, 8 * 600], BF16, "tokfm")
    tv = tok_fm.rearrange("p (c el) -> p c el", c=8)
    for e in range(6):
        for ch in range(8):
            ps = p.bankf(ch % 2)[:, 0:100]
            p.mm(ps, Bmv[:, e, ch * 128:(ch + 1) * 128], Atv[:, e, :],
                 True, True)
            p.cp(tv[:, ch, e * 100:(e + 1) * 100], ps, eng="vector")
    val = p.sbuf([128, 5 * D], BF16, "val")
    vv = val.rearrange("p (c d) -> p c d", c=5)
    for elc in range(5):
        rows = min(128, 600 - elc * 128)
        for fo in range(2):
            ps = p.bankf(2 + fo % 2)[0:rows, :]
            fcols = slice(fo * 512, (fo + 1) * 512)
            pairs = [(tv[:, ki, elc * 128:elc * 128 + rows],
                      wtwT[:, ki, fcols]) for ki in range(8)]
            pairs.append((ones1[:, 0:rows], wtb[:, fcols]))
            p.mm_acc(ps, pairs)
            p.cp(vv[0:rows, elc, fcols], ps, eng="vector")

    p.dma(wdwT, wdwT_h.rearrange("(k p) n -> p k n", p=128), eng="gpsimd")

    iot6m = p.sbuf([128, 6], F32, "iot6m")
    p.iota(iot6m, [[1, 6]], base=-6, channel_multiplier=0)
    gl = p.sbuf([128, 6], F32, "gl")
    cand = p.sbuf([128, 6], F32, "cand")
    mio = p.sbuf([128, 6], F32, "mio")
    oh1 = p.sbuf([128, 6], F32, "oh1")
    v1 = p.sbuf([128, 1], F32, "v1")
    i1 = p.sbuf([128, 1], F32, "i1")
    r1 = p.sbuf([128, 1], F32, "r1")
    r2 = p.sbuf([128, 1], F32, "r2")
    att = p.sbuf([128, 600], F32, "att")
    attb = p.sbuf([128, 4 * 600], BF16, "attb")
    abv = attb.rearrange("p (j el) -> p j el", j=4)
    attT = p.sbuf([128, 5 * 512], BF16, "attT")
    atv = attT.rearrange("p (c j) -> p c j", c=5)
    delin = p.sbuf([128, 8 * BLK], BF16, "delin")
    dv = delin.rearrange("p (c t) -> p c t", c=8)
    expd = p.sbuf([128, BLK], F32, "expd")
    zs = p.sbuf([128, BLK], F32, "zs")
    gpre = p.sbuf([128, 8 * BLK], BF16, "gpre")
    gv = gpre.rearrange("p (c t) -> p c t", c=8)
    sfm = p.sbuf([128, 8 * BLK], BF16, "sfm")
    sv = sfm.rearrange("p (c t) -> p c t", c=8)
    sgb = p.sbuf([128, BLK], BF16, "sgb")
    xtl = p.sbuf([128, D], F32, "xtl")
    EPSF = float(np.finfo(np.float32).eps)

    x3v = x_h.rearrange("(t p) d -> t p d", p=128)
    o3 = o_x.rearrange("(t p) d -> t p d", p=128)
    for nb in range(NBLK):
        tokb = slice(nb * BLK, (nb + 1) * BLK)
        for j in range(4):
            t = nb * 4 + j
            tok = slice(t * 128, (t + 1) * 128)
            psg = p.bankf(0)[:, 0:6]
            p.mm_acc(psg, [(xnv[:, ki, tok], wgate[:, ki, :])
                           for ki in range(8)])
            p.cp(gl, psg)
            p.red(v1, gl, ALU.max)
            p.ts(cand, gl, v1, None, op0=ALU.is_ge)
            p.tt(mio, cand, iot6m, ALU.mult)
            p.ts(mio, mio, 6.0, None, op0=ALU.add)
            p.red(i1, mio, ALU.min)
            p.ts(i1, i1, -6.0, None, op0=ALU.add)
            p.ts(oh1, iot6m, i1, None, op0=ALU.is_equal)
            ps1 = p.bankf(1)
            ps2 = p.bankf(0)[:, 128:216]
            p.mm_acc(ps1, [(xnv[:, ki, tok], tv[:, ki, 0:512])
                           for ki in range(8)])
            p.mm_acc(ps2, [(xnv[:, ki, tok], tv[:, ki, 512:600])
                           for ki in range(8)])
            p.ts(att[:, 0:512], ps1, 1.0 / 32.0, None, op0=ALU.mult)
            p.ts(att[:, 512:600], ps2, 1.0 / 32.0, None, op0=ALU.mult)
            for e in range(6):
                sl = att[:, e * 100:(e + 1) * 100]
                p.red(r1, sl, ALU.max)
                p.ts(r1, r1, -1.0, None, op0=ALU.mult)
                p.act(sl, sl, AF.Exp, bias=r1, accum_out=r2)
                p.recip(r2, r2)
                p.ts(r2, r2, oh1[:, e:e + 1], None, op0=ALU.mult)
                p.ts(abv[:, j, e * 100:(e + 1) * 100], sl, r2, None,
                     op0=ALU.mult)
            p.memset(bass.AP(attb.tensor, attb.offset + j * 600,
                             [attb.ap[0], [100, 6], [1, 1]]), 0.0)
        for j in range(4):
            for elc in range(5):
                rows = min(128, 600 - elc * 128)
                pb = p.bankb(elc % 2)[0:rows, 0:128]
                p.tp(pb, abv[:, j, elc * 128:elc * 128 + rows], idsb)
                p.cp(atv[0:rows, elc, j * 128:(j + 1) * 128], pb,
                     eng="vector")
        for ch in range(8):
            ps = p.bankf(2 + ch % 2)
            pairs = []
            for elc in range(5):
                rows = min(128, 600 - elc * 128)
                pairs.append((vv[0:rows, elc, ch * 128:(ch + 1) * 128],
                              atv[0:rows, elc, :]))
            p.mm_acc(ps, pairs)
            p.tt(dv[:, ch, :], ps, xnv[:, ch, tokb], ALU.add)
        for ch in range(8):
            ps = p.bankf(4 + ch % 2)
            p.mm_acc(ps, [(wp[:, ki, ch * 128:(ch + 1) * 128],
                           catv[:, ki, tokb]) for ki in range(8)])
            p.ts(sv[:, ch, :], ps, bp[:, ch:ch + 1], None, op0=ALU.add)
        for ch in range(8):
            ps = p.bankf(2 + ch % 2)
            p.mm_acc(ps, [(wdwT[:, ki, ch * 128:(ch + 1) * 128],
                           dv[:, ki, :]) for ki in range(8)])
            p.act(expd, ps, AF.Exp, scale=sbc[:, 0:1],
                  bias=wdbs[:, ch:ch + 1])
            p.ts(zs, expd, 0.0, None, op0=ALU.is_equal)
            p.stt(expd, zs, EPSF, expd, op0=ALU.mult, op1=ALU.add)
            p.act(expd, expd, AF.Ln)
            p.tt(gv[:, ch, :], expd, xnv[:, ch, tokb], ALU.add)
        for j in range(4):
            t = nb * 4 + j
            p.dma(xtl, x3v[t])
            for ch in range(8):
                cols = slice(j * 128, (j + 1) * 128)
                p.act(zs[:, 0:128], gv[:, ch, cols], AF.Silu)
                p.tt(sgb[:, cols.start:cols.stop], zs[:, 0:128],
                     sv[:, ch, cols], ALU.mult)
                pb = p.bankb(ch % 2)[:, 0:128]
                p.tp(pb, sgb[:, cols.start:cols.stop], idsb)
                p.tt(xtl[:, ch * 128:(ch + 1) * 128],
                     xtl[:, ch * 128:(ch + 1) * 128], pb, ALU.add)
            p.dma(o3[t], xtl)
    return p


def run_l2(inp, x_full):
    ident, slt = _consts()
    xs = [np.ascontiguousarray(x_full[c * NT:(c + 1) * NT]) for c in range(NCORES)]
    im = [dict(x=xs[c], wx=inp["L2_attn_Wx"],
               bx=np.ascontiguousarray(inp["L2_attn_bx"].T), ident=ident)
          for c in range(NCORES)]
    r1 = _run("l2e1", build_l2e1, im)

    h_full = np.concatenate([r["h"] for r in r1], axis=2)
    hn_full = np.concatenate([r["hn"] for r in r1], axis=2)
    im = []
    for c in range(NCORES):
        b = c // 4
        hs = []
        hns = []
        for k in range(4):
            h = 4 * (c % 4) + k
            hs.append(h_full[h][:, b * L:(b + 1) * L])
            hns.append(hn_full[h][:, b * L:(b + 1) * L])
        im.append(dict(hn=np.ascontiguousarray(np.stack(hns)),
                       h=np.ascontiguousarray(np.stack(hs)), ident=ident))
    r2 = _run("l2e2", build_l2e2, im)

    out_full = np.zeros((D, NTOK), dtype=BF)
    for c in range(NCORES):
        b = c // 4
        for k in range(4):
            h = 4 * (c % 4) + k
            out_full[h * 64:(h + 1) * 64, b * L:(b + 1) * L] = r2[c]["out"][k]
    bp_arr = np.ascontiguousarray(inp["L2_attn_bp"].reshape(8, 128).T)
    wdb_arr = np.ascontiguousarray(inp["L2_ffn_wd_b"].reshape(8, 128).T)
    At = np.ascontiguousarray(inp["L2_ffn_A"].transpose(0, 2, 1))
    im = []
    for c in range(NCORES):
        cols = slice(c * NT, (c + 1) * NT)
        im.append(dict(
            cat=np.ascontiguousarray(out_full[:, cols].reshape(8, 128, NT)),
            xnfm=r1[c]["xnfm"], wp=inp["L2_attn_Wp"], bp=bp_arr,
            At=At, Bm=inp["L2_ffn_B"], wgate=inp["L2_ffn_w_gate"],
            wtwT=np.ascontiguousarray(inp["L2_ffn_wt_w"].T),
            wtb=np.ascontiguousarray(inp["L2_ffn_wt_b"][None, :]),
            wdwT=np.ascontiguousarray(inp["L2_ffn_wd_w"].T),
            wdb=wdb_arr,
            scal=np.ascontiguousarray(inp["L2_ffn_scale"][None, :]),
            x=xs[c], ident=ident))
    r3 = _run("l2e3", build_l2e3, im)
    x3 = np.concatenate([r["xout"] for r in r3], axis=0)
    return x3, r2





DIP = 4240
CONVD = 2176


def build_l3f1():
    p = Prog("l3f1")
    x = p.dram_in("x", (NT, D))
    ipw_h = p.dram_in("ipw", (DIP, D))
    xnw_h = p.dram_in("xnw", (1, D))
    xnb_h = p.dram_in("xnb", (1, D))
    ident_h = p.dram_in("ident", (128, 128), BF16)
    o_xnfm = p.dram_out("xnfm", (8, 128, NT), BF16)
    o_z = p.dram_out("z", (16, 128, NT), BF16)
    o_xbc = p.dram_out("xbc", (17, 128, NT), BF16)
    o_dtr = p.dram_out("dtr", (16, NT))

    idsb = p.sbuf([128, 128], BF16, "idsb")
    p.dma(idsb, ident_h)
    wnT = p.sbuf([128, 8 * DIP], BF16, "wnT")
    wv = wnT.rearrange("p (k n) -> p k n", k=8)
    wrow = p.sbuf([128, D], F32, "wrow")
    wsq = p.sbuf([128, D], F32, "wsq")
    wst = p.sbuf([128, 4], F32, "wst")
    wbf = p.sbuf([128, D], BF16, "wbf")
    NCHW = (DIP + 127) // 128
    for rc in range(NCHW):
        rows = min(128, DIP - rc * 128)
        p.dma(wrow[0:rows, :], ipw_h[rc * 128: rc * 128 + rows, :])
        rn, rn2, dv = wst[:, 0:1], wst[:, 1:2], wst[:, 2:3]
        p.act(wsq[0:rows, 0:512], wrow[0:rows, 0:512], AF.Square,
              accum_out=rn[0:rows])
        p.act(wsq[0:rows, 512:1024], wrow[0:rows, 512:1024], AF.Square,
              accum_out=rn2[0:rows])
        p.tt(rn[0:rows], rn[0:rows], rn2[0:rows], ALU.add)
        p.act(rn[0:rows], rn[0:rows], AF.Sqrt)
        p.ts(rn[0:rows], rn[0:rows], 1e-6, None, op0=ALU.max)
        p.recip(dv[0:rows], rn[0:rows])
        p.ts(wbf[0:rows, :], wrow[0:rows, :], dv[0:rows], 1.0 / 32.0,
             op0=ALU.mult, op1=ALU.mult)
        for ch in range(8):
            pb = p.bankb(ch % 2)[:, 0:rows]
            p.tp(pb, wbf[0:rows, ch * 128:(ch + 1) * 128], idsb)
            p.cp(wv[:, ch, rc * 128: rc * 128 + rows], pb, eng="vector")

    xnw = p.sbuf([1, D], BF16, "xnw")
    xnb = p.sbuf([1, D], BF16, "xnb")
    p.dma(xnw, xnw_h, eng="gpsimd")
    p.dma(xnb, xnb_h, eng="gpsimd")
    ones1 = p.sbuf([1, 128], BF16, "ones1")
    p.memset(ones1, 1.0)
    wb_ = p.sbuf([128, D], F32, "wb_")
    bb_ = p.sbuf([128, D], F32, "bb_")
    for fo in range(2):
        ps = p.bankf(0)
        p.mm(ps, ones1, xnw[:, fo * 512:(fo + 1) * 512], True, True)
        p.cp(wb_[:, fo * 512:(fo + 1) * 512], ps, eng="vector")
        p.mm(ps, ones1, xnb[:, fo * 512:(fo + 1) * 512], True, True)
        p.cp(bb_[:, fo * 512:(fo + 1) * 512], ps, eng="vector")

    xnfm = p.sbuf([128, 8 * NT], BF16, "xnfm")
    xnv = xnfm.rearrange("p (c t) -> p c t", c=8)
    ufm = p.sbuf([128, 8 * NT], BF16, "ufm")
    uv = ufm.rearrange("p (c t) -> p c t", c=8)
    xt = [p.sbuf([128, D], F32, f"xt{i}") for i in range(2)]
    xc = [p.sbuf([128, D], F32, f"xc{i}") for i in range(2)]
    ubf = [p.sbuf([128, D], BF16, f"ubf{i}") for i in range(2)]
    xnbf = [p.sbuf([128, D], BF16, f"xnbf{i}") for i in range(2)]
    xsq = p.sbuf([128, D], F32, "xsq")
    statsall = p.sbuf([128, TT * 8], F32, "stats")
    x3 = x.rearrange("(t p) d -> t p d", p=128)

    def ln(src, dst, st):
        nm, vr, sd, rstd, vr2 = (st[:, i:i + 1] for i in range(5))
        p.red(nm, src, ALU.add)
        p.ts(nm, nm, -1.0 / D, None, op0=ALU.mult)
        p.ts(dst, src, nm, None, op0=ALU.add)
        p.act(xsq[:, 0:512], dst[:, 0:512], AF.Square, accum_out=vr)
        p.act(xsq[:, 512:1024], dst[:, 512:1024], AF.Square, accum_out=vr2)
        p.tt(vr, vr, vr2, ALU.add)
        p.act(sd, vr, AF.Sqrt, bias=p.constf(1e-5), scale=1.0 / D)
        p.recip(rstd, sd)
        p.ts(dst, dst, rstd, None, op0=ALU.mult)

    for t in range(TT):
        bi = t % 2
        p.dma(xt[bi], x3[t])
        st = statsall[:, t * 8:(t + 1) * 8]
        ln(xt[bi], xc[bi], st)
        p.cp(xnbf[bi], xc[bi])
        ln(xc[bi], xt[bi], st)
        p.tt(xt[bi], xt[bi], wb_, ALU.mult)
        p.tt(xt[bi], xt[bi], bb_, ALU.add)
        p.cp(ubf[bi], xt[bi])
        for ch in range(8):
            pb = p.bankb(ch % 2)[:, 0:128]
            p.tp(pb, xnbf[bi][:, ch * 128:(ch + 1) * 128], idsb)
            p.cp(xnv[:, ch, t * 128:(t + 1) * 128], pb, eng="vector")
            pb2 = p.bankb(ch % 2)[:, 128:256]
            p.tp(pb2, ubf[bi][:, ch * 128:(ch + 1) * 128], idsb)
            p.cp(uv[:, ch, t * 128:(t + 1) * 128], pb2, eng="scalar")
    p.dma(o_xnfm.rearrange("c p t -> p c t"), xnv)

    ob = [p.sbuf([128, BLK], BF16, f"ob{i}") for i in range(4)]
    dtrb = p.sbuf([16, NT], F32, "dtrb")
    for nb in range(NBLK):
        tok = slice(nb * BLK, (nb + 1) * BLK)
        for po in range(34):
            rows = min(128, DIP - po * 128)
            ps = p.bankf(po % 4)[0:rows, :]
            p.mm_acc(ps, [(wv[:, ki, po * 128:po * 128 + rows],
                           uv[:, ki, tok]) for ki in range(8)])
            if po < 16:
                o = ob[po % 4]
                p.cp(o, ps, eng="scalar" if po % 2 else "vector")
                p.dma(o_z[po][:, tok], o)
            elif po < 33:
                o = ob[po % 4]
                p.cp(o, ps, eng="scalar" if po % 2 else "vector")
                p.dma(o_xbc[po - 16][:, tok], o)
            else:
                p.cp(dtrb[:, tok], ps, eng="vector")
    p.dma(o_dtr, dtrb)
    return p


def build_l3f2():
    p = Prog("l3f2")
    xh_h = p.dram_in("xh", (4, 128, L), BF16)
    bc_h = p.dram_in("bc", (128, L), BF16)
    cwx_h = p.dram_in("cwx", (4, 128, 4))
    cbx_h = p.dram_in("cbx", (4, 128, 1))
    cwbc_h = p.dram_in("cwbc", (128, 4))
    cbbc_h = p.dram_in("cbbc", (128, 1))
    dtr_h = p.dram_in("dtr", (4, L))
    dtb_h = p.dram_in("dtb", (4, 32, 1))
    alg_h = p.dram_in("alg", (4, 32, 1))
    drep_h = p.dram_in("drep", (4, 128, 1))
    z_h = p.dram_in("z", (4, 128, L), BF16)
    maskt_h = p.dram_in("maskt", (2, 128, 256))
    ident_h = p.dram_in("ident", (128, 128), BF16)
    identf_h = p.dram_in("identf", (128, 128))
    o_g = p.dram_out("g", (4, 128, L), BF16)

    idsb = p.sbuf([128, 128], BF16, "idsb")
    p.dma(idsb, ident_h)
    idf = p.sbuf([128, 128], F32, "idf")
    p.dma(idf, identf_h)
    masks = p.sbuf([128, 2 * 256], F32, "masks")
    p.dma(masks.rearrange("p (m t) -> p m t", m=2),
          maskt_h.rearrange("m p t -> p m t"))
    mk = masks.rearrange("p (m t) -> p m t", m=2)
    ones1 = p.sbuf([1, 128], BF16, "ones1")
    p.memset(ones1, 1.0)
    ones64 = p.sbuf([1, 64], BF16, "ones64")
    p.memset(ones64, 1.0)

    bcin = p.sbuf([128, L], BF16, "bcin")
    p.dma(bcin, bc_h)
    cwbc = p.sbuf([128, 4], F32, "cwbc")
    p.dma(cwbc, cwbc_h)
    cbbc = p.sbuf([128, 1], F32, "cbbc")
    p.dma(cbbc, cbbc_h)
    acc = p.sbuf([128, L], F32, "acc")
    p.ts(acc, bcin, cwbc[:, 3:4], None, op0=ALU.mult)
    for j in (2, 1, 0):
        sh = 3 - j
        p.stt(acc[:, sh:L], bcin[:, 0:L - sh], cwbc[:, j:j + 1],
              acc[:, sh:L], op0=ALU.mult, op1=ALU.add)
    bcf = p.sbuf([128, L], BF16, "bcf")
    p.act(bcf, acc, AF.Silu, bias=cbbc[:, 0:1])
    cfm = p.sbuf([64, L], BF16, "cfm")
    p.cp(cfm, bcf[64:128, :])

    xin = p.sbuf([128, L], BF16, "xin")
    xfm = p.sbuf([128, L], BF16, "xfm")
    zfm = p.sbuf([128, L], BF16, "zfm")
    cwx = p.sbuf([128, 4], F32, "cwx")
    cbx = p.sbuf([128, 1], F32, "cbx")
    dts = p.sbuf([32, 256], F32, "dts")
    sp1 = p.sbuf([32, 256], F32, "sp1")
    sp2 = p.sbuf([32, 256], F32, "sp2")
    dtn = p.sbuf([32, 256], F32, "dtn")
    dA = p.sbuf([32, 256], F32, "dA")
    cum = p.sbuf([32, 256], F32, "cum")
    ncum = p.sbuf([32, 256], F32, "ncum")
    dte = p.sbuf([32, 256], F32, "dte")
    wgt = p.sbuf([32, 256], F32, "wgt")
    cdec = p.sbuf([32, 1], F32, "cdec")
    zer = p.sbuf([32, 256], F32, "zer")
    p.memset(zer, 0.0)
    dtb = p.sbuf([32, 1], F32, "dtb")
    alg = p.sbuf([32, 1], F32, "alg")
    Acol = p.sbuf([32, 1], F32, "Acol")
    drep = p.sbuf([128, 1], F32, "drep")
    dtflat = p.sbuf([1, 256], F32, "dtflat")
    dtflatb = p.sbuf([1, 256], BF16, "dtflatb")
    cumflat = p.sbuf([1, 256], F32, "cumflat")
    cumhib = p.sbuf([1, 256], BF16, "cumhib")
    cumlo = p.sbuf([1, 256], F32, "cumlo")
    cumlob = p.sbuf([1, 256], BF16, "cumlob")
    ncumT = p.sbuf([128, 2 * 32], F32, "ncumT")
    dtT = p.sbuf([128, 2 * 32], F32, "dtT")
    wT = p.sbuf([128, 2 * 32], F32, "wT")
    cdecb = p.sbuf([128, 32], F32, "cdecb")
    cdrow = p.sbuf([1, 32], F32, "cdrow")
    cdrowb = p.sbuf([1, 32], BF16, "cdrowb")
    cdrlo = p.sbuf([1, 32], F32, "cdrlo")
    cdrlob = p.sbuf([1, 32], BF16, "cdrlob")
    Bd = p.sbuf([64, 256], BF16, "Bd")
    Ce = p.sbuf([64, 256], BF16, "Ce")
    xctm = p.sbuf([128, 2 * 128], BF16, "xctm")
    btmw = p.sbuf([128, 2 * 64], BF16, "btmw")
    dec = p.sbuf([128, 256], F32, "dec")
    attb = p.sbuf([128, 256], BF16, "attb")
    hst = [p.sbuf([128, 64], F32, f"hst{i}") for i in range(2)]
    hsb = p.sbuf([128, 64], BF16, "hsb")
    hTb = p.sbuf([64, 128], BF16, "hTb")
    ysb = p.sbuf([128, 256], F32, "ysb")
    zsil = p.sbuf([128, 256], F32, "zsil")
    gsb = p.sbuf([128, 256], BF16, "gsb")

    CH = 256
    for bh in range(4):
        p.dma(xin, xh_h[bh])
        p.dma(zfm, z_h[bh])
        p.dma(cwx, cwx_h[bh])
        p.dma(cbx[:, 0:1], cbx_h[bh])
        p.dma(dtb, dtb_h[bh])
        p.dma(alg, alg_h[bh])
        p.dma(drep, drep_h[bh])
        p.ts(acc, xin, cwx[:, 3:4], None, op0=ALU.mult)
        for j in (2, 1, 0):
            sh = 3 - j
            p.stt(acc[:, sh:L], xin[:, 0:L - sh], cwx[:, j:j + 1],
                  acc[:, sh:L], op0=ALU.mult, op1=ALU.add)
        p.act(xfm, acc, AF.Silu, bias=cbx[:, 0:1])
        p.act(Acol, alg, AF.Exp)
        p.ts(Acol, Acol, -1.0, None, op0=ALU.mult)
        p.dma(dts, dtr_h[bh].rearrange("(c t) -> c t", t=256))
        p.act(sp1, dts, AF.Abs, bias=dtb[:, 0:1])
        p.act(sp1, sp1, AF.Exp, scale=-1.0)
        p.act(sp1, sp1, AF.Ln, bias=p.constf(1.0))
        p.act(sp2, dts, AF.Relu, bias=dtb[:, 0:1])
        p.tt(dtn, sp1, sp2, ALU.add)
        p.ts(dA, dtn, Acol[:, 0:1], None, op0=ALU.mult)
        p.scan(cum, zer, dA, 0.0, op0=ALU.add, op1=ALU.add)
        p.ts(ncum, cum, -1.0, None, op0=ALU.mult)
        p.act(dte, ncum, AF.Exp, bias=cum[:, 255:256])
        p.tt(wgt, dte, dtn, ALU.mult)
        p.act(cdec, cum[:, 255:256], AF.Exp)
        for half in range(2):
            sl = slice(half * 128, (half + 1) * 128)
            pb = p.bankf(0)[:, 0:32]
            p.tp(pb, ncum[:, sl], idf)
            p.cp(ncumT[:, half * 32:(half + 1) * 32], pb, eng="vector")
            p.tp(pb, dtn[:, sl], idf)
            p.cp(dtT[:, half * 32:(half + 1) * 32], pb, eng="vector")
            p.tp(pb, wgt[:, sl], idf)
            p.cp(wT[:, half * 32:(half + 1) * 32], pb, eng="vector")
        pbc = p.bankf(0)[0:1, 0:32]
        p.tp(pbc, cdec, idf)
        p.cp(cdrow, pbc, eng="vector")
        p.cp(cdrowb, cdrow)
        p.tt(cdrlo, cdrow, cdrowb, ALU.subtract)
        p.cp(cdrlob, cdrlo)
        pcb = p.bankf(0)[:, 32:64]
        p.mm(pcb, ones1, cdrowb, True, False)
        p.mm(pcb, ones1, cdrlob, False, True)
        p.cp(cdecb, pcb, eng="vector")
        p.memset(hst[0], 0.0)

        for c in range(32):
            cols = slice(c * CH, (c + 1) * CH)
            p.dma(dtflat, dtn[c:c + 1, :], eng="gpsimd")
            p.cp(dtflatb, dtflat)
            p.dma(cumflat, cum[c:c + 1, :], eng="gpsimd")
            p.cp(cumhib, cumflat)
            p.tt(cumlo, cumflat, cumhib, ALU.subtract)
            p.cp(cumlob, cumlo)
            pb1 = p.bankf(1)[0:64, 0:256]
            p.mm(pb1, ones64, dtflatb, True, True)
            p.tt(Bd, bcf[0:64, cols], pb1, ALU.mult)
            pb2 = p.bankf(1)[0:64, 256:512]
            p.mm(pb2, ones64, cumhib, True, False)
            p.mm(pb2, ones64, cumlob, False, True)
            p.act(Ce, pb2, AF.Exp)
            p.tt(Ce, Ce, cfm[:, cols], ALU.mult)
            for half in range(2):
                sl = slice(c * CH + half * 128, c * CH + (half + 1) * 128)
                pb = p.bankb(0)[:, 0:128]
                p.tp(pb, xfm[:, sl], idsb)
                p.cp(xctm[:, half * 128:(half + 1) * 128], pb, eng="vector")
                pb3 = p.bankb(1)[:, 0:64]
                p.tp(pb3, bcf[0:64, sl], idsb)
                p.cp(btmw[:, half * 64:(half + 1) * 64], pb3, eng="vector")
                p.ts(btmw[:, half * 64:(half + 1) * 64],
                     btmw[:, half * 64:(half + 1) * 64],
                     wT[:, half * 32 + c:half * 32 + c + 1], None,
                     op0=ALU.mult)
            py = p.bankf(3)[:, 0:256]
            pst = p.bankf(4)[:, 0:64]
            for jt in range(2):
                jl = slice(jt * 128, (jt + 1) * 128)
                psc2 = p.bankf(5)[:, 0:256]
                p.mm(psc2, Bd[:, jl], cfm[:, cols], True, True)
                pcum = p.bankf(2)[:, 0:256]
                p.mm(pcum, ones1, cumhib, True, False)
                p.mm(pcum, ones1, cumlob, False, True)
                p.tt(pcum, pcum, mk[:, jt, :], ALU.add)
                p.act(dec, pcum, AF.Exp,
                      bias=ncumT[:, jt * 32 + c:jt * 32 + c + 1])
                p.tt(dec, dec, psc2, ALU.mult)
                p.ts(attb, dec, dtT[:, jt * 32 + c:jt * 32 + c + 1], None,
                     op0=ALU.mult)
                p.mm(py, xctm[:, jl], attb, jt == 0, False)
                p.mm(pst, xctm[:, jl], btmw[:, jt * 64:(jt + 1) * 64],
                     jt == 0, jt == 1)
            h0 = hst[c % 2]
            h1 = hst[(c + 1) % 2]
            p.cp(hsb, h0)
            pbh = p.bankb(0)[0:64, 0:128]
            p.tp(pbh, hsb, idsb)
            p.cp(hTb, pbh, eng="vector")
            p.mm(py, hTb, Ce, False, True)
            p.stt(h1, h0, cdecb[:, c:c + 1], pst, op0=ALU.mult, op1=ALU.add)
            p.stt(ysb, xfm[:, cols], drep[:, 0:1], py, op0=ALU.mult,
                  op1=ALU.add)
            p.act(zsil, zfm[:, cols], AF.Silu)
            p.tt(gsb, ysb, zsil, ALU.mult)
            p.dma(o_g[bh][:, cols], gsb)
    return p


def build_l3f3():
    p = Prog("l3f3")
    g_h = p.dram_in("g", (16, 128, NT), BF16)
    rmsw_h = p.dram_in("rmsw", (128, 16))
    opw_h = p.dram_in("opw", (D, DI))
    ident_h = p.dram_in("ident", (128, 128), BF16)
    o_s = p.dram_out("s", (NT, D), BF16)

    idsb = p.sbuf([128, 128], BF16, "idsb")
    p.dma(idsb, ident_h)
    g = p.sbuf([128, 16 * NT], BF16, "g")
    gv = g.rearrange("p (c t) -> p c t", c=16)
    p.dma(gv, g_h.rearrange("c p t -> p c t"))
    rmsw = p.sbuf([128, 16], F32, "rmsw")
    p.dma(rmsw, rmsw_h)

    wnT = p.sbuf([128, 16 * D], BF16, "wnT")
    wv = wnT.rearrange("p (k n) -> p k n", k=16)
    wrow = p.sbuf([128, DI], F32, "wrow")
    wsq = p.sbuf([128, DI], F32, "wsq")
    wst = p.sbuf([128, 4], F32, "wst")
    wbf = p.sbuf([128, DI], BF16, "wbf")
    for rc in range(8):
        p.dma(wrow, opw_h[rc * 128:(rc + 1) * 128, :])
        rn, rn2, dv_ = wst[:, 0:1], wst[:, 1:2], wst[:, 2:3]
        for q4 in range(4):
            tgt = rn if q4 == 0 else rn2
            p.act(wsq[:, q4 * 512:(q4 + 1) * 512],
                  wrow[:, q4 * 512:(q4 + 1) * 512], AF.Square,
                  accum_out=tgt)
            if q4 > 0:
                p.tt(rn, rn, rn2, ALU.add)
        p.act(rn, rn, AF.Sqrt)
        p.ts(rn, rn, 1e-6, None, op0=ALU.max)
        p.recip(dv_, rn)
        p.ts(wbf, wrow, dv_, None, op0=ALU.mult)
        for ch in range(16):
            pb = p.bankb(ch % 2)[:, 0:128]
            p.tp(pb, wbf[:, ch * 128:(ch + 1) * 128], idsb)
            p.cp(wv[:, ch, rc * 128:(rc + 1) * 128], pb, eng="vector")

    gsq = p.sbuf([128, BLK], BF16, "gsq")
    onesb = p.sbuf([128, 1], BF16, "onesb")
    p.memset(onesb, 1.0)
    rr = p.sbuf([1, BLK], F32, "rr")
    rrb = p.sbuf([1, BLK], BF16, "rrb")
    ghat = p.sbuf([128, 16 * BLK], BF16, "ghat")
    gh = ghat.rearrange("p (c t) -> p c t", c=16)
    rstdb = p.sbuf([128, BLK], F32, "rstdb")
    sbf = p.sbuf([128, D], BF16, "sbf")
    ones1 = p.sbuf([1, 128], BF16, "ones1")
    p.memset(ones1, 1.0)
    s3 = o_s.rearrange("(t p) d -> t p d", p=128)
    for nb in range(NBLK):
        tok = slice(nb * BLK, (nb + 1) * BLK)
        psn = p.bankf(0)[0:1, :]
        for ch in range(16):
            p.act(gsq, gv[:, ch, tok], AF.Square)
            p.op("tensor", lambda a=onesb, b=gsq, ps=psn, st=(ch == 0),
                 sp=(ch == 15): p.nc.tensor.matmul(ps, a, b, start=st,
                                                   stop=sp),
                 reads=[onesb, gsq], writes=[psn])
        p.act(rr, psn, AF.Sqrt, scale=1.0 / DI, bias=p.constf(1e-5))
        p.recip(rr, rr)
        p.cp(rrb, rr)
        prb = p.bankf(1)
        p.mm(prb, ones1, rrb, True, True)
        p.cp(rstdb, prb, eng="scalar")
        for ch in range(16):
            p.tt(gh[:, ch, :], gv[:, ch, tok], rstdb, ALU.mult)
            p.ts(gh[:, ch, :], gh[:, ch, :], rmsw[:, ch:ch + 1], None,
                 op0=ALU.mult)
        for j in range(4):
            t = nb * 4 + j
            for fo in range(2):
                ps = p.bankf(2 + fo)
                fcols = slice(fo * 512, (fo + 1) * 512)
                p.mm_acc(ps, [(gh[:, ki, j * 128:(j + 1) * 128],
                               wv[:, ki, fcols]) for ki in range(16)])
                p.cp(sbf[:, fcols], ps, eng="vector")
            p.dma(s3[t], sbf)
    return p


def build_l3g1():
    p = Prog("l3g1")
    xnfm_h = p.dram_in("xnfm", (8, 128, NT), BF16)
    wln1_h = p.dram_in("wln1", (128, 8))
    wq_h = p.dram_in("wq", (D, D))
    wkva_h = p.dram_in("wkva", (D, 160))
    wkvb_h = p.dram_in("wkvb", (128, 512))
    cs_h = p.dram_in("cs", (NT, 32))
    ident_h = p.dram_in("ident", (128, 128), BF16)
    o_q = p.dram_out("q", (8, 128, NT), BF16)
    o_kv = p.dram_out("kv", (4, 128, NT), BF16)
    o_kr = p.dram_out("kr", (32, NT), BF16)

    idsb = p.sbuf([128, 128], BF16, "idsb")
    p.dma(idsb, ident_h)
    xnfm = p.sbuf([128, 8 * NT], BF16, "xnfm")
    p.dma(xnfm.rearrange("p (c t) -> p c t", c=8),
          xnfm_h.rearrange("c p t -> p c t"))
    xnv = xnfm.rearrange("p (c t) -> p c t", c=8)
    wln1 = p.sbuf([128, 8], F32, "wln1")
    p.dma(wln1, wln1_h)
    wq = p.wload(wq_h, D, D, "wq")
    wkva = p.wload(wkva_h, D, 160, "wkva")
    wkvb = p.wload(wkvb_h, 128, 512, "wkvb")
    cs = p.sbuf([128, TT * 32], F32, "cs")
    p.dma(cs.rearrange("p (t k) -> p t k", t=TT),
          cs_h.rearrange("(t p) k -> p t k", p=128))
    csv = cs.rearrange("p (t k) -> p t k", t=TT)

    xn1 = p.sbuf([128, 8 * NT], BF16, "xn1")
    x1v = xn1.rearrange("p (c t) -> p c t", c=8)
    xsq = p.sbuf([128, BLK], BF16, "xsq")
    onesb = p.sbuf([128, 1], BF16, "onesb")
    p.memset(onesb, 1.0)
    ones1 = p.sbuf([1, 128], BF16, "ones1")
    p.memset(ones1, 1.0)
    rr = p.sbuf([1, BLK], F32, "rr")
    rrb = p.sbuf([1, BLK], BF16, "rrb")
    rstdb = p.sbuf([128, BLK], F32, "rstdb")
    for nb in range(NBLK):
        tok = slice(nb * BLK, (nb + 1) * BLK)
        psn = p.bankf(0)[0:1, :]
        for ch in range(8):
            p.act(xsq, xnv[:, ch, tok], AF.Square)
            p.op("tensor", lambda a=onesb, b=xsq, ps=psn, st=(ch == 0),
                 sp=(ch == 7): p.nc.tensor.matmul(ps, a, b, start=st, stop=sp),
                 reads=[onesb, xsq], writes=[psn])
        p.act(rr, psn, AF.Sqrt, scale=1.0 / D, bias=p.constf(1e-5))
        p.recip(rr, rr)
        p.cp(rrb, rr)
        prb = p.bankf(1)
        p.mm(prb, ones1, rrb, True, True)
        p.cp(rstdb, prb, eng="scalar")
        for ch in range(8):
            p.tt(x1v[:, ch, tok], xnv[:, ch, tok], rstdb, ALU.mult)
            p.ts(x1v[:, ch, tok], x1v[:, ch, tok], wln1[:, ch:ch + 1], None,
                 op0=ALU.mult)

    qfm = p.sbuf([128, 8 * NT], BF16, "qfm")
    qv = qfm.rearrange("p (c t) -> p c t", c=8)
    rtm = p.sbuf([128, 64], BF16, "rtm")
    rot = p.sbuf([128, 32], BF16, "rot")
    for nb in range(NBLK):
        tok = slice(nb * BLK, (nb + 1) * BLK)
        for po in range(8):
            ps = p.bankf(po % 2)
            p.mm_acc(ps, [(wq[:, ki, po * 128:(po + 1) * 128],
                           xnv2 := x1v[:, ki, tok]) for ki in range(8)])
            p.smul(qv[:, po, tok], ps, 0.125)
    tmpf = p.sbuf([128, 16], F32, "tmpf")
    for h in range(16):
        ch = h // 2
        rw = (h % 2) * 64
        for t in range(TT):
            tok = slice(t * 128, (t + 1) * 128)
            pb = p.bankb(0)[:, 0:64]
            p.tp(pb, qv[rw:rw + 64, ch, tok], idsb)
            p.cp(rtm, pb, eng="vector")
            xe = rtm[:, 32:64:2]
            xo = rtm[:, 33:64:2]
            co = csv[:, t, 0:16]
            si = csv[:, t, 16:32]
            re_ = rot[:, 0:32:2]
            ro_ = rot[:, 1:32:2]
            p.tt(re_, xe, co, ALU.mult)
            p.tt(ro_, xo, co, ALU.mult)
            p.tt(tmpf, xo, si, ALU.mult)
            p.tt(re_, re_, tmpf, ALU.subtract)
            p.tt(tmpf, xe, si, ALU.mult)
            p.tt(ro_, ro_, tmpf, ALU.add)
            p.cp(rtm[:, 32:64], rot)
            pb2 = p.bankb(1)[0:64, 0:128]
            p.tp(pb2, rtm, idsb)
            p.cp(qv[rw:rw + 64, ch, tok], pb2, eng="vector")
    p.dma(o_q.rearrange("c p t -> p c t"), qv)

    ckv = p.sbuf([128, 2 * NT], BF16, "ckv")
    cv = ckv.rearrange("p (c t) -> p c t", c=2)
    for nb in range(NBLK):
        tok = slice(nb * BLK, (nb + 1) * BLK)
        ps = p.bankf(0)
        p.mm_acc(ps, [(wkva[:, ki, 0:128], x1v[:, ki, tok])
                      for ki in range(8)])
        p.cp(cv[:, 0, tok], ps, eng="vector")
        ps2 = p.bankf(1)[0:32, :]
        p.mm_acc(ps2, [(wkva[:, ki, 128:160], x1v[:, ki, tok])
                       for ki in range(8)])
        p.cp(cv[0:32, 1, tok], ps2, eng="vector")
        for po in range(4):
            ps3 = p.bankf(2 + po % 2)
            p.mm(ps3, wkvb[:, 0, po * 128:(po + 1) * 128], cv[:, 0, tok],
                 True, True)
            p.op("vector", lambda o=None, ps_=ps3, dst=o_kv, poo=po, tk=tok:
                 None) if False else None
            ob = p.sbuf([128, BLK], BF16)
            p.cp(ob, ps3, eng="scalar")
            p.dma(o_kv[po][:, tok], ob)
    for t in range(TT):
        tok = slice(t * 128, (t + 1) * 128)
        pb = p.bankb(0)[:, 0:32]
        p.tp(pb, cv[0:32, 1, tok], idsb)
        p.cp(rtm[:, 0:32], pb, eng="vector")
        xe = rtm[:, 0:32:2]
        xo = rtm[:, 1:32:2]
        co = csv[:, t, 0:16]
        si = csv[:, t, 16:32]
        re_ = rot[:, 0:32:2]
        ro_ = rot[:, 1:32:2]
        p.tt(re_, xe, co, ALU.mult)
        p.tt(ro_, xo, co, ALU.mult)
        p.tt(tmpf, xo, si, ALU.mult)
        p.tt(re_, re_, tmpf, ALU.subtract)
        p.tt(tmpf, xe, si, ALU.mult)
        p.tt(ro_, ro_, tmpf, ALU.add)
        pb2 = p.bankb(1)[0:32, 0:128]
        p.tp(pb2, rot, idsb)
        krb = p.sbuf([32, 128], BF16)
        p.cp(krb, pb2, eng="vector")
        p.dma(o_kr[:, tok], krb)
    return p


def build_l3g2():
    p = Prog("l3g2")
    q_h = p.dram_in("q", (4, 64, L), BF16)
    k_h = p.dram_in("k", (4, 64, L), BF16)
    v_h = p.dram_in("v", (4, 32, L), BF16)
    masks_h = p.dram_in("masks", (4, 128, 512))
    maskkm_h = p.dram_in("maskkm", (128, 128))
    ident_h = p.dram_in("ident", (128, 128), BF16)
    identf_h = p.dram_in("identf", (128, 128))
    o_out = p.dram_out("out", (4, 32, L), BF16)

    idsb = p.sbuf([128, 128], BF16, "idsb")
    p.dma(idsb, ident_h)
    idf = p.sbuf([128, 128], F32, "idf")
    p.dma(idf, identf_h)
    masks = p.sbuf([128, 4 * 512], F32, "masks")
    p.dma(masks.rearrange("p (m t) -> p m t", m=4),
          masks_h.rearrange("m p t -> p m t"))
    mk = masks.rearrange("p (m t) -> p m t", m=4)
    mkm = p.sbuf([128, 128], F32, "mkm")
    p.dma(mkm, maskkm_h)

    qa = p.sbuf([65, L], BF16, "qa")
    ka = p.sbuf([65, L], BF16, "ka")
    va = p.sbuf([128, 64 * 33], BF16, "va")
    vv = va.rearrange("p (c n) -> p c n", c=64)
    vb = p.sbuf([32, L], BF16, "vb")
    NR = 4
    PT = [p.sbuf([128, 512], BF16, f"PT{i}") for i in range(NR)]
    mcol = [p.sbuf([128, 16], F32, f"mcol{i}") for i in range(2)]
    m_ = [p.sbuf([128, 1], F32, f"m{i}") for i in range(2)]
    mneg = [p.sbuf([128, 1], BF16, f"mn{i}") for i in range(2)]
    rl = p.sbuf([1, 512], F32, "rl")
    rlb = p.sbuf([1, 512], BF16, "rlb")
    rdbc = p.sbuf([32, 512], F32, "rdbc")
    ob = p.sbuf([32, 512], BF16, "ob")
    ones1 = p.sbuf([1, 32], BF16, "ones1")
    p.memset(ones1, 1.0)

    for bh in range(4):
        p.dma(qa[0:64, :], q_h[bh])
        p.dma(ka[0:64, :], k_h[bh])
        p.memset(ka[64:65, :], 1.0)
        p.dma(vb, v_h[bh])
        p.memset(bass.AP(va.tensor, va.offset + 32,
                         [va.ap[0], [33, 64], [1, 1]]), 1.0)
        for kc in range(64):
            pb = p.bankb(kc % 2)[:, 0:32]
            p.tp(pb, vb[:, kc * 128:(kc + 1) * 128], idsb)
            p.cp(vv[:, kc, 0:32], pb, eng="vector")
        for qb in range(16):
            for j in range(4):
                qt = qb * 4 + j
                qi = qt % 2
                mc = mcol[qi]
                nkb = qt // 4 + 1
                for kb in range(nkb):
                    pS = p.bankf(kb % 2)
                    p.mm(pS, qa[0:64, qt * 128:(qt + 1) * 128],
                         ka[0:64, kb * 512:(kb + 1) * 512], True, True)
                    if kb == qt // 4:
                        p.tt(pS, pS, mk[:, qt % 4, :], ALU.add)
                    p.red(mc[:, kb:kb + 1], pS, ALU.max)
                p.red(m_[qi], mc[:, 0:nkb], ALU.max)
                p.ts(m_[qi], m_[qi], -1.0, None, op0=ALU.mult)
                p.cp(mneg[qi], m_[qi])
                pmr = p.bankb(0)[0:1, 128:256]
                p.tp(pmr, mneg[qi], idsb)
                p.cp(qa[64:65, qt * 128:(qt + 1) * 128], pmr, eng="vector")
            qs0 = qb * 512
            pacc = p.bankf(2 + qb % 2)[0:33, :]
            nkc = 4 * qb + 4
            first = True
            for kc in range(nkc):
                ri = kc % NR
                if kc < 4 * qb:
                    cols = slice(qs0, qs0 + 512)
                    width = 512
                    off = 0
                else:
                    j = kc - 4 * qb
                    off = j * 128
                    cols = slice(qs0 + off, qs0 + 512)
                    width = 512 - off
                pS = p.bankf(4 + kc % 2)[:, 0:width]
                p.mm(pS, ka[:, kc * 128:(kc + 1) * 128], qa[:, cols],
                     True, True)
                if kc >= 4 * qb:
                    p.tt(pS[:, 0:128], pS[:, 0:128], mkm, ALU.add)
                p.act(PT[ri][:, 0:width], pS, AF.Exp)
                p.mm(pacc[:, off:512], vv[:, kc, :], PT[ri][:, 0:width],
                     first, kc == nkc - 1)
                first = False
            p.recip(rl, pacc[32:33, :])
            p.cp(rlb, rl)
            prb = p.bankf(4)[0:32, :]
            p.mm(prb, ones1, rlb, True, True)
            p.cp(rdbc, prb, eng="scalar")
            p.tt(ob, pacc[0:32, :], rdbc, ALU.mult)
            p.dma(o_out[bh][:, qs0:qs0 + 512], ob)
    return p


def build_l3g3():
    p = Prog("l3g3")
    y_h = p.dram_in("y", (4, 128, NT), BF16)
    wonz_h = p.dram_in("wonz", (512, D))
    xnfm_h = p.dram_in("xnfm", (8, 128, NT), BF16)
    x_h = p.dram_in("x", (NT, D))
    wln2_h = p.dram_in("wln2", (1, D))
    wgate_h = p.dram_in("wgate", (D, 8))
    ebias_h = p.dram_in("ebias", (1, 8))
    wgm_h = p.dram_in("wgm", (9, D, 512))
    wum_h = p.dram_in("wum", (9, D, 512))
    wdm_h = p.dram_in("wdm", (9, 512, D))
    s_h = p.dram_in("s", (NT, D), BF16)
    ident_h = p.dram_in("ident", (128, 128), BF16)
    o_x = p.dram_out("xout", (NT, D))

    idsb = p.sbuf([128, 128], BF16, "idsb")
    p.dma(idsb, ident_h)
    y = p.sbuf([128, 4 * NT], BF16, "y")
    yv = y.rearrange("p (c t) -> p c t", c=4)
    p.dma(yv, y_h.rearrange("c p t -> p c t"))
    wonz = p.wload(wonz_h, 512, D, "wonz")
    xnfm = p.sbuf([128, 8 * NT], BF16, "xnfm")
    p.dma(xnfm.rearrange("p (c t) -> p c t", c=8),
          xnfm_h.rearrange("c p t -> p c t"))
    xnv = xnfm.rearrange("p (c t) -> p c t", c=8)
    wgate = p.wload(wgate_h, D, 8, "wgate")
    ones1 = p.sbuf([1, 128], BF16, "ones1")
    p.memset(ones1, 1.0)
    wln2 = p.sbuf([1, D], BF16, "wln2")
    p.dma(wln2, wln2_h, eng="gpsimd")
    ebias = p.sbuf([1, 8], F32, "ebias")
    p.dma(ebias, ebias_h)
    ebb = p.sbuf([128, 8], F32, "ebb")
    ebhib = p.sbuf([1, 8], BF16, "ebhib")
    eblo = p.sbuf([1, 8], F32, "eblo")
    eblob = p.sbuf([1, 8], BF16, "eblob")
    p.cp(ebhib, ebias)
    p.tt(eblo, ebias, ebhib, ALU.subtract)
    p.cp(eblob, eblo)
    peb = p.bankf(0)[:, 0:8]
    p.mm(peb, ones1, ebhib, True, False)
    p.mm(peb, ones1, eblob, False, True)
    p.cp(ebb, peb)
    wlnb = p.sbuf([128, D], F32, "wlnb")
    for fo in range(2):
        ps = p.bankf(1)
        p.mm(ps, ones1, wln2[:, fo * 512:(fo + 1) * 512], True, True)
        p.cp(wlnb[:, fo * 512:(fo + 1) * 512], ps, eng="vector")

    x1i = p.sbuf([128, TT * D], BF16, "x1i")
    x1v = x1i.rearrange("p (t d) -> p t d", t=TT)
    xn2 = p.sbuf([128, 8 * NT], BF16, "xn2")
    x2v = xn2.rearrange("p (c t) -> p c t", c=8)
    xt = p.sbuf([128, D], F32, "xt")
    xsq = p.sbuf([128, D], F32, "xsq")
    st8 = p.sbuf([128, 8], F32, "st8")
    xnbf = p.sbuf([128, D], BF16, "xnbf")
    x3 = x_h.rearrange("(t p) d -> t p d", p=128)
    for t in range(TT):
        tok = slice(t * 128, (t + 1) * 128)
        p.dma(xt, x3[t])
        nm, vr, sd, rstd, vr2 = (st8[:, i:i + 1] for i in range(5))
        p.red(nm, xt, ALU.add)
        p.ts(nm, nm, -1.0 / D, None, op0=ALU.mult)
        p.ts(xt, xt, nm, None, op0=ALU.add)
        p.act(xsq[:, 0:512], xt[:, 0:512], AF.Square, accum_out=vr)
        p.act(xsq[:, 512:1024], xt[:, 512:1024], AF.Square, accum_out=vr2)
        p.tt(vr, vr, vr2, ALU.add)
        p.act(sd, vr, AF.Sqrt, bias=p.constf(1e-5), scale=1.0 / D)
        p.recip(rstd, sd)
        p.ts(xt, xt, rstd, None, op0=ALU.mult)
        for fo in range(2):
            ps = p.bankf(2 + fo)
            fcols = slice(fo * 512, (fo + 1) * 512)
            p.mm_acc(ps, [(yv[:, ki, tok], wonz[:, ki, fcols])
                          for ki in range(4)])
            p.tt(x1v[:, t, fcols], ps, xt[:, fcols], ALU.add)
        x1t = x1v[:, t, :]
        p.act(xsq[:, 0:512], x1t[:, 0:512], AF.Square, accum_out=vr)
        p.act(xsq[:, 512:1024], x1t[:, 512:1024], AF.Square, accum_out=vr2)
        p.tt(vr, vr, vr2, ALU.add)
        p.act(sd, vr, AF.Sqrt, bias=p.constf(1e-5), scale=1.0 / D)
        p.recip(rstd, sd)
        p.ts(xsq, x1t, rstd, None, op0=ALU.mult)
        p.tt(xnbf, xsq, wlnb, ALU.mult)
        for ch in range(8):
            pb = p.bankb(ch % 2)[:, 0:128]
            p.tp(pb, xnbf[:, ch * 128:(ch + 1) * 128], idsb)
            p.cp(x2v[:, ch, tok], pb, eng="vector")

    cwall = p.sbuf([128, TT * 8], F32, "cwall")
    iot8m = p.sbuf([128, 8], F32, "iot8m")
    p.iota(iot8m, [[1, 8]], base=-8, channel_multiplier=0)
    gp = p.sbuf([128, 8], F32, "gp")
    gp2 = p.sbuf([128, 8], F32, "gp2")
    cand = p.sbuf([128, 8], F32, "cand")
    mio = p.sbuf([128, 8], F32, "mio")
    oh1 = p.sbuf([128, 8], F32, "oh1")
    oh2 = p.sbuf([128, 8], F32, "oh2")
    v1 = p.sbuf([128, 1], F32, "v1")
    v2 = p.sbuf([128, 1], F32, "v2")
    i1 = p.sbuf([128, 1], F32, "i1")
    i2 = p.sbuf([128, 1], F32, "i2")
    r1 = p.sbuf([128, 1], F32, "r1")
    r2 = p.sbuf([128, 1], F32, "r2")
    vs = p.sbuf([128, 1], F32, "vs")
    BIG = 1.0e9
    for t in range(TT):
        tok = slice(t * 128, (t + 1) * 128)
        ps2 = p.bankf(0)[:, 0:8]
        p.mm_acc(ps2, [(x2v[:, ki, tok], wgate[:, ki, :]) for ki in range(8)])
        p.tt(gp, ps2, ebb, ALU.add)
        p.red(r1, gp, ALU.max)
        p.ts(r1, r1, -1.0, None, op0=ALU.mult)
        p.act(gp, gp, AF.Exp, bias=r1, accum_out=r2)
        p.recip(r2, r2)
        p.ts(gp, gp, r2, None, op0=ALU.mult)
        p.red(v1, gp, ALU.max)
        p.ts(cand, gp, v1, None, op0=ALU.is_ge)
        p.tt(mio, cand, iot8m, ALU.mult)
        p.ts(mio, mio, 8.0, None, op0=ALU.add)
        p.red(i1, mio, ALU.min)
        p.ts(i1, i1, -8.0, None, op0=ALU.add)
        p.ts(oh1, iot8m, i1, None, op0=ALU.is_equal)
        p.stt(gp2, oh1, -BIG, gp, op0=ALU.mult, op1=ALU.add)
        p.red(v2, gp2, ALU.max)
        p.ts(cand, gp2, v2, None, op0=ALU.is_ge)
        p.tt(mio, cand, iot8m, ALU.mult)
        p.ts(mio, mio, 8.0, None, op0=ALU.add)
        p.red(i2, mio, ALU.min)
        p.ts(i2, i2, -8.0, None, op0=ALU.add)
        p.ts(oh2, iot8m, i2, None, op0=ALU.is_equal)
        p.tt(vs, v1, v2, ALU.add)
        p.recip(vs, vs)
        cw = cwall[:, t * 8:(t + 1) * 8]
        p.ts(cw, oh1, v1, None, op0=ALU.mult)
        p.stt(cw, oh2, v2, cw, op0=ALU.mult, op1=ALU.add)
        p.ts(cw, cw, vs, None, op0=ALU.mult)

    acc = p.sbuf([128, TT * D], BF16, "acc")
    accv = acc.rearrange("p (t d) -> p t d", t=TT)
    wgm = p.sbuf([128, 8 * 512], BF16, "wgm")
    wgv = wgm.rearrange("p (k n) -> p k n", k=8)
    wum = p.sbuf([128, 8 * 512], BF16, "wum")
    wuv = wum.rearrange("p (k n) -> p k n", k=8)
    wdm = p.sbuf([128, 4 * D], BF16, "wdm")
    wdv = wdm.rearrange("p (k n) -> p k n", k=4)
    geb = p.sbuf([128, 4 * BLK], BF16, "geb")
    gev = geb.rearrange("p (k n) -> p k n", k=4)
    silb = p.sbuf([128, BLK], F32, "silb")
    for e in range(9):
        p.dma(wgv, wgm_h[e].rearrange("(k p) n -> p k n", p=128), eng="gpsimd")
        p.dma(wuv, wum_h[e].rearrange("(k p) n -> p k n", p=128), eng="gpsimd")
        p.dma(wdv, wdm_h[e].rearrange("(k p) n -> p k n", p=128), eng="gpsimd")
        for nb in range(NBLK):
            tok = slice(nb * BLK, (nb + 1) * BLK)
            for po in range(4):
                psg = p.bankf(po % 2)
                p.mm_acc(psg, [(wgv[:, ki, po * 128:(po + 1) * 128],
                                x2v[:, ki, tok]) for ki in range(8)])
                p.act(silb, psg, AF.Silu)
                psu = p.bankf(2 + po % 2)
                p.mm_acc(psu, [(wuv[:, ki, po * 128:(po + 1) * 128],
                                x2v[:, ki, tok]) for ki in range(8)])
                p.tt(gev[:, po, :], silb, psu, ALU.mult)
            for j in range(4):
                tg = nb * 4 + j
                for fo in range(2):
                    ps = p.bankf(4 + fo)
                    fcols = slice(fo * BLK, (fo + 1) * BLK)
                    p.mm_acc(ps, [(gev[:, ki, j * 128:(j + 1) * 128],
                                   wdv[:, ki, fcols]) for ki in range(4)])
                    dst = accv[:, tg, fcols]
                    if e == 0:
                        p.ts(dst, ps, cwall[:, tg * 8:tg * 8 + 1], None,
                             op0=ALU.mult)
                    elif e < 8:
                        p.stt(dst, ps, cwall[:, tg * 8 + e:tg * 8 + e + 1],
                              dst, op0=ALU.mult, op1=ALU.add)
                    else:
                        p.stt(dst, ps, 1.0, dst, op0=ALU.mult, op1=ALU.add)

    stl = p.sbuf([128, D], BF16, "stl")
    gtl = xsq
    sS = s_h.rearrange("(t p) d -> t p d", p=128)
    o3 = o_x.rearrange("(t p) d -> t p d", p=128)
    for t in range(TT):
        p.dma(xt, x3[t])
        p.dma(stl, sS[t], eng="gpsimd")
        p.tt(gtl, accv[:, t, :], x1v[:, t, :], ALU.add)
        p.act(gtl, gtl, AF.Silu)
        p.tt(gtl, gtl, stl, ALU.mult)
        p.tt(gtl, gtl, xt, ALU.add)
        p.dma(o3[t], gtl)
    return p


def run_l3(inp, x_full):
    ident, slt = _consts()
    identf = np.eye(128, dtype=np.float32)
    xs = [np.ascontiguousarray(x_full[c * NT:(c + 1) * NT]) for c in range(NCORES)]
    im = [dict(x=xs[c], ipw=inp["L3_attn_in_proj_w"],
               xnw=np.ascontiguousarray(inp["L3_attn_x_norm_w"][None, :]),
               xnb=np.ascontiguousarray(inp["L3_attn_x_norm_b"][None, :]),
               ident=ident) for c in range(NCORES)]
    rf1 = _run("l3f1", build_l3f1, im)

    xbc_full = np.concatenate([r["xbc"] for r in rf1], axis=2).reshape(17 * 128, NTOK)
    z_full = np.concatenate([r["z"] for r in rf1], axis=2).reshape(DI, NTOK)
    dtr_full = np.concatenate([r["dtr"] for r in rf1], axis=1)
    cw_all = inp["L3_attn_conv_w"]
    cb_all = inp["L3_attn_conv_b"]
    maskt = np.full((2, 128, 256), -1.0e30, dtype=np.float32)
    for pp in range(128):
        maskt[0, pp, pp:] = 0.0
        maskt[1, pp, 128 + pp:] = 0.0
    im = []
    for c in range(NCORES):
        b = c // 4
        hs = [4 * (c % 4) + kk for kk in range(4)]
        colb = slice(b * L, (b + 1) * L)
        xh = np.stack([xbc_full[128 * h:128 * (h + 1), colb] for h in hs])
        bc = xbc_full[2048:2176, colb]
        cwx = np.stack([cw_all[128 * h:128 * (h + 1)] for h in hs])
        cbx = np.stack([cb_all[128 * h:128 * (h + 1), None] for h in hs])
        zz = np.stack([z_full[128 * h:128 * (h + 1), colb] for h in hs])
        dtr = np.stack([dtr_full[h, colb] for h in hs])
        dtb = np.stack([np.full((32, 1), inp["L3_attn_dt_bias"][h],
                                np.float32) for h in hs])
        alg = np.stack([np.full((32, 1), inp["L3_attn_A_log"][h], np.float32)
                        for h in hs])
        drep = np.stack([np.full((128, 1), inp["L3_attn_D"][h], np.float32)
                         for h in hs])
        im.append(dict(xh=np.ascontiguousarray(xh),
                       bc=np.ascontiguousarray(bc), cwx=cwx, cbx=cbx,
                       cwbc=np.ascontiguousarray(cw_all[2048:2176]),
                       cbbc=np.ascontiguousarray(cb_all[2048:2176, None]),
                       dtr=np.ascontiguousarray(dtr), dtb=dtb, alg=alg,
                       drep=drep, z=np.ascontiguousarray(zz), maskt=maskt,
                       ident=ident, identf=identf))
    rf2 = _run("l3f2", build_l3f2, im)

    g_full = np.zeros((DI, NTOK), dtype=BF)
    for c in range(NCORES):
        b = c // 4
        for kk in range(4):
            h = 4 * (c % 4) + kk
            g_full[128 * h:128 * (h + 1), b * L:(b + 1) * L] = rf2[c]["g"][kk]
    rmsw = np.ascontiguousarray(inp["L3_attn_rms_w"].reshape(16, 128).T)
    im = []
    for c in range(NCORES):
        cols = slice(c * NT, (c + 1) * NT)
        im.append(dict(g=np.ascontiguousarray(
                           g_full[:, cols].reshape(16, 128, NT)),
                       rmsw=rmsw, opw=inp["L3_attn_out_proj_w"], ident=ident))
    rf3 = _run("l3f3", build_l3f3, im)

    THETA = 1.0e5
    freqs = (1.0 / THETA ** (np.arange(0, 32, 2, dtype=np.float32) / 32.0))
    im = []
    for c in range(NCORES):
        pos = (np.arange(NT, dtype=np.float32) + (c % 4) * NT)
        ang = np.float32(pos[:, None]) * np.float32(freqs[None, :])
        cs = np.concatenate([np.cos(ang), np.sin(ang)], axis=1).astype(np.float32)
        im.append(dict(xnfm=rf1[c]["xnfm"],
                       wln1=np.ascontiguousarray(
                           inp["L3_ffn_w_ln1"].reshape(8, 128).T),
                       wq=inp["L3_ffn_Wq"], wkva=inp["L3_ffn_Wkva"],
                       wkvb=inp["L3_ffn_Wkvb"], cs=cs, ident=ident))
    rg1 = _run("l3g1", build_l3g1, im)

    q_full = np.concatenate([r["q"] for r in rg1], axis=2).reshape(D, NTOK)
    kv_full = np.concatenate([r["kv"] for r in rg1], axis=2).reshape(512, NTOK)
    kr_full = np.concatenate([r["kr"] for r in rg1], axis=1)
    masks4 = np.zeros((4, 128, 512), dtype=np.float32)
    for m4 in range(4):
        for pp in range(128):
            masks4[m4, pp, m4 * 128 + pp + 1:] = -1.0e30
    maskkm = np.zeros((128, 128), dtype=np.float32)
    for pp in range(128):
        maskkm[pp, :pp] = -1.0e30
    identf = np.eye(128, dtype=np.float32)
    im = []
    for c in range(NCORES):
        b = c // 4
        colb = slice(b * L, (b + 1) * L)
        qs, ks, vsl = [], [], []
        for kk in range(4):
            h = 4 * (c % 4) + kk
            qs.append(q_full[64 * h:64 * (h + 1), colb])
            ks.append(np.concatenate([kv_full[32 * h:32 * (h + 1), colb],
                                      kr_full[:, colb]], axis=0))
            vsl.append(kv_full[32 * h:32 * (h + 1), colb])
        im.append(dict(q=np.ascontiguousarray(np.stack(qs)),
                       k=np.ascontiguousarray(np.stack(ks)),
                       v=np.ascontiguousarray(np.stack(vsl)),
                       masks=masks4, maskkm=maskkm, ident=ident,
                       identf=identf))
    rg2 = _run("l3g2", build_l3g2, im)

    yout_full = np.zeros((512, NTOK), dtype=BF)
    for c in range(NCORES):
        b = c // 4
        for kk in range(4):
            h = 4 * (c % 4) + kk
            yout_full[32 * h:32 * (h + 1), b * L:(b + 1) * L] = rg2[c]["out"][kk]
    wonz = np.ascontiguousarray(
        inp["L3_ffn_Wo"].reshape(16, 64, D)[:, :32].reshape(512, D))
    wgm = np.concatenate([inp["L3_ffn_Wg"], inp["L3_ffn_sWg"]], axis=0)
    wum = np.concatenate([inp["L3_ffn_Wu"], inp["L3_ffn_sWu"]], axis=0)
    wdm = np.concatenate([inp["L3_ffn_Wd"], inp["L3_ffn_sWd"]], axis=0)
    im = []
    for c in range(NCORES):
        cols = slice(c * NT, (c + 1) * NT)
        im.append(dict(
            y=np.ascontiguousarray(yout_full[:, cols].reshape(4, 128, NT)),
            wonz=wonz, xnfm=rf1[c]["xnfm"], x=xs[c],
            wln2=np.ascontiguousarray(inp["L3_ffn_w_ln2"][None, :]),
            wgate=inp["L3_ffn_Wgate"],
            ebias=np.ascontiguousarray(inp["L3_ffn_expert_bias"][None, :]),
            wgm=wgm, wum=wum, wdm=wdm, s=rf3[c]["s"], ident=ident))
    rg3 = _run("l3g3", build_l3g3, im)
    x4 = np.concatenate([r["xout"] for r in rg3], axis=0)
    return x4, rf3




def _consts():
    ident = np.eye(128, dtype=BF)
    slt = (np.arange(128)[:, None] < np.arange(128)[None, :]).astype(BF)
    return ident, slt


def run_l0(inp, x_full):
    ident, slt = _consts()
    med = np.ascontiguousarray(inp["L0_attn_mediator"][0])
    xs = [np.ascontiguousarray(x_full[c * NT:(c + 1) * NT]) for c in range(NCORES)]

    im_a = [dict(x=xs[c], wq=inp["L0_attn_Wq"], wkv=inp["L0_attn_Wkv"],
                 med=med, wg=inp["L0_ffn_Wg"], ident=ident)
            for c in range(NCORES)]
    ra = _run("l0a", build_l0a, im_a)

    mx_all = np.stack([r["mx"] for r in ra])
    se_all = np.stack([r["se"] for r in ra])
    accv_all = np.stack([r["accv"] for r in ra]).reshape(8, 64, 16, 64)
    S_full = np.concatenate([r["S"] for r in ra], axis=1)

    im_b = []
    for c in range(NCORES):
        b = c // 4
        cs = slice(4 * b, 4 * b + 4)
        im_b.append(dict(
            qfm=ra[c]["qfm"], med=med, wo=inp["L0_attn_Wo"],
            bo=np.ascontiguousarray(inp["L0_attn_bo"][None, :]),
            mx=np.ascontiguousarray(mx_all[cs].transpose(2, 1, 0)),
            se=np.ascontiguousarray(se_all[cs].transpose(2, 1, 0)),
            accv=np.ascontiguousarray(accv_all[cs].transpose(2, 0, 1, 3)),
            ident=ident))
    rb = _run("l0b", build_l0b, im_b)

    S_pm = np.ascontiguousarray(S_full.reshape(8, 128, 128))
    im_c = []
    for c in range(NCORES):
        colsel = np.zeros((128, 16), dtype=np.float32)
        for t in range(16):
            colsel[c * 16 + t, t] = 1.0
        b1t = np.ascontiguousarray(
            inp["L0_ffn_b1"].reshape(8, 32, 128).transpose(0, 2, 1))
        im_c.append(dict(
            xnfm=ra[c]["xnfm"], S=S_pm, colsel=colsel, s=rb[c]["s"], x=xs[c],
            w1=inp["L0_ffn_W1"], b1=b1t, w2=inp["L0_ffn_W2"],
            b2=inp["L0_ffn_b2"], slt=slt))
    rc = _run("l0c", build_l0c, im_c)
    return np.concatenate([r["xout"] for r in rc], axis=0), ra, rb


def kernel(**inputs):
    inp = {k: np.ascontiguousarray(np.asarray(v, dtype=np.float32))
           for k, v in inputs.items()}
    x = np.ascontiguousarray(inp["L0_attn_x"].reshape(NTOK, D))
    x1, _, _ = run_l0(inp, x)
    x2, _ = run_l1(inp, x1)
    x3, _ = run_l2(inp, x2)
    x4, _ = run_l3(inp, x3)
    return x4.reshape(B, L, D)
